# version 02
# baseline (speedup 1.0000x reference)
import numpy as np
import ml_dtypes
import concourse.bass as bass
import concourse.mybir as mybir
from concourse.bass_utils import run_bass_kernel_spmd

F32 = mybir.dt.float32
BF16 = mybir.dt.bfloat16
AF = mybir.ActivationFunctionType
ALU = mybir.AluOpType
AX = mybir.AxisListType

ENGINES = ("sync", "scalar", "vector", "tensor", "gpsimd")

NCORES = 8
B = 2
SEQ = 4096
DM = 2048
NDM = DM // 128
TOK = 1024
BIG_NEG = -60000.0


class _Sem:
    __slots__ = ("h", "n")

    def __init__(self, h):
        self.h = h
        self.n = 0


SELF_SYNC = ("vector", "scalar", "gpsimd")


class Prog:
    def __init__(self, nc):
        self.nc = nc
        self.ops = {e: [] for e in ENGINES}
        self._done = {e: {} for e in ENGINES}
        self._esem = {}
        self._dmaflow = {}

    def sem(self, name):
        return _Sem(self.nc.alloc_semaphore(name))

    def engine_sem(self, eng):
        if eng not in self._esem:
            self._esem[eng] = self.sem(f"es_{eng}")
        return self._esem[eng]

    def add(self, eng, fn, waits=(), inc=None, inc_n=1, is_dma=False):
        wl = []
        for w in waits:
            if isinstance(w, tuple):
                s, v = w
            else:
                s, v = w, w.n
            if v > 0:
                wl.append((s, v))
        if not is_dma and eng in SELF_SYNC:
            es = self.engine_sem(eng)
            if es.n > 0:
                wl.append((es, es.n))
            if inc is None:
                inc, inc_n = es, 1
            else:
                assert inc is es, f"compute op on {eng} must inc its engine sem"
        if inc is not None:
            inc.n += inc_n
        self.ops[eng].append((wl, fn, inc, inc_n))

    def wait(self, eng, *waits):
        wl = [(w[0], w[1]) if isinstance(w, tuple) else (w, w.n) for w in waits]
        self.ops[eng].append(([w for w in wl if w[1] > 0], lambda e: None, None, 0))

    def dma(self, eng, out, in_, waits=(), inc=None):
        self.add(eng, lambda e, o=out, i=in_: e.dma_start(out=o, in_=i), list(waits),
                 inc, 16, is_dma=True)

    def build(self):
        with self.nc.Block() as block:
            for eng in ENGINES:
                if not self.ops[eng]:
                    continue

                def run(engine, _ops=self.ops[eng], _eng=eng):
                    done = self._done[_eng]
                    for wl, fn, inc, inc_n in _ops:
                        for s, v in wl:
                            if done.get(id(s), 0) < v:
                                engine.wait_ge(s.h, v)
                                done[id(s)] = v
                        instr = fn(engine)
                        if inc is not None:
                            assert instr is not None, "fn must return the last instruction"
                            instr.then_inc(inc.h, inc_n)

                getattr(block, eng)(run)


def _bf(x):
    return np.asarray(x, dtype=ml_dtypes.bfloat16)


SCALARS = {"eps": 0, "one": 1, "lneps": 2, "ln05": 3}
SCALAR_VALS = [1e-5, 1.0, 1e-5, float(np.log(0.5))]


def host_consts():
    sc = np.tile(np.array(SCALAR_VALS, np.float32)[None, :], (128, 1))
    return {
        "ones128_bf": _bf(np.ones((128, 1))),
        "ones1x128_f": np.ones((1, 128), np.float32),
        "ident_bf": _bf(np.eye(128)),
        "scal": np.ascontiguousarray(sc),
    }



class LNKit:
    def __init__(self, nc, pr, mm, one_bf, one_f, eps_ap, ntok=TOK, ctx=None):
        if ctx is None:
            a = nc.alloc_sbuf_tensor
        else:
            def a(n, sh, dt):
                return ctx.enter_context(nc.sbuf_tensor(n, sh, dt))
        self.pr, self.mm = pr, mm
        self.one_bf, self.one_f, self.eps_ap = one_bf, one_f, eps_ap
        self.ntok = ntok
        uid = len(getattr(nc, "_lnkit_ids", [])) if hasattr(nc, "_lnkit_ids") else 0
        if not hasattr(nc, "_lnkit_ids"):
            nc._lnkit_ids = []
        nc._lnkit_ids.append(uid)
        self.stat_m = a(f"ln_m{uid}", [1, ntok], F32)
        self.stat_v = a(f"ln_v{uid}", [1, ntok], F32)
        self.rstd = a(f"ln_r{uid}", [1, ntok], F32)
        self.mrstd = a(f"ln_mr{uid}", [1, ntok], F32)
        self.ybf = a(f"ln_yb{uid}", [128, ntok], BF16)
        self.ysq = a(f"ln_ys{uid}", [128, ntok], BF16)

    def emit(self, buf, ps, resid_ready, cast_to=None, spill=None, spill_sem=None):
        pr, mm = self.pr, self.mm
        ve = pr.engine_sem("vector")
        se = pr.engine_sem("scalar")
        NT = self.ntok
        stat_mms = []
        for c in range(NDM):
            pr.add("vector", lambda e, c=c: e.tensor_copy(
                self.ybf[:], buf[:, c * NT:(c + 1) * NT]),
                   waits=[resid_ready] + ([(mm, stat_mms[-1])] if c else []), inc=ve)
            pr.add("scalar", lambda e: e.activation(self.ysq[:], self.ybf[:], AF.Square),
                   waits=[(ve, ve.n)], inc=se)

            def statfn(e, c=c):
                last = None
                for j in range(NT // 512):
                    last = e.matmul(ps[0:1, 512 * j:512 * (j + 1)], self.one_bf[:],
                                    self.ybf[:, 512 * j:512 * (j + 1)],
                                    start=(c == 0), stop=(c == NDM - 1))
                for j in range(NT // 512):
                    last = e.matmul(ps[0:1, NT + 512 * j:NT + 512 * (j + 1)], self.one_bf[:],
                                    self.ysq[:, 512 * j:512 * (j + 1)],
                                    start=(c == 0), stop=(c == NDM - 1))
                return last
            pr.add("tensor", statfn, waits=[(se, se.n), (ve, ve.n)], inc=mm)
            stat_mms.append(mm.n)
        pr.add("vector", lambda e: e.tensor_scalar_mul(self.stat_m[:], ps[0:1, 0:NT],
                                                       1.0 / DM),
               waits=[(mm, mm.n)], inc=ve)
        pr.add("vector", lambda e: e.tensor_scalar_mul(self.stat_v[:], ps[0:1, NT:2 * NT],
                                                       1.0 / DM), waits=[], inc=ve)
        pr.add("vector", lambda e: e.tensor_mul(self.mrstd[:], self.stat_m[:], self.stat_m[:]),
               waits=[], inc=ve)
        pr.add("vector", lambda e: e.tensor_sub(self.stat_v[:], self.stat_v[:], self.mrstd[:]),
               waits=[], inc=ve)
        pr.add("scalar", lambda e: e.activation(self.rstd[:], self.stat_v[:], AF.Sqrt,
                                                bias=self.eps_ap),
               waits=[(ve, ve.n)], inc=se)
        pr.add("vector", lambda e: e.reciprocal(self.rstd[:], self.rstd[:]),
               waits=[(se, se.n)], inc=ve)
        pr.add("vector", lambda e: e.tensor_mul(self.mrstd[:], self.stat_m[:], self.rstd[:]),
               waits=[], inc=ve)

        def bcfn(e):
            last = None
            for j in range(NT // 512):
                last = e.matmul(ps[:, 512 * j:512 * (j + 1)], self.one_f[:],
                                self.rstd[0:1, 512 * j:512 * (j + 1)], start=True, stop=True)
            for j in range(NT // 512):
                last = e.matmul(ps[:, NT + 512 * j:NT + 512 * (j + 1)], self.one_f[:],
                                self.mrstd[0:1, 512 * j:512 * (j + 1)],
                                start=True, stop=True)
            return last
        pr.add("tensor", bcfn, waits=[(ve, ve.n)], inc=mm)
        mm_bc = mm.n
        for c in range(NDM):
            sl = np.s_[:, c * NT:(c + 1) * NT]
            pr.add("vector", lambda e, sl=sl: e.tensor_mul(buf[sl], buf[sl], ps[:, 0:NT]),
                   waits=[(mm, mm_bc)], inc=ve)
            pr.add("vector", lambda e, sl=sl: e.tensor_sub(buf[sl], buf[sl],
                                                           ps[:, NT:2 * NT]),
                   waits=[], inc=ve)
            if cast_to is not None:
                pr.add("vector", lambda e, sl=sl: e.tensor_copy(cast_to[sl], buf[sl]),
                       waits=[], inc=ve)
            if spill is not None:
                pr.dma("sync", spill.rearrange("(c p) t -> c p t", p=128)[c], buf[sl],
                       waits=[(ve, ve.n)], inc=spill_sem)


def build_mlp1():
    nc = bass.Bass("TRN2", target_bir_lowering=False, debug=False)
    x3T = nc.dram_tensor("x3T", [DM, TOK], F32, kind="ExternalInput").ap()
    wT = nc.dram_tensor("wT", [DM, DM], F32, kind="ExternalInput").ap()
    biasc = nc.dram_tensor("biasc", [128, NDM], F32, kind="ExternalInput").ap()
    ones128_bf = nc.dram_tensor("ones128_bf", [128, 1], BF16, kind="ExternalInput").ap()
    ones1x128_f = nc.dram_tensor("ones1x128_f", [1, 128], F32, kind="ExternalInput").ap()
    scal = nc.dram_tensor("scal", [128, len(SCALAR_VALS)], F32, kind="ExternalInput").ap()
    outT = nc.dram_tensor("outT", [DM, TOK], F32, kind="ExternalOutput").ap()

    a = nc.alloc_sbuf_tensor
    xs = a("xs", [128, NDM * TOK], F32)
    xb = a("xb", [128, NDM * TOK], BF16)
    wb = a("wb", [128, NDM * DM], BF16)
    sgn = a("sgn", [128, DM], BF16)
    absw = a("absw", [128, DM], BF16)
    bias_sb = a("bias_sb", [128, NDM], F32)
    one_bf = a("one_bf", [128, 1], BF16)
    one_f = a("one_f", [1, 128], F32)
    sc_sb = a("sc_sb", [128, len(SCALAR_VALS)], F32)
    al_rowf = a("al_rowf", [1, DM], F32)
    stat_m = a("stat_m", [1, TOK], F32)
    stat_v = a("stat_v", [1, TOK], F32)
    rstd = a("rstd", [1, TOK], F32)
    mrstd = a("mrstd", [1, TOK], F32)
    ybf = a("ybf", [128, TOK], BF16)
    ysq = a("ysq", [128, TOK], BF16)
    res2 = [a("res_a", [128, TOK], F32), a("res_b", [128, TOK], F32)]

    p = nc.alloc_psum_tensor
    ps_d = [p("ps_d0", [128, 512], F32), p("ps_d1", [128, 512], F32)]
    ps_big = p("ps_big", [128, 2048], F32)

    pr = Prog(nc)
    ld = pr.sem("ld")
    ldg = pr.sem("ldg")
    mm = pr.sem("mm")
    ve = pr.engine_sem("vector")
    se = pr.engine_sem("scalar")
    stq2 = [pr.sem("stq_a"), pr.sem("stq_b")]

    x3c = x3T.rearrange("(c p) t -> c p t", p=128)
    wc = wT.rearrange("(c p) o -> c p o", p=128)
    for c in range(NDM):
        pr.dma("sync", xs[:, c * TOK:(c + 1) * TOK], x3c[c], inc=ld)
    pr.dma("sync", bias_sb[:], biasc[:], inc=ld)
    pr.dma("sync", one_f[:], ones1x128_f[:], inc=ld)
    pr.dma("sync", sc_sb[:], scal[:], inc=ld)
    ld_all = ld.n
    pr.dma("gpsimd", one_bf[:], ones128_bf[:], inc=ldg)
    for c in range(NDM):
        pr.dma("gpsimd", xb[:, c * TOK:(c + 1) * TOK], x3c[c], inc=ldg)
    ldg_x = ldg.n
    for c in range(NDM):
        pr.dma("gpsimd", wb[:, c * DM:(c + 1) * DM], wc[c], inc=ldg)
    ldg_w = ldg.n

    alpha_mm = []
    for c in range(NDM):
        pr.add("scalar",
               lambda e, c=c: e.activation(absw[:], wb[:, c * DM:(c + 1) * DM], AF.Abs),
               waits=[(ldg, ldg_w)] + ([(mm, alpha_mm[-1])] if c else []), inc=se)
        pr.add("tensor", lambda e, c=c: [
            e.matmul(ps_big[0:1, 512 * j:512 * (j + 1)], one_bf[:],
                     absw[:, 512 * j:512 * (j + 1)],
                     start=(c == 0), stop=(c == NDM - 1)) for j in range(4)][-1],
               waits=[(se, se.n)], inc=mm)
        alpha_mm.append(mm.n)
    pr.add("vector", lambda e: e.tensor_scalar_mul(al_rowf[:], ps_big[0:1, :], 1.0 / DM),
           waits=[(mm, mm.n)], inc=ve)
    pr.add("tensor", lambda e: [
        e.matmul(ps_big[:, 512 * j:512 * (j + 1)], one_f[:],
                 al_rowf[0:1, 512 * j:512 * (j + 1)], start=True, stop=True)
        for j in range(4)][-1],
           waits=[(ve, ve.n)], inc=mm)
    mm_albc = mm.n

    wbin_ve = []
    for c in range(NDM):
        pr.add("scalar",
               lambda e, c=c: e.activation(sgn[:], wb[:, c * DM:(c + 1) * DM], AF.Sign),
               waits=[(ve, wbin_ve[-1])] if c else [], inc=se)
        pr.add("vector", lambda e, c=c: e.tensor_tensor(
            wb[:, c * DM:(c + 1) * DM], sgn[:], ps_big[:, :DM], ALU.mult),
               waits=[(se, se.n), (mm, mm_albc)], inc=ve)
        wbin_ve.append(ve.n)
    ve_wbin = ve.n

    resid_ve = []
    k = 0
    for oc in range(NDM):
        for tt in range(2):
            ps = ps_d[k % 2]

            def mmfn(e, oc=oc, tt=tt, ps=ps):
                last = None
                for ic in range(NDM):
                    last = e.matmul(ps[:], wb[:, ic * DM + 128 * oc: ic * DM + 128 * (oc + 1)],
                                    xb[:, ic * TOK + 512 * tt: ic * TOK + 512 * (tt + 1)],
                                    start=(ic == 0), stop=(ic == NDM - 1))
                return last
            waits = [(ve, ve_wbin), (ldg, ldg_w)]
            if k >= 2:
                waits.append((ve, resid_ve[k - 2]))
            pr.add("tensor", mmfn, waits=waits, inc=mm)
            sl = np.s_[:, oc * TOK + 512 * tt: oc * TOK + 512 * (tt + 1)]
            pr.add("vector", lambda e, sl=sl, oc=oc, ps=ps: e.scalar_tensor_tensor(
                xs[sl], xs[sl], bias_sb[:, oc:oc + 1], ps[:], ALU.add, ALU.add),
                   waits=[(mm, mm.n), (ld, ld_all)], inc=ve)
            resid_ve.append(ve.n)
            k += 1
    ve_resid = ve.n

    stat_mm = []
    for c in range(NDM):
        pr.add("vector", lambda e, c=c: e.tensor_copy(ybf[:], xs[:, c * TOK:(c + 1) * TOK]),
               waits=[(ve, ve_resid)] + ([(mm, stat_mm[-1])] if c else [(mm, mm_albc)]),
               inc=ve)
        pr.add("scalar", lambda e: e.activation(ysq[:], ybf[:], AF.Square),
               waits=[(ve, ve.n)], inc=se)

        def statfn(e, c=c):
            last = None
            for j in range(2):
                last = e.matmul(ps_big[0:1, 512 * j:512 * (j + 1)], one_bf[:],
                                ybf[:, 512 * j:512 * (j + 1)],
                                start=(c == 0), stop=(c == NDM - 1))
            for j in range(2):
                last = e.matmul(ps_big[0:1, TOK + 512 * j:TOK + 512 * (j + 1)], one_bf[:],
                                ysq[:, 512 * j:512 * (j + 1)],
                                start=(c == 0), stop=(c == NDM - 1))
            return last
        pr.add("tensor", statfn, waits=[(se, se.n), (ve, ve.n)], inc=mm)
        stat_mm.append(mm.n)

    pr.add("vector", lambda e: e.tensor_scalar_mul(stat_m[:], ps_big[0:1, 0:TOK], 1.0 / DM),
           waits=[(mm, mm.n)], inc=ve)
    pr.add("vector", lambda e: e.tensor_scalar_mul(stat_v[:], ps_big[0:1, TOK:2 * TOK], 1.0 / DM),
           waits=[], inc=ve)
    pr.add("vector", lambda e: e.tensor_mul(mrstd[:], stat_m[:], stat_m[:]), waits=[], inc=ve)
    pr.add("vector", lambda e: e.tensor_sub(stat_v[:], stat_v[:], mrstd[:]), waits=[], inc=ve)
    pr.add("scalar", lambda e: e.activation(rstd[:], stat_v[:], AF.Sqrt,
                                            bias=sc_sb[0:1, SCALARS["eps"]:SCALARS["eps"] + 1]),
           waits=[(ve, ve.n), (ld, ld_all)], inc=se)
    pr.add("vector", lambda e: e.reciprocal(rstd[:], rstd[:]), waits=[(se, se.n)], inc=ve)
    pr.add("vector", lambda e: e.tensor_mul(mrstd[:], stat_m[:], rstd[:]), waits=[], inc=ve)
    ve_stats = ve.n

    def bcfn(e):
        last = None
        for j in range(2):
            last = e.matmul(ps_big[:, 512 * j:512 * (j + 1)], one_f[:],
                            rstd[0:1, 512 * j:512 * (j + 1)], start=True, stop=True)
        for j in range(2):
            last = e.matmul(ps_big[:, TOK + 512 * j:TOK + 512 * (j + 1)], one_f[:],
                            mrstd[0:1, 512 * j:512 * (j + 1)], start=True, stop=True)
        return last
    pr.add("tensor", bcfn, waits=[(ve, ve_stats)], inc=mm)
    mm_bc = mm.n

    outc = outT.rearrange("(c p) t -> c p t", p=128)
    for c in range(NDM):
        res = res2[c % 2]
        sq = stq2[c % 2]
        waits = [(mm, mm_bc)]
        if c >= 2:
            waits.append((sq, sq.n))
        pr.add("vector", lambda e, c=c, res=res: e.tensor_mul(
            res[:], xs[:, c * TOK:(c + 1) * TOK], ps_big[:, 0:TOK]), waits=waits, inc=ve)
        pr.add("vector", lambda e, res=res: e.tensor_sub(res[:], res[:], ps_big[:, TOK:2 * TOK]),
               waits=[], inc=ve)
        pr.dma("sync", outc[c], res[:], waits=[(ve, ve.n)], inc=sq)
    pr.wait("sync", (stq2[0], stq2[0].n), (stq2[1], stq2[1].n))
    pr.build()
    return nc







def _mlp1_in_maps(x3T_slices, wT, bias, consts):
    biasc = np.ascontiguousarray(bias.reshape(NDM, 128).T).astype(np.float32)
    return [dict(
        x3T=np.ascontiguousarray(x3T_slices[cid]),
        wT=wT, biasc=biasc,
        ones128_bf=consts["ones128_bf"],
        ones1x128_f=consts["ones1x128_f"],
        scal=consts["scal"],
    ) for cid in range(NCORES)]


def _slice_stream(xT_full):
    out = []
    for cid in range(NCORES):
        b, q = cid // 4, cid % 4
        out.append(np.ascontiguousarray(xT_full[b][:, q * TOK:(q + 1) * TOK]))
    return out



from contextlib import ExitStack
import ml_dtypes

SEQ = 4096
HQ = 4
DH = 128
NCK0 = SEQ // 128
GATE_NORM = 16.0
RSC = float(1.0 / np.sqrt(128.0))


def build_attn0():
    nc = bass.Bass("TRN2", target_bir_lowering=False, debug=False)

    def di(n, sh, dt=F32):
        return nc.dram_tensor(n, sh, dt, kind="ExternalInput").ap()
    x0T = di("x0T", [DM, SEQ])
    wq = di("wq", [DM, 512])
    wk = di("wk", [DM, 128])
    wv = di("wv", [DM, 128])
    cosT = di("cosT", [128, SEQ], BF16)
    sinT = di("sinT", [128, SEQ], BF16)
    rotm = di("rotm", [128, 128], BF16)
    maskc = di("maskc", [128, 4 * 512], BF16)
    triu = di("triu", [128, 128])
    ident = di("ident", [128, 128], BF16)
    ident_f32 = di("ident_f32", [128, 128], F32)
    ones_f = di("ones_f", [128, 128], F32)
    ones128_bf = di("ones128_bf", [128, 1], BF16)
    ones1x128_f = di("ones1x128_f", [1, 128], F32)
    scal = di("scal", [128, len(SCALAR_VALS)])
    oT_d = nc.dram_tensor("oT_d", [512, SEQ], BF16, kind="ExternalOutput").ap()

    a = nc.alloc_sbuf_tensor
    sc_sb = a("sc_sb", [128, len(SCALAR_VALS)], F32)
    idb = a("idb", [128, 128], BF16)
    idf = a("idf", [128, 128], F32)
    onesf = a("onesf", [128, 128], F32)
    one_bf = a("one_bf", [128, 1], BF16)
    one1f = a("one1f", [1, 128], F32)
    tri_sb = a("tri_sb", [128, 128], F32)
    maskc_sb = a("maskc_sb", [128, 4 * 512], BF16)
    cos_sb = a("cos_sb", [128, SEQ], BF16)
    sin_sb = a("sin_sb", [128, SEQ], BF16)
    rot_sb = a("rot_sb", [128, 128], BF16)
    qT = a("qT", [128, HQ * SEQ], BF16)
    kT = a("kT", [128, SEQ], BF16)
    v_sb = a("v_sb", [128, NCK0 * 129], BF16)
    spc = a("spc", [128, 128], F32)
    csc = a("csc", [128, 128], F32)
    tkc = a("tkc", [128, 128], F32)
    tqc = a("tqc", [128, 128], F32)
    negc16 = a("negc16", [128, 1], F32)
    ktl = a("ktl", [128, 128], BF16)
    kt2 = a("kt2", [128, 128], BF16)
    kt2t = a("kt2t", [128, 128], BF16)
    qtl = a("qtl", [128, 128], BF16)
    ebf = a("ebf", [128, 128], BF16)
    mgl = a("mgl", [128, 128], BF16)
    s_st = a("s_st", [128, 128], F32)
    s_bf = a("s_bf", [128, 128], BF16)
    rrow = a("rrow", [1, 128], F32)
    dcol = a("dcol", [128, 1], F32)
    cend = a("cend", [128, 1], F32)
    tmpb = a("tmpb", [128, 512], BF16)
    den_col = a("den_col", [128, 1], F32)
    y_sb = a("y_sb", [128, 128], F32)
    ocmb = a("ocmb", [128, 128], BF16)
    obuf = a("obuf", [128, 512], BF16)

    p = nc.alloc_psum_tensor
    ps_a = p("ps_a", [128, 512], F32)
    ps_b = p("ps_b", [128, 512], F32)
    ps_t0 = p("ps_t0", [128, 512], BF16)
    ps_t1 = p("ps_t1", [128, 512], BF16)
    ps_sc = p("ps_sc", [128, 2048], F32)

    pr = Prog(nc)
    ldc = pr.sem("ldc")
    ldx = pr.sem("ldx")
    mm = pr.sem("mm")
    ve = pr.engine_sem("vector")
    se = pr.engine_sem("scalar")
    so = pr.sem("so")

    pr.dma("sync", sc_sb[:], scal[:], inc=ldc)
    pr.dma("sync", idf[:], ident_f32[:], inc=ldc)
    pr.dma("sync", onesf[:], ones_f[:], inc=ldc)
    pr.dma("sync", one1f[:], ones1x128_f[:], inc=ldc)
    pr.dma("sync", tri_sb[:], triu[:], inc=ldc)
    pr.dma("sync", cos_sb[:], cosT[:], inc=ldc)
    pr.dma("sync", sin_sb[:], sinT[:], inc=ldc)
    pr.dma("sync", rot_sb[:], rotm[:], inc=ldc)
    ldc_all = ldc.n

    ctx0 = ExitStack()
    wq_sb = ctx0.enter_context(nc.sbuf_tensor("wq_sb", [128, NDM * 512], BF16))
    wk_sb = ctx0.enter_context(nc.sbuf_tensor("wk_sb", [128, NDM * 128], BF16))
    wv_sb = ctx0.enter_context(nc.sbuf_tensor("wv_sb", [128, NDM * 128], BF16))
    xtg = [ctx0.enter_context(nc.sbuf_tensor(f"xtg{s}", [128, NDM * 512], BF16))
           for s in range(2)]

    pr.dma("gpsimd", idb[:], ident[:], inc=ldx)
    pr.dma("gpsimd", one_bf[:], ones128_bf[:], inc=ldx)
    pr.dma("gpsimd", maskc_sb[:], maskc[:], inc=ldx)
    wqc = wq.rearrange("(c p) t -> c p t", p=128)
    wkc = wk.rearrange("(c p) t -> c p t", p=128)
    wvc = wv.rearrange("(c p) t -> c p t", p=128)
    for c in range(NDM):
        pr.dma("gpsimd", wq_sb[:, c * 512:(c + 1) * 512], wqc[c], inc=ldx)
        pr.dma("gpsimd", wk_sb[:, c * 128:(c + 1) * 128], wkc[c], inc=ldx)
        pr.dma("gpsimd", wv_sb[:, c * 128:(c + 1) * 128], wvc[c], inc=ldx)
    ldx_all = ldx.n
    x0c = x0T.rearrange("(c p) t -> c p t", p=128)
    lx = [pr.sem("lx0"), pr.sem("lx1")]

    pr.add("vector", lambda e: e.memset(v_sb[:], 1.0), waits=[], inc=ve)
    pr.add("vector", lambda e: e.memset(s_st[:], 0.0), waits=[], inc=ve)
    pr.add("vector", lambda e: e.memset(s_bf[:], 0.0), waits=[], inc=ve)

    mm_tg_done = []
    for tg in range(8):
        sl = tg % 2
        for c in range(NDM):
            w = [(mm, mm_tg_done[tg - 2])] if (tg >= 2 and c == 0) else []
            pr.dma("gpsimd", xtg[sl][:, c * 512:(c + 1) * 512],
                   x0c[c][:, 512 * tg:512 * (tg + 1)], waits=w, inc=lx[sl])
        lx_full = lx[sl].n
        xv = xtg[sl]
        pp = 0
        for hs in range(HQ):
            ps = ps_a if pp % 2 == 0 else ps_b
            pp += 1

            def qfn(e, hs=hs, xv=xv, ps=ps):
                last = None
                for ic in range(NDM):
                    last = e.matmul(ps[:],
                                    wq_sb[:, ic * 512 + 128 * hs: ic * 512 + 128 * (hs + 1)],
                                    xv[:, ic * 512:(ic + 1) * 512],
                                    start=(ic == 0), stop=(ic == NDM - 1))
                return last
            pr.add("tensor", qfn,
                   waits=[(ldx, ldx_all), (lx[sl], lx_full), (ve, ve.n)], inc=mm)
            pr.add("vector", lambda e, hs=hs, tg=tg, ps=ps: e.tensor_copy(
                qT[:, hs * SEQ + 512 * tg: hs * SEQ + 512 * (tg + 1)], ps[:]),
                   waits=[(mm, mm.n)], inc=ve)
        ps = ps_a if pp % 2 == 0 else ps_b
        pp += 1

        def kfn(e, xv=xv, ps=ps):
            last = None
            for ic in range(NDM):
                last = e.matmul(ps[:],
                                wk_sb[:, ic * 128:(ic + 1) * 128],
                                xv[:, ic * 512:(ic + 1) * 512],
                                start=(ic == 0), stop=(ic == NDM - 1))
            return last
        pr.add("tensor", kfn, waits=[(ve, ve.n)], inc=mm)
        pr.add("vector", lambda e, tg=tg, ps=ps: e.tensor_copy(
            kT[:, 512 * tg:512 * (tg + 1)], ps[:]), waits=[(mm, mm.n)], inc=ve)
        ps = ps_a if pp % 2 == 0 else ps_b

        def vfn(e, xv=xv, ps=ps):
            last = None
            for ic in range(NDM):
                last = e.matmul(ps[:],
                                wv_sb[:, ic * 128:(ic + 1) * 128],
                                xv[:, ic * 512:(ic + 1) * 512],
                                start=(ic == 0), stop=(ic == NDM - 1))
            return last
        pr.add("tensor", vfn, waits=[(ve, ve.n)], inc=mm)
        pr.add("vector", lambda e, ps=ps: e.tensor_copy(obuf[:], ps[:]),
               waits=[(mm, mm.n)], inc=ve)
        for j in range(4):
            ck = 4 * tg + j
            pst = ps_t0 if j % 2 == 0 else ps_t1
            pr.add("tensor", lambda e, j=j, pst=pst: e.transpose(
                pst[:, 0:128], obuf[:, 128 * j:128 * (j + 1)], idb[:]),
                   waits=[(ve, ve.n)], inc=mm)
            pr.add("vector", lambda e, ck=ck, pst=pst: e.tensor_copy(
                v_sb[:, ck * 129:ck * 129 + 128], pst[:, 0:128]),
                   waits=[(mm, mm.n)], inc=ve)
        mm_tg_done.append(mm.n)
    ve_proj = ve.n
    ctx0.close()

    o_gla = a("o_gla", [128, NCK0 * HQ * 128], BF16)
    rq_col = a("rq_col", [128, NCK0 * HQ], F32)
    p_exp = a("p_exp", [128, 32 * 512], BF16)

    for ck in range(NCK0):
        ksl = np.s_[:, 128 * ck:128 * (ck + 1)]
        pr.add("scalar", lambda e, ksl=ksl: e.activation(spc[:], kT[ksl], AF.Exp,
                                                         scale=-1.0),
               waits=[(ve, ve_proj), (ldc, ldc_all)], inc=se)
        pr.add("scalar", lambda e: e.activation(
            spc[:], spc[:], AF.Ln, bias=sc_sb[:, SCALARS["one"]:SCALARS["one"] + 1]),
               waits=[], inc=se)
        pr.add("vector", lambda e: e.tensor_tensor_scan(
            csc[:], onesf[:], spc[:], 0.0, ALU.mult, ALU.add),
               waits=[(se, se.n)], inc=ve)
        pr.add("vector", lambda e: e.tensor_copy(cend[:], csc[:, 127:128]), waits=[], inc=ve)
        pr.add("vector", lambda e: e.tensor_scalar_mul(negc16[:], cend[:],
                                                       -1.0 / GATE_NORM),
               waits=[], inc=ve)
        pr.add("scalar", lambda e: e.activation(dcol[:], cend[:], AF.Exp,
                                                scale=-1.0 / GATE_NORM),
               waits=[(ve, ve.n)], inc=se)
        pr.add("scalar", lambda e, ksl=ksl: e.activation(ebf[:], kT[ksl], AF.Exp),
               waits=[], inc=se)
        pr.add("tensor", lambda e: e.matmul(ps_sc[0:1, 0:128], one_bf[:], ebf[:],
                                            start=True, stop=True),
               waits=[(se, se.n), (ve, ve.n)], inc=mm)
        pr.add("vector", lambda e: e.reciprocal(rrow[:], ps_sc[0:1, 0:128]),
               waits=[(mm, mm.n)], inc=ve)
        pr.add("tensor", lambda e: e.matmul(ps_sc[:, 1024:1152], one1f[:], rrow[:],
                                            start=True, stop=True),
               waits=[(ve, ve.n)], inc=mm)
        mm_rk = mm.n
        pr.add("vector", lambda e, ksl=ksl: e.scalar_tensor_tensor(
            tkc[:], csc[:], 1.0 / GATE_NORM, kT[ksl], ALU.mult, ALU.add),
               waits=[], inc=ve)
        pr.add("scalar", lambda e: e.activation(ktl[:], tkc[:], AF.Exp),
               waits=[(ve, ve.n)], inc=se)
        pr.add("vector", lambda e: e.tensor_tensor(ktl[:], ktl[:], ps_sc[:, 1024:1152],
                                                   ALU.mult),
               waits=[(se, se.n), (mm, mm_rk)], inc=ve)
        pr.add("scalar", lambda e: e.activation(kt2[:], tkc[:], AF.Exp,
                                                bias=negc16[:, 0:1]),
               waits=[(ve, ve.n)], inc=se)
        pr.add("vector", lambda e: e.tensor_tensor(kt2[:], kt2[:], ps_sc[:, 1024:1152],
                                                   ALU.mult),
               waits=[(se, se.n)], inc=ve)
        pr.add("tensor", lambda e: e.transpose(ps_t0[:, 0:128], kt2[:], idb[:]),
               waits=[(ve, ve.n)], inc=mm)
        pr.add("vector", lambda e: e.tensor_copy(kt2t[:], ps_t0[:, 0:128]),
               waits=[(mm, mm.n)], inc=ve)
        for h in range(HQ):
            qsl = np.s_[:, h * SEQ + 128 * ck: h * SEQ + 128 * (ck + 1)]
            pr.add("scalar", lambda e, qsl=qsl: e.activation(ebf[:], qT[qsl], AF.Exp),
                   waits=[(ve, ve.n)], inc=se)
            pr.add("tensor", lambda e: e.matmul(ps_sc[0:1, 0:128], one_bf[:], ebf[:],
                                                start=True, stop=True),
                   waits=[(se, se.n)], inc=mm)
            pr.add("vector", lambda e: e.reciprocal(rrow[:], ps_sc[0:1, 0:128]),
                   waits=[(mm, mm.n)], inc=ve)
            pr.add("tensor", lambda e: e.transpose(ps_sc[:, 1024:1025], rrow[:],
                                                   idf[0:1, 0:1]),
                   waits=[(ve, ve.n)], inc=mm)
            pr.add("vector", lambda e, ck=ck, h=h: e.tensor_copy(
                rq_col[:, ck * HQ + h: ck * HQ + h + 1], ps_sc[:, 1024:1025]),
                   waits=[(mm, mm.n)], inc=ve)
            pr.add("vector", lambda e, qsl=qsl: e.scalar_tensor_tensor(
                tqc[:], csc[:], -1.0 / GATE_NORM, qT[qsl], ALU.mult, ALU.add),
                   waits=[], inc=ve)
            pr.add("scalar", lambda e: e.activation(
                qtl[:], tqc[:], AF.Exp,
                bias=sc_sb[:, SCALARS["ln05"]:SCALARS["ln05"] + 1]),
                   waits=[(ve, ve.n)], inc=se)
            pr.add("tensor", lambda e: e.matmul(ps_sc[:, 256:384], ktl[:], qtl[:],
                                                start=True, stop=True),
                   waits=[(se, se.n)], inc=mm)
            pr.add("vector", lambda e: e.tensor_tensor(mgl[:], ps_sc[:, 256:384],
                                                       tri_sb[:], ALU.mult),
                   waits=[(mm, mm.n)], inc=ve)

            def ofn(e, ck=ck):
                e.matmul(ps_sc[:, 512:640], mgl[:], v_sb[:, ck * 129:ck * 129 + 128],
                         start=True, stop=False)
                return e.matmul(ps_sc[:, 512:640], qtl[:], s_bf[:],
                                start=False, stop=True)
            pr.add("tensor", ofn, waits=[(ve, ve.n), (se, se.n)], inc=mm)
            pr.add("scalar", lambda e, ck=ck, h=h: e.activation(
                o_gla[:, (ck * HQ + h) * 128:(ck * HQ + h + 1) * 128],
                ps_sc[:, 512:640], AF.Copy), waits=[(mm, mm.n)], inc=se)
        def cfn(e, ck=ck):
            return e.matmul(ps_sc[:, 1536:1664], kt2t[:],
                            v_sb[:, ck * 129:ck * 129 + 128], start=True, stop=True)
        pr.add("tensor", cfn, waits=[(ve, ve.n), (se, se.n)], inc=mm)
        pr.add("vector", lambda e: e.scalar_tensor_tensor(
            s_st[:], s_st[:], dcol[:, 0:1], ps_sc[:, 1536:1664], ALU.mult, ALU.add),
               waits=[(mm, mm.n)], inc=ve)
        pr.add("vector", lambda e: e.tensor_copy(s_bf[:], s_st[:]), waits=[], inc=ve)
    ve_gla = ve.n
    se_gla = se.n

    chunks = [(qT, h * SEQ + 512 * J, 512 * J) for h in range(HQ) for J in range(8)]
    chunks += [(kT, 512 * J, 512 * J) for J in range(8)]
    tmp_ve = []
    for i, (buf, c0, t0) in enumerate(chunks):
        ps = ps_a if i % 2 == 0 else ps_b
        sl = np.s_[:, c0:c0 + 512]
        if i < 2:
            w = [(ve, ve_gla), (se, se_gla)]
        else:
            w = [(ve, tmp_ve[i - 2])]
        pr.add("tensor", lambda e, sl=sl, ps=ps, buf=buf: e.matmul(
            ps[:], rot_sb[:], buf[sl], start=True, stop=True), waits=w, inc=mm)
        pr.add("vector", lambda e, sl=sl, buf=buf, t0=t0: e.tensor_tensor(
            buf[sl], buf[sl], cos_sb[:, t0:t0 + 512], ALU.mult),
               waits=[(mm, mm.n)], inc=ve)
        pr.add("vector", lambda e, ps=ps, t0=t0: e.tensor_tensor(
            tmpb[:], sin_sb[:, t0:t0 + 512], ps[:], ALU.mult),
               waits=[], inc=ve)
        tmp_ve.append(ve.n)
        pr.add("vector", lambda e, sl=sl, buf=buf: e.tensor_tensor(
            buf[sl], buf[sl], tmpb[:], ALU.add), waits=[], inc=ve)
    ve_rope = ve.n

    last_av_mm = [mm.n]
    for h in range(HQ):
        for J in range(8):
            for kg in range(J + 1):
                def sfn(e, h=h, J=J, kg=kg):
                    last = None
                    for s4 in range(4):
                        ki = 4 * kg + s4
                        last = e.matmul(ps_sc[:, 512 * s4:512 * (s4 + 1)],
                                        kT[:, 128 * ki:128 * (ki + 1)],
                                        qT[:, h * SEQ + 512 * J: h * SEQ + 512 * (J + 1)],
                                        start=True, stop=(kg != J))
                    if kg == J:
                        for s4 in range(4):
                            last = e.matmul(ps_sc[:, 512 * s4:512 * (s4 + 1)], idb[:],
                                            maskc_sb[:, 512 * s4:512 * (s4 + 1)],
                                            start=False, stop=True)
                    return last
                w = [(ve, ve_rope), (se, se.n)]
                if kg == 0:
                    w.append((mm, last_av_mm[-1]))
                pr.add("tensor", sfn, waits=w, inc=mm)
                mm_s = mm.n
                for half in range(2):
                    pr.add("scalar", lambda e, kg=kg, half=half: e.activation(
                        p_exp[:, 2048 * kg + 1024 * half: 2048 * kg + 1024 * (half + 1)],
                        ps_sc[:, 1024 * half:1024 * (half + 1)], AF.Exp, scale=RSC),
                           waits=[(mm, mm_s)], inc=se)
            se_exp = se.n
            for s in range(4):
                nki = 4 * J + s + 1
                ps = ps_a if s % 2 == 0 else ps_b

                def avfn(e, s=s, nki=nki, ps=ps):
                    last = None
                    for ki in range(nki):
                        last = e.matmul(
                            ps[:, 0:129],
                            p_exp[:, 512 * ki + 128 * s: 512 * ki + 128 * (s + 1)],
                            v_sb[:, ki * 129:(ki + 1) * 129],
                            start=(ki == 0), stop=(ki == nki - 1))
                    return last
                pr.add("tensor", avfn, waits=[(se, se_exp), (ve, ve.n)], inc=mm)
                last_av_mm.append(mm.n)
                pr.add("vector", lambda e, ps=ps: e.reciprocal(den_col[:], ps[:, 128:129]),
                       waits=[(mm, mm.n)], inc=ve)
                pr.add("vector", lambda e: e.tensor_scalar_mul(den_col[:], den_col[:], 0.5),
                       waits=[], inc=ve)
                pr.add("scalar", lambda e, ps=ps: e.activation(
                    y_sb[:], ps[:, 0:128], AF.Copy, scale=den_col[:, 0:1]),
                       waits=[(ve, ve.n)], inc=se)
                ck_g = 4 * J + s
                pr.add("vector", lambda e, ck_g=ck_g, h=h: e.scalar_tensor_tensor(
                    ocmb[:], o_gla[:, (ck_g * HQ + h) * 128:(ck_g * HQ + h + 1) * 128],
                    rq_col[:, ck_g * HQ + h: ck_g * HQ + h + 1], y_sb[:],
                    ALU.mult, ALU.add), waits=[(se, se.n)], inc=ve)
                pst = ps_t0 if s % 2 == 0 else ps_t1
                pr.add("tensor", lambda e, pst=pst: e.transpose(pst[:, 0:128], ocmb[:],
                                                                idb[:]),
                       waits=[(ve, ve.n)], inc=mm)
                pr.add("vector", lambda e, s=s, pst=pst: e.tensor_copy(
                    obuf[:, 128 * s:128 * (s + 1)], pst[:, 0:128]),
                       waits=[(mm, mm.n), (so, so.n)], inc=ve)
            pr.dma("sync", oT_d[128 * h:128 * (h + 1), 512 * J:512 * (J + 1)], obuf[:],
                   waits=[(ve, ve.n)], inc=so)
    pr.wait("sync", (so, so.n))
    pr.build()
    return nc


def attn0_in_maps(x0, inputs, consts):
    Wq = np.asarray(inputs["L0_attn_Wq"], np.float32)
    Wk = np.asarray(inputs["L0_attn_Wk"], np.float32)
    Wv = np.asarray(inputs["L0_attn_Wv"], np.float32)
    inv = 1.0 / (10000.0 ** (np.arange(0, 128, 2, dtype=np.float32) / 128.0))
    freqs = np.arange(SEQ, dtype=np.float32)[:, None] * inv[None, :]
    cos64 = np.ascontiguousarray(np.cos(freqs).T)
    sin64 = np.ascontiguousarray(np.sin(freqs).T)
    cosT = np.concatenate([cos64, cos64], 0).astype(ml_dtypes.bfloat16)
    sinT = np.concatenate([sin64, sin64], 0).astype(ml_dtypes.bfloat16)
    rotv = np.zeros((128, 128), np.float32)
    rotv[np.arange(64) + 64, np.arange(64)] = -1.0
    rotv[np.arange(64), np.arange(64) + 64] = 1.0
    kk = np.arange(128)
    qq = np.arange(512)
    maskv = np.zeros((128, 4 * 512), np.float32)
    for s in range(4):
        maskv[:, 512 * s:512 * (s + 1)] = np.where(
            (128 * s + kk[:, None]) > qq[None, :], BIG_NEG, 0.0)
    x0T = [np.ascontiguousarray(np.asarray(x0[b], np.float32).T) for b in range(2)]
    maps = []
    for cid in range(8):
        b, g = cid // 4, cid % 4
        maps.append(dict(
            x0T=x0T[b],
            wq=np.ascontiguousarray(Wq[:, 512 * g:512 * (g + 1)]),
            wk=np.ascontiguousarray(Wk[:, 128 * g:128 * (g + 1)]),
            wv=np.ascontiguousarray(Wv[:, 128 * g:128 * (g + 1)]),
            cosT=cosT, sinT=sinT, rotm=rotv.astype(ml_dtypes.bfloat16),
            maskc=maskv.astype(ml_dtypes.bfloat16),
            triu=np.triu(np.ones((128, 128), np.float32)),
            ident=consts["ident_bf"], ident_f32=np.eye(128, dtype=np.float32),
            ones_f=np.ones((128, 128), np.float32),
            ones128_bf=consts["ones128_bf"], ones1x128_f=consts["ones1x128_f"],
            scal=consts["scal"]))
    return maps

from contextlib import ExitStack

DFF = 5632
NFC = 44
NBLK, BFC = 4, 11


def build_mlp0(debug=False):
    nc = bass.Bass("TRN2", target_bir_lowering=False, debug=False)

    def di(n, sh, dt=F32):
        return nc.dram_tensor(n, sh, dt, kind="ExternalInput").ap()
    oT = di("oT", [DM, TOK], BF16)
    x0T = di("x0T", [DM, TOK])
    Wo_n = di("Wo_n", [DM, DM])
    gup = di("gup", [2 * NFC, DM, 128])
    agT = di("agT", [DM, 128])
    auT = di("auT", [DM, 128])
    wrT = di("wrT", [DM, 8])
    bgT = di("bgT", [16, 8 * DFF])
    buT = di("buT", [16, 8 * DFF])
    adT_r = di("adT_r", [128, DFF])
    bdT = di("bdT", [128, DM])
    wdT_q = di("wdT_q", [4, NFC, 128, 512])
    pattern2 = di("pattern2", [8, 128], BF16)
    ident_f32 = di("ident_f32", [128, 128], F32)
    ident = di("ident", [128, 128], BF16)
    ones128_bf = di("ones128_bf", [128, 1], BF16)
    ones1x128_f = di("ones1x128_f", [1, 128], F32)
    scal = di("scal", [128, len(SCALAR_VALS)])
    x2T = nc.dram_tensor("x2T", [DM, TOK], F32, kind="ExternalOutput").ap()
    kind_dbg = "ExternalOutput" if debug else "Internal"
    x1spill = nc.dram_tensor("x1spill", [DM, TOK], F32, kind=kind_dbg).ap()
    y2spill = nc.dram_tensor("y2spill", [DM, TOK], F32, kind=kind_dbg).ap()
    hspill = nc.dram_tensor("hspill", [DFF, 512], BF16, kind="Internal").ap()

    a = nc.alloc_sbuf_tensor

    pat2 = a("pat2", [8, 128], BF16)
    idb = a("idb", [128, 128], BF16)
    idf = a("idf", [128, 128], F32)
    one_bf = a("one_bf", [128, 1], BF16)
    one_f = a("one_f", [1, 128], F32)
    sc_sb = a("sc_sb", [128, len(SCALAR_VALS)], F32)
    stat_m = a("stat_m", [1, TOK], F32)
    stat_v = a("stat_v", [1, TOK], F32)
    rstd = a("rstd", [1, TOK], F32)
    mrstd = a("mrstd", [1, TOK], F32)
    ybf = a("ybf", [128, TOK], BF16)
    ysq = a("ysq", [128, TOK], BF16)
    xb = a("xb", [128, NDM * TOK], BF16)

    p = nc.alloc_psum_tensor
    ps_a = p("ps_a", [128, 512], F32)
    ps_b = p("ps_b", [128, 512], F32)
    ps_cf = p("ps_cf", [128, 512], F32)
    ps_tr = p("ps_tr", [128, 1024], BF16)
    ps_dn = p("ps_dn", [128, 2048], F32)

    pr = Prog(nc)
    ldc = pr.sem("ldc")
    ldga = pr.sem("ldga")
    ldwo = pr.sem("ldwo")
    ldsm = pr.sem("ldsm")
    gws = [pr.sem(f"gw{i}") for i in range(4)]
    bgs = pr.sem("bgs")
    bus = pr.sem("bus")
    wds = [pr.sem(f"wds{i}") for i in range(2)]
    hhs = [pr.sem(f"hhs{i}") for i in range(2)]
    xls = [pr.sem(f"xls{i}") for i in range(2)]
    stx1 = pr.sem("stx1")
    sth = pr.sem("sth")
    sy2 = [pr.sem(f"sy2{i}") for i in range(2)]
    sout = pr.sem("sout")
    ldy2 = pr.sem("ldy2")
    mm = pr.sem("mm")
    ve = pr.engine_sem("vector")
    se = pr.engine_sem("scalar")

    def barrier(*waits):
        for eng in ("sync", "scalar", "vector", "tensor", "gpsimd"):
            pr.wait(eng, *waits)

    x0c = x0T.rearrange("(c p) t -> c p t", p=128)
    oc_ = oT.rearrange("(c p) t -> c p t", p=128)
    woc = Wo_n.rearrange("(c p) t -> c p t", p=128)

    def emit_ln(buf, resid_ready, cast_to=None, spill=None, spill_sem=None):
        stat_mms = []
        for c in range(NDM):
            pr.add("vector", lambda e, c=c: e.tensor_copy(ybf[:], buf[:, c * TOK:(c + 1) * TOK]),
                   waits=[resid_ready] + ([(mm, stat_mms[-1])] if c else []), inc=ve)
            pr.add("scalar", lambda e: e.activation(ysq[:], ybf[:], AF.Square),
                   waits=[(ve, ve.n)], inc=se)

            def statfn(e, c=c):
                last = None
                for j in range(2):
                    last = e.matmul(ps_dn[0:1, 512 * j:512 * (j + 1)], one_bf[:],
                                    ybf[:, 512 * j:512 * (j + 1)],
                                    start=(c == 0), stop=(c == NDM - 1))
                for j in range(2):
                    last = e.matmul(ps_dn[0:1, TOK + 512 * j:TOK + 512 * (j + 1)], one_bf[:],
                                    ysq[:, 512 * j:512 * (j + 1)],
                                    start=(c == 0), stop=(c == NDM - 1))
                return last
            pr.add("tensor", statfn, waits=[(se, se.n), (ve, ve.n)], inc=mm)
            stat_mms.append(mm.n)
        pr.add("vector", lambda e: e.tensor_scalar_mul(stat_m[:], ps_dn[0:1, 0:TOK], 1.0 / DM),
               waits=[(mm, mm.n)], inc=ve)
        pr.add("vector", lambda e: e.tensor_scalar_mul(stat_v[:], ps_dn[0:1, TOK:2 * TOK],
                                                       1.0 / DM), waits=[], inc=ve)
        pr.add("vector", lambda e: e.tensor_mul(mrstd[:], stat_m[:], stat_m[:]), waits=[], inc=ve)
        pr.add("vector", lambda e: e.tensor_sub(stat_v[:], stat_v[:], mrstd[:]), waits=[], inc=ve)
        pr.add("scalar", lambda e: e.activation(
            rstd[:], stat_v[:], AF.Sqrt, bias=sc_sb[0:1, SCALARS["eps"]:SCALARS["eps"] + 1]),
               waits=[(ve, ve.n), (ldc, ldc.n)], inc=se)
        pr.add("vector", lambda e: e.reciprocal(rstd[:], rstd[:]), waits=[(se, se.n)], inc=ve)
        pr.add("vector", lambda e: e.tensor_mul(mrstd[:], stat_m[:], rstd[:]), waits=[], inc=ve)

        def bcfn(e):
            last = None
            for j in range(2):
                last = e.matmul(ps_dn[:, 512 * j:512 * (j + 1)], one_f[:],
                                rstd[0:1, 512 * j:512 * (j + 1)], start=True, stop=True)
            for j in range(2):
                last = e.matmul(ps_dn[:, TOK + 512 * j:TOK + 512 * (j + 1)], one_f[:],
                                mrstd[0:1, 512 * j:512 * (j + 1)], start=True, stop=True)
            return last
        pr.add("tensor", bcfn, waits=[(ve, ve.n)], inc=mm)
        mm_bc = mm.n
        for c in range(NDM):
            sl = np.s_[:, c * TOK:(c + 1) * TOK]
            pr.add("vector", lambda e, sl=sl: e.tensor_mul(buf[sl], buf[sl], ps_dn[:, 0:TOK]),
                   waits=[(mm, mm_bc)], inc=ve)
            pr.add("vector", lambda e, sl=sl: e.tensor_sub(buf[sl], buf[sl],
                                                           ps_dn[:, TOK:2 * TOK]),
                   waits=[], inc=ve)
            if cast_to is not None:
                pr.add("vector", lambda e, sl=sl: e.tensor_copy(cast_to[sl], buf[sl]),
                       waits=[], inc=ve)
            if spill is not None:
                pr.dma("sync", spill.rearrange("(c p) t -> c p t", p=128)[c], buf[sl],
                       waits=[(ve, ve.n)], inc=spill_sem)

    ctxA = ExitStack()
    xs = ctxA.enter_context(nc.sbuf_tensor("xs", [128, NDM * TOK], F32))
    ob = ctxA.enter_context(nc.sbuf_tensor("ob", [128, NDM * TOK], BF16))
    woh = ctxA.enter_context(nc.sbuf_tensor("woh", [128, NDM * TOK], BF16))

    for c in range(NDM):
        pr.dma("sync", xs[:, c * TOK:(c + 1) * TOK], x0c[c], inc=ldc)
    pr.dma("sync", one_f[:], ones1x128_f[:], inc=ldc)
    pr.dma("sync", sc_sb[:], scal[:], inc=ldc)
    pr.dma("sync", idf[:], ident_f32[:], inc=ldc)
    ld_x0 = ldc.n
    pr.dma("gpsimd", one_bf[:], ones128_bf[:], inc=ldga)
    pr.dma("gpsimd", idb[:], ident[:], inc=ldga)
    pr.dma("gpsimd", pat2[:], pattern2[:], inc=ldga)
    for c in range(NDM):
        pr.dma("gpsimd", ob[:, c * TOK:(c + 1) * TOK], oc_[c], inc=ldga)
    ldga_all = ldga.n

    resid_mm = []
    resid_ve = []
    k = 0
    for h in range(2):
        for c in range(NDM):
            w = [(mm, resid_mm[-1])] if (h == 1 and c == 0) else []
            pr.dma("gpsimd", woh[:, c * TOK:(c + 1) * TOK],
                   woc[c][:, 1024 * h:1024 * (h + 1)], waits=w, inc=ldwo)
        ldwo_h = ldwo.n
        for dmc_i in range(8):
            dmc = 8 * h + dmc_i
            for tt in range(2):
                ps = ps_a if k % 2 == 0 else ps_b

                def mmfn(e, dmc_i=dmc_i, tt=tt, ps=ps):
                    last = None
                    for ocn in range(NDM):
                        last = e.matmul(
                            ps[:],
                            woh[:, ocn * TOK + 128 * dmc_i: ocn * TOK + 128 * (dmc_i + 1)],
                            ob[:, ocn * TOK + 512 * tt: ocn * TOK + 512 * (tt + 1)],
                            start=(ocn == 0), stop=(ocn == NDM - 1))
                    return last
                waits = [(ldwo, ldwo_h), (ldga, ldga_all)]
                if k >= 2:
                    waits.append((ve, resid_ve[k - 2]))
                pr.add("tensor", mmfn, waits=waits, inc=mm)
                resid_mm.append(mm.n)
                sl = np.s_[:, dmc * TOK + 512 * tt: dmc * TOK + 512 * (tt + 1)]
                pr.add("vector", lambda e, sl=sl, ps=ps: e.tensor_add(xs[sl], xs[sl], ps[:]),
                       waits=[(mm, mm.n), (ldc, ld_x0)], inc=ve)
                resid_ve.append(ve.n)
                k += 1

    emit_ln(xs, (ve, ve.n), cast_to=xb, spill=x1spill, spill_sem=stx1)
    ve_ln1 = ve.n
    stx1_all = stx1.n
    ctxA.close()

    barrier((ve, ve_ln1), (stx1, stx1_all))
    ctxB = ExitStack()

    def B(n, sh, dt):
        return ctxB.enter_context(nc.sbuf_tensor(n, sh, dt))
    gb = B("gb", [128, BFC * 512], BF16)
    ub = B("ub", [128, BFC * 512], BF16)
    agb = B("agb", [128, NDM * 128], BF16)
    aub = B("aub", [128, NDM * 128], BF16)
    wrb = B("wrb", [128, NDM * 8], BF16)
    bgb = B("bgb", [16, BFC * 128], BF16)
    bub = B("bub", [16, BFC * 128], BF16)
    adb = B("adb", [128, DFF], BF16)
    bdb = B("bdb", [128, DM], BF16)
    cg_sb = B("cg_sb", [16, 8 * 512], BF16)
    cu_sb = B("cu_sb", [16, 8 * 512], BF16)
    he_sb = [B(f"he{i}", [128, 512], BF16) for i in range(2)]
    tg_sb = [B(f"tg{i}", [128, 512], BF16) for i in range(2)]
    wpat = B("wpat", [128, 4 * 128], F32)
    cw_sb = B("cw_sb", [128, 4 * 128], BF16)
    cwT_sb = B("cwT_sb", [128, 4 * 128], BF16)
    gw = [B(f"gw{i}", [128, DM], BF16) for i in range(4)]
    wd_t = [B(f"wdt{i}", [128, 512], BF16) for i in range(2)]
    hh_t = [B(f"hht{i}", [128, 512], BF16) for i in range(2)]
    rlog = B("rlog", [128, 8], F32)
    rsum = B("rsum", [128, 1], F32)
    rmax = B("rmax", [128, 1], F32)
    rp2 = B("rp2", [128, 8], F32)
    rmask = B("rmask", [128, 8], F32)
    rmask2 = B("rmask2", [128, 8], F32)
    rw_sb = B("rw_sb", [128, 8], F32)
    wfm = B("wfm", [8, 512], BF16)
    x1rl = [B(f"x1rl{i}", [128, 512], F32) for i in range(2)]

    hs_d = hspill.rearrange("(c p) t -> c p t", p=128)
    agc = agT.rearrange("(c p) t -> c p t", p=128)
    auc = auT.rearrange("(c p) t -> c p t", p=128)
    wrc = wrT.rearrange("(c p) t -> c p t", p=128)
    for c in range(NDM):
        pr.dma("gpsimd", agb[:, c * 128:(c + 1) * 128], agc[c], inc=ldsm)
        pr.dma("gpsimd", aub[:, c * 128:(c + 1) * 128], auc[c], inc=ldsm)
        pr.dma("gpsimd", wrb[:, c * 8:(c + 1) * 8], wrc[c], inc=ldsm)
    pr.dma("gpsimd", adb[:], adT_r[:], inc=ldsm)
    pr.dma("gpsimd", bdb[:], bdT[:], inc=ldsm)
    ldsm_all = ldsm.n

    x1s = x1spill.rearrange("(c p) t -> c p t", p=128)
    y2s_d = y2spill.rearrange("(c p) t -> c p t", p=128)

    for tt in range(2):
        barrier((ve, ve.n), (mm, mm.n), (se, se.n),
                (sth, sth.n), (sy2[0], sy2[0].n), (sy2[1], sy2[1].n))

        first_r = True
        for ttt in range(4):
            tok0 = 512 * tt + 128 * ttt

            def rlfn(e, tok0=tok0):
                last = None
                for ic in range(NDM):
                    last = e.matmul(ps_cf[:, 0:8],
                                    xb[:, ic * TOK + tok0: ic * TOK + tok0 + 128],
                                    wrb[:, ic * 8:(ic + 1) * 8],
                                    start=(ic == 0), stop=(ic == NDM - 1))
                return last
            waits = [(ldsm, ldsm_all), (ve, ve.n)] if first_r else [(ve, ve.n)]
            first_r = False
            pr.add("tensor", rlfn, waits=waits, inc=mm)
            pr.add("scalar", lambda e: e.activation(rlog[:], ps_cf[:, 0:8], AF.Exp),
                   waits=[(mm, mm.n)], inc=se)
            pr.add("vector", lambda e: e.reduce_sum(rsum[:], rlog[:], axis=AX.X),
                   waits=[(se, se.n)], inc=ve)
            pr.add("vector", lambda e: e.reciprocal(rsum[:], rsum[:]), waits=[], inc=ve)
            pr.add("vector", lambda e: e.tensor_scalar_mul(rlog[:], rlog[:], rsum[:, 0:1]),
                   waits=[], inc=ve)
            pr.add("vector", lambda e: e.reduce_max(rmax[:], rlog[:], axis=AX.X),
                   waits=[], inc=ve)
            pr.add("vector", lambda e: e.tensor_single_scalar(rmask[:], rlog[:], rmax[:, 0:1],
                                                              ALU.is_ge), waits=[], inc=ve)
            pr.add("vector", lambda e: e.scalar_tensor_tensor(rp2[:], rmask[:], BIG_NEG,
                                                              rlog[:], ALU.mult, ALU.add),
                   waits=[], inc=ve)
            pr.add("vector", lambda e: e.reduce_max(rmax[:], rp2[:], axis=AX.X),
                   waits=[], inc=ve)
            pr.add("vector", lambda e: e.tensor_single_scalar(rmask2[:], rp2[:], rmax[:, 0:1],
                                                              ALU.is_ge), waits=[], inc=ve)
            pr.add("vector", lambda e: e.tensor_add(rmask[:], rmask[:], rmask2[:]),
                   waits=[], inc=ve)
            pr.add("vector", lambda e: e.tensor_mul(rw_sb[:], rlog[:], rmask[:]),
                   waits=[], inc=ve)
            ve_rw = ve.n
            pr.add("tensor", lambda e: e.transpose(ps_cf[0:8, 128:256], rw_sb[:], idf[:]),
                   waits=[(ve, ve_rw)], inc=mm)
            pr.add("vector", lambda e, ttt=ttt: e.tensor_copy(
                wfm[:, 128 * ttt:128 * (ttt + 1)], ps_cf[0:8, 128:256]),
                   waits=[(mm, mm.n)], inc=ve)
        for ttt in range(4):
            pr.add("tensor", lambda e, ttt=ttt: e.matmul(
                ps_cf[:, 128:256], wfm[:, 128 * ttt:128 * (ttt + 1)], pat2[:],
                start=True, stop=True), waits=[(ve, ve.n)], inc=mm)
            pr.add("vector", lambda e, ttt=ttt: e.tensor_copy(
                wpat[:, 128 * ttt:128 * (ttt + 1)], ps_cf[:, 128:256]),
                   waits=[(mm, mm.n)], inc=ve)
        ve_wpat = ve.n

        cf_done = []
        for e_i in range(8):
            def cgfn(e, e_i=e_i, tt=tt):
                last = None
                for ic in range(NDM):
                    last = e.matmul(ps_a[0:16, :],
                                    agb[:, ic * 128 + 16 * e_i: ic * 128 + 16 * e_i + 16],
                                    xb[:, ic * TOK + 512 * tt: ic * TOK + 512 * (tt + 1)],
                                    start=(ic == 0), stop=(ic == NDM - 1))
                return last

            def cufn(e, e_i=e_i, tt=tt):
                last = None
                for ic in range(NDM):
                    last = e.matmul(ps_b[0:16, :],
                                    aub[:, ic * 128 + 16 * e_i: ic * 128 + 16 * e_i + 16],
                                    xb[:, ic * TOK + 512 * tt: ic * TOK + 512 * (tt + 1)],
                                    start=(ic == 0), stop=(ic == NDM - 1))
                return last
            waits = [(se, cf_done[-1])] if e_i else [(ve, ve_wpat)]
            pr.add("tensor", cgfn, waits=waits, inc=mm)
            pr.add("tensor", cufn, waits=[], inc=mm)
            mm_c = mm.n
            pr.add("scalar", lambda e, e_i=e_i: e.activation(
                cg_sb[:, 512 * e_i:512 * (e_i + 1)], ps_a[0:16, :], AF.Copy, scale=2.0),
                   waits=[(mm, mm_c)], inc=se)
            pr.add("scalar", lambda e, e_i=e_i: e.activation(
                cu_sb[:, 512 * e_i:512 * (e_i + 1)], ps_b[0:16, :], AF.Copy, scale=2.0),
                   waits=[], inc=se)
            cf_done.append(se.n)
        se_cf = se.n

        first_cf = True
        cf_mms = []
        for blk in range(NBLK):
            base_ve = []
            base_mms = []
            for j in range(BFC):
                fc = blk * BFC + j
                sg, su = gws[2 * (j % 2)], gws[2 * (j % 2) + 1]
                bg, bu = gw[2 * (j % 2)], gw[2 * (j % 2) + 1]
                w = [(mm, base_mms[j - 2])] if j >= 2 else [(ve, ve.n), (mm, mm.n)]
                pr.dma("gpsimd", bg[:].rearrange("p (c f) -> p c f", f=128),
                       gup[2 * fc].rearrange("(c p) f -> p c f", p=128), waits=w, inc=sg)
                pr.dma("gpsimd", bu[:].rearrange("p (c f) -> p c f", f=128),
                       gup[2 * fc + 1].rearrange("(c p) f -> p c f", p=128), inc=su)

                def gmm(e, bg=bg, tt=tt):
                    last = None
                    for ic in range(NDM):
                        last = e.matmul(ps_a[:], bg[:, ic * 128:(ic + 1) * 128],
                                        xb[:, ic * TOK + 512 * tt: ic * TOK + 512 * (tt + 1)],
                                        start=(ic == 0), stop=(ic == NDM - 1))
                    return last

                def umm(e, bu=bu, tt=tt):
                    last = None
                    for ic in range(NDM):
                        last = e.matmul(ps_b[:], bu[:, ic * 128:(ic + 1) * 128],
                                        xb[:, ic * TOK + 512 * tt: ic * TOK + 512 * (tt + 1)],
                                        start=(ic == 0), stop=(ic == NDM - 1))
                    return last
                waits = [(sg, sg.n), (su, su.n), (se, se_cf)]
                if j >= 1:
                    waits.append((ve, base_ve[-1]))
                pr.add("tensor", gmm, waits=waits, inc=mm)
                pr.add("tensor", umm, waits=[], inc=mm)
                mm_b = mm.n
                base_mms.append(mm_b)
                pr.add("vector", lambda e, j=j: e.tensor_copy(
                    gb[:, 512 * j:512 * (j + 1)], ps_a[:]),
                       waits=[(mm, mm_b), (sth, sth.n)], inc=ve)
                pr.add("vector", lambda e, j=j: e.tensor_copy(
                    ub[:, 512 * j:512 * (j + 1)], ps_b[:]), waits=[], inc=ve)
                base_ve.append(ve.n)
            ve_base = ve.n

            ex_ve = []
            for e_i in range(8):
                pr.dma("gpsimd", bgb[:], bgT[0:16, e_i * DFF + blk * BFC * 128:
                                             e_i * DFF + (blk + 1) * BFC * 128],
                       waits=[(mm, mm.n)], inc=bgs)
                pr.dma("gpsimd", bub[:], buT[0:16, e_i * DFF + blk * BFC * 128:
                                             e_i * DFF + (blk + 1) * BFC * 128],
                       inc=bus)
                ldg_be = [(bgs, bgs.n), (bus, bus.n)]
                for j in range(BFC):
                    fc = blk * BFC + j

                    def gemm(e, e_i=e_i, j=j):
                        e.matmul(ps_a[:], idb[:], gb[:, 512 * j:512 * (j + 1)],
                                 start=True, stop=False)
                        return e.matmul(ps_a[:], bgb[0:16, 128 * j:128 * (j + 1)],
                                        cg_sb[:, 512 * e_i:512 * (e_i + 1)],
                                        start=False, stop=True)

                    def uemm(e, e_i=e_i, j=j):
                        e.matmul(ps_b[:], idb[:], ub[:, 512 * j:512 * (j + 1)],
                                 start=True, stop=False)
                        return e.matmul(ps_b[:], bub[0:16, 128 * j:128 * (j + 1)],
                                        cu_sb[:, 512 * e_i:512 * (e_i + 1)],
                                        start=False, stop=True)
                    waits = [(ve, ve_base)] + ldg_be
                    if ex_ve:
                        waits.append((ve, ex_ve[-1]))
                    pr.add("tensor", gemm, waits=waits, inc=mm)
                    pr.add("tensor", uemm, waits=[], inc=mm)
                    mm_e = mm.n
                    par = j % 2
                    pr.add("scalar", lambda e, par=par: e.activation(
                        tg_sb[par][:], ps_a[:], AF.Silu), waits=[(mm, mm_e)], inc=se)
                    he_w = [(se, se.n)]
                    if len(cf_mms) >= 2:
                        he_w.append((mm, cf_mms[-2]))
                    pr.add("vector", lambda e, par=par: e.tensor_tensor(
                        he_sb[par][:], tg_sb[par][:], ps_b[:], ALU.mult),
                           waits=he_w, inc=ve)
                    ex_ve.append(ve.n)
                    is_last = (blk == NBLK - 1 and e_i == 7 and j == BFC - 1)

                    def cfmm(e, e_i=e_i, fc=fc, par=par, first=first_cf, is_last=is_last):
                        last = None
                        for ttt in range(4):
                            last = e.matmul(
                                ps_cf[:, 128 * ttt + 16 * e_i: 128 * ttt + 16 * e_i + 16],
                                he_sb[par][:, 128 * ttt:128 * (ttt + 1)],
                                adb[:, 128 * fc + 16 * e_i: 128 * fc + 16 * e_i + 16],
                                start=(first and ttt == 0), stop=(is_last and ttt == 3))
                        return last
                    pr.add("tensor", cfmm, waits=[(ve, ve.n)], inc=mm)
                    cf_mms.append(mm.n)
                    first_cf = False

            h_ve = []
            for j in range(BFC):
                fc = blk * BFC + j
                par = j % 2
                hw = [(ve, ex_ve[-1])]
                if j >= 2:
                    hw.append((ve, h_ve[-2]))
                pr.add("scalar", lambda e, j=j, par=par: e.activation(
                    tg_sb[par][:], gb[:, 512 * j:512 * (j + 1)], AF.Silu),
                       waits=hw, inc=se)
                pr.add("vector", lambda e, j=j, par=par: e.tensor_tensor(
                    gb[:, 512 * j:512 * (j + 1)], tg_sb[par][:],
                    ub[:, 512 * j:512 * (j + 1)], ALU.mult), waits=[(se, se.n)], inc=ve)
                h_ve.append(ve.n)
                pr.dma("sync", hs_d[fc], gb[:, 512 * j:512 * (j + 1)],
                       waits=[(ve, ve.n)], inc=sth)
        sth_all = sth.n

        mm_cf = cf_mms[-1]
        for ttt in range(4):
            pr.add("vector", lambda e, ttt=ttt: e.tensor_tensor(
                cw_sb[:, 128 * ttt:128 * (ttt + 1)], ps_cf[:, 128 * ttt:128 * (ttt + 1)],
                wpat[:, 128 * ttt:128 * (ttt + 1)], ALU.mult),
                   waits=[(mm, mm_cf)], inc=ve)
            pr.add("tensor", lambda e, ttt=ttt: e.transpose(
                ps_tr[:, 128 * ttt:128 * (ttt + 1)], cw_sb[:, 128 * ttt:128 * (ttt + 1)],
                idb[:]), waits=[(ve, ve.n)], inc=mm)
            pr.add("vector", lambda e, ttt=ttt: e.tensor_copy(
                cwT_sb[:, 128 * ttt:128 * (ttt + 1)], ps_tr[:, 128 * ttt:128 * (ttt + 1)]),
                   waits=[(mm, mm.n)], inc=ve)
        ve_cw = ve.n

        dn_all = []
        for q in range(4):
            for fc in range(NFC):
                buf = wd_t[fc % 2]
                hbuf = hh_t[fc % 2]
                sw = wds[fc % 2]
                sh_ = hhs[fc % 2]
                if len(dn_all) >= 2:
                    w = [(mm, dn_all[-2] if fc % 2 == len(dn_all) % 2 else dn_all[-1])]
                elif dn_all:
                    w = [(mm, dn_all[-1])]
                else:
                    w = []
                if fc < 2:
                    w = w + [(ve, ve_cw), (sth, sth_all)]
                pr.dma("gpsimd", buf[:], wdT_q[q, fc], waits=w, inc=sw)
                pr.dma("scalar", hbuf[:], hs_d[fc], waits=w, inc=sh_)

                def dmm(e, fc=fc, buf=buf, hbuf=hbuf):
                    last = None
                    for i in range(4):
                        last = e.matmul(ps_dn[:, 512 * i:512 * (i + 1)],
                                        buf[:, 128 * i:128 * (i + 1)],
                                        hbuf[:],
                                        start=(fc == 0), stop=False)
                    return last
                dw = [(sw, sw.n), (sh_, sh_.n)]
                if fc == 0:
                    dw.append((ve, ve.n))
                pr.add("tensor", dmm, waits=dw, inc=mm)
                dn_all.append(mm.n)

            def delmm(e, q=q):
                last = None
                for i in range(4):
                    for ttt in range(4):
                        last = e.matmul(
                            ps_dn[:, 512 * i + 128 * ttt: 512 * i + 128 * (ttt + 1)],
                            bdb[:, (4 * q + i) * 128:(4 * q + i + 1) * 128],
                            cwT_sb[:, 128 * ttt:128 * (ttt + 1)],
                            start=False, stop=(ttt == 3))
                return last
            pr.add("tensor", delmm, waits=[(ve, ve_cw)], inc=mm)
            mm_q = mm.n
            for i in range(4):
                dmc = 4 * q + i
                buf = x1rl[i % 2]
                sx = xls[i % 2]
                sy = sy2[i % 2]
                pr.dma("sync", buf[:], x1s[dmc][:, 512 * tt:512 * (tt + 1)],
                       waits=[(sy, sy.n)], inc=sx)
                pr.add("vector", lambda e, i=i, buf=buf: e.tensor_add(
                    buf[:], buf[:], ps_dn[:, 512 * i:512 * (i + 1)]),
                       waits=[(mm, mm_q), (sx, sx.n)], inc=ve)
                pr.dma("sync", y2s_d[dmc][:, 512 * tt:512 * (tt + 1)], buf[:],
                       waits=[(ve, ve.n)], inc=sy)

    ctxB.close()

    barrier((sy2[0], sy2[0].n), (sy2[1], sy2[1].n), (ve, ve.n), (mm, mm.n), (se, se.n))
    ctxC = ExitStack()
    y2s = ctxC.enter_context(nc.sbuf_tensor("y2s", [128, NDM * TOK], F32))
    for c in range(NDM):
        pr.dma("sync", y2s[:, c * TOK:(c + 1) * TOK], y2s_d[c], inc=ldy2)
    emit_ln(y2s, (ldy2, ldy2.n), cast_to=None, spill=x2T, spill_sem=sout)
    pr.wait("sync", (sout, sout.n))
    ctxC.close()
    pr.build()
    return nc


def mlp0_in_maps(oT_slices, x0T_slices, inputs, consts):
    import ml_dtypes
    Wo = np.asarray(inputs["L0_attn_Wo"], np.float32)
    Wg = np.asarray(inputs["L0_ffn_Wg"], np.float32)
    Wu = np.asarray(inputs["L0_ffn_Wu"], np.float32)
    Wd = np.asarray(inputs["L0_ffn_Wd"], np.float32)
    Wr = np.asarray(inputs["L0_ffn_Wr"], np.float32)
    Ag = np.asarray(inputs["L0_ffn_Ag"], np.float32)
    Bg = np.asarray(inputs["L0_ffn_Bg"], np.float32)
    Au = np.asarray(inputs["L0_ffn_Au"], np.float32)
    Bu = np.asarray(inputs["L0_ffn_Bu"], np.float32)
    Ad = np.asarray(inputs["L0_ffn_Ad"], np.float32)
    Bd = np.asarray(inputs["L0_ffn_Bd"], np.float32)

    WgT = Wg.T
    WuT = Wu.T
    g_ch = WgT.reshape(DM, NFC, 128).transpose(1, 0, 2)
    u_ch = WuT.reshape(DM, NFC, 128).transpose(1, 0, 2)
    gupv = np.empty((2 * NFC, DM, 128), np.float32)
    gupv[0::2] = g_ch
    gupv[1::2] = u_ch
    agT = np.concatenate([Ag[e].T for e in range(8)], axis=1)
    auT = np.concatenate([Au[e].T for e in range(8)], axis=1)
    wrT = Wr.T
    bgT = np.concatenate([Bg[e].T for e in range(8)], axis=1)
    buT = np.concatenate([Bu[e].T for e in range(8)], axis=1)
    adT = np.concatenate([Ad[e].T for e in range(8)], axis=1)
    adT_r = np.ascontiguousarray(
        adT.reshape(NFC, 128, 128).transpose(1, 0, 2).reshape(128, NFC * 128))
    bdT = np.concatenate([Bd[e].T for e in range(8)], axis=0)
    wdT = Wd.T
    wdT_q = np.ascontiguousarray(
        wdT.reshape(NFC, 128, 4, 512).transpose(2, 0, 1, 3))
    pattern2 = np.zeros((8, 128), np.float32)
    for e in range(8):
        pattern2[e, 16 * e:16 * (e + 1)] = 2.0
    shared = dict(
        Wo_n=Wo, gup=gupv, agT=np.ascontiguousarray(agT), auT=np.ascontiguousarray(auT),
        wrT=np.ascontiguousarray(wrT), bgT=np.ascontiguousarray(bgT),
        buT=np.ascontiguousarray(buT), adT_r=adT_r, bdT=np.ascontiguousarray(bdT),
        wdT_q=wdT_q, pattern2=pattern2.astype(ml_dtypes.bfloat16),
        ident_f32=np.eye(128, dtype=np.float32),
        ident=consts["ident_bf"], ones128_bf=consts["ones128_bf"],
        ones1x128_f=consts["ones1x128_f"], scal=consts["scal"])
    maps = []
    for cid in range(8):
        m = dict(shared)
        m["oT"] = np.ascontiguousarray(oT_slices[cid]).astype(ml_dtypes.bfloat16) \
            if oT_slices[cid].dtype != ml_dtypes.bfloat16 else np.ascontiguousarray(oT_slices[cid])
        m["x0T"] = np.ascontiguousarray(x0T_slices[cid])
        maps.append(m)
    return maps

from contextlib import ExitStack
import ml_dtypes

INTER = 4096
NH = 64
HD = 64
NST = 128
CONV_DIM = INTER + 2 * NST
D_IN = 2 * INTER + 2 * NST + NH
NCH = CONV_DIM // 128
NCK = 8
HALO = 128
TOKH = TOK + HALO


def build_attn1a():
    nc = bass.Bass("TRN2", target_bir_lowering=False, debug=False)

    def di(n, sh, dt=F32):
        return nc.dram_tensor(n, sh, dt, kind="ExternalInput").ap()
    x2Th = di("x2Th", [DM, TOKH])
    w_in = di("w_in", [DM, D_IN])
    convw = di("convw", [128, NCH * 4])
    convb = di("convb", [128, NCH])
    dtb_bc = di("dtb_bc", [128, NH])
    a_col = di("a_col", [NH, 1])
    triu = di("triu", [128, 128])
    masklt = di("masklt", [128, 128])
    ident = di("ident", [128, 128], BF16)
    ident_f32 = di("ident_f32", [128, 128], F32)
    ones_f = di("ones_f", [128, 128], F32)
    ones1x128_f = di("ones1x128_f", [1, 128], F32)
    scal = di("scal", [128, len(SCALAR_VALS)])

    def do(n, sh, dt=F32):
        return nc.dram_tensor(n, sh, dt, kind="ExternalOutput").ap()
    gate_d = do("gate_d", [NCK, 128, INTER], BF16)
    xtok_d = do("xtok_d", [NCK, 128, INTER], BF16)
    intra_d = do("intra_d", [NCK, 128, INTER], BF16)
    cfm_d = do("cfm_d", [128, TOK], BF16)
    gtok_d = do("gtok_d", [NCK, 128, NH])
    tc_d = do("tc_d", [NCK, 128, INTER])
    dch_d = do("dch_d", [NCK, 128, NH])
    uspan_d = do("uspan_d", [128, INTER])
    dspan_d = do("dspan_d", [128, NH])

    a = nc.alloc_sbuf_tensor
    sc_sb = a("sc_sb", [128, len(SCALAR_VALS)], F32)
    idb = a("idb", [128, 128], BF16)
    idf = a("idf", [128, 128], F32)
    onef = a("onef", [128, 128], F32)
    one1f = a("one1f", [1, 128], F32)
    tri_sb = a("tri_sb", [128, 128], F32)
    cw_sb = a("cw_sb", [128, NCH * 4], F32)
    cb_sb = a("cb_sb", [128, NCH], F32)
    dtb_sb = a("dtb_sb", [128, NH], F32)
    acol_sb = a("acol_sb", [NH, 1], F32)
    dt_tok = a("dt_tok", [128, NCK * NH], F32)
    dt_fm = a("dt_fm", [NH, TOK], F32)
    da_fm = a("da_fm", [NH, TOK], F32)
    ca_fm = a("ca_fm", [NH, TOK], F32)
    gt_tmp = a("gt_tmp", [128, NH], F32)
    ca_tok = a("ca_tok", [128, NCK * NH], F32)
    tfm = a("tfm", [NH, 128], F32)
    mlt_sb = a("mlt_sb", [128, 128], F32)
    cs_sb = a("cs_sb", [128, 128], F32)
    d1_sb = a("d1_sb", [128, 128], F32)
    l_sb = a("l_sb", [128, 128], F32)
    m_sb = a("m_sb", [128, 128], BF16)
    scores_sb = a("scores_sb", [128, 128], F32)
    ftok = a("ftok", [128, NCK * NH], F32)
    dce_col = a("dce_col", [NH, 1], F32)
    dch_sb = a("dch_sb", [128, NCK * NH], F32)
    drow64 = a("drow64", [1, NH], F32)
    dsp_sb = a("dsp_sb", [128, NH], F32)
    zrow = a("zrow", [128, NH], F32)
    z2 = a("z2", [128, NH], F32)
    z3 = a("z3", [128, NH], F32)
    ev1 = a("ev1", [128, 512], BF16)
    ev2 = a("ev2", [128, 512], BF16)
    xconv = a("xconv", [128, NCH * TOK], BF16)
    xbc_dram = nc.dram_tensor("xbc_dram", [NCH, 128, TOKH], BF16, kind="Internal").ap()

    p = nc.alloc_psum_tensor
    ps_a = p("ps_a", [128, 512], F32)
    ps_b = p("ps_b", [128, 512], F32)
    ps_t2 = [p("ps_t0", [128, 512], BF16), p("ps_t1", [128, 512], BF16)]
    ps_f = p("ps_f", [128, 512], F32)
    ps_g = p("ps_g", [128, 1536], F32)

    pr = Prog(nc)
    ldc = pr.sem("ldc")
    ldx = pr.sem("ldx")
    wins = [pr.sem(f"win{i}") for i in range(2)]
    mm = pr.sem("mm")
    ve = pr.engine_sem("vector")
    se = pr.engine_sem("scalar")
    sg = pr.sem("sg")
    sxb = pr.sem("sxb")
    sx = pr.sem("sx")
    si = pr.sem("si")
    stc = pr.sem("stc")
    sm = pr.sem("sm")

    ctx0 = ExitStack()
    xb2 = ctx0.enter_context(nc.sbuf_tensor("xb2", [128, NDM * TOKH], BF16))
    win_t = [ctx0.enter_context(nc.sbuf_tensor(f"winb{i}", [128, NDM * 512], BF16))
             for i in range(2)]

    x2c = x2Th.rearrange("(c p) t -> c p t", p=128)
    for c in range(NDM):
        pr.dma("gpsimd", xb2[:, c * TOKH:(c + 1) * TOKH], x2c[c], inc=ldx)
    pr.dma("gpsimd", idb[:], ident[:], inc=ldx)
    ldx_all = ldx.n
    pr.dma("sync", sc_sb[:], scal[:], inc=ldc)
    pr.dma("sync", idf[:], ident_f32[:], inc=ldc)
    pr.dma("sync", onef[:], ones_f[:], inc=ldc)
    pr.dma("sync", one1f[:], ones1x128_f[:], inc=ldc)
    pr.dma("sync", tri_sb[:], triu[:], inc=ldc)
    pr.dma("sync", mlt_sb[:], masklt[:], inc=ldc)
    pr.dma("sync", cw_sb[:], convw[:], inc=ldc)
    pr.dma("sync", cb_sb[:], convb[:], inc=ldc)
    pr.dma("sync", dtb_sb[:], dtb_bc[:], inc=ldc)
    pr.dma("sync", acol_sb[:], a_col[:], inc=ldc)
    ldc_all = ldc.n

    winc = w_in.rearrange("(c p) t -> c p t", p=128)

    last_gate_mm = []
    for g in range(8):
        buf = win_t[g % 2]
        sem = wins[g % 2]
        w = [(mm, last_gate_mm[g - 2])] if g >= 2 else []
        for ic in range(NDM):
            pr.dma("gpsimd", buf[:, ic * 512:(ic + 1) * 512],
                   winc[ic][:, 512 * g:512 * (g + 1)], waits=w if ic == 0 else [], inc=sem)
        for ttt in range(NCK):
            ps = ps_a if ttt % 2 == 0 else ps_b

            def gfn(e, ttt=ttt, buf=buf, ps=ps):
                last = None
                for ic in range(NDM):
                    last = e.matmul(ps[:],
                                    xb2[:, ic * TOKH + HALO + 128 * ttt:
                                        ic * TOKH + HALO + 128 * (ttt + 1)],
                                    buf[:, ic * 512:(ic + 1) * 512],
                                    start=(ic == 0), stop=(ic == NDM - 1))
                return last
            waits = [(sem, sem.n), (ldx, ldx_all), (se, se.n)]
            pr.add("tensor", gfn, waits=waits, inc=mm)
            buf_ev = ev1 if ttt % 2 == 0 else ev2
            pr.add("scalar", lambda e, ps=ps, buf_ev=buf_ev: e.activation(
                buf_ev[:], ps[:], AF.Silu), waits=[(mm, mm.n), (sg, sg.n)], inc=se)
            pr.dma("sync", gate_d[ttt][:, 512 * g:512 * (g + 1)], buf_ev[:],
                   waits=[(se, se.n)], inc=sg)
        last_gate_mm.append(mm.n)

    tok_tiles = [(0, 512), (512, 512), (1024, 128)]
    last_xbc_mm = []
    for fc in range(NCH):
        buf = win_t[fc % 2]
        sem = wins[fc % 2]
        w = [(mm, last_xbc_mm[fc - 2])] if fc >= 2 else [(mm, mm.n)]
        for ic in range(NDM):
            pr.dma("gpsimd", buf[:, ic * 512:ic * 512 + 128],
                   winc[ic][:, INTER + 128 * fc:INTER + 128 * (fc + 1)],
                   waits=w if ic == 0 else [], inc=sem)
        for ti, (t0, tw) in enumerate(tok_tiles):
            ps = ps_a if ti % 2 == 0 else ps_b

            def xfn(e, t0=t0, tw=tw, buf=buf, ps=ps):
                last = None
                for ic in range(NDM):
                    last = e.matmul(ps[:, 0:tw],
                                    buf[:, ic * 512:ic * 512 + 128],
                                    xb2[:, ic * TOKH + t0: ic * TOKH + t0 + tw],
                                    start=(ic == 0), stop=(ic == NDM - 1))
                return last
            pr.add("tensor", xfn, waits=[(sem, sem.n), (ve, ve.n)], inc=mm)
            evb = ev1 if ti % 2 == 0 else ev2
            pr.add("vector", lambda e, tw=tw, ps=ps, evb=evb: e.tensor_copy(
                evb[:, 0:tw], ps[:, 0:tw]),
                   waits=[(mm, mm.n), (sxb, sxb.n), (sg, sg.n)], inc=ve)
            pr.dma("sync", xbc_dram[fc][:, t0:t0 + tw], evb[:, 0:tw],
                   waits=[(ve, ve.n)], inc=sxb)
        last_xbc_mm.append(mm.n)

    buf = win_t[0]
    sem = wins[0]
    for ic in range(NDM):
        pr.dma("gpsimd", buf[:, ic * 512:ic * 512 + NH],
               winc[ic][:, 2 * INTER + 2 * NST: 2 * INTER + 2 * NST + NH],
               waits=[(mm, mm.n)] if ic == 0 else [], inc=sem)
    for ttt in range(NCK):
        def dfn(e, ttt=ttt, buf=buf):
            last = None
            for ic in range(NDM):
                last = e.matmul(ps_f[:, 0:NH],
                                xb2[:, ic * TOKH + HALO + 128 * ttt:
                                    ic * TOKH + HALO + 128 * (ttt + 1)],
                                buf[:, ic * 512:ic * 512 + NH],
                                start=(ic == 0), stop=(ic == NDM - 1))
            return last
        pr.add("tensor", dfn, waits=[(sem, sem.n), (ve, ve.n)], inc=mm)
        pr.add("vector", lambda e: e.tensor_add(zrow[:], ps_f[:, 0:NH], dtb_sb[:]),
               waits=[(mm, mm.n), (ldc, ldc_all)], inc=ve)
        pr.add("scalar", lambda e: e.activation(z2[:], zrow[:], AF.Abs),
               waits=[(ve, ve.n)], inc=se)
        pr.add("scalar", lambda e: e.activation(z2[:], z2[:], AF.Exp, scale=-1.0),
               waits=[], inc=se)
        pr.add("scalar", lambda e: e.activation(
            z2[:], z2[:], AF.Ln, bias=sc_sb[:, SCALARS["one"]:SCALARS["one"] + 1]),
               waits=[], inc=se)
        pr.add("vector", lambda e: e.tensor_scalar_max(z3[:], zrow[:], 0.0),
               waits=[(se, se.n)], inc=ve)
        pr.add("vector", lambda e, ttt=ttt: e.tensor_add(
            dt_tok[:, ttt * NH:(ttt + 1) * NH], z3[:], z2[:]), waits=[], inc=ve)
    ve_dt = ve.n
    ctx0.close()

    xtokc = a("xtokc", [128, INTER], BF16)
    xt_sb = a("xt_sb", [128, INTER], BF16)
    btok = a("btok", [128, 128], BF16)
    mask_sb = a("mask_sb", [128, 128], BF16)
    evf = [a(f"evf{i}", [128, 512], F32) for i in range(2)]
    s_sb = a("s_sb", [128, INTER], F32)
    convacc = a("convacc", [128, TOK], F32)
    xbt = [a(f"xbt{i}", [128, TOKH], BF16) for i in range(2)]
    lxb = [pr.sem(f"lxb{i}") for i in range(2)]
    conv_ve = []
    for fc in range(NCH):
        buf = xbt[fc % 2]
        sem = lxb[fc % 2]
        w = [(sxb, sxb.n)] if fc < 2 else [(ve, conv_ve[fc - 2])]
        pr.dma("scalar", buf[:], xbc_dram[fc], waits=w, inc=sem)
        pr.add("vector", lambda e, fc=fc, buf=buf: e.tensor_scalar(
            convacc[:], buf[:, 125:125 + TOK],
            cw_sb[:, 4 * fc:4 * fc + 1], cb_sb[:, fc:fc + 1], ALU.mult, ALU.add),
               waits=[(sem, sem.n), (se, se.n)], inc=ve)
        for kk in range(1, 4):
            pr.add("vector", lambda e, fc=fc, kk=kk, buf=buf: e.scalar_tensor_tensor(
                convacc[:], buf[:, 125 + kk: 125 + kk + TOK],
                cw_sb[:, 4 * fc + kk:4 * fc + kk + 1], convacc[:], ALU.mult, ALU.add),
                   waits=[], inc=ve)
        pr.add("scalar", lambda e, fc=fc: e.activation(
            xconv[:, fc * TOK:(fc + 1) * TOK], convacc[:], AF.Silu),
               waits=[(ve, ve.n)], inc=se)
        conv_ve.append(ve.n)
    se_conv = se.n
    pr.dma("sync", cfm_d[:], xconv[:, 33 * TOK:34 * TOK], waits=[(se, se_conv)], inc=sm)

    for ttt in range(NCK):
        pr.add("tensor", lambda e, ttt=ttt: e.transpose(
            ps_f[0:NH, 0:128], dt_tok[:, ttt * NH:(ttt + 1) * NH], idf[:]),
               waits=[(ve, ve.n)], inc=mm)
        pr.add("vector", lambda e, ttt=ttt: e.tensor_copy(
            dt_fm[:, 128 * ttt:128 * (ttt + 1)], ps_f[0:NH, 0:128]),
               waits=[(mm, mm.n)], inc=ve)
    pr.add("vector", lambda e: e.tensor_scalar_mul(da_fm[:], dt_fm[:], acol_sb[:, 0:1]),
           waits=[], inc=ve)
    for ttt in range(NCK):
        pr.add("vector", lambda e, ttt=ttt: e.tensor_tensor_scan(
            ca_fm[:, 128 * ttt:128 * (ttt + 1)], onef[0:NH, :],
            da_fm[:, 128 * ttt:128 * (ttt + 1)], 0.0, ALU.mult, ALU.add),
               waits=[], inc=ve)
    ve_ca = ve.n
    for ttt in range(NCK):
        pr.add("tensor", lambda e, ttt=ttt: e.transpose(
            ps_f[0:128, 0:NH], ca_fm[:, 128 * ttt:128 * (ttt + 1)], idf[0:NH, 0:NH]),
               waits=[(ve, ve_ca), (se, se.n), (ve, ve.n)], inc=mm)
        pr.add("vector", lambda e, ttt=ttt: e.tensor_copy(
            ca_tok[:, ttt * NH:(ttt + 1) * NH], ps_f[0:128, 0:NH]),
               waits=[(mm, mm.n)], inc=ve)
        pr.add("scalar", lambda e, ttt=ttt: e.activation(
            gt_tmp[:], ca_tok[:, ttt * NH:(ttt + 1) * NH], AF.Exp),
               waits=[(ve, ve.n), (sm, sm.n)], inc=se)
        pr.dma("sync", gtok_d[ttt], gt_tmp[:], waits=[(se, se.n)], inc=sm)
        pr.add("vector", lambda e, ttt=ttt: e.tensor_scalar(
            tfm[:], ca_fm[:, 128 * ttt:128 * (ttt + 1)],
            ca_fm[:, 128 * ttt + 127:128 * ttt + 128], None, ALU.subtract),
               waits=[], inc=ve)
        pr.add("tensor", lambda e: e.transpose(ps_f[:, 64:64 + NH], tfm[:],
                                               idf[0:NH, 0:NH]),
               waits=[(ve, ve.n)], inc=mm)
        pr.add("scalar", lambda e, ttt=ttt: e.activation(
            ftok[:, ttt * NH:(ttt + 1) * NH], ps_f[:, 64:64 + NH], AF.Exp, scale=-1.0),
               waits=[(mm, mm.n)], inc=se)
        pr.add("vector", lambda e, ttt=ttt: e.tensor_mul(
            ftok[:, ttt * NH:(ttt + 1) * NH], ftok[:, ttt * NH:(ttt + 1) * NH],
            dt_tok[:, ttt * NH:(ttt + 1) * NH]), waits=[(se, se.n)], inc=ve)
        pr.add("scalar", lambda e, ttt=ttt: e.activation(
            dce_col[:], ca_fm[:, 128 * ttt + 127:128 * ttt + 128], AF.Exp),
               waits=[(ve, ve_ca)], inc=se)
        pr.add("tensor", lambda e: e.transpose(ps_f[0:1, 128:128 + NH], dce_col[:],
                                               idf[0:NH, 0:NH]),
               waits=[(se, se.n)], inc=mm)
        pr.add("vector", lambda e: e.tensor_copy(drow64[:], ps_f[0:1, 128:128 + NH]),
               waits=[(mm, mm.n)], inc=ve)
        pr.add("tensor", lambda e: e.matmul(ps_f[:, 256:256 + NH], one1f[:], drow64[:],
                                            start=True, stop=True),
               waits=[(ve, ve.n)], inc=mm)
        pr.add("vector", lambda e, ttt=ttt: e.tensor_copy(
            dch_sb[:, ttt * NH:(ttt + 1) * NH], ps_f[:, 256:256 + NH]),
               waits=[(mm, mm.n)], inc=ve)
        pr.dma("sync", dch_d[ttt], dch_sb[:, ttt * NH:(ttt + 1) * NH],
               waits=[(ve, ve.n)], inc=sm)
    pr.add("vector", lambda e: e.tensor_copy(dsp_sb[:], dch_sb[:, 0:NH]), waits=[], inc=ve)
    for ttt in range(1, NCK):
        pr.add("vector", lambda e, ttt=ttt: e.tensor_mul(
            dsp_sb[:], dsp_sb[:], dch_sb[:, ttt * NH:(ttt + 1) * NH]), waits=[], inc=ve)
    pr.dma("sync", dspan_d[:], dsp_sb[:], waits=[(ve, ve.n)], inc=sm)
    pr.add("vector", lambda e: e.memset(s_sb[:], 0.0), waits=[], inc=ve)

    for ttt in range(NCK):
        tr_ves = []
        for j in range(32):
            slot = j % 2
            w = [(se, se_conv)]
            if len(tr_ves) >= 2:
                w.append((ve, tr_ves[-2]))
            pr.add("tensor", lambda e, j=j, ttt=ttt, slot=slot: e.transpose(
                ps_t2[slot][:, 0:128],
                xconv[:, j * TOK + 128 * ttt: j * TOK + 128 * (ttt + 1)], idb[:]),
                   waits=w, inc=mm)
            pr.add("vector", lambda e, j=j, slot=slot: e.tensor_copy(
                xtokc[:, 128 * j:128 * (j + 1)], ps_t2[slot][:, 0:128]),
                   waits=[(mm, mm.n), (sx, sx.n)], inc=ve)
            tr_ves.append(ve.n)
        pr.dma("sync", xtok_d[ttt], xtokc[:], waits=[(ve, ve.n)], inc=sx)
        pr.add("vector", lambda e, ttt=ttt: e.tensor_tensor(
            xt_sb[:].rearrange("p (h q) -> p h q", h=NH),
            xtokc[:].rearrange("p (h q) -> p h q", h=NH),
            ftok[:, ttt * NH:(ttt + 1) * NH].unsqueeze(2).broadcast_to([128, NH, HD]),
            ALU.mult), waits=[], inc=ve)
        ve_xt = ve.n
        pr.add("tensor", lambda e, ttt=ttt: e.transpose(
            ps_t2[0][:, 0:128], xconv[:, 32 * TOK + 128 * ttt:32 * TOK + 128 * (ttt + 1)],
            idb[:]), waits=[(ve, ve.n)], inc=mm)
        pr.add("vector", lambda e: e.tensor_copy(btok[:], ps_t2[0][:, 0:128]),
               waits=[(mm, mm.n)], inc=ve)
        ve_btok = ve.n
        pr.add("vector", lambda e, ttt=ttt: e.tensor_tensor(
            s_sb[:].rearrange("p (h q) -> p h q", h=NH),
            s_sb[:].rearrange("p (h q) -> p h q", h=NH),
            dch_sb[:, ttt * NH:(ttt + 1) * NH].unsqueeze(2).broadcast_to([128, NH, HD]),
            ALU.mult), waits=[(ve, ve_btok)], inc=ve)
        for i in range(8):
            bank = i % 3

            def tcfn(e, i=i, bank=bank):
                return e.matmul(ps_g[:, 512 * bank:512 * (bank + 1)], btok[:],
                                xt_sb[:, 512 * i:512 * (i + 1)], start=True, stop=True)
            w = [(ve, ve_btok)]
            if i >= 3:
                w.append((ve, ve.n))
            pr.add("tensor", tcfn, waits=w, inc=mm)
            mm_tc = mm.n
            pr.add("vector", lambda e, i=i, bank=bank: e.tensor_add(
                s_sb[:, 512 * i:512 * (i + 1)], s_sb[:, 512 * i:512 * (i + 1)],
                ps_g[:, 512 * bank:512 * (bank + 1)]),
                   waits=[(mm, mm_tc)], inc=ve)
            pr.add("vector", lambda e, i=i, bank=bank: e.tensor_copy(
                evf[i % 2][:], ps_g[:, 512 * bank:512 * (bank + 1)]),
                   waits=[(stc, stc.n)], inc=ve)
            pr.dma("sync", tc_d[ttt][:, 512 * i:512 * (i + 1)], evf[i % 2][:],
                   waits=[(ve, ve.n)], inc=stc)
        pr.add("tensor", lambda e, ttt=ttt: e.matmul(
            ps_f[:, 0:128], xconv[:, 32 * TOK + 128 * ttt:32 * TOK + 128 * (ttt + 1)],
            xconv[:, 33 * TOK + 128 * ttt:33 * TOK + 128 * (ttt + 1)],
            start=True, stop=True), waits=[(ve, ve.n)], inc=mm)
        pr.add("vector", lambda e: e.tensor_copy(scores_sb[:], ps_f[:, 0:128]),
               waits=[(mm, mm.n)], inc=ve)
        ve_sc = ve.n
        for grp in range(8):
            ps = ps_a if grp % 2 == 0 else ps_b
            for hh in range(8):
                h = 8 * grp + hh
                pr.add("vector", lambda e, ttt=ttt, h=h: e.tensor_scalar_mul(
                    cs_sb[:], onef[:], ca_tok[:, ttt * NH + h: ttt * NH + h + 1]),
                       waits=[(ve, ve_sc)], inc=ve)
                pr.add("tensor", lambda e: e.transpose(ps_f[:, 128:256], cs_sb[:], idf[:]),
                       waits=[(ve, ve.n)], inc=mm)
                pr.add("vector", lambda e: e.scalar_tensor_tensor(
                    d1_sb[:], cs_sb[:], -1.0, ps_f[:, 128:256], ALU.mult, ALU.add),
                       waits=[(mm, mm.n)], inc=ve)
                pr.add("vector", lambda e: e.tensor_add(d1_sb[:], d1_sb[:], mlt_sb[:]),
                       waits=[], inc=ve)
                pr.add("scalar", lambda e: e.activation(l_sb[:], d1_sb[:], AF.Exp),
                       waits=[(ve, ve.n)], inc=se)
                pr.add("vector", lambda e, ttt=ttt, h=h: e.scalar_tensor_tensor(
                    m_sb[:], l_sb[:], dt_tok[:, ttt * NH + h: ttt * NH + h + 1],
                    scores_sb[:], ALU.mult, ALU.mult),
                       waits=[(se, se.n)], inc=ve)

                def imfn(e, h=h, hh=hh, ps=ps):
                    return e.matmul(ps[:, 64 * hh:64 * (hh + 1)], m_sb[:],
                                    xtokc[:, 64 * h:64 * (h + 1)],
                                    start=(hh == 0), stop=(hh == 7))
                pr.add("tensor", imfn, waits=[(ve, ve.n)], inc=mm)
            buf_ev = ev1 if grp % 2 == 0 else ev2
            pr.add("scalar", lambda e, ps=ps, buf_ev=buf_ev: e.activation(
                buf_ev[:], ps[:], AF.Copy),
                   waits=[(mm, mm.n), (si, si.n), (sg, sg.n), (sxb, sxb.n)], inc=se)
            pr.dma("sync", intra_d[ttt][:, 512 * grp:512 * (grp + 1)], buf_ev[:],
                   waits=[(se, se.n)], inc=si)

    pr.dma("sync", uspan_d[:], s_sb[:], waits=[(ve, ve.n)], inc=sm)
    pr.wait("sync", (sm, sm.n), (sg, sg.n), (sx, sx.n), (si, si.n), (stc, stc.n))
    pr.build()
    return nc


def attn1a_in_maps(x2T_full, inputs, consts):
    W_in = np.asarray(inputs["L1_attn_W_in"], np.float32)
    conv_w = np.asarray(inputs["L1_attn_conv_w"], np.float32)
    conv_b = np.asarray(inputs["L1_attn_conv_b"], np.float32)
    dt_bias = np.asarray(inputs["L1_attn_dt_bias"], np.float32)
    A = np.asarray(inputs["L1_attn_A"], np.float32)
    convw_r = np.ascontiguousarray(
        conv_w.reshape(NCH, 128, 4).transpose(1, 0, 2).reshape(128, NCH * 4))
    convb_r = np.ascontiguousarray(conv_b.reshape(NCH, 128).T)
    dtb = np.tile(dt_bias[None, :], (128, 1)).astype(np.float32)
    triu = np.triu(np.ones((128, 128), np.float32))
    ii = np.arange(128)
    mlt = np.where(ii[:, None] > ii[None, :], np.float32(-1e30), np.float32(0.0))
    shared = dict(
        masklt=np.ascontiguousarray(mlt.astype(np.float32)),
        w_in=W_in, convw=convw_r, convb=convb_r, dtb_bc=dtb,
        a_col=np.ascontiguousarray(A[:, None]),
        triu=triu, ident=consts["ident_bf"],
        ident_f32=np.eye(128, dtype=np.float32),
        ones_f=np.ones((128, 128), np.float32),
        ones1x128_f=consts["ones1x128_f"], scal=consts["scal"])
    maps = []
    for cid in range(8):
        b, q = cid // 4, cid % 4
        m = dict(shared)
        sl = np.zeros((DM, TOKH), np.float32)
        if q > 0:
            sl[:, 0:HALO] = x2T_full[b][:, q * TOK - HALO:q * TOK]
        sl[:, HALO:] = x2T_full[b][:, q * TOK:(q + 1) * TOK]
        m["x2Th"] = sl
        maps.append(m)
    return maps


def build_attn1b():
    nc = bass.Bass("TRN2", target_bir_lowering=False, debug=False)

    def di(n, sh, dt=F32):
        return nc.dram_tensor(n, sh, dt, kind="ExternalInput").ap()
    x2T = di("x2T", [DM, TOK])
    gate_d = di("gate_d", [NCK, 128, INTER], BF16)
    xtok_d = di("xtok_d", [NCK, 128, INTER], BF16)
    intra_d = di("intra_d", [NCK, 128, INTER], BF16)
    cfm_d = di("cfm_d", [128, TOK], BF16)
    gtok_d = di("gtok_d", [NCK, 128, NH])
    tc_d = di("tc_d", [NCK, 128, INTER])
    dch_d = di("dch_d", [NCK, 128, NH])
    u_in = di("u_in", [3, 128, INTER])
    din_bc = di("din_bc", [3, 128, NH])
    d_bc = di("d_bc", [128, NH])
    nw_bc = di("nw_bc", [128, INTER], BF16)
    wout = di("wout", [INTER, DM])
    ident = di("ident", [128, 128], BF16)
    ones128_bf = di("ones128_bf", [128, 1], BF16)
    ones1x128_f = di("ones1x128_f", [1, 128], F32)
    scal = di("scal", [128, len(SCALAR_VALS)])
    x3T = nc.dram_tensor("x3T", [DM, TOK], F32, kind="ExternalOutput").ap()
    zbT_dram = nc.dram_tensor("zbT_dram", [32, 128, TOK], BF16, kind="Internal").ap()

    a = nc.alloc_sbuf_tensor
    sc_sb = a("sc_sb", [128, len(SCALAR_VALS)], F32)
    idb = a("idb", [128, 128], BF16)
    one_bf = a("one_bf", [128, 1], BF16)
    one_f = a("one_f", [1, 128], F32)
    dbc_sb = a("dbc_sb", [128, NH], F32)
    nw_sb = a("nw_sb", [128, INTER], BF16)
    gl_sb = a("gl_sb", [128, NCK * NH], F32)
    dchl_sb = a("dchl_sb", [128, NCK * NH], F32)
    din_sb = a("din_sb", [128, 3 * NH], F32)
    cfm_sb = a("cfm_sb", [128, TOK], BF16)
    s_sb = a("s_sb", [128, INTER], F32)
    sbf = a("sbf", [128, INTER], BF16)
    tcl = a("tcl", [128, INTER], F32)
    u_sb = tcl
    intral = a("intral", [128, INTER], BF16)
    xtokl = a("xtokl", [128, INTER], BF16)
    gatel = a("gatel", [128, INTER], BF16)
    zc = a("zc", [128, INTER], F32)
    tmp512 = a("tmp512", [128, 512], F32)
    zcol = a("zcol", [128, 1], F32)
    znb = a("znb", [128, INTER], BF16)
    zrow_big = a("zrow_big", [128, 32 * 128], BF16)
    y3 = a("y3", [128, NDM * TOK], F32)
    wq = [a(f"wq{i}", [128, 1024], BF16) for i in range(2)]
    zq = [a(f"zq{i}", [128, 512], BF16) for i in range(2)]

    p = nc.alloc_psum_tensor
    ps_y = p("ps_y", [128, 1024], F32)
    ps_t2 = [p("ps_t0", [128, 512], BF16), p("ps_t1", [128, 512], BF16)]
    ps_w = p("ps_w", [128, 2048], F32)

    pr = Prog(nc)
    ldc = pr.sem("ldc")
    ldu = pr.sem("ldu")
    ldk = pr.sem("ldk")
    lw = [pr.sem(f"lw{i}") for i in range(2)]
    lz = [pr.sem(f"lz{i}") for i in range(2)]
    szt = pr.sem("szt")
    sout = pr.sem("sout")
    mm = pr.sem("mm")
    ve = pr.engine_sem("vector")
    se = pr.engine_sem("scalar")
    lnk = LNKit(nc, pr, mm, one_bf, one_f,
                sc_sb[0:1, SCALARS["eps"]:SCALARS["eps"] + 1])

    pr.dma("sync", sc_sb[:], scal[:], inc=ldc)
    pr.dma("sync", dbc_sb[:], d_bc[:], inc=ldc)
    pr.dma("sync", nw_sb[:], nw_bc[:], inc=ldc)
    pr.dma("sync", cfm_sb[:], cfm_d[:], inc=ldc)
    for c in range(NCK):
        pr.dma("sync", gl_sb[:, c * NH:(c + 1) * NH], gtok_d[c], inc=ldc)
        pr.dma("sync", dchl_sb[:, c * NH:(c + 1) * NH], dch_d[c], inc=ldc)
    for jj in range(3):
        pr.dma("sync", din_sb[:, jj * NH:(jj + 1) * NH], din_bc[jj], inc=ldc)
    ldg2 = pr.sem("ldg2")
    pr.dma("gpsimd", idb[:], ident[:], inc=ldg2)
    pr.dma("gpsimd", one_bf[:], ones128_bf[:], inc=ldg2)
    pr.dma("sync", one_f[:], ones1x128_f[:], inc=ldc)
    x2c = x2T.rearrange("(c p) t -> c p t", p=128)
    for c in range(NDM):
        pr.dma("sync", y3[:, c * TOK:(c + 1) * TOK], x2c[c], inc=ldc)
    ldc_all = ldc.n
    pr.wait("tensor", (ldg2, ldg2.n))

    def hview(t, col0):
        return t[:, col0:col0 + NH].unsqueeze(2).broadcast_to([128, NH, HD])

    def hview8(t, col0):
        return t[:, col0:col0 + 8].unsqueeze(2).broadcast_to([128, 8, HD])

    def as_hq(apx, h):
        return apx.rearrange("p (h q) -> p h q", h=h)

    pr.add("vector", lambda e: e.memset(s_sb[:], 0.0), waits=[(ldc, ldc_all)], inc=ve)
    for jj in range(3):
        pr.dma("scalar", u_sb[:], u_in[jj], waits=[(ve, ve.n)], inc=ldu)
        pr.add("vector", lambda e, jj=jj: e.tensor_tensor(
            as_hq(s_sb[:], NH), as_hq(s_sb[:], NH), hview(din_sb, jj * NH), ALU.mult),
               waits=[], inc=ve)
        pr.add("vector", lambda e: e.tensor_add(s_sb[:], s_sb[:], u_sb[:]),
               waits=[(ldu, ldu.n)], inc=ve)

    for ck in range(NCK):
        w = [(ve, ve.n), (mm, mm.n)]
        pr.dma("sync", tcl[:], tc_d[ck], waits=w, inc=ldk)
        pr.dma("scalar", intral[:], intra_d[ck], waits=w, inc=ldk)
        pr.dma("scalar", xtokl[:], xtok_d[ck], waits=w, inc=ldk)
        pr.dma("scalar", gatel[:], gate_d[ck], waits=w, inc=ldk)
        ldk_ck = ldk.n
        pr.add("vector", lambda e: e.tensor_copy(sbf[:], s_sb[:]), waits=[], inc=ve)
        pr.add("vector", lambda e, ck=ck: e.tensor_tensor(
            as_hq(s_sb[:], NH), as_hq(s_sb[:], NH), hview(dchl_sb, ck * NH), ALU.mult),
               waits=[], inc=ve)
        pr.add("vector", lambda e: e.tensor_add(s_sb[:], s_sb[:], tcl[:]),
               waits=[(ldk, ldk_ck)], inc=ve)
        ve_snap = ve.n
        for i in range(8):
            pcol = 512 * (i % 2)

            def yfn(e, ck=ck, i=i, pcol=pcol):
                return e.matmul(ps_y[:, pcol:pcol + 512],
                                cfm_sb[:, 128 * ck:128 * (ck + 1)],
                                sbf[:, 512 * i:512 * (i + 1)], start=True, stop=True)
            yw = [(ve, ve_snap)]
            if i >= 2:
                yw.append((ve, ve.n))
            pr.add("tensor", yfn, waits=yw, inc=mm)
            mm_y = mm.n
            pr.add("vector", lambda e, ck=ck, i=i, pcol=pcol: e.tensor_tensor(
                as_hq(zc[:, 512 * i:512 * (i + 1)], 8),
                as_hq(ps_y[:, pcol:pcol + 512], 8),
                hview8(gl_sb, ck * NH + 8 * i), ALU.mult), waits=[(mm, mm_y)], inc=ve)
            pr.add("vector", lambda e, i=i: e.tensor_add(
                zc[:, 512 * i:512 * (i + 1)], zc[:, 512 * i:512 * (i + 1)],
                intral[:, 512 * i:512 * (i + 1)]), waits=[], inc=ve)
            pr.add("vector", lambda e, i=i: e.tensor_tensor(
                as_hq(tmp512[:], 8), as_hq(xtokl[:, 512 * i:512 * (i + 1)], 8),
                hview8(dbc_sb, 8 * i), ALU.mult), waits=[], inc=ve)
            pr.add("vector", lambda e, i=i: e.tensor_add(
                zc[:, 512 * i:512 * (i + 1)], zc[:, 512 * i:512 * (i + 1)], tmp512[:]),
                   waits=[], inc=ve)
            pr.add("vector", lambda e, i=i: e.tensor_mul(
                zc[:, 512 * i:512 * (i + 1)], zc[:, 512 * i:512 * (i + 1)],
                gatel[:, 512 * i:512 * (i + 1)]), waits=[], inc=ve)
        pr.add("scalar", lambda e: e.activation(intral[:], zc[:], AF.Square),
               waits=[(ve, ve.n)], inc=se)
        pr.add("vector", lambda e: e.reduce_sum(zcol[:], intral[:], axis=AX.X),
               waits=[(se, se.n)], inc=ve)
        pr.add("scalar", lambda e: e.activation(
            zcol[:], zcol[:], AF.Sqrt, scale=1.0 / INTER,
            bias=sc_sb[:, SCALARS["eps"]:SCALARS["eps"] + 1]),
               waits=[(ve, ve.n)], inc=se)
        pr.add("vector", lambda e: e.reciprocal(zcol[:], zcol[:]), waits=[(se, se.n)], inc=ve)
        pr.add("vector", lambda e: e.scalar_tensor_tensor(
            znb[:], zc[:], zcol[:, 0:1], nw_sb[:], ALU.mult, ALU.mult), waits=[], inc=ve)
        ve_znb = ve.n
        tr_ves = []
        for j in range(32):
            slot = j % 2
            tw = [(ve, ve_znb)]
            if len(tr_ves) >= 2:
                tw.append((ve, tr_ves[-2]))
            pr.add("tensor", lambda e, j=j, slot=slot: e.transpose(
                ps_t2[slot][:, 0:128], znb[:, 128 * j:128 * (j + 1)],
                idb[:]), waits=tw, inc=mm)
            pr.add("vector", lambda e, j=j, slot=slot: e.tensor_copy(
                zrow_big[:, 128 * j:128 * (j + 1)], ps_t2[slot][:, 0:128]),
                   waits=[(mm, mm.n), (szt, szt.n)], inc=ve)
            tr_ves.append(ve.n)
        pr.dma("sync", zbT_dram[:, :, 128 * ck:128 * (ck + 1)].rearrange("c p f -> p c f"),
               zrow_big[:].rearrange("p (c f) -> p c f", f=128),
               waits=[(ve, ve.n)], inc=szt)

    woutc = wout.rearrange("(c p) t -> c p t", p=128)
    wv_all = []
    for tt2 in range(2):
        for qd in range(4):
            for ic in range(32):
                bw = wq[ic % 2]
                bz = zq[ic % 2]
                sw = lw[ic % 2]
                sz = lz[ic % 2]
                w = [(mm, wv_all[-2])] if len(wv_all) >= 2 else [(szt, szt.n)]
                pr.dma("gpsimd", bw[:, 0:512], woutc[ic][:, 512 * qd:512 * (qd + 1)],
                       waits=w, inc=sw)
                pr.dma("scalar", bz[:], zbT_dram[ic][:, 512 * tt2:512 * (tt2 + 1)],
                       waits=w, inc=sz)

                def wfn(e, ic=ic, bw=bw, bz=bz):
                    last = None
                    for t4 in range(4):
                        last = e.matmul(ps_w[:, 512 * t4:512 * (t4 + 1)],
                                        bw[:, 128 * t4:128 * (t4 + 1)], bz[:],
                                        start=(ic == 0), stop=(ic == 31))
                    return last
                ww = [(sw, sw.n), (sz, sz.n)]
                if ic == 0:
                    ww.append((ve, ve.n))
                pr.add("tensor", wfn, waits=ww, inc=mm)
                wv_all.append(mm.n)
            mm_pass = mm.n
            for t4 in range(4):
                dmc = 4 * qd + t4
                pr.add("vector", lambda e, dmc=dmc, t4=t4, tt2=tt2: e.tensor_add(
                    y3[:, dmc * TOK + 512 * tt2: dmc * TOK + 512 * (tt2 + 1)],
                    y3[:, dmc * TOK + 512 * tt2: dmc * TOK + 512 * (tt2 + 1)],
                    ps_w[:, 512 * t4:512 * (t4 + 1)]), waits=[(mm, mm_pass)], inc=ve)

    lnk.emit(y3, ps_w, (ve, ve.n), spill=x3T, spill_sem=sout)
    pr.wait("sync", (sout, sout.n))
    pr.build()
    return nc


def attn1b_in_maps(x2T_full, a_outs, inputs, consts):
    D = np.asarray(inputs["L1_attn_D"], np.float32)
    nw = np.asarray(inputs["L1_attn_norm_weight"], np.float32)
    W_out = np.asarray(inputs["L1_attn_W_out"], np.float32)
    d_bc = np.tile(D[None, :], (128, 1)).astype(np.float32)
    nw_bc = np.tile(nw[None, :], (128, 1)).astype(ml_dtypes.bfloat16)
    maps = []
    for cid in range(8):
        b, q = cid // 4, cid % 4
        grp = [4 * b + j for j in range(4)]
        u_in = np.zeros((3, 128, INTER), np.float32)
        din = np.ones((3, 128, NH), np.float32)
        srcs = [a_outs[grp[c]] for c in range(4)]
        if q >= 1:
            for j in range(q):
                u_in[3 - q + j] = srcs[j]["uspan_d"]
            for j in range(1, q):
                din[3 - q + j] = srcs[j]["dspan_d"]
        m = dict(
            x2T=np.ascontiguousarray(x2T_full[b][:, q * TOK:(q + 1) * TOK]),
            gate_d=a_outs[cid]["gate_d"], xtok_d=a_outs[cid]["xtok_d"],
            intra_d=a_outs[cid]["intra_d"], cfm_d=a_outs[cid]["cfm_d"],
            gtok_d=a_outs[cid]["gtok_d"], tc_d=a_outs[cid]["tc_d"],
            dch_d=a_outs[cid]["dch_d"], u_in=u_in, din_bc=din,
            d_bc=d_bc, nw_bc=nw_bc, wout=W_out,
            ident=consts["ident_bf"], ones128_bf=consts["ones128_bf"],
            ones1x128_f=consts["ones1x128_f"], scal=consts["scal"])
        maps.append(m)
    return maps

from contextlib import ExitStack
from concourse.aot_env import allow_ici_all_to_all
import ml_dtypes

RG8 = [[0, 1, 2, 3, 4, 5, 6, 7]]
RG4 = [[0, 1, 2, 3], [4, 5, 6, 7]]
CM_WLO, CM_WHI = 0, 1
CM_HSEL = 2
CM_S1, CM_C1, CM_S2, CM_C2 = 6, 7, 8, 9
CM_M0, CM_M1, CM_M2 = 10, 11, 12


def build_full(debug=False):
    nc = bass.Bass("TRN2", target_bir_lowering=False, debug=False)

    def di(n, sh, dt=F32):
        return nc.dram_tensor(n, sh, dt, kind="ExternalInput").ap()

    kdbg = "ExternalOutput" if debug else "Internal"

    def dint(n, sh, dt=F32, dbg=False):
        return nc.dram_tensor(n, sh, dt, kind=kdbg if dbg else "Internal").ap()

    triu = di("triu", [128, 128])
    masklt = di("masklt", [128, 128])
    ident = di("ident", [128, 128], BF16)
    ident_f32 = di("ident_f32", [128, 128], F32)
    ones_f = di("ones_f", [128, 128], F32)
    ones128_bf = di("ones128_bf", [128, 1], BF16)
    ones1x128_f = di("ones1x128_f", [1, 128], F32)
    scal = di("scal", [128, len(SCALAR_VALS)])
    cmask = di("cmask", [128, 16])
    x0T = di("x0T", [DM, SEQ])
    wq = di("wq", [DM, 512])
    wk = di("wk", [DM, 128])
    wv = di("wv", [DM, 128])
    cosT = di("cosT", [128, SEQ], BF16)
    sinT = di("sinT", [128, SEQ], BF16)
    rotm = di("rotm", [128, 128], BF16)
    maskc = di("maskc", [128, 4 * 512], BF16)
    x0Ts = di("x0Ts", [DM, TOK])
    Wo_n = di("Wo_n", [DM, DM])
    gup = di("gup", [2 * NFC, DM, 128])
    agT = di("agT", [DM, 128])
    auT = di("auT", [DM, 128])
    wrT = di("wrT", [DM, 8])
    bgT = di("bgT", [16, 8 * DFF])
    buT = di("buT", [16, 8 * DFF])
    adT_r = di("adT_r", [128, DFF])
    bdT = di("bdT", [128, DM])
    wdT_q = di("wdT_q", [4, NFC, 128, 512])
    pattern2 = di("pattern2", [8, 128], BF16)
    w_in = di("w_in", [DM, D_IN])
    convw = di("convw", [128, NCH * 4])
    convb = di("convb", [128, NCH])
    dtb_bc = di("dtb_bc", [128, NH])
    a_col = di("a_col", [NH, 1])
    d_bc = di("d_bc", [128, NH])
    nw_bc = di("nw_bc", [128, INTER], BF16)
    wout = di("wout", [INTER, DM])
    wT = di("wT", [DM, DM])
    biasc = di("biasc", [128, NDM])

    oT_snd = dint("oT_snd", [8 * 512, TOK], BF16)
    oT_rcv = dint("oT_rcv", [8 * 512, TOK], BF16)
    x1spill = dint("x1spill", [DM, TOK])
    y2spill = dint("y2spill", [DM, TOK])
    hspill = dint("hspill", [DFF, 512], BF16)
    x2T_d = dint("x2T_d", [DM, TOK], dbg=True)
    halo_snd = dint("halo_snd", [DM, HALO])
    halo_gat = dint("halo_gat", [4, DM, HALO])
    xbc_dram = dint("xbc_dram", [NCH, 128, TOKH], BF16)
    gate_d = dint("gate_d", [NCK, 128, INTER], BF16)
    xtok_d = dint("xtok_d", [NCK, 128, INTER], BF16)
    intra_d = dint("intra_d", [NCK, 128, INTER], BF16)
    cfm_d = dint("cfm_d", [128, TOK], BF16)
    gtok_d = dint("gtok_d", [NCK, 128, NH])
    tc_d = dint("tc_d", [NCK, 128, INTER])
    dch_d = dint("dch_d", [NCK, 128, NH])
    uspan_d = dint("uspan_d", [128, INTER])
    dspan_d = dint("dspan_d", [128, NH])
    uspan_gat = dint("uspan_gat", [4, 128, INTER])
    dspan_gat = dint("dspan_gat", [4, 128, NH])
    dbg_sin = dint("dbg_sin", [128, INTER], dbg=True)
    dbg_wl = dint("dbg_wl", [128, 4 * NH + 3 * NH], dbg=True)
    zbT_dram = dint("zbT_dram", [32, 128, TOK], BF16)
    x3T_d = dint("x3T_d", [DM, TOK], dbg=True)
    outT = nc.dram_tensor("outT", [DM, TOK], F32, kind="ExternalOutput").ap()

    pr = Prog(nc)
    ve = pr.engine_sem("vector")
    se = pr.engine_sem("scalar")
    cc = pr.sem("cc")

    a = nc.alloc_sbuf_tensor
    sc_sb = a("sc_sb", [128, len(SCALAR_VALS)], F32)
    idb = a("idb", [128, 128], BF16)
    idf = a("idf", [128, 128], F32)
    onesf = a("onesf", [128, 128], F32)
    one_bf = a("one_bf", [128, 1], BF16)
    one1f = a("one1f", [1, 128], F32)
    tri_sb = a("tri_sb", [128, 128], F32)
    mlt_sb = a("mlt_sb", [128, 128], F32)
    cm_sb = a("cm_sb", [128, 16], F32)
    ldc0 = pr.sem("ldc0")
    ldg0 = pr.sem("ldg0")
    pr.dma("sync", sc_sb[:], scal[:], inc=ldc0)
    pr.dma("sync", idf[:], ident_f32[:], inc=ldc0)
    pr.dma("sync", onesf[:], ones_f[:], inc=ldc0)
    pr.dma("sync", one1f[:], ones1x128_f[:], inc=ldc0)
    pr.dma("sync", tri_sb[:], triu[:], inc=ldc0)
    pr.dma("sync", mlt_sb[:], masklt[:], inc=ldc0)
    pr.dma("sync", cm_sb[:], cmask[:], inc=ldc0)
    ldc0_all = ldc0.n
    pr.dma("gpsimd", idb[:], ident[:], inc=ldg0)
    pr.dma("gpsimd", one_bf[:], ones128_bf[:], inc=ldg0)
    ldg0_all = ldg0.n

    def cmcol(j):
        return cm_sb[:, j:j + 1]

    def barrier(*waits):
        for eng in ("sync", "scalar", "vector", "tensor", "gpsimd"):
            pr.wait(eng, *waits)

    def phase_attn0():
        ctxP = ExitStack()

        def a(n, sh, dt):
            return ctxP.enter_context(nc.sbuf_tensor("a0" + n, sh, dt))
        maskc_sb = a("maskc_sb", [128, 4 * 512], BF16)
        cos_sb = a("cos_sb", [128, SEQ], BF16)
        sin_sb = a("sin_sb", [128, SEQ], BF16)
        rot_sb = a("rot_sb", [128, 128], BF16)
        qT = a("qT", [128, HQ * SEQ], BF16)
        kT = a("kT", [128, SEQ], BF16)
        v_sb = a("v_sb", [128, NCK0 * 129], BF16)
        spc = a("spc", [128, 128], F32)
        csc = a("csc", [128, 128], F32)
        tkc = a("tkc", [128, 128], F32)
        tqc = a("tqc", [128, 128], F32)
        negc16 = a("negc16", [128, 1], F32)
        ktl = a("ktl", [128, 128], BF16)
        kt2 = a("kt2", [128, 128], BF16)
        kt2t = a("kt2t", [128, 128], BF16)
        qtl = a("qtl", [128, 128], BF16)
        ebf = a("ebf", [128, 128], BF16)
        mgl = a("mgl", [128, 128], BF16)
        s_st = a("s_st", [128, 128], F32)
        s_bf = a("s_bf", [128, 128], BF16)
        rrow = a("rrow", [1, 128], F32)
        dcol = a("dcol", [128, 1], F32)
        cend = a("cend", [128, 1], F32)
        tmpb = a("tmpb", [128, 512], BF16)
        den_col = a("den_col", [128, 1], F32)
        y_sb = a("y_sb", [128, 128], F32)
        ocmb = a("ocmb", [128, 128], BF16)
        obuf = a("obuf", [128, 512], BF16)

        def pp(n, sh, dt):
            return ctxP.enter_context(nc.psum_tensor("a0" + n, sh, dt))
        ps_a = pp("ps_a", [128, 512], F32)
        ps_b = pp("ps_b", [128, 512], F32)
        ps_t0 = pp("ps_t0", [128, 512], BF16)
        ps_t1 = pp("ps_t1", [128, 512], BF16)
        ps_sc = pp("ps_sc", [128, 2048], F32)

        ldc = pr.sem("a0ldc")
        ldx = pr.sem("a0ldx")
        mm = pr.sem("a0mm")
        so = pr.sem("a0so")

        pr.dma("sync", cos_sb[:], cosT[:], inc=ldc)
        pr.dma("sync", sin_sb[:], sinT[:], inc=ldc)
        pr.dma("sync", rot_sb[:], rotm[:], inc=ldc)
        ldc_all = ldc.n

        ctx0 = ExitStack()
        wq_sb = ctx0.enter_context(nc.sbuf_tensor("a0wq_sb", [128, NDM * 512], BF16))
        wk_sb = ctx0.enter_context(nc.sbuf_tensor("a0wk_sb", [128, NDM * 128], BF16))
        wv_sb = ctx0.enter_context(nc.sbuf_tensor("a0wv_sb", [128, NDM * 128], BF16))
        xtg = [ctx0.enter_context(nc.sbuf_tensor(f"a0xtg{s}", [128, NDM * 512], BF16))
               for s in range(2)]

        pr.dma("gpsimd", maskc_sb[:], maskc[:], inc=ldx)
        wqc = wq.rearrange("(c p) t -> c p t", p=128)
        wkc = wk.rearrange("(c p) t -> c p t", p=128)
        wvc = wv.rearrange("(c p) t -> c p t", p=128)
        for c in range(NDM):
            pr.dma("gpsimd", wq_sb[:, c * 512:(c + 1) * 512], wqc[c], inc=ldx)
            pr.dma("gpsimd", wk_sb[:, c * 128:(c + 1) * 128], wkc[c], inc=ldx)
            pr.dma("gpsimd", wv_sb[:, c * 128:(c + 1) * 128], wvc[c], inc=ldx)
        ldx_all = ldx.n
        x0c = x0T.rearrange("(c p) t -> c p t", p=128)
        lx = [pr.sem("a0lx0"), pr.sem("a0lx1")]

        pr.add("vector", lambda e: e.memset(v_sb[:], 1.0), waits=[], inc=ve)
        pr.add("vector", lambda e: e.memset(s_st[:], 0.0), waits=[], inc=ve)
        pr.add("vector", lambda e: e.memset(s_bf[:], 0.0), waits=[], inc=ve)

        mm_tg_done = []
        for tg in range(8):
            sl = tg % 2
            for c in range(NDM):
                w = [(mm, mm_tg_done[tg - 2])] if (tg >= 2 and c == 0) else []
                pr.dma("gpsimd", xtg[sl][:, c * 512:(c + 1) * 512],
                       x0c[c][:, 512 * tg:512 * (tg + 1)], waits=w, inc=lx[sl])
            lx_full = lx[sl].n
            xv = xtg[sl]
            pp_i = 0
            for hs in range(HQ):
                ps = ps_a if pp_i % 2 == 0 else ps_b
                pp_i += 1

                def qfn(e, hs=hs, xv=xv, ps=ps):
                    last = None
                    for ic in range(NDM):
                        last = e.matmul(ps[:],
                                        wq_sb[:, ic * 512 + 128 * hs: ic * 512 + 128 * (hs + 1)],
                                        xv[:, ic * 512:(ic + 1) * 512],
                                        start=(ic == 0), stop=(ic == NDM - 1))
                    return last
                pr.add("tensor", qfn,
                       waits=[(ldx, ldx_all), (ldg0, ldg0_all), (lx[sl], lx_full),
                              (ve, ve.n)], inc=mm)
                pr.add("vector", lambda e, hs=hs, tg=tg, ps=ps: e.tensor_copy(
                    qT[:, hs * SEQ + 512 * tg: hs * SEQ + 512 * (tg + 1)], ps[:]),
                       waits=[(mm, mm.n)], inc=ve)
            ps = ps_a if pp_i % 2 == 0 else ps_b
            pp_i += 1

            def kfn(e, xv=xv, ps=ps):
                last = None
                for ic in range(NDM):
                    last = e.matmul(ps[:],
                                    wk_sb[:, ic * 128:(ic + 1) * 128],
                                    xv[:, ic * 512:(ic + 1) * 512],
                                    start=(ic == 0), stop=(ic == NDM - 1))
                return last
            pr.add("tensor", kfn, waits=[(ve, ve.n)], inc=mm)
            pr.add("vector", lambda e, tg=tg, ps=ps: e.tensor_copy(
                kT[:, 512 * tg:512 * (tg + 1)], ps[:]), waits=[(mm, mm.n)], inc=ve)
            ps = ps_a if pp_i % 2 == 0 else ps_b

            def vfn(e, xv=xv, ps=ps):
                last = None
                for ic in range(NDM):
                    last = e.matmul(ps[:],
                                    wv_sb[:, ic * 128:(ic + 1) * 128],
                                    xv[:, ic * 512:(ic + 1) * 512],
                                    start=(ic == 0), stop=(ic == NDM - 1))
                return last
            pr.add("tensor", vfn, waits=[(ve, ve.n)], inc=mm)
            pr.add("vector", lambda e, ps=ps: e.tensor_copy(obuf[:], ps[:]),
                   waits=[(mm, mm.n)], inc=ve)
            for j in range(4):
                ck = 4 * tg + j
                pst = ps_t0 if j % 2 == 0 else ps_t1
                pr.add("tensor", lambda e, j=j, pst=pst: e.transpose(
                    pst[:, 0:128], obuf[:, 128 * j:128 * (j + 1)], idb[:]),
                       waits=[(ve, ve.n)], inc=mm)
                pr.add("vector", lambda e, ck=ck, pst=pst: e.tensor_copy(
                    v_sb[:, ck * 129:ck * 129 + 128], pst[:, 0:128]),
                       waits=[(mm, mm.n)], inc=ve)
            mm_tg_done.append(mm.n)
        ve_proj = ve.n
        ctx0.close()

        o_gla = a("o_gla", [128, NCK0 * HQ * 128], BF16)
        rq_col = a("rq_col", [128, NCK0 * HQ], F32)
        p_exp = a("p_exp", [128, 32 * 512], BF16)

        for ck in range(NCK0):
            ksl = np.s_[:, 128 * ck:128 * (ck + 1)]
            pr.add("scalar", lambda e, ksl=ksl: e.activation(spc[:], kT[ksl], AF.Exp,
                                                             scale=-1.0),
                   waits=[(ve, ve_proj), (ldc, ldc_all), (ldc0, ldc0_all)], inc=se)
            pr.add("scalar", lambda e: e.activation(
                spc[:], spc[:], AF.Ln, bias=sc_sb[:, SCALARS["one"]:SCALARS["one"] + 1]),
                   waits=[], inc=se)
            pr.add("vector", lambda e: e.tensor_tensor_scan(
                csc[:], onesf[:], spc[:], 0.0, ALU.mult, ALU.add),
                   waits=[(se, se.n)], inc=ve)
            pr.add("vector", lambda e: e.tensor_copy(cend[:], csc[:, 127:128]),
                   waits=[], inc=ve)
            pr.add("vector", lambda e: e.tensor_scalar_mul(negc16[:], cend[:],
                                                           -1.0 / GATE_NORM),
                   waits=[], inc=ve)
            pr.add("scalar", lambda e: e.activation(dcol[:], cend[:], AF.Exp,
                                                    scale=-1.0 / GATE_NORM),
                   waits=[(ve, ve.n)], inc=se)
            pr.add("scalar", lambda e, ksl=ksl: e.activation(ebf[:], kT[ksl], AF.Exp),
                   waits=[], inc=se)
            pr.add("tensor", lambda e: e.matmul(ps_sc[0:1, 0:128], one_bf[:], ebf[:],
                                                start=True, stop=True),
                   waits=[(se, se.n), (ve, ve.n)], inc=mm)
            pr.add("vector", lambda e: e.reciprocal(rrow[:], ps_sc[0:1, 0:128]),
                   waits=[(mm, mm.n)], inc=ve)
            pr.add("tensor", lambda e: e.matmul(ps_sc[:, 1024:1152], one1f[:], rrow[:],
                                                start=True, stop=True),
                   waits=[(ve, ve.n)], inc=mm)
            mm_rk = mm.n
            pr.add("vector", lambda e, ksl=ksl: e.scalar_tensor_tensor(
                tkc[:], csc[:], 1.0 / GATE_NORM, kT[ksl], ALU.mult, ALU.add),
                   waits=[], inc=ve)
            pr.add("scalar", lambda e: e.activation(ktl[:], tkc[:], AF.Exp),
                   waits=[(ve, ve.n)], inc=se)
            pr.add("vector", lambda e: e.tensor_tensor(ktl[:], ktl[:],
                                                       ps_sc[:, 1024:1152], ALU.mult),
                   waits=[(se, se.n), (mm, mm_rk)], inc=ve)
            pr.add("scalar", lambda e: e.activation(kt2[:], tkc[:], AF.Exp,
                                                    bias=negc16[:, 0:1]),
                   waits=[(ve, ve.n)], inc=se)
            pr.add("vector", lambda e: e.tensor_tensor(kt2[:], kt2[:],
                                                       ps_sc[:, 1024:1152], ALU.mult),
                   waits=[(se, se.n)], inc=ve)
            pr.add("tensor", lambda e: e.transpose(ps_t0[:, 0:128], kt2[:], idb[:]),
                   waits=[(ve, ve.n)], inc=mm)
            pr.add("vector", lambda e: e.tensor_copy(kt2t[:], ps_t0[:, 0:128]),
                   waits=[(mm, mm.n)], inc=ve)
            for h in range(HQ):
                qsl = np.s_[:, h * SEQ + 128 * ck: h * SEQ + 128 * (ck + 1)]
                pr.add("scalar", lambda e, qsl=qsl: e.activation(ebf[:], qT[qsl], AF.Exp),
                       waits=[(ve, ve.n)], inc=se)
                pr.add("tensor", lambda e: e.matmul(ps_sc[0:1, 0:128], one_bf[:], ebf[:],
                                                    start=True, stop=True),
                       waits=[(se, se.n)], inc=mm)
                pr.add("vector", lambda e: e.reciprocal(rrow[:], ps_sc[0:1, 0:128]),
                       waits=[(mm, mm.n)], inc=ve)
                pr.add("tensor", lambda e: e.transpose(ps_sc[:, 1024:1025], rrow[:],
                                                       idf[0:1, 0:1]),
                       waits=[(ve, ve.n)], inc=mm)
                pr.add("vector", lambda e, ck=ck, h=h: e.tensor_copy(
                    rq_col[:, ck * HQ + h: ck * HQ + h + 1], ps_sc[:, 1024:1025]),
                       waits=[(mm, mm.n)], inc=ve)
                pr.add("vector", lambda e, qsl=qsl: e.scalar_tensor_tensor(
                    tqc[:], csc[:], -1.0 / GATE_NORM, qT[qsl], ALU.mult, ALU.add),
                       waits=[], inc=ve)
                pr.add("scalar", lambda e: e.activation(
                    qtl[:], tqc[:], AF.Exp,
                    bias=sc_sb[:, SCALARS["ln05"]:SCALARS["ln05"] + 1]),
                       waits=[(ve, ve.n)], inc=se)
                pr.add("tensor", lambda e: e.matmul(ps_sc[:, 256:384], ktl[:], qtl[:],
                                                    start=True, stop=True),
                       waits=[(se, se.n)], inc=mm)
                pr.add("vector", lambda e: e.tensor_tensor(mgl[:], ps_sc[:, 256:384],
                                                           tri_sb[:], ALU.mult),
                       waits=[(mm, mm.n)], inc=ve)

                def ofn(e, ck=ck):
                    e.matmul(ps_sc[:, 512:640], mgl[:],
                             v_sb[:, ck * 129:ck * 129 + 128], start=True, stop=False)
                    return e.matmul(ps_sc[:, 512:640], qtl[:], s_bf[:],
                                    start=False, stop=True)
                pr.add("tensor", ofn, waits=[(ve, ve.n), (se, se.n)], inc=mm)
                pr.add("scalar", lambda e, ck=ck, h=h: e.activation(
                    o_gla[:, (ck * HQ + h) * 128:(ck * HQ + h + 1) * 128],
                    ps_sc[:, 512:640], AF.Copy), waits=[(mm, mm.n)], inc=se)

            def cfn(e, ck=ck):
                return e.matmul(ps_sc[:, 1536:1664], kt2t[:],
                                v_sb[:, ck * 129:ck * 129 + 128], start=True, stop=True)
            pr.add("tensor", cfn, waits=[(ve, ve.n), (se, se.n)], inc=mm)
            pr.add("vector", lambda e: e.scalar_tensor_tensor(
                s_st[:], s_st[:], dcol[:, 0:1], ps_sc[:, 1536:1664], ALU.mult, ALU.add),
                   waits=[(mm, mm.n)], inc=ve)
            pr.add("vector", lambda e: e.tensor_copy(s_bf[:], s_st[:]), waits=[], inc=ve)
        ve_gla = ve.n
        se_gla = se.n

        chunks = [(qT, h * SEQ + 512 * J, 512 * J) for h in range(HQ) for J in range(8)]
        chunks += [(kT, 512 * J, 512 * J) for J in range(8)]
        tmp_ve = []
        for i, (buf, c0, t0) in enumerate(chunks):
            ps = ps_a if i % 2 == 0 else ps_b
            sl = np.s_[:, c0:c0 + 512]
            if i < 2:
                w = [(ve, ve_gla), (se, se_gla)]
            else:
                w = [(ve, tmp_ve[i - 2])]
            pr.add("tensor", lambda e, sl=sl, ps=ps, buf=buf: e.matmul(
                ps[:], rot_sb[:], buf[sl], start=True, stop=True), waits=w, inc=mm)
            pr.add("vector", lambda e, sl=sl, buf=buf, t0=t0: e.tensor_tensor(
                buf[sl], buf[sl], cos_sb[:, t0:t0 + 512], ALU.mult),
                   waits=[(mm, mm.n)], inc=ve)
            pr.add("vector", lambda e, ps=ps, t0=t0: e.tensor_tensor(
                tmpb[:], sin_sb[:, t0:t0 + 512], ps[:], ALU.mult),
                   waits=[], inc=ve)
            tmp_ve.append(ve.n)
            pr.add("vector", lambda e, sl=sl, buf=buf: e.tensor_tensor(
                buf[sl], buf[sl], tmpb[:], ALU.add), waits=[], inc=ve)
        ve_rope = ve.n

        last_av_mm = [mm.n]
        for h in range(HQ):
            for J in range(8):
                for kg in range(J + 1):
                    def sfn(e, h=h, J=J, kg=kg):
                        last = None
                        for s4 in range(4):
                            ki = 4 * kg + s4
                            last = e.matmul(ps_sc[:, 512 * s4:512 * (s4 + 1)],
                                            kT[:, 128 * ki:128 * (ki + 1)],
                                            qT[:, h * SEQ + 512 * J: h * SEQ + 512 * (J + 1)],
                                            start=True, stop=(kg != J))
                        if kg == J:
                            for s4 in range(4):
                                last = e.matmul(ps_sc[:, 512 * s4:512 * (s4 + 1)], idb[:],
                                                maskc_sb[:, 512 * s4:512 * (s4 + 1)],
                                                start=False, stop=True)
                        return last
                    w = [(ve, ve_rope), (se, se.n)]
                    if kg == 0:
                        w.append((mm, last_av_mm[-1]))
                    pr.add("tensor", sfn, waits=w, inc=mm)
                    mm_s = mm.n
                    for half in range(2):
                        pr.add("scalar", lambda e, kg=kg, half=half: e.activation(
                            p_exp[:, 2048 * kg + 1024 * half: 2048 * kg + 1024 * (half + 1)],
                            ps_sc[:, 1024 * half:1024 * (half + 1)], AF.Exp, scale=RSC),
                               waits=[(mm, mm_s)], inc=se)
                se_exp = se.n
                for s in range(4):
                    nki = 4 * J + s + 1
                    ps = ps_a if s % 2 == 0 else ps_b

                    def avfn(e, s=s, nki=nki, ps=ps):
                        last = None
                        for ki in range(nki):
                            last = e.matmul(
                                ps[:, 0:129],
                                p_exp[:, 512 * ki + 128 * s: 512 * ki + 128 * (s + 1)],
                                v_sb[:, ki * 129:(ki + 1) * 129],
                                start=(ki == 0), stop=(ki == nki - 1))
                        return last
                    pr.add("tensor", avfn, waits=[(se, se_exp), (ve, ve.n)], inc=mm)
                    last_av_mm.append(mm.n)
                    pr.add("vector", lambda e, ps=ps: e.reciprocal(den_col[:],
                                                                   ps[:, 128:129]),
                           waits=[(mm, mm.n)], inc=ve)
                    pr.add("vector", lambda e: e.tensor_scalar_mul(den_col[:],
                                                                   den_col[:], 0.5),
                           waits=[], inc=ve)
                    pr.add("scalar", lambda e, ps=ps: e.activation(
                        y_sb[:], ps[:, 0:128], AF.Copy, scale=den_col[:, 0:1]),
                           waits=[(ve, ve.n)], inc=se)
                    ck_g = 4 * J + s
                    pr.add("vector", lambda e, ck_g=ck_g, h=h: e.scalar_tensor_tensor(
                        ocmb[:], o_gla[:, (ck_g * HQ + h) * 128:(ck_g * HQ + h + 1) * 128],
                        rq_col[:, ck_g * HQ + h: ck_g * HQ + h + 1], y_sb[:],
                        ALU.mult, ALU.add), waits=[(se, se.n)], inc=ve)
                    pst = ps_t0 if s % 2 == 0 else ps_t1
                    pr.add("tensor", lambda e, pst=pst: e.transpose(
                        pst[:, 0:128], ocmb[:], idb[:]), waits=[(ve, ve.n)], inc=mm)
                    pr.add("vector", lambda e, s=s, pst=pst: e.tensor_copy(
                        obuf[:, 128 * s:128 * (s + 1)], pst[:, 0:128]),
                           waits=[(mm, mm.n), (so, so.n)], inc=ve)
                jb = J // 2
                off = 512 * (J % 2)
                r0 = 512 * jb + 128 * h
                pr.dma("sync", oT_snd[r0:r0 + 128, off:off + 512], obuf[:],
                       waits=[(ve, ve.n)], inc=so)
                pr.dma("sync", oT_snd[2048 + r0:2048 + r0 + 128, off:off + 512], obuf[:],
                       waits=[], inc=so)
        ctxP.close()
        return so

    a0so = phase_attn0()
    barrier((a0so, a0so.n))

    def cc_a2a(e):
        with allow_ici_all_to_all():
            return e.collective_compute(
                "AllToAll", mybir.AluOpType.bypass, replica_groups=RG8,
                ins=[oT_snd.tensor.ap().opt()], outs=[oT_rcv.tensor.ap().opt()])
    pr.add("gpsimd", cc_a2a, waits=[(a0so, a0so.n)], inc=cc, inc_n=1, is_dma=True)

    def phase_mlp0():
        ctxP = ExitStack()

        def a(n, sh, dt):
            return ctxP.enter_context(nc.sbuf_tensor("m0" + n, sh, dt))
        pat2 = a("pat2", [8, 128], BF16)
        stat_m = a("stat_m", [1, TOK], F32)
        stat_v = a("stat_v", [1, TOK], F32)
        rstd = a("rstd", [1, TOK], F32)
        mrstd = a("mrstd", [1, TOK], F32)
        ybf = a("ybf", [128, TOK], BF16)
        ysq = a("ysq", [128, TOK], BF16)
        xb = a("xb", [128, NDM * TOK], BF16)

        def pp(n, sh, dt):
            return ctxP.enter_context(nc.psum_tensor("m0" + n, sh, dt))
        ps_a = pp("ps_a", [128, 512], F32)
        ps_b = pp("ps_b", [128, 512], F32)
        ps_cf = pp("ps_cf", [128, 512], F32)
        ps_tr = pp("ps_tr", [128, 1024], BF16)
        ps_dn = pp("ps_dn", [128, 2048], F32)

        ldc = pr.sem("m0ldc")
        ldga = pr.sem("m0ldga")
        ldwo = pr.sem("m0ldwo")
        ldsm = pr.sem("m0ldsm")
        gws = [pr.sem(f"m0gw{i}") for i in range(4)]
        bgs = pr.sem("m0bgs")
        bus = pr.sem("m0bus")
        wds = [pr.sem(f"m0wds{i}") for i in range(2)]
        hhs = [pr.sem(f"m0hhs{i}") for i in range(2)]
        xls = [pr.sem(f"m0xls{i}") for i in range(2)]
        stx1 = pr.sem("m0stx1")
        sth = pr.sem("m0sth")
        sy2 = [pr.sem(f"m0sy2{i}") for i in range(2)]
        sout = pr.sem("m0sout")
        shalo = pr.sem("m0shalo")
        ldy2 = pr.sem("m0ldy2")
        mm = pr.sem("m0mm")

        x0c = x0Ts.rearrange("(c p) t -> c p t", p=128)
        olo = oT_rcv[0:2048].rearrange("(c p) t -> c p t", p=128)
        ohi = oT_rcv[2048:4096].rearrange("(c p) t -> c p t", p=128)
        woc = Wo_n.rearrange("(c p) t -> c p t", p=128)

        def emit_ln(buf, resid_ready, cast_to=None, spill=None, spill_sem=None):
            stat_mms = []
            for c in range(NDM):
                pr.add("vector", lambda e, c=c: e.tensor_copy(
                    ybf[:], buf[:, c * TOK:(c + 1) * TOK]),
                       waits=[resid_ready] + ([(mm, stat_mms[-1])] if c else []), inc=ve)
                pr.add("scalar", lambda e: e.activation(ysq[:], ybf[:], AF.Square),
                       waits=[(ve, ve.n)], inc=se)

                def statfn(e, c=c):
                    last = None
                    for j in range(2):
                        last = e.matmul(ps_dn[0:1, 512 * j:512 * (j + 1)], one_bf[:],
                                        ybf[:, 512 * j:512 * (j + 1)],
                                        start=(c == 0), stop=(c == NDM - 1))
                    for j in range(2):
                        last = e.matmul(ps_dn[0:1, TOK + 512 * j:TOK + 512 * (j + 1)],
                                        one_bf[:], ysq[:, 512 * j:512 * (j + 1)],
                                        start=(c == 0), stop=(c == NDM - 1))
                    return last
                pr.add("tensor", statfn, waits=[(se, se.n), (ve, ve.n)], inc=mm)
                stat_mms.append(mm.n)
            pr.add("vector", lambda e: e.tensor_scalar_mul(stat_m[:], ps_dn[0:1, 0:TOK],
                                                           1.0 / DM),
                   waits=[(mm, mm.n)], inc=ve)
            pr.add("vector", lambda e: e.tensor_scalar_mul(
                stat_v[:], ps_dn[0:1, TOK:2 * TOK], 1.0 / DM), waits=[], inc=ve)
            pr.add("vector", lambda e: e.tensor_mul(mrstd[:], stat_m[:], stat_m[:]),
                   waits=[], inc=ve)
            pr.add("vector", lambda e: e.tensor_sub(stat_v[:], stat_v[:], mrstd[:]),
                   waits=[], inc=ve)
            pr.add("scalar", lambda e: e.activation(
                rstd[:], stat_v[:], AF.Sqrt,
                bias=sc_sb[0:1, SCALARS["eps"]:SCALARS["eps"] + 1]),
                   waits=[(ve, ve.n), (ldc0, ldc0_all)], inc=se)
            pr.add("vector", lambda e: e.reciprocal(rstd[:], rstd[:]),
                   waits=[(se, se.n)], inc=ve)
            pr.add("vector", lambda e: e.tensor_mul(mrstd[:], stat_m[:], rstd[:]),
                   waits=[], inc=ve)

            def bcfn(e):
                last = None
                for j in range(2):
                    last = e.matmul(ps_dn[:, 512 * j:512 * (j + 1)], one1f[:],
                                    rstd[0:1, 512 * j:512 * (j + 1)],
                                    start=True, stop=True)
                for j in range(2):
                    last = e.matmul(ps_dn[:, TOK + 512 * j:TOK + 512 * (j + 1)], one1f[:],
                                    mrstd[0:1, 512 * j:512 * (j + 1)],
                                    start=True, stop=True)
                return last
            pr.add("tensor", bcfn, waits=[(ve, ve.n)], inc=mm)
            mm_bc = mm.n
            for c in range(NDM):
                sl = np.s_[:, c * TOK:(c + 1) * TOK]
                pr.add("vector", lambda e, sl=sl: e.tensor_mul(buf[sl], buf[sl],
                                                               ps_dn[:, 0:TOK]),
                       waits=[(mm, mm_bc)], inc=ve)
                pr.add("vector", lambda e, sl=sl: e.tensor_sub(buf[sl], buf[sl],
                                                               ps_dn[:, TOK:2 * TOK]),
                       waits=[], inc=ve)
                if cast_to is not None:
                    pr.add("vector", lambda e, sl=sl: e.tensor_copy(cast_to[sl], buf[sl]),
                           waits=[], inc=ve)
                if spill is not None:
                    pr.dma("sync", spill.rearrange("(c p) t -> c p t", p=128)[c], buf[sl],
                           waits=[(ve, ve.n)], inc=spill_sem)

        ctxA = ExitStack()
        xs = ctxA.enter_context(nc.sbuf_tensor("m0xs", [128, NDM * TOK], F32))
        ob = ctxA.enter_context(nc.sbuf_tensor("m0ob", [128, NDM * TOK], BF16))
        obt = [ctxA.enter_context(nc.sbuf_tensor(f"m0obt{i}", [128, TOK], BF16))
               for i in range(2)]
        woh = ctxA.enter_context(nc.sbuf_tensor("m0woh", [128, NDM * TOK], BF16))

        for c in range(NDM):
            pr.dma("sync", xs[:, c * TOK:(c + 1) * TOK], x0c[c], inc=ldc)
        ld_x0 = ldc.n
        pr.dma("gpsimd", pat2[:], pattern2[:], inc=ldga)
        lo_s = pr.sem("m0lo")
        ohs = [pr.sem(f"m0oh{i}") for i in range(2)]
        cmb_ve = []
        for c in range(NDM):
            sl = np.s_[:, c * TOK:(c + 1) * TOK]
            slot = c % 2
            w = [(cc, 1)] if c == 0 else []
            pr.dma("gpsimd", ob[sl], olo[c], waits=w, inc=lo_s)
            lo_c = lo_s.n
            wh = [(ve, cmb_ve[c - 2])] if c >= 2 else []
            pr.dma("gpsimd", obt[slot][:], ohi[c], waits=wh, inc=ohs[slot])
            hi_c = ohs[slot].n
            pr.add("vector", lambda e, sl=sl: e.tensor_scalar_mul(
                ob[sl], ob[sl], cmcol(CM_WLO)),
                   waits=[(lo_s, lo_c), (ldc0, ldc0_all)], inc=ve)
            pr.add("vector", lambda e, sl=sl, slot=slot: e.scalar_tensor_tensor(
                ob[sl], obt[slot][:], cmcol(CM_WHI), ob[sl], ALU.mult, ALU.add),
                   waits=[(ohs[slot], hi_c)], inc=ve)
            cmb_ve.append(ve.n)
        ldga_all = ldga.n
        ve_cmb = ve.n

        resid_mm = []
        resid_ve = []
        k = 0
        for h in range(2):
            for c in range(NDM):
                w = [(mm, resid_mm[-1])] if (h == 1 and c == 0) else []
                pr.dma("gpsimd", woh[:, c * TOK:(c + 1) * TOK],
                       woc[c][:, 1024 * h:1024 * (h + 1)], waits=w, inc=ldwo)
            ldwo_h = ldwo.n
            for dmc_i in range(8):
                dmc = 8 * h + dmc_i
                for tt in range(2):
                    ps = ps_a if k % 2 == 0 else ps_b

                    def mmfn(e, dmc_i=dmc_i, tt=tt, ps=ps):
                        last = None
                        for ocn in range(NDM):
                            last = e.matmul(
                                ps[:],
                                woh[:, ocn * TOK + 128 * dmc_i: ocn * TOK + 128 * (dmc_i + 1)],
                                ob[:, ocn * TOK + 512 * tt: ocn * TOK + 512 * (tt + 1)],
                                start=(ocn == 0), stop=(ocn == NDM - 1))
                        return last
                    waits = [(ldwo, ldwo_h), (ldga, ldga_all), (ve, ve_cmb)]
                    if k >= 2:
                        waits.append((ve, resid_ve[k - 2]))
                    pr.add("tensor", mmfn, waits=waits, inc=mm)
                    resid_mm.append(mm.n)
                    sl = np.s_[:, dmc * TOK + 512 * tt: dmc * TOK + 512 * (tt + 1)]
                    pr.add("vector", lambda e, sl=sl, ps=ps: e.tensor_add(xs[sl], xs[sl],
                                                                          ps[:]),
                           waits=[(mm, mm.n), (ldc, ld_x0)], inc=ve)
                    resid_ve.append(ve.n)
                    k += 1

        emit_ln(xs, (ve, ve.n), cast_to=xb, spill=x1spill, spill_sem=stx1)
        ve_ln1 = ve.n
        stx1_all = stx1.n
        ctxA.close()

        barrier((ve, ve_ln1), (stx1, stx1_all))
        ctxB = ExitStack()

        def B(n, sh, dt):
            return ctxB.enter_context(nc.sbuf_tensor("m0" + n, sh, dt))
        gb = B("gb", [128, BFC * 512], BF16)
        ub = B("ub", [128, BFC * 512], BF16)
        agb = B("agb", [128, NDM * 128], BF16)
        aub = B("aub", [128, NDM * 128], BF16)
        wrb = B("wrb", [128, NDM * 8], BF16)
        bgb = B("bgb", [16, BFC * 128], BF16)
        bub = B("bub", [16, BFC * 128], BF16)
        adb = B("adb", [128, DFF], BF16)
        bdb = B("bdb", [128, DM], BF16)
        cg_sb = B("cg_sb", [16, 8 * 512], BF16)
        cu_sb = B("cu_sb", [16, 8 * 512], BF16)
        he_sb = [B(f"he{i}", [128, 512], BF16) for i in range(2)]
        tg_sb = [B(f"tg{i}", [128, 512], BF16) for i in range(2)]
        wpat = B("wpat", [128, 4 * 128], F32)
        cw_sb = B("cw_sb", [128, 4 * 128], BF16)
        cwT_sb = B("cwT_sb", [128, 4 * 128], BF16)
        gw = [B(f"gw{i}", [128, DM], BF16) for i in range(4)]
        wd_t = [B(f"wdt{i}", [128, 512], BF16) for i in range(2)]
        hh_t = [B(f"hht{i}", [128, 512], BF16) for i in range(2)]
        rlog = B("rlog", [128, 8], F32)
        rsum = B("rsum", [128, 1], F32)
        rmax = B("rmax", [128, 1], F32)
        rp2 = B("rp2", [128, 8], F32)
        rmask = B("rmask", [128, 8], F32)
        rmask2 = B("rmask2", [128, 8], F32)
        rw_sb = B("rw_sb", [128, 8], F32)
        wfm = B("wfm", [8, 512], BF16)
        x1rl = [B(f"x1rl{i}", [128, 512], F32) for i in range(2)]

        hs_d = hspill.rearrange("(c p) t -> c p t", p=128)
        agc = agT.rearrange("(c p) t -> c p t", p=128)
        auc = auT.rearrange("(c p) t -> c p t", p=128)
        wrc = wrT.rearrange("(c p) t -> c p t", p=128)
        for c in range(NDM):
            pr.dma("gpsimd", agb[:, c * 128:(c + 1) * 128], agc[c], inc=ldsm)
            pr.dma("gpsimd", aub[:, c * 128:(c + 1) * 128], auc[c], inc=ldsm)
            pr.dma("gpsimd", wrb[:, c * 8:(c + 1) * 8], wrc[c], inc=ldsm)
        pr.dma("gpsimd", adb[:], adT_r[:], inc=ldsm)
        pr.dma("gpsimd", bdb[:], bdT[:], inc=ldsm)
        ldsm_all = ldsm.n

        x1s = x1spill.rearrange("(c p) t -> c p t", p=128)
        y2s_d = y2spill.rearrange("(c p) t -> c p t", p=128)

        for tt in range(2):
            barrier((ve, ve.n), (mm, mm.n), (se, se.n),
                    (sth, sth.n), (sy2[0], sy2[0].n), (sy2[1], sy2[1].n))
            first_r = True
            for ttt in range(4):
                tok0 = 512 * tt + 128 * ttt

                def rlfn(e, tok0=tok0):
                    last = None
                    for ic in range(NDM):
                        last = e.matmul(ps_cf[:, 0:8],
                                        xb[:, ic * TOK + tok0: ic * TOK + tok0 + 128],
                                        wrb[:, ic * 8:(ic + 1) * 8],
                                        start=(ic == 0), stop=(ic == NDM - 1))
                    return last
                waits = [(ldsm, ldsm_all), (ve, ve.n)] if first_r else [(ve, ve.n)]
                first_r = False
                pr.add("tensor", rlfn, waits=waits, inc=mm)
                pr.add("scalar", lambda e: e.activation(rlog[:], ps_cf[:, 0:8], AF.Exp),
                       waits=[(mm, mm.n)], inc=se)
                pr.add("vector", lambda e: e.reduce_sum(rsum[:], rlog[:], axis=AX.X),
                       waits=[(se, se.n)], inc=ve)
                pr.add("vector", lambda e: e.reciprocal(rsum[:], rsum[:]), waits=[], inc=ve)
                pr.add("vector", lambda e: e.tensor_scalar_mul(rlog[:], rlog[:],
                                                               rsum[:, 0:1]),
                       waits=[], inc=ve)
                pr.add("vector", lambda e: e.reduce_max(rmax[:], rlog[:], axis=AX.X),
                       waits=[], inc=ve)
                pr.add("vector", lambda e: e.tensor_single_scalar(
                    rmask[:], rlog[:], rmax[:, 0:1], ALU.is_ge), waits=[], inc=ve)
                pr.add("vector", lambda e: e.scalar_tensor_tensor(
                    rp2[:], rmask[:], BIG_NEG, rlog[:], ALU.mult, ALU.add),
                       waits=[], inc=ve)
                pr.add("vector", lambda e: e.reduce_max(rmax[:], rp2[:], axis=AX.X),
                       waits=[], inc=ve)
                pr.add("vector", lambda e: e.tensor_single_scalar(
                    rmask2[:], rp2[:], rmax[:, 0:1], ALU.is_ge), waits=[], inc=ve)
                pr.add("vector", lambda e: e.tensor_add(rmask[:], rmask[:], rmask2[:]),
                       waits=[], inc=ve)
                pr.add("vector", lambda e: e.tensor_mul(rw_sb[:], rlog[:], rmask[:]),
                       waits=[], inc=ve)
                ve_rw = ve.n
                pr.add("tensor", lambda e: e.transpose(ps_cf[0:8, 128:256], rw_sb[:],
                                                       idf[:]),
                       waits=[(ve, ve_rw)], inc=mm)
                pr.add("vector", lambda e, ttt=ttt: e.tensor_copy(
                    wfm[:, 128 * ttt:128 * (ttt + 1)], ps_cf[0:8, 128:256]),
                       waits=[(mm, mm.n)], inc=ve)
            for ttt in range(4):
                pr.add("tensor", lambda e, ttt=ttt: e.matmul(
                    ps_cf[:, 128:256], wfm[:, 128 * ttt:128 * (ttt + 1)], pat2[:],
                    start=True, stop=True), waits=[(ve, ve.n)], inc=mm)
                pr.add("vector", lambda e, ttt=ttt: e.tensor_copy(
                    wpat[:, 128 * ttt:128 * (ttt + 1)], ps_cf[:, 128:256]),
                       waits=[(mm, mm.n)], inc=ve)
            ve_wpat = ve.n

            cf_done = []
            for e_i in range(8):
                def cgfn(e, e_i=e_i, tt=tt):
                    last = None
                    for ic in range(NDM):
                        last = e.matmul(ps_a[0:16, :],
                                        agb[:, ic * 128 + 16 * e_i: ic * 128 + 16 * e_i + 16],
                                        xb[:, ic * TOK + 512 * tt: ic * TOK + 512 * (tt + 1)],
                                        start=(ic == 0), stop=(ic == NDM - 1))
                    return last

                def cufn(e, e_i=e_i, tt=tt):
                    last = None
                    for ic in range(NDM):
                        last = e.matmul(ps_b[0:16, :],
                                        aub[:, ic * 128 + 16 * e_i: ic * 128 + 16 * e_i + 16],
                                        xb[:, ic * TOK + 512 * tt: ic * TOK + 512 * (tt + 1)],
                                        start=(ic == 0), stop=(ic == NDM - 1))
                    return last
                waits = [(se, cf_done[-1])] if e_i else [(ve, ve_wpat)]
                pr.add("tensor", cgfn, waits=waits, inc=mm)
                pr.add("tensor", cufn, waits=[], inc=mm)
                mm_c = mm.n
                pr.add("scalar", lambda e, e_i=e_i: e.activation(
                    cg_sb[:, 512 * e_i:512 * (e_i + 1)], ps_a[0:16, :], AF.Copy,
                    scale=2.0), waits=[(mm, mm_c)], inc=se)
                pr.add("scalar", lambda e, e_i=e_i: e.activation(
                    cu_sb[:, 512 * e_i:512 * (e_i + 1)], ps_b[0:16, :], AF.Copy,
                    scale=2.0), waits=[], inc=se)
                cf_done.append(se.n)
            se_cf = se.n

            first_cf = True
            cf_mms = []
            for blk in range(NBLK):
                base_ve = []
                base_mms = []
                for j in range(BFC):
                    fc = blk * BFC + j
                    sg_, su = gws[2 * (j % 2)], gws[2 * (j % 2) + 1]
                    bg, bu = gw[2 * (j % 2)], gw[2 * (j % 2) + 1]
                    w = [(mm, base_mms[j - 2])] if j >= 2 else [(ve, ve.n), (mm, mm.n)]
                    pr.dma("gpsimd", bg[:].rearrange("p (c f) -> p c f", f=128),
                           gup[2 * fc].rearrange("(c p) f -> p c f", p=128),
                           waits=w, inc=sg_)
                    pr.dma("gpsimd", bu[:].rearrange("p (c f) -> p c f", f=128),
                           gup[2 * fc + 1].rearrange("(c p) f -> p c f", p=128), inc=su)

                    def gmm(e, bg=bg, tt=tt):
                        last = None
                        for ic in range(NDM):
                            last = e.matmul(ps_a[:], bg[:, ic * 128:(ic + 1) * 128],
                                            xb[:, ic * TOK + 512 * tt: ic * TOK + 512 * (tt + 1)],
                                            start=(ic == 0), stop=(ic == NDM - 1))
                        return last

                    def umm(e, bu=bu, tt=tt):
                        last = None
                        for ic in range(NDM):
                            last = e.matmul(ps_b[:], bu[:, ic * 128:(ic + 1) * 128],
                                            xb[:, ic * TOK + 512 * tt: ic * TOK + 512 * (tt + 1)],
                                            start=(ic == 0), stop=(ic == NDM - 1))
                        return last
                    waits = [(sg_, sg_.n), (su, su.n), (se, se_cf)]
                    if j >= 1:
                        waits.append((ve, base_ve[-1]))
                    pr.add("tensor", gmm, waits=waits, inc=mm)
                    pr.add("tensor", umm, waits=[], inc=mm)
                    mm_b = mm.n
                    base_mms.append(mm_b)
                    pr.add("vector", lambda e, j=j: e.tensor_copy(
                        gb[:, 512 * j:512 * (j + 1)], ps_a[:]),
                           waits=[(mm, mm_b), (sth, sth.n)], inc=ve)
                    pr.add("vector", lambda e, j=j: e.tensor_copy(
                        ub[:, 512 * j:512 * (j + 1)], ps_b[:]), waits=[], inc=ve)
                    base_ve.append(ve.n)
                ve_base = ve.n

                ex_ve = []
                for e_i in range(8):
                    pr.dma("gpsimd", bgb[:], bgT[0:16, e_i * DFF + blk * BFC * 128:
                                                 e_i * DFF + (blk + 1) * BFC * 128],
                           waits=[(mm, mm.n)], inc=bgs)
                    pr.dma("gpsimd", bub[:], buT[0:16, e_i * DFF + blk * BFC * 128:
                                                 e_i * DFF + (blk + 1) * BFC * 128],
                           inc=bus)
                    ldg_be = [(bgs, bgs.n), (bus, bus.n)]
                    for j in range(BFC):
                        fc = blk * BFC + j

                        def gemm(e, e_i=e_i, j=j):
                            e.matmul(ps_a[:], idb[:], gb[:, 512 * j:512 * (j + 1)],
                                     start=True, stop=False)
                            return e.matmul(ps_a[:], bgb[0:16, 128 * j:128 * (j + 1)],
                                            cg_sb[:, 512 * e_i:512 * (e_i + 1)],
                                            start=False, stop=True)

                        def uemm(e, e_i=e_i, j=j):
                            e.matmul(ps_b[:], idb[:], ub[:, 512 * j:512 * (j + 1)],
                                     start=True, stop=False)
                            return e.matmul(ps_b[:], bub[0:16, 128 * j:128 * (j + 1)],
                                            cu_sb[:, 512 * e_i:512 * (e_i + 1)],
                                            start=False, stop=True)
                        waits = [(ve, ve_base)] + ldg_be
                        if ex_ve:
                            waits.append((ve, ex_ve[-1]))
                        pr.add("tensor", gemm, waits=waits, inc=mm)
                        pr.add("tensor", uemm, waits=[], inc=mm)
                        mm_e = mm.n
                        par = j % 2
                        pr.add("scalar", lambda e, par=par: e.activation(
                            tg_sb[par][:], ps_a[:], AF.Silu), waits=[(mm, mm_e)], inc=se)
                        he_w = [(se, se.n)]
                        if len(cf_mms) >= 2:
                            he_w.append((mm, cf_mms[-2]))
                        pr.add("vector", lambda e, par=par: e.tensor_tensor(
                            he_sb[par][:], tg_sb[par][:], ps_b[:], ALU.mult),
                               waits=he_w, inc=ve)
                        ex_ve.append(ve.n)
                        is_last = (blk == NBLK - 1 and e_i == 7 and j == BFC - 1)

                        def cfmm(e, e_i=e_i, fc=fc, par=par, first=first_cf,
                                 is_last=is_last):
                            last = None
                            for ttt in range(4):
                                last = e.matmul(
                                    ps_cf[:, 128 * ttt + 16 * e_i: 128 * ttt + 16 * e_i + 16],
                                    he_sb[par][:, 128 * ttt:128 * (ttt + 1)],
                                    adb[:, 128 * fc + 16 * e_i: 128 * fc + 16 * e_i + 16],
                                    start=(first and ttt == 0), stop=(is_last and ttt == 3))
                            return last
                        pr.add("tensor", cfmm, waits=[(ve, ve.n)], inc=mm)
                        cf_mms.append(mm.n)
                        first_cf = False

                h_ve = []
                for j in range(BFC):
                    fc = blk * BFC + j
                    par = j % 2
                    hw = [(ve, ex_ve[-1])]
                    if j >= 2:
                        hw.append((ve, h_ve[-2]))
                    pr.add("scalar", lambda e, j=j, par=par: e.activation(
                        tg_sb[par][:], gb[:, 512 * j:512 * (j + 1)], AF.Silu),
                           waits=hw, inc=se)
                    pr.add("vector", lambda e, j=j, par=par: e.tensor_tensor(
                        gb[:, 512 * j:512 * (j + 1)], tg_sb[par][:],
                        ub[:, 512 * j:512 * (j + 1)], ALU.mult),
                           waits=[(se, se.n)], inc=ve)
                    h_ve.append(ve.n)
                    pr.dma("sync", hs_d[fc], gb[:, 512 * j:512 * (j + 1)],
                           waits=[(ve, ve.n)], inc=sth)
            sth_all = sth.n

            mm_cf = cf_mms[-1]
            for ttt in range(4):
                pr.add("vector", lambda e, ttt=ttt: e.tensor_tensor(
                    cw_sb[:, 128 * ttt:128 * (ttt + 1)], ps_cf[:, 128 * ttt:128 * (ttt + 1)],
                    wpat[:, 128 * ttt:128 * (ttt + 1)], ALU.mult),
                       waits=[(mm, mm_cf)], inc=ve)
                pr.add("tensor", lambda e, ttt=ttt: e.transpose(
                    ps_tr[:, 128 * ttt:128 * (ttt + 1)], cw_sb[:, 128 * ttt:128 * (ttt + 1)],
                    idb[:]), waits=[(ve, ve.n)], inc=mm)
                pr.add("vector", lambda e, ttt=ttt: e.tensor_copy(
                    cwT_sb[:, 128 * ttt:128 * (ttt + 1)],
                    ps_tr[:, 128 * ttt:128 * (ttt + 1)]),
                       waits=[(mm, mm.n)], inc=ve)
            ve_cw = ve.n

            dn_all = []
            for q in range(4):
                for fc in range(NFC):
                    buf = wd_t[fc % 2]
                    hbuf = hh_t[fc % 2]
                    sw = wds[fc % 2]
                    sh_ = hhs[fc % 2]
                    if len(dn_all) >= 2:
                        w = [(mm, dn_all[-2] if fc % 2 == len(dn_all) % 2 else dn_all[-1])]
                    elif dn_all:
                        w = [(mm, dn_all[-1])]
                    else:
                        w = []
                    if fc < 2:
                        w = w + [(ve, ve_cw), (sth, sth_all)]
                    pr.dma("gpsimd", buf[:], wdT_q[q, fc], waits=w, inc=sw)
                    pr.dma("scalar", hbuf[:], hs_d[fc], waits=w, inc=sh_)

                    def dmm(e, fc=fc, buf=buf, hbuf=hbuf):
                        last = None
                        for i in range(4):
                            last = e.matmul(ps_dn[:, 512 * i:512 * (i + 1)],
                                            buf[:, 128 * i:128 * (i + 1)],
                                            hbuf[:],
                                            start=(fc == 0), stop=False)
                        return last
                    dw = [(sw, sw.n), (sh_, sh_.n)]
                    if fc == 0:
                        dw.append((ve, ve.n))
                    pr.add("tensor", dmm, waits=dw, inc=mm)
                    dn_all.append(mm.n)

                def delmm(e, q=q):
                    last = None
                    for i in range(4):
                        for ttt in range(4):
                            last = e.matmul(
                                ps_dn[:, 512 * i + 128 * ttt: 512 * i + 128 * (ttt + 1)],
                                bdb[:, (4 * q + i) * 128:(4 * q + i + 1) * 128],
                                cwT_sb[:, 128 * ttt:128 * (ttt + 1)],
                                start=False, stop=(ttt == 3))
                    return last
                pr.add("tensor", delmm, waits=[(ve, ve_cw)], inc=mm)
                mm_q = mm.n
                for i in range(4):
                    dmc = 4 * q + i
                    buf = x1rl[i % 2]
                    sx_ = xls[i % 2]
                    sy = sy2[i % 2]
                    pr.dma("sync", buf[:], x1s[dmc][:, 512 * tt:512 * (tt + 1)],
                           waits=[(sy, sy.n)], inc=sx_)
                    pr.add("vector", lambda e, i=i, buf=buf: e.tensor_add(
                        buf[:], buf[:], ps_dn[:, 512 * i:512 * (i + 1)]),
                           waits=[(mm, mm_q), (sx_, sx_.n)], inc=ve)
                    pr.dma("sync", y2s_d[dmc][:, 512 * tt:512 * (tt + 1)], buf[:],
                           waits=[(ve, ve.n)], inc=sy)

        ctxB.close()

        barrier((sy2[0], sy2[0].n), (sy2[1], sy2[1].n), (ve, ve.n), (mm, mm.n),
                (se, se.n))
        ctxC = ExitStack()
        y2s = ctxC.enter_context(nc.sbuf_tensor("m0y2s", [128, NDM * TOK], F32))
        for c in range(NDM):
            pr.dma("sync", y2s[:, c * TOK:(c + 1) * TOK], y2s_d[c], inc=ldy2)
        emit_ln(y2s, (ldy2, ldy2.n), cast_to=None, spill=x2T_d, spill_sem=sout)
        hc = halo_snd.rearrange("(c p) t -> c p t", p=128)
        for c in range(NDM):
            pr.dma("sync", hc[c], y2s[:, c * TOK + TOK - HALO: (c + 1) * TOK],
                   waits=[(ve, ve.n)], inc=shalo)
        ctxC.close()
        ctxP.close()
        return sout, shalo

    m0sout, m0shalo = phase_mlp0()
    barrier((m0sout, m0sout.n), (m0shalo, m0shalo.n))

    def cc_halo(e):
        return e.collective_compute(
            "AllGather", mybir.AluOpType.bypass, replica_groups=RG4,
            ins=[halo_snd.tensor.ap().opt()], outs=[halo_gat.tensor.ap().opt()])
    pr.add("gpsimd", cc_halo, waits=[(m0shalo, m0shalo.n)], inc=cc, inc_n=1, is_dma=True)

    def phase_attn1a():
        ctxP = ExitStack()

        def a(n, sh, dt):
            return ctxP.enter_context(nc.sbuf_tensor("a1" + n, sh, dt))
        cw_sb = a("cw_sb", [128, NCH * 4], F32)
        cb_sb = a("cb_sb", [128, NCH], F32)
        dtb_sb = a("dtb_sb", [128, NH], F32)
        acol_sb = a("acol_sb", [NH, 1], F32)
        dt_tok = a("dt_tok", [128, NCK * NH], F32)
        dt_fm = a("dt_fm", [NH, TOK], F32)
        da_fm = a("da_fm", [NH, TOK], F32)
        ca_fm = a("ca_fm", [NH, TOK], F32)
        gt_tmp = a("gt_tmp", [128, NH], F32)
        ca_tok = a("ca_tok", [128, NCK * NH], F32)
        tfm = a("tfm", [NH, 128], F32)
        cs_sb = a("cs_sb", [128, 128], F32)
        d1_sb = a("d1_sb", [128, 128], F32)
        l_sb = a("l_sb", [128, 128], F32)
        m_sb = a("m_sb", [128, 128], BF16)
        scores_sb = a("scores_sb", [128, 128], F32)
        ftok = a("ftok", [128, NCK * NH], F32)
        dce_col = a("dce_col", [NH, 1], F32)
        dch_sb = a("dch_sb", [128, NCK * NH], F32)
        drow64 = a("drow64", [1, NH], F32)
        dsp_sb = a("dsp_sb", [128, NH], F32)
        zrow = a("zrow", [128, NH], F32)
        z2 = a("z2", [128, NH], F32)
        z3 = a("z3", [128, NH], F32)
        ev1 = a("ev1", [128, 512], BF16)
        ev2 = a("ev2", [128, 512], BF16)
        xconv = a("xconv", [128, NCH * TOK], BF16)

        def pp(n, sh, dt):
            return ctxP.enter_context(nc.psum_tensor("a1" + n, sh, dt))
        ps_a = pp("ps_a", [128, 512], F32)
        ps_b = pp("ps_b", [128, 512], F32)
        ps_t2 = [pp("ps_t0", [128, 512], BF16), pp("ps_t1", [128, 512], BF16)]
        ps_f = pp("ps_f", [128, 512], F32)
        ps_g = pp("ps_g", [128, 1536], F32)

        ldc = pr.sem("a1ldc")
        ldx = pr.sem("a1ldx")
        wins = [pr.sem(f"a1win{i}") for i in range(2)]
        mm = pr.sem("a1mm")
        sg = pr.sem("a1sg")
        sxb = pr.sem("a1sxb")
        sx = pr.sem("a1sx")
        si = pr.sem("a1si")
        stc = pr.sem("a1stc")
        sm = pr.sem("a1sm")

        ctx0 = ExitStack()
        xb2 = ctx0.enter_context(nc.sbuf_tensor("a1xb2", [128, NDM * TOKH], BF16))
        win_t = [ctx0.enter_context(nc.sbuf_tensor(f"a1winb{i}", [128, NDM * 512], BF16))
                 for i in range(2)]
        hg_sb = ctx0.enter_context(nc.sbuf_tensor("a1hg_sb", [128, 4 * NDM * HALO], BF16))

        x2c = x2T_d.rearrange("(c p) t -> c p t", p=128)
        for c in range(NDM):
            pr.dma("gpsimd", xb2[:, c * TOKH + HALO:(c + 1) * TOKH], x2c[c], inc=ldx)
        for m in range(4):
            pr.dma("gpsimd", hg_sb[:, m * NDM * HALO:(m + 1) * NDM * HALO].rearrange(
                "p (c t) -> p c t", t=HALO),
                   halo_gat[m].rearrange("(c p) t -> p c t", p=128),
                   waits=[(cc, 2)] if m == 0 else [], inc=ldx)
        ldx_all = ldx.n
        for c in range(NDM):
            dst = np.s_[:, c * TOKH: c * TOKH + HALO]
            pr.add("vector", lambda e, dst=dst, c=c: e.tensor_scalar_mul(
                xb2[dst], hg_sb[:, c * HALO:(c + 1) * HALO], cmcol(CM_HSEL)),
                   waits=[(ldx, ldx_all), (ldc0, ldc0_all)], inc=ve)
            for m in range(1, 4):
                pr.add("vector", lambda e, dst=dst, c=c, m=m: e.scalar_tensor_tensor(
                    xb2[dst], hg_sb[:, m * NDM * HALO + c * HALO:
                                    m * NDM * HALO + (c + 1) * HALO],
                    cmcol(CM_HSEL + m), xb2[dst], ALU.mult, ALU.add),
                       waits=[], inc=ve)
        ve_halo = ve.n
        pr.dma("sync", cw_sb[:], convw[:], inc=ldc)
        pr.dma("sync", cb_sb[:], convb[:], inc=ldc)
        pr.dma("sync", dtb_sb[:], dtb_bc[:], inc=ldc)
        pr.dma("sync", acol_sb[:], a_col[:], inc=ldc)
        ldc_all = ldc.n

        winc = w_in.rearrange("(c p) t -> c p t", p=128)

        last_gate_mm = []
        for g in range(8):
            buf = win_t[g % 2]
            sem = wins[g % 2]
            w = [(mm, last_gate_mm[g - 2])] if g >= 2 else []
            for ic in range(NDM):
                pr.dma("gpsimd", buf[:, ic * 512:(ic + 1) * 512],
                       winc[ic][:, 512 * g:512 * (g + 1)],
                       waits=w if ic == 0 else [], inc=sem)
            for ttt in range(NCK):
                ps = ps_a if ttt % 2 == 0 else ps_b

                def gfn(e, ttt=ttt, buf=buf, ps=ps):
                    last = None
                    for ic in range(NDM):
                        last = e.matmul(ps[:],
                                        xb2[:, ic * TOKH + HALO + 128 * ttt:
                                            ic * TOKH + HALO + 128 * (ttt + 1)],
                                        buf[:, ic * 512:(ic + 1) * 512],
                                        start=(ic == 0), stop=(ic == NDM - 1))
                    return last
                waits = [(sem, sem.n), (ldx, ldx_all), (ve, ve_halo), (se, se.n)]
                pr.add("tensor", gfn, waits=waits, inc=mm)
                buf_ev = ev1 if ttt % 2 == 0 else ev2
                pr.add("scalar", lambda e, ps=ps, buf_ev=buf_ev: e.activation(
                    buf_ev[:], ps[:], AF.Silu), waits=[(mm, mm.n), (sg, sg.n)], inc=se)
                pr.dma("sync", gate_d[ttt][:, 512 * g:512 * (g + 1)], buf_ev[:],
                       waits=[(se, se.n)], inc=sg)
            last_gate_mm.append(mm.n)

        tok_tiles = [(0, 512), (512, 512), (1024, 128)]
        last_xbc_mm = []
        for fc in range(NCH):
            buf = win_t[fc % 2]
            sem = wins[fc % 2]
            w = [(mm, last_xbc_mm[fc - 2])] if fc >= 2 else [(mm, mm.n)]
            for ic in range(NDM):
                pr.dma("gpsimd", buf[:, ic * 512:ic * 512 + 128],
                       winc[ic][:, INTER + 128 * fc:INTER + 128 * (fc + 1)],
                       waits=w if ic == 0 else [], inc=sem)
            for ti, (t0, tw) in enumerate(tok_tiles):
                ps = ps_a if ti % 2 == 0 else ps_b

                def xfn(e, t0=t0, tw=tw, buf=buf, ps=ps):
                    last = None
                    for ic in range(NDM):
                        last = e.matmul(ps[:, 0:tw],
                                        buf[:, ic * 512:ic * 512 + 128],
                                        xb2[:, ic * TOKH + t0: ic * TOKH + t0 + tw],
                                        start=(ic == 0), stop=(ic == NDM - 1))
                    return last
                pr.add("tensor", xfn, waits=[(sem, sem.n), (ve, ve.n)], inc=mm)
                evb = ev1 if ti % 2 == 0 else ev2
                pr.add("vector", lambda e, tw=tw, ps=ps, evb=evb: e.tensor_copy(
                    evb[:, 0:tw], ps[:, 0:tw]),
                       waits=[(mm, mm.n), (sxb, sxb.n), (sg, sg.n)], inc=ve)
                pr.dma("sync", xbc_dram[fc][:, t0:t0 + tw], evb[:, 0:tw],
                       waits=[(ve, ve.n)], inc=sxb)
            last_xbc_mm.append(mm.n)

        buf = win_t[0]
        sem = wins[0]
        for ic in range(NDM):
            pr.dma("gpsimd", buf[:, ic * 512:ic * 512 + NH],
                   winc[ic][:, 2 * INTER + 2 * NST: 2 * INTER + 2 * NST + NH],
                   waits=[(mm, mm.n)] if ic == 0 else [], inc=sem)
        for ttt in range(NCK):
            def dfn(e, ttt=ttt, buf=buf):
                last = None
                for ic in range(NDM):
                    last = e.matmul(ps_f[:, 0:NH],
                                    xb2[:, ic * TOKH + HALO + 128 * ttt:
                                        ic * TOKH + HALO + 128 * (ttt + 1)],
                                    buf[:, ic * 512:ic * 512 + NH],
                                    start=(ic == 0), stop=(ic == NDM - 1))
                return last
            pr.add("tensor", dfn, waits=[(sem, sem.n), (ve, ve.n)], inc=mm)
            pr.add("vector", lambda e: e.tensor_add(zrow[:], ps_f[:, 0:NH], dtb_sb[:]),
                   waits=[(mm, mm.n), (ldc, ldc_all)], inc=ve)
            pr.add("scalar", lambda e: e.activation(z2[:], zrow[:], AF.Abs),
                   waits=[(ve, ve.n)], inc=se)
            pr.add("scalar", lambda e: e.activation(z2[:], z2[:], AF.Exp, scale=-1.0),
                   waits=[], inc=se)
            pr.add("scalar", lambda e: e.activation(
                z2[:], z2[:], AF.Ln, bias=sc_sb[:, SCALARS["one"]:SCALARS["one"] + 1]),
                   waits=[], inc=se)
            pr.add("vector", lambda e: e.tensor_scalar_max(z3[:], zrow[:], 0.0),
                   waits=[(se, se.n)], inc=ve)
            pr.add("vector", lambda e, ttt=ttt: e.tensor_add(
                dt_tok[:, ttt * NH:(ttt + 1) * NH], z3[:], z2[:]), waits=[], inc=ve)
        ctx0.close()

        xtokc = a("xtokc", [128, INTER], BF16)
        xt_sb = a("xt_sb", [128, INTER], BF16)
        btok = a("btok", [128, 128], BF16)
        evf = [a(f"evf{i}", [128, 512], F32) for i in range(2)]
        s_sb = a("s_sb", [128, INTER], F32)
        convacc = a("convacc", [128, TOK], F32)
        xbt = [a(f"xbt{i}", [128, TOKH], BF16) for i in range(2)]
        lxb = [pr.sem(f"a1lxb{i}") for i in range(2)]
        conv_ve = []
        for fc in range(NCH):
            buf = xbt[fc % 2]
            sem = lxb[fc % 2]
            w = [(sxb, sxb.n)] if fc < 2 else [(ve, conv_ve[fc - 2])]
            pr.dma("scalar", buf[:], xbc_dram[fc], waits=w, inc=sem)
            pr.add("vector", lambda e, fc=fc, buf=buf: e.tensor_scalar(
                convacc[:], buf[:, 125:125 + TOK],
                cw_sb[:, 4 * fc:4 * fc + 1], cb_sb[:, fc:fc + 1], ALU.mult, ALU.add),
                   waits=[(sem, sem.n), (se, se.n)], inc=ve)
            for kk in range(1, 4):
                pr.add("vector", lambda e, fc=fc, kk=kk, buf=buf: e.scalar_tensor_tensor(
                    convacc[:], buf[:, 125 + kk: 125 + kk + TOK],
                    cw_sb[:, 4 * fc + kk:4 * fc + kk + 1], convacc[:], ALU.mult, ALU.add),
                       waits=[], inc=ve)
            pr.add("scalar", lambda e, fc=fc: e.activation(
                xconv[:, fc * TOK:(fc + 1) * TOK], convacc[:], AF.Silu),
                   waits=[(ve, ve.n)], inc=se)
            conv_ve.append(ve.n)
        se_conv = se.n
        pr.dma("sync", cfm_d[:], xconv[:, 33 * TOK:34 * TOK], waits=[(se, se_conv)],
               inc=sm)

        for ttt in range(NCK):
            pr.add("tensor", lambda e, ttt=ttt: e.transpose(
                ps_f[0:NH, 0:128], dt_tok[:, ttt * NH:(ttt + 1) * NH], idf[:]),
                   waits=[(ve, ve.n)], inc=mm)
            pr.add("vector", lambda e, ttt=ttt: e.tensor_copy(
                dt_fm[:, 128 * ttt:128 * (ttt + 1)], ps_f[0:NH, 0:128]),
                   waits=[(mm, mm.n)], inc=ve)
        pr.add("vector", lambda e: e.tensor_scalar_mul(da_fm[:], dt_fm[:],
                                                       acol_sb[:, 0:1]),
               waits=[], inc=ve)
        for ttt in range(NCK):
            pr.add("vector", lambda e, ttt=ttt: e.tensor_tensor_scan(
                ca_fm[:, 128 * ttt:128 * (ttt + 1)], onesf[0:NH, :],
                da_fm[:, 128 * ttt:128 * (ttt + 1)], 0.0, ALU.mult, ALU.add),
                   waits=[], inc=ve)
        ve_ca = ve.n
        for ttt in range(NCK):
            pr.add("tensor", lambda e, ttt=ttt: e.transpose(
                ps_f[0:128, 0:NH], ca_fm[:, 128 * ttt:128 * (ttt + 1)], idf[0:NH, 0:NH]),
                   waits=[(ve, ve_ca), (se, se.n), (ve, ve.n)], inc=mm)
            pr.add("vector", lambda e, ttt=ttt: e.tensor_copy(
                ca_tok[:, ttt * NH:(ttt + 1) * NH], ps_f[0:128, 0:NH]),
                   waits=[(mm, mm.n)], inc=ve)
            pr.add("scalar", lambda e, ttt=ttt: e.activation(
                gt_tmp[:], ca_tok[:, ttt * NH:(ttt + 1) * NH], AF.Exp),
                   waits=[(ve, ve.n), (sm, sm.n)], inc=se)
            pr.dma("sync", gtok_d[ttt], gt_tmp[:], waits=[(se, se.n)], inc=sm)
            pr.add("vector", lambda e, ttt=ttt: e.tensor_scalar(
                tfm[:], ca_fm[:, 128 * ttt:128 * (ttt + 1)],
                ca_fm[:, 128 * ttt + 127:128 * ttt + 128], None, ALU.subtract),
                   waits=[], inc=ve)
            pr.add("tensor", lambda e: e.transpose(ps_f[:, 64:64 + NH], tfm[:],
                                                   idf[0:NH, 0:NH]),
                   waits=[(ve, ve.n)], inc=mm)
            pr.add("scalar", lambda e, ttt=ttt: e.activation(
                ftok[:, ttt * NH:(ttt + 1) * NH], ps_f[:, 64:64 + NH], AF.Exp,
                scale=-1.0), waits=[(mm, mm.n)], inc=se)
            pr.add("vector", lambda e, ttt=ttt: e.tensor_mul(
                ftok[:, ttt * NH:(ttt + 1) * NH], ftok[:, ttt * NH:(ttt + 1) * NH],
                dt_tok[:, ttt * NH:(ttt + 1) * NH]), waits=[(se, se.n)], inc=ve)
            pr.add("scalar", lambda e, ttt=ttt: e.activation(
                dce_col[:], ca_fm[:, 128 * ttt + 127:128 * ttt + 128], AF.Exp),
                   waits=[(ve, ve_ca)], inc=se)
            pr.add("tensor", lambda e: e.transpose(ps_f[0:1, 128:128 + NH], dce_col[:],
                                                   idf[0:NH, 0:NH]),
                   waits=[(se, se.n)], inc=mm)
            pr.add("vector", lambda e: e.tensor_copy(drow64[:], ps_f[0:1, 128:128 + NH]),
                   waits=[(mm, mm.n)], inc=ve)
            pr.add("tensor", lambda e: e.matmul(ps_f[:, 256:256 + NH], one1f[:],
                                                drow64[:], start=True, stop=True),
                   waits=[(ve, ve.n)], inc=mm)
            pr.add("vector", lambda e, ttt=ttt: e.tensor_copy(
                dch_sb[:, ttt * NH:(ttt + 1) * NH], ps_f[:, 256:256 + NH]),
                   waits=[(mm, mm.n)], inc=ve)
            pr.dma("sync", dch_d[ttt], dch_sb[:, ttt * NH:(ttt + 1) * NH],
                   waits=[(ve, ve.n)], inc=sm)
        pr.add("vector", lambda e: e.tensor_copy(dsp_sb[:], dch_sb[:, 0:NH]),
               waits=[], inc=ve)
        for ttt in range(1, NCK):
            pr.add("vector", lambda e, ttt=ttt: e.tensor_mul(
                dsp_sb[:], dsp_sb[:], dch_sb[:, ttt * NH:(ttt + 1) * NH]),
                   waits=[], inc=ve)
        pr.dma("sync", dspan_d[:], dsp_sb[:], waits=[(ve, ve.n)], inc=sm)
        pr.add("vector", lambda e: e.memset(s_sb[:], 0.0), waits=[], inc=ve)

        for ttt in range(NCK):
            tr_ves = []
            for j in range(32):
                slot = j % 2
                w = [(se, se_conv)]
                if len(tr_ves) >= 2:
                    w.append((ve, tr_ves[-2]))
                pr.add("tensor", lambda e, j=j, ttt=ttt, slot=slot: e.transpose(
                    ps_t2[slot][:, 0:128],
                    xconv[:, j * TOK + 128 * ttt: j * TOK + 128 * (ttt + 1)], idb[:]),
                       waits=w, inc=mm)
                pr.add("vector", lambda e, j=j, slot=slot: e.tensor_copy(
                    xtokc[:, 128 * j:128 * (j + 1)], ps_t2[slot][:, 0:128]),
                       waits=[(mm, mm.n), (sx, sx.n)], inc=ve)
                tr_ves.append(ve.n)
            pr.dma("sync", xtok_d[ttt], xtokc[:], waits=[(ve, ve.n)], inc=sx)
            pr.add("vector", lambda e, ttt=ttt: e.tensor_tensor(
                xt_sb[:].rearrange("p (h q) -> p h q", h=NH),
                xtokc[:].rearrange("p (h q) -> p h q", h=NH),
                ftok[:, ttt * NH:(ttt + 1) * NH].unsqueeze(2).broadcast_to([128, NH, HD]),
                ALU.mult), waits=[], inc=ve)
            pr.add("tensor", lambda e, ttt=ttt: e.transpose(
                ps_t2[0][:, 0:128],
                xconv[:, 32 * TOK + 128 * ttt:32 * TOK + 128 * (ttt + 1)],
                idb[:]), waits=[(ve, ve.n)], inc=mm)
            pr.add("vector", lambda e: e.tensor_copy(btok[:], ps_t2[0][:, 0:128]),
                   waits=[(mm, mm.n)], inc=ve)
            ve_btok = ve.n
            pr.add("vector", lambda e, ttt=ttt: e.tensor_tensor(
                s_sb[:].rearrange("p (h q) -> p h q", h=NH),
                s_sb[:].rearrange("p (h q) -> p h q", h=NH),
                dch_sb[:, ttt * NH:(ttt + 1) * NH].unsqueeze(2).broadcast_to(
                    [128, NH, HD]),
                ALU.mult), waits=[(ve, ve_btok)], inc=ve)
            for i in range(8):
                bank = i % 3

                def tcfn(e, i=i, bank=bank):
                    return e.matmul(ps_g[:, 512 * bank:512 * (bank + 1)], btok[:],
                                    xt_sb[:, 512 * i:512 * (i + 1)],
                                    start=True, stop=True)
                w = [(ve, ve_btok)]
                if i >= 3:
                    w.append((ve, ve.n))
                pr.add("tensor", tcfn, waits=w, inc=mm)
                mm_tc = mm.n
                pr.add("vector", lambda e, i=i, bank=bank: e.tensor_add(
                    s_sb[:, 512 * i:512 * (i + 1)], s_sb[:, 512 * i:512 * (i + 1)],
                    ps_g[:, 512 * bank:512 * (bank + 1)]),
                       waits=[(mm, mm_tc)], inc=ve)
                pr.add("vector", lambda e, i=i, bank=bank: e.tensor_copy(
                    evf[i % 2][:], ps_g[:, 512 * bank:512 * (bank + 1)]),
                       waits=[(stc, stc.n)], inc=ve)
                pr.dma("sync", tc_d[ttt][:, 512 * i:512 * (i + 1)], evf[i % 2][:],
                       waits=[(ve, ve.n)], inc=stc)
            pr.add("tensor", lambda e, ttt=ttt: e.matmul(
                ps_f[:, 0:128],
                xconv[:, 32 * TOK + 128 * ttt:32 * TOK + 128 * (ttt + 1)],
                xconv[:, 33 * TOK + 128 * ttt:33 * TOK + 128 * (ttt + 1)],
                start=True, stop=True), waits=[(ve, ve.n)], inc=mm)
            pr.add("vector", lambda e: e.tensor_copy(scores_sb[:], ps_f[:, 0:128]),
                   waits=[(mm, mm.n)], inc=ve)
            ve_sc = ve.n
            for grp in range(8):
                ps = ps_a if grp % 2 == 0 else ps_b
                for hh in range(8):
                    h = 8 * grp + hh
                    pr.add("vector", lambda e, ttt=ttt, h=h: e.tensor_scalar_mul(
                        cs_sb[:], onesf[:], ca_tok[:, ttt * NH + h: ttt * NH + h + 1]),
                           waits=[(ve, ve_sc)], inc=ve)
                    pr.add("tensor", lambda e: e.transpose(ps_f[:, 128:256], cs_sb[:],
                                                           idf[:]),
                           waits=[(ve, ve.n)], inc=mm)
                    pr.add("vector", lambda e: e.scalar_tensor_tensor(
                        d1_sb[:], cs_sb[:], -1.0, ps_f[:, 128:256], ALU.mult, ALU.add),
                           waits=[(mm, mm.n)], inc=ve)
                    pr.add("vector", lambda e: e.tensor_add(d1_sb[:], d1_sb[:],
                                                            mlt_sb[:]),
                           waits=[], inc=ve)
                    pr.add("scalar", lambda e: e.activation(l_sb[:], d1_sb[:], AF.Exp),
                           waits=[(ve, ve.n)], inc=se)
                    pr.add("vector", lambda e, ttt=ttt, h=h: e.scalar_tensor_tensor(
                        m_sb[:], l_sb[:], dt_tok[:, ttt * NH + h: ttt * NH + h + 1],
                        scores_sb[:], ALU.mult, ALU.mult),
                           waits=[(se, se.n)], inc=ve)

                    def imfn(e, h=h, hh=hh, ps=ps):
                        return e.matmul(ps[:, 64 * hh:64 * (hh + 1)], m_sb[:],
                                        xtokc[:, 64 * h:64 * (h + 1)],
                                        start=(hh == 0), stop=(hh == 7))
                    pr.add("tensor", imfn, waits=[(ve, ve.n)], inc=mm)
                buf_ev = ev1 if grp % 2 == 0 else ev2
                pr.add("scalar", lambda e, ps=ps, buf_ev=buf_ev: e.activation(
                    buf_ev[:], ps[:], AF.Copy),
                       waits=[(mm, mm.n), (si, si.n), (sg, sg.n), (sxb, sxb.n)], inc=se)
                pr.dma("sync", intra_d[ttt][:, 512 * grp:512 * (grp + 1)], buf_ev[:],
                       waits=[(se, se.n)], inc=si)

        pr.dma("sync", uspan_d[:], s_sb[:], waits=[(ve, ve.n)], inc=sm)
        ctxP.close()
        return sm, sg, sx, si, stc, sxb

    a1sm, a1sg, a1sx, a1si, a1stc, a1sxb = phase_attn1a()
    barrier((a1sm, a1sm.n), (a1sg, a1sg.n), (a1sx, a1sx.n), (a1si, a1si.n),
            (a1stc, a1stc.n), (a1sxb, a1sxb.n))

    def cc_usp(e):
        return e.collective_compute(
            "AllGather", mybir.AluOpType.bypass, replica_groups=RG4,
            ins=[uspan_d.tensor.ap().opt()], outs=[uspan_gat.tensor.ap().opt()])
    pr.add("gpsimd", cc_usp, waits=[(a1sm, a1sm.n)], inc=cc, inc_n=1, is_dma=True)

    def cc_dsp(e):
        return e.collective_compute(
            "AllGather", mybir.AluOpType.bypass, replica_groups=RG4,
            ins=[dspan_d.tensor.ap().opt()], outs=[dspan_gat.tensor.ap().opt()])
    pr.add("gpsimd", cc_dsp, waits=[], inc=cc, inc_n=1, is_dma=True)

    def phase_attn1b():
        ctxP = ExitStack()
        mm = pr.sem("b1mm")
        lnk = LNKit(nc, pr, mm, one_bf, one1f,
                    sc_sb[0:1, SCALARS["eps"]:SCALARS["eps"] + 1], ctx=ctxP)

        def a(n, sh, dt):
            return ctxP.enter_context(nc.sbuf_tensor("b1" + n, sh, dt))
        dbc_sb = a("dbc_sb", [128, NH], F32)
        nw_sb = a("nw_sb", [128, INTER], BF16)
        gl_sb = a("gl_sb", [128, NCK * NH], F32)
        dchl_sb = a("dchl_sb", [128, NCK * NH], F32)
        dg_sb = a("dg_sb", [128, 4 * NH], F32)
        wl_sb = a("wl_sb", [128, 3 * NH], F32)
        f2_sb = a("f2_sb", [128, NH], F32)
        cfm_sb = a("cfm_sb", [128, TOK], BF16)
        s_sb = a("s_sb", [128, INTER], F32)
        sbf = a("sbf", [128, INTER], BF16)
        tcl = a("tcl", [128, INTER], F32)
        u_sb = tcl
        intral = a("intral", [128, INTER], BF16)
        xtokl = a("xtokl", [128, INTER], BF16)
        gatel = a("gatel", [128, INTER], BF16)
        zc = a("zc", [128, INTER], F32)
        tmp512 = a("tmp512", [128, 512], F32)
        zcol = a("zcol", [128, 1], F32)
        znb = a("znb", [128, INTER], BF16)
        zrow_big = a("zrow_big", [128, 32 * 128], BF16)
        y3 = a("y3", [128, NDM * TOK], F32)
        wq_ = [a(f"wq{i}", [128, 1024], BF16) for i in range(2)]
        zq = [a(f"zq{i}", [128, 512], BF16) for i in range(2)]

        def pp(n, sh, dt):
            return ctxP.enter_context(nc.psum_tensor("b1" + n, sh, dt))
        ps_y = pp("ps_y", [128, 1024], F32)
        ps_t2 = [pp("ps_t0", [128, 512], BF16), pp("ps_t1", [128, 512], BF16)]
        ps_w = pp("ps_w", [128, 2048], F32)

        ldc = pr.sem("b1ldc")
        ldu = pr.sem("b1ldu")
        ldk = pr.sem("b1ldk")
        lw = [pr.sem(f"b1lw{i}") for i in range(2)]
        lz = [pr.sem(f"b1lz{i}") for i in range(2)]
        szt = pr.sem("b1szt")
        sout = pr.sem("b1sout")

        pr.dma("sync", dbc_sb[:], d_bc[:], inc=ldc)
        pr.dma("sync", nw_sb[:], nw_bc[:], inc=ldc)
        pr.dma("sync", cfm_sb[:], cfm_d[:], inc=ldc)
        for c in range(NCK):
            pr.dma("sync", gl_sb[:, c * NH:(c + 1) * NH], gtok_d[c], inc=ldc)
            pr.dma("sync", dchl_sb[:, c * NH:(c + 1) * NH], dch_d[c], inc=ldc)
        for m in range(4):
            pr.dma("sync", dg_sb[:, m * NH:(m + 1) * NH], dspan_gat[m],
                   waits=[(cc, 4)] if m == 0 else [], inc=ldc)
        x2c = x2T_d.rearrange("(c p) t -> c p t", p=128)
        for c in range(NDM):
            pr.dma("sync", y3[:, c * TOK:(c + 1) * TOK], x2c[c], inc=ldc)
        ldc_all = ldc.n

        def hview(t, col0):
            return t[:, col0:col0 + NH].unsqueeze(2).broadcast_to([128, NH, HD])

        def hview8(t, col0):
            return t[:, col0:col0 + 8].unsqueeze(2).broadcast_to([128, 8, HD])

        def as_hq(apx, h):
            return apx.rearrange("p (h q) -> p h q", h=h)

        pr.add("vector", lambda e: e.tensor_scalar(
            wl_sb[:, 0:NH], dg_sb[:, NH:2 * NH], cmcol(CM_S1), cmcol(CM_C1),
            ALU.mult, ALU.add), waits=[(ldc, ldc_all), (ldc0, ldc0_all)], inc=ve)
        pr.add("vector", lambda e: e.tensor_scalar(
            f2_sb[:], dg_sb[:, 2 * NH:3 * NH], cmcol(CM_S2), cmcol(CM_C2),
            ALU.mult, ALU.add), waits=[], inc=ve)
        pr.add("vector", lambda e: e.tensor_mul(wl_sb[:, 0:NH], wl_sb[:, 0:NH],
                                                f2_sb[:]),
               waits=[], inc=ve)
        pr.add("vector", lambda e: e.tensor_scalar_mul(wl_sb[:, 0:NH], wl_sb[:, 0:NH],
                                                       cmcol(CM_M0)), waits=[], inc=ve)
        pr.add("vector", lambda e: e.tensor_scalar_mul(wl_sb[:, NH:2 * NH], f2_sb[:],
                                                       cmcol(CM_M1)), waits=[], inc=ve)
        pr.add("vector", lambda e: e.tensor_scalar_mul(
            wl_sb[:, 2 * NH:3 * NH], onesf[:, 0:NH], cmcol(CM_M2)), waits=[], inc=ve)

        pr.add("vector", lambda e: e.memset(s_sb[:], 0.0), waits=[], inc=ve)
        for m in range(3):
            pr.dma("scalar", u_sb[:], uspan_gat[m],
                   waits=[(cc, 3), (ve, ve.n)], inc=ldu)
            pr.add("vector", lambda e, m=m: e.tensor_tensor(
                as_hq(u_sb[:], NH), as_hq(u_sb[:], NH), hview(wl_sb, m * NH), ALU.mult),
                   waits=[(ldu, ldu.n)], inc=ve)
            pr.add("vector", lambda e: e.tensor_add(s_sb[:], s_sb[:], u_sb[:]),
                   waits=[], inc=ve)

        sdbg = pr.sem("b1sdbg")
        if debug:
            pr.dma("sync", dbg_sin[:], s_sb[:], waits=[(ve, ve.n)], inc=sdbg)
            pr.dma("sync", dbg_wl[:, 0:4 * NH], dg_sb[:], waits=[(ve, ve.n)], inc=sdbg)
            pr.dma("sync", dbg_wl[:, 4 * NH:], wl_sb[:], waits=[(ve, ve.n)], inc=sdbg)

        for ck in range(NCK):
            w = [(ve, ve.n), (mm, mm.n), (sdbg, sdbg.n)]
            pr.dma("sync", tcl[:], tc_d[ck], waits=w, inc=ldk)
            pr.dma("scalar", intral[:], intra_d[ck], waits=w, inc=ldk)
            pr.dma("scalar", xtokl[:], xtok_d[ck], waits=w, inc=ldk)
            pr.dma("scalar", gatel[:], gate_d[ck], waits=w, inc=ldk)
            ldk_ck = ldk.n
            pr.add("vector", lambda e: e.tensor_copy(sbf[:], s_sb[:]),
                   waits=[(sdbg, sdbg.n)], inc=ve)
            pr.add("vector", lambda e, ck=ck: e.tensor_tensor(
                as_hq(s_sb[:], NH), as_hq(s_sb[:], NH), hview(dchl_sb, ck * NH),
                ALU.mult), waits=[], inc=ve)
            pr.add("vector", lambda e: e.tensor_add(s_sb[:], s_sb[:], tcl[:]),
                   waits=[(ldk, ldk_ck)], inc=ve)
            ve_snap = ve.n
            for i in range(8):
                pcol = 512 * (i % 2)

                def yfn(e, ck=ck, i=i, pcol=pcol):
                    return e.matmul(ps_y[:, pcol:pcol + 512],
                                    cfm_sb[:, 128 * ck:128 * (ck + 1)],
                                    sbf[:, 512 * i:512 * (i + 1)],
                                    start=True, stop=True)
                yw = [(ve, ve_snap)]
                if i >= 2:
                    yw.append((ve, ve.n))
                pr.add("tensor", yfn, waits=yw, inc=mm)
                mm_y = mm.n
                pr.add("vector", lambda e, ck=ck, i=i, pcol=pcol: e.tensor_tensor(
                    as_hq(zc[:, 512 * i:512 * (i + 1)], 8),
                    as_hq(ps_y[:, pcol:pcol + 512], 8),
                    hview8(gl_sb, ck * NH + 8 * i), ALU.mult), waits=[(mm, mm_y)],
                       inc=ve)
                pr.add("vector", lambda e, i=i: e.tensor_add(
                    zc[:, 512 * i:512 * (i + 1)], zc[:, 512 * i:512 * (i + 1)],
                    intral[:, 512 * i:512 * (i + 1)]), waits=[], inc=ve)
                pr.add("vector", lambda e, i=i: e.tensor_tensor(
                    as_hq(tmp512[:], 8), as_hq(xtokl[:, 512 * i:512 * (i + 1)], 8),
                    hview8(dbc_sb, 8 * i), ALU.mult), waits=[], inc=ve)
                pr.add("vector", lambda e, i=i: e.tensor_add(
                    zc[:, 512 * i:512 * (i + 1)], zc[:, 512 * i:512 * (i + 1)],
                    tmp512[:]), waits=[], inc=ve)
                pr.add("vector", lambda e, i=i: e.tensor_mul(
                    zc[:, 512 * i:512 * (i + 1)], zc[:, 512 * i:512 * (i + 1)],
                    gatel[:, 512 * i:512 * (i + 1)]), waits=[], inc=ve)
            pr.add("scalar", lambda e: e.activation(intral[:], zc[:], AF.Square),
                   waits=[(ve, ve.n)], inc=se)
            pr.add("vector", lambda e: e.reduce_sum(zcol[:], intral[:], axis=AX.X),
                   waits=[(se, se.n)], inc=ve)
            pr.add("scalar", lambda e: e.activation(
                zcol[:], zcol[:], AF.Sqrt, scale=1.0 / INTER,
                bias=sc_sb[:, SCALARS["eps"]:SCALARS["eps"] + 1]),
                   waits=[(ve, ve.n)], inc=se)
            pr.add("vector", lambda e: e.reciprocal(zcol[:], zcol[:]),
                   waits=[(se, se.n)], inc=ve)
            pr.add("vector", lambda e: e.scalar_tensor_tensor(
                znb[:], zc[:], zcol[:, 0:1], nw_sb[:], ALU.mult, ALU.mult),
                   waits=[], inc=ve)
            ve_znb = ve.n
            tr_ves = []
            for j in range(32):
                slot = j % 2
                tw = [(ve, ve_znb)]
                if len(tr_ves) >= 2:
                    tw.append((ve, tr_ves[-2]))
                pr.add("tensor", lambda e, j=j, slot=slot: e.transpose(
                    ps_t2[slot][:, 0:128], znb[:, 128 * j:128 * (j + 1)],
                    idb[:]), waits=tw, inc=mm)
                pr.add("vector", lambda e, j=j, slot=slot: e.tensor_copy(
                    zrow_big[:, 128 * j:128 * (j + 1)], ps_t2[slot][:, 0:128]),
                       waits=[(mm, mm.n), (szt, szt.n)], inc=ve)
                tr_ves.append(ve.n)
            pr.dma("sync",
                   zbT_dram[:, :, 128 * ck:128 * (ck + 1)].rearrange("c p f -> p c f"),
                   zrow_big[:].rearrange("p (c f) -> p c f", f=128),
                   waits=[(ve, ve.n)], inc=szt)

        woutc = wout.rearrange("(c p) t -> c p t", p=128)
        wv_all = []
        for tt2 in range(2):
            for qd in range(4):
                for ic in range(32):
                    bw = wq_[ic % 2]
                    bz = zq[ic % 2]
                    sw = lw[ic % 2]
                    sz = lz[ic % 2]
                    w = [(mm, wv_all[-2])] if len(wv_all) >= 2 else [(szt, szt.n)]
                    pr.dma("gpsimd", bw[:, 0:512], woutc[ic][:, 512 * qd:512 * (qd + 1)],
                           waits=w, inc=sw)
                    pr.dma("scalar", bz[:], zbT_dram[ic][:, 512 * tt2:512 * (tt2 + 1)],
                           waits=w, inc=sz)

                    def wfn(e, ic=ic, bw=bw, bz=bz):
                        last = None
                        for t4 in range(4):
                            last = e.matmul(ps_w[:, 512 * t4:512 * (t4 + 1)],
                                            bw[:, 128 * t4:128 * (t4 + 1)], bz[:],
                                            start=(ic == 0), stop=(ic == 31))
                        return last
                    ww = [(sw, sw.n), (sz, sz.n)]
                    if ic == 0:
                        ww.append((ve, ve.n))
                    pr.add("tensor", wfn, waits=ww, inc=mm)
                    wv_all.append(mm.n)
                mm_pass = mm.n
                for t4 in range(4):
                    dmc = 4 * qd + t4
                    pr.add("vector", lambda e, dmc=dmc, t4=t4, tt2=tt2: e.tensor_add(
                        y3[:, dmc * TOK + 512 * tt2: dmc * TOK + 512 * (tt2 + 1)],
                        y3[:, dmc * TOK + 512 * tt2: dmc * TOK + 512 * (tt2 + 1)],
                        ps_w[:, 512 * t4:512 * (t4 + 1)]), waits=[(mm, mm_pass)],
                           inc=ve)

        lnk.emit(y3, ps_w, (ve, ve.n), spill=x3T_d, spill_sem=sout)
        ctxP.close()
        return sout

    b1sout = phase_attn1b()
    barrier((b1sout, b1sout.n))

    def phase_mlp1():
        ctxP = ExitStack()

        def a(n, sh, dt):
            return ctxP.enter_context(nc.sbuf_tensor("m1" + n, sh, dt))
        xs = a("xs", [128, NDM * TOK], F32)
        xb = a("xb", [128, NDM * TOK], BF16)
        wb = a("wb", [128, NDM * DM], BF16)
        sgn = a("sgn", [128, DM], BF16)
        absw = a("absw", [128, DM], BF16)
        bias_sb = a("bias_sb", [128, NDM], F32)
        al_rowf = a("al_rowf", [1, DM], F32)
        stat_m = a("stat_m", [1, TOK], F32)
        stat_v = a("stat_v", [1, TOK], F32)
        rstd = a("rstd", [1, TOK], F32)
        mrstd = a("mrstd", [1, TOK], F32)
        ybf = a("ybf", [128, TOK], BF16)
        ysq = a("ysq", [128, TOK], BF16)
        res2 = [a("res_a", [128, TOK], F32), a("res_b", [128, TOK], F32)]

        def pp(n, sh, dt):
            return ctxP.enter_context(nc.psum_tensor("m1" + n, sh, dt))
        ps_d = [pp("ps_d0", [128, 512], F32), pp("ps_d1", [128, 512], F32)]
        ps_big = pp("ps_big", [128, 2048], F32)

        ld = pr.sem("m1ld")
        ldg = pr.sem("m1ldg")
        mm = pr.sem("m1mm")
        stq2 = [pr.sem("m1stq_a"), pr.sem("m1stq_b")]

        x3c = x3T_d.rearrange("(c p) t -> c p t", p=128)
        wc = wT.rearrange("(c p) o -> c p o", p=128)
        for c in range(NDM):
            pr.dma("sync", xs[:, c * TOK:(c + 1) * TOK], x3c[c], inc=ld)
        pr.dma("sync", bias_sb[:], biasc[:], inc=ld)
        ld_all = ld.n
        for c in range(NDM):
            pr.dma("gpsimd", xb[:, c * TOK:(c + 1) * TOK], x3c[c], inc=ldg)
        ldg_x = ldg.n
        for c in range(NDM):
            pr.dma("gpsimd", wb[:, c * DM:(c + 1) * DM], wc[c], inc=ldg)
        ldg_w = ldg.n

        alpha_mm = []
        for c in range(NDM):
            pr.add("scalar",
                   lambda e, c=c: e.activation(absw[:], wb[:, c * DM:(c + 1) * DM],
                                               AF.Abs),
                   waits=[(ldg, ldg_w)] + ([(mm, alpha_mm[-1])] if c else []), inc=se)
            pr.add("tensor", lambda e, c=c: [
                e.matmul(ps_big[0:1, 512 * j:512 * (j + 1)], one_bf[:],
                         absw[:, 512 * j:512 * (j + 1)],
                         start=(c == 0), stop=(c == NDM - 1)) for j in range(4)][-1],
                   waits=[(se, se.n)], inc=mm)
            alpha_mm.append(mm.n)
        pr.add("vector", lambda e: e.tensor_scalar_mul(al_rowf[:], ps_big[0:1, :],
                                                       1.0 / DM),
               waits=[(mm, mm.n)], inc=ve)
        pr.add("tensor", lambda e: [
            e.matmul(ps_big[:, 512 * j:512 * (j + 1)], one1f[:],
                     al_rowf[0:1, 512 * j:512 * (j + 1)], start=True, stop=True)
            for j in range(4)][-1],
               waits=[(ve, ve.n)], inc=mm)
        mm_albc = mm.n

        wbin_ve = []
        for c in range(NDM):
            pr.add("scalar",
                   lambda e, c=c: e.activation(sgn[:], wb[:, c * DM:(c + 1) * DM],
                                               AF.Sign),
                   waits=[(ve, wbin_ve[-1])] if c else [], inc=se)
            pr.add("vector", lambda e, c=c: e.tensor_tensor(
                wb[:, c * DM:(c + 1) * DM], sgn[:], ps_big[:, :DM], ALU.mult),
                   waits=[(se, se.n), (mm, mm_albc)], inc=ve)
            wbin_ve.append(ve.n)
        ve_wbin = ve.n

        resid_ve = []
        k = 0
        for oc in range(NDM):
            for tt in range(2):
                ps = ps_d[k % 2]

                def mmfn(e, oc=oc, tt=tt, ps=ps):
                    last = None
                    for ic in range(NDM):
                        last = e.matmul(ps[:],
                                        wb[:, ic * DM + 128 * oc: ic * DM + 128 * (oc + 1)],
                                        xb[:, ic * TOK + 512 * tt: ic * TOK + 512 * (tt + 1)],
                                        start=(ic == 0), stop=(ic == NDM - 1))
                    return last
                waits = [(ve, ve_wbin), (ldg, ldg_w)]
                if k >= 2:
                    waits.append((ve, resid_ve[k - 2]))
                pr.add("tensor", mmfn, waits=waits, inc=mm)
                sl = np.s_[:, oc * TOK + 512 * tt: oc * TOK + 512 * (tt + 1)]
                pr.add("vector", lambda e, sl=sl, oc=oc, ps=ps: e.scalar_tensor_tensor(
                    xs[sl], xs[sl], bias_sb[:, oc:oc + 1], ps[:], ALU.add, ALU.add),
                       waits=[(mm, mm.n), (ld, ld_all)], inc=ve)
                resid_ve.append(ve.n)
                k += 1
        ve_resid = ve.n

        stat_mm = []
        for c in range(NDM):
            pr.add("vector", lambda e, c=c: e.tensor_copy(ybf[:],
                                                          xs[:, c * TOK:(c + 1) * TOK]),
                   waits=[(ve, ve_resid)] + ([(mm, stat_mm[-1])] if c else
                                             [(mm, mm_albc)]),
                   inc=ve)
            pr.add("scalar", lambda e: e.activation(ysq[:], ybf[:], AF.Square),
                   waits=[(ve, ve.n)], inc=se)

            def statfn(e, c=c):
                last = None
                for j in range(2):
                    last = e.matmul(ps_big[0:1, 512 * j:512 * (j + 1)], one_bf[:],
                                    ybf[:, 512 * j:512 * (j + 1)],
                                    start=(c == 0), stop=(c == NDM - 1))
                for j in range(2):
                    last = e.matmul(ps_big[0:1, TOK + 512 * j:TOK + 512 * (j + 1)],
                                    one_bf[:], ysq[:, 512 * j:512 * (j + 1)],
                                    start=(c == 0), stop=(c == NDM - 1))
                return last
            pr.add("tensor", statfn, waits=[(se, se.n), (ve, ve.n)], inc=mm)
            stat_mm.append(mm.n)

        pr.add("vector", lambda e: e.tensor_scalar_mul(stat_m[:], ps_big[0:1, 0:TOK],
                                                       1.0 / DM),
               waits=[(mm, mm.n)], inc=ve)
        pr.add("vector", lambda e: e.tensor_scalar_mul(stat_v[:],
                                                       ps_big[0:1, TOK:2 * TOK],
                                                       1.0 / DM), waits=[], inc=ve)
        pr.add("vector", lambda e: e.tensor_mul(mrstd[:], stat_m[:], stat_m[:]),
               waits=[], inc=ve)
        pr.add("vector", lambda e: e.tensor_sub(stat_v[:], stat_v[:], mrstd[:]),
               waits=[], inc=ve)
        pr.add("scalar", lambda e: e.activation(
            rstd[:], stat_v[:], AF.Sqrt,
            bias=sc_sb[0:1, SCALARS["eps"]:SCALARS["eps"] + 1]),
               waits=[(ve, ve.n), (ld, ld_all)], inc=se)
        pr.add("vector", lambda e: e.reciprocal(rstd[:], rstd[:]), waits=[(se, se.n)],
               inc=ve)
        pr.add("vector", lambda e: e.tensor_mul(mrstd[:], stat_m[:], rstd[:]),
               waits=[], inc=ve)
        ve_stats = ve.n

        def bcfn(e):
            last = None
            for j in range(2):
                last = e.matmul(ps_big[:, 512 * j:512 * (j + 1)], one1f[:],
                                rstd[0:1, 512 * j:512 * (j + 1)], start=True, stop=True)
            for j in range(2):
                last = e.matmul(ps_big[:, TOK + 512 * j:TOK + 512 * (j + 1)], one1f[:],
                                mrstd[0:1, 512 * j:512 * (j + 1)], start=True, stop=True)
            return last
        pr.add("tensor", bcfn, waits=[(ve, ve_stats)], inc=mm)
        mm_bc = mm.n

        outc = outT.rearrange("(c p) t -> c p t", p=128)
        for c in range(NDM):
            res = res2[c % 2]
            sq = stq2[c % 2]
            waits = [(mm, mm_bc)]
            if c >= 2:
                waits.append((sq, sq.n))
            pr.add("vector", lambda e, c=c, res=res: e.tensor_mul(
                res[:], xs[:, c * TOK:(c + 1) * TOK], ps_big[:, 0:TOK]),
                   waits=waits, inc=ve)
            pr.add("vector", lambda e, res=res: e.tensor_sub(res[:], res[:],
                                                             ps_big[:, TOK:2 * TOK]),
                   waits=[], inc=ve)
            pr.dma("sync", outc[c], res[:], waits=[(ve, ve.n)], inc=sq)
        pr.wait("sync", (stq2[0], stq2[0].n), (stq2[1], stq2[1].n))
        ctxP.close()

    phase_mlp1()
    pr.build()
    return nc


def full_in_maps(inputs, consts):
    x0 = np.asarray(inputs["L0_attn_hidden_states"], np.float32)
    x0T = [np.ascontiguousarray(x0[b].T) for b in range(2)]
    a0maps = attn0_in_maps(x0, inputs, consts)
    dummy = [np.zeros((DM, TOK), np.float32)] * NCORES
    m0maps = mlp0_in_maps(dummy, dummy, inputs, consts)
    a1maps = attn1a_in_maps([np.zeros((DM, SEQ), np.float32)] * 2, inputs, consts)
    D = np.asarray(inputs["L1_attn_D"], np.float32)
    nw = np.asarray(inputs["L1_attn_norm_weight"], np.float32)
    W_out = np.asarray(inputs["L1_attn_W_out"], np.float32)
    wT = np.ascontiguousarray(np.asarray(inputs["L1_ffn_weight"], np.float32).T)
    bias = np.asarray(inputs["L1_ffn_bias"], np.float32)
    biasc = np.ascontiguousarray(bias.reshape(NDM, 128).T).astype(np.float32)
    maps = []
    for cid in range(NCORES):
        b, q = cid // 4, cid % 4
        m = dict(a0maps[cid])
        m0 = dict(m0maps[cid])
        del m0["oT"]
        m0["x0Ts"] = np.ascontiguousarray(x0T[b][:, q * TOK:(q + 1) * TOK])
        del m0["x0T"]
        m.update(m0)
        a1 = dict(a1maps[cid])
        del a1["x2Th"]
        del a1["triu"]
        m.update(a1)
        m["d_bc"] = np.tile(D[None, :], (128, 1)).astype(np.float32)
        m["nw_bc"] = np.tile(nw[None, :], (128, 1)).astype(ml_dtypes.bfloat16)
        m["wout"] = W_out
        m["wT"] = wT
        m["biasc"] = biasc
        cm = np.zeros((16,), np.float32)
        cm[CM_WLO] = 1.0 if b == 0 else 0.0
        cm[CM_WHI] = 1.0 - cm[CM_WLO]
        if q >= 1:
            cm[CM_HSEL + (q - 1)] = 1.0
        cm[CM_S1] = 1.0 if 1 < q else 0.0
        cm[CM_C1] = 1.0 - cm[CM_S1]
        cm[CM_S2] = 1.0 if 2 < q else 0.0
        cm[CM_C2] = 1.0 - cm[CM_S2]
        cm[CM_M0] = 1.0 if 0 < q else 0.0
        cm[CM_M1] = 1.0 if 1 < q else 0.0
        cm[CM_M2] = 1.0 if 2 < q else 0.0
        m["cmask"] = np.tile(cm[None, :], (128, 1)).astype(np.float32)
        maps.append(m)
    return maps


def kernel(**inputs):
    consts = host_consts()
    nc = build_full()
    maps = full_in_maps(inputs, consts)
    res = run_bass_kernel_spmd(nc, maps, core_ids=list(range(NCORES)))
    out = np.zeros((B, SEQ, DM), np.float32)
    for cid in range(NCORES):
        b, q = cid // 4, cid % 4
        out[b, TOK * q:TOK * (q + 1), :] = np.asarray(res.results[cid]["outT"],
                                                      np.float32).T
    return out


# version 03
# speedup vs baseline: 1.0054x; 1.0054x over previous
import numpy as np
import ml_dtypes
import concourse.bass as bass
import concourse.mybir as mybir
from concourse.bass_utils import run_bass_kernel_spmd

F32 = mybir.dt.float32
BF16 = mybir.dt.bfloat16
AF = mybir.ActivationFunctionType
ALU = mybir.AluOpType
AX = mybir.AxisListType

ENGINES = ("sync", "scalar", "vector", "tensor", "gpsimd")

NCORES = 8
B = 2
SEQ = 4096
DM = 2048
NDM = DM // 128
TOK = 1024
BIG_NEG = -60000.0


class _Sem:
    __slots__ = ("h", "n")

    def __init__(self, h):
        self.h = h
        self.n = 0


SELF_SYNC = ("vector", "scalar", "gpsimd")


class Prog:
    def __init__(self, nc):
        self.nc = nc
        self.ops = {e: [] for e in ENGINES}
        self._done = {e: {} for e in ENGINES}
        self._esem = {}
        self._dmaflow = {}

    def sem(self, name):
        return _Sem(self.nc.alloc_semaphore(name))

    def engine_sem(self, eng):
        if eng not in self._esem:
            self._esem[eng] = self.sem(f"es_{eng}")
        return self._esem[eng]

    def add(self, eng, fn, waits=(), inc=None, inc_n=1, is_dma=False):
        wl = []
        for w in waits:
            if isinstance(w, tuple):
                s, v = w
            else:
                s, v = w, w.n
            if v > 0:
                wl.append((s, v))
        if not is_dma and eng in SELF_SYNC:
            es = self.engine_sem(eng)
            if es.n > 0:
                wl.append((es, es.n))
            if inc is None:
                inc, inc_n = es, 1
            else:
                assert inc is es, f"compute op on {eng} must inc its engine sem"
        if inc is not None:
            inc.n += inc_n
        self.ops[eng].append((wl, fn, inc, inc_n))

    def wait(self, eng, *waits):
        wl = [(w[0], w[1]) if isinstance(w, tuple) else (w, w.n) for w in waits]
        self.ops[eng].append(([w for w in wl if w[1] > 0], lambda e: None, None, 0))

    def dma(self, eng, out, in_, waits=(), inc=None):
        self.add(eng, lambda e, o=out, i=in_: e.dma_start(out=o, in_=i), list(waits),
                 inc, 16, is_dma=True)

    def build(self):
        with self.nc.Block() as block:
            for eng in ENGINES:
                if not self.ops[eng]:
                    continue

                def run(engine, _ops=self.ops[eng], _eng=eng):
                    done = self._done[_eng]
                    for wl, fn, inc, inc_n in _ops:
                        for s, v in wl:
                            if done.get(id(s), 0) < v:
                                engine.wait_ge(s.h, v)
                                done[id(s)] = v
                        instr = fn(engine)
                        if inc is not None:
                            assert instr is not None, "fn must return the last instruction"
                            instr.then_inc(inc.h, inc_n)

                getattr(block, eng)(run)


def _bf(x):
    return np.asarray(x, dtype=ml_dtypes.bfloat16)


SCALARS = {"eps": 0, "one": 1, "lneps": 2, "ln05": 3}
SCALAR_VALS = [1e-5, 1.0, 1e-5, float(np.log(0.5))]


def host_consts():
    sc = np.tile(np.array(SCALAR_VALS, np.float32)[None, :], (128, 1))
    return {
        "ones128_bf": _bf(np.ones((128, 1))),
        "ones1x128_f": np.ones((1, 128), np.float32),
        "ident_bf": _bf(np.eye(128)),
        "scal": np.ascontiguousarray(sc),
    }



class LNKit:
    def __init__(self, nc, pr, mm, one_bf, one_f, eps_ap, ntok=TOK, ctx=None):
        if ctx is None:
            a = nc.alloc_sbuf_tensor
        else:
            def a(n, sh, dt):
                return ctx.enter_context(nc.sbuf_tensor(n, sh, dt))
        self.pr, self.mm = pr, mm
        self.one_bf, self.one_f, self.eps_ap = one_bf, one_f, eps_ap
        self.ntok = ntok
        uid = len(getattr(nc, "_lnkit_ids", [])) if hasattr(nc, "_lnkit_ids") else 0
        if not hasattr(nc, "_lnkit_ids"):
            nc._lnkit_ids = []
        nc._lnkit_ids.append(uid)
        self.stat_m = a(f"ln_m{uid}", [1, ntok], F32)
        self.stat_v = a(f"ln_v{uid}", [1, ntok], F32)
        self.rstd = a(f"ln_r{uid}", [1, ntok], F32)
        self.mrstd = a(f"ln_mr{uid}", [1, ntok], F32)
        self.ybf = a(f"ln_yb{uid}", [128, ntok], BF16)
        self.ysq = a(f"ln_ys{uid}", [128, ntok], BF16)

    def emit(self, buf, ps, resid_ready, cast_to=None, spill=None, spill_sem=None):
        pr, mm = self.pr, self.mm
        ve = pr.engine_sem("vector")
        se = pr.engine_sem("scalar")
        NT = self.ntok
        stat_mms = []
        for c in range(NDM):
            pr.add("vector", lambda e, c=c: e.tensor_copy(
                self.ybf[:], buf[:, c * NT:(c + 1) * NT]),
                   waits=[resid_ready] + ([(mm, stat_mms[-1])] if c else []), inc=ve)
            pr.add("scalar", lambda e: e.activation(self.ysq[:], self.ybf[:], AF.Square),
                   waits=[(ve, ve.n)], inc=se)

            def statfn(e, c=c):
                last = None
                for j in range(NT // 512):
                    last = e.matmul(ps[0:1, 512 * j:512 * (j + 1)], self.one_bf[:],
                                    self.ybf[:, 512 * j:512 * (j + 1)],
                                    start=(c == 0), stop=(c == NDM - 1))
                for j in range(NT // 512):
                    last = e.matmul(ps[0:1, NT + 512 * j:NT + 512 * (j + 1)], self.one_bf[:],
                                    self.ysq[:, 512 * j:512 * (j + 1)],
                                    start=(c == 0), stop=(c == NDM - 1))
                return last
            pr.add("tensor", statfn, waits=[(se, se.n), (ve, ve.n)], inc=mm)
            stat_mms.append(mm.n)
        pr.add("vector", lambda e: e.tensor_scalar_mul(self.stat_m[:], ps[0:1, 0:NT],
                                                       1.0 / DM),
               waits=[(mm, mm.n)], inc=ve)
        pr.add("vector", lambda e: e.tensor_scalar_mul(self.stat_v[:], ps[0:1, NT:2 * NT],
                                                       1.0 / DM), waits=[], inc=ve)
        pr.add("vector", lambda e: e.tensor_mul(self.mrstd[:], self.stat_m[:], self.stat_m[:]),
               waits=[], inc=ve)
        pr.add("vector", lambda e: e.tensor_sub(self.stat_v[:], self.stat_v[:], self.mrstd[:]),
               waits=[], inc=ve)
        pr.add("scalar", lambda e: e.activation(self.rstd[:], self.stat_v[:], AF.Sqrt,
                                                bias=self.eps_ap),
               waits=[(ve, ve.n)], inc=se)
        pr.add("vector", lambda e: e.reciprocal(self.rstd[:], self.rstd[:]),
               waits=[(se, se.n)], inc=ve)
        pr.add("vector", lambda e: e.tensor_mul(self.mrstd[:], self.stat_m[:], self.rstd[:]),
               waits=[], inc=ve)

        def bcfn(e):
            last = None
            for j in range(NT // 512):
                last = e.matmul(ps[:, 512 * j:512 * (j + 1)], self.one_f[:],
                                self.rstd[0:1, 512 * j:512 * (j + 1)], start=True, stop=True)
            for j in range(NT // 512):
                last = e.matmul(ps[:, NT + 512 * j:NT + 512 * (j + 1)], self.one_f[:],
                                self.mrstd[0:1, 512 * j:512 * (j + 1)],
                                start=True, stop=True)
            return last
        pr.add("tensor", bcfn, waits=[(ve, ve.n)], inc=mm)
        mm_bc = mm.n
        for c in range(NDM):
            sl = np.s_[:, c * NT:(c + 1) * NT]
            pr.add("vector", lambda e, sl=sl: e.tensor_mul(buf[sl], buf[sl], ps[:, 0:NT]),
                   waits=[(mm, mm_bc)], inc=ve)
            pr.add("vector", lambda e, sl=sl: e.tensor_sub(buf[sl], buf[sl],
                                                           ps[:, NT:2 * NT]),
                   waits=[], inc=ve)
            if cast_to is not None:
                pr.add("vector", lambda e, sl=sl: e.tensor_copy(cast_to[sl], buf[sl]),
                       waits=[], inc=ve)
            if spill is not None:
                pr.dma("sync", spill.rearrange("(c p) t -> c p t", p=128)[c], buf[sl],
                       waits=[(ve, ve.n)], inc=spill_sem)


def build_mlp1():
    nc = bass.Bass("TRN2", target_bir_lowering=False, debug=False)
    x3T = nc.dram_tensor("x3T", [DM, TOK], F32, kind="ExternalInput").ap()
    wT = nc.dram_tensor("wT", [DM, DM], F32, kind="ExternalInput").ap()
    biasc = nc.dram_tensor("biasc", [128, NDM], F32, kind="ExternalInput").ap()
    ones128_bf = nc.dram_tensor("ones128_bf", [128, 1], BF16, kind="ExternalInput").ap()
    ones1x128_f = nc.dram_tensor("ones1x128_f", [1, 128], F32, kind="ExternalInput").ap()
    scal = nc.dram_tensor("scal", [128, len(SCALAR_VALS)], F32, kind="ExternalInput").ap()
    outT = nc.dram_tensor("outT", [DM, TOK], F32, kind="ExternalOutput").ap()

    a = nc.alloc_sbuf_tensor
    xs = a("xs", [128, NDM * TOK], F32)
    xb = a("xb", [128, NDM * TOK], BF16)
    wb = a("wb", [128, NDM * DM], BF16)
    sgn = a("sgn", [128, DM], BF16)
    absw = a("absw", [128, DM], BF16)
    bias_sb = a("bias_sb", [128, NDM], F32)
    one_bf = a("one_bf", [128, 1], BF16)
    one_f = a("one_f", [1, 128], F32)
    sc_sb = a("sc_sb", [128, len(SCALAR_VALS)], F32)
    al_rowf = a("al_rowf", [1, DM], F32)
    stat_m = a("stat_m", [1, TOK], F32)
    stat_v = a("stat_v", [1, TOK], F32)
    rstd = a("rstd", [1, TOK], F32)
    mrstd = a("mrstd", [1, TOK], F32)
    ybf = a("ybf", [128, TOK], BF16)
    ysq = a("ysq", [128, TOK], BF16)
    res2 = [a("res_a", [128, TOK], F32), a("res_b", [128, TOK], F32)]

    p = nc.alloc_psum_tensor
    ps_d = [p("ps_d0", [128, 512], F32), p("ps_d1", [128, 512], F32)]
    ps_big = p("ps_big", [128, 2048], F32)

    pr = Prog(nc)
    ld = pr.sem("ld")
    ldg = pr.sem("ldg")
    mm = pr.sem("mm")
    ve = pr.engine_sem("vector")
    se = pr.engine_sem("scalar")
    stq2 = [pr.sem("stq_a"), pr.sem("stq_b")]

    x3c = x3T.rearrange("(c p) t -> c p t", p=128)
    wc = wT.rearrange("(c p) o -> c p o", p=128)
    for c in range(NDM):
        pr.dma("sync", xs[:, c * TOK:(c + 1) * TOK], x3c[c], inc=ld)
    pr.dma("sync", bias_sb[:], biasc[:], inc=ld)
    pr.dma("sync", one_f[:], ones1x128_f[:], inc=ld)
    pr.dma("sync", sc_sb[:], scal[:], inc=ld)
    ld_all = ld.n
    pr.dma("gpsimd", one_bf[:], ones128_bf[:], inc=ldg)
    for c in range(NDM):
        pr.dma("gpsimd", xb[:, c * TOK:(c + 1) * TOK], x3c[c], inc=ldg)
    ldg_x = ldg.n
    for c in range(NDM):
        pr.dma("gpsimd", wb[:, c * DM:(c + 1) * DM], wc[c], inc=ldg)
    ldg_w = ldg.n

    alpha_mm = []
    for c in range(NDM):
        pr.add("scalar",
               lambda e, c=c: e.activation(absw[:], wb[:, c * DM:(c + 1) * DM], AF.Abs),
               waits=[(ldg, ldg_w)] + ([(mm, alpha_mm[-1])] if c else []), inc=se)
        pr.add("tensor", lambda e, c=c: [
            e.matmul(ps_big[0:1, 512 * j:512 * (j + 1)], one_bf[:],
                     absw[:, 512 * j:512 * (j + 1)],
                     start=(c == 0), stop=(c == NDM - 1)) for j in range(4)][-1],
               waits=[(se, se.n)], inc=mm)
        alpha_mm.append(mm.n)
    pr.add("vector", lambda e: e.tensor_scalar_mul(al_rowf[:], ps_big[0:1, :], 1.0 / DM),
           waits=[(mm, mm.n)], inc=ve)
    pr.add("tensor", lambda e: [
        e.matmul(ps_big[:, 512 * j:512 * (j + 1)], one_f[:],
                 al_rowf[0:1, 512 * j:512 * (j + 1)], start=True, stop=True)
        for j in range(4)][-1],
           waits=[(ve, ve.n)], inc=mm)
    mm_albc = mm.n

    wbin_ve = []
    for c in range(NDM):
        pr.add("scalar",
               lambda e, c=c: e.activation(sgn[:], wb[:, c * DM:(c + 1) * DM], AF.Sign),
               waits=[(ve, wbin_ve[-1])] if c else [], inc=se)
        pr.add("vector", lambda e, c=c: e.tensor_tensor(
            wb[:, c * DM:(c + 1) * DM], sgn[:], ps_big[:, :DM], ALU.mult),
               waits=[(se, se.n), (mm, mm_albc)], inc=ve)
        wbin_ve.append(ve.n)
    ve_wbin = ve.n

    resid_ve = []
    k = 0
    for oc in range(NDM):
        for tt in range(2):
            ps = ps_d[k % 2]

            def mmfn(e, oc=oc, tt=tt, ps=ps):
                last = None
                for ic in range(NDM):
                    last = e.matmul(ps[:], wb[:, ic * DM + 128 * oc: ic * DM + 128 * (oc + 1)],
                                    xb[:, ic * TOK + 512 * tt: ic * TOK + 512 * (tt + 1)],
                                    start=(ic == 0), stop=(ic == NDM - 1))
                return last
            waits = [(ve, ve_wbin), (ldg, ldg_w)]
            if k >= 2:
                waits.append((ve, resid_ve[k - 2]))
            pr.add("tensor", mmfn, waits=waits, inc=mm)
            sl = np.s_[:, oc * TOK + 512 * tt: oc * TOK + 512 * (tt + 1)]
            pr.add("vector", lambda e, sl=sl, oc=oc, ps=ps: e.scalar_tensor_tensor(
                xs[sl], xs[sl], bias_sb[:, oc:oc + 1], ps[:], ALU.add, ALU.add),
                   waits=[(mm, mm.n), (ld, ld_all)], inc=ve)
            resid_ve.append(ve.n)
            k += 1
    ve_resid = ve.n

    stat_mm = []
    for c in range(NDM):
        pr.add("vector", lambda e, c=c: e.tensor_copy(ybf[:], xs[:, c * TOK:(c + 1) * TOK]),
               waits=[(ve, ve_resid)] + ([(mm, stat_mm[-1])] if c else [(mm, mm_albc)]),
               inc=ve)
        pr.add("scalar", lambda e: e.activation(ysq[:], ybf[:], AF.Square),
               waits=[(ve, ve.n)], inc=se)

        def statfn(e, c=c):
            last = None
            for j in range(2):
                last = e.matmul(ps_big[0:1, 512 * j:512 * (j + 1)], one_bf[:],
                                ybf[:, 512 * j:512 * (j + 1)],
                                start=(c == 0), stop=(c == NDM - 1))
            for j in range(2):
                last = e.matmul(ps_big[0:1, TOK + 512 * j:TOK + 512 * (j + 1)], one_bf[:],
                                ysq[:, 512 * j:512 * (j + 1)],
                                start=(c == 0), stop=(c == NDM - 1))
            return last
        pr.add("tensor", statfn, waits=[(se, se.n), (ve, ve.n)], inc=mm)
        stat_mm.append(mm.n)

    pr.add("vector", lambda e: e.tensor_scalar_mul(stat_m[:], ps_big[0:1, 0:TOK], 1.0 / DM),
           waits=[(mm, mm.n)], inc=ve)
    pr.add("vector", lambda e: e.tensor_scalar_mul(stat_v[:], ps_big[0:1, TOK:2 * TOK], 1.0 / DM),
           waits=[], inc=ve)
    pr.add("vector", lambda e: e.tensor_mul(mrstd[:], stat_m[:], stat_m[:]), waits=[], inc=ve)
    pr.add("vector", lambda e: e.tensor_sub(stat_v[:], stat_v[:], mrstd[:]), waits=[], inc=ve)
    pr.add("scalar", lambda e: e.activation(rstd[:], stat_v[:], AF.Sqrt,
                                            bias=sc_sb[0:1, SCALARS["eps"]:SCALARS["eps"] + 1]),
           waits=[(ve, ve.n), (ld, ld_all)], inc=se)
    pr.add("vector", lambda e: e.reciprocal(rstd[:], rstd[:]), waits=[(se, se.n)], inc=ve)
    pr.add("vector", lambda e: e.tensor_mul(mrstd[:], stat_m[:], rstd[:]), waits=[], inc=ve)
    ve_stats = ve.n

    def bcfn(e):
        last = None
        for j in range(2):
            last = e.matmul(ps_big[:, 512 * j:512 * (j + 1)], one_f[:],
                            rstd[0:1, 512 * j:512 * (j + 1)], start=True, stop=True)
        for j in range(2):
            last = e.matmul(ps_big[:, TOK + 512 * j:TOK + 512 * (j + 1)], one_f[:],
                            mrstd[0:1, 512 * j:512 * (j + 1)], start=True, stop=True)
        return last
    pr.add("tensor", bcfn, waits=[(ve, ve_stats)], inc=mm)
    mm_bc = mm.n

    outc = outT.rearrange("(c p) t -> c p t", p=128)
    for c in range(NDM):
        res = res2[c % 2]
        sq = stq2[c % 2]
        waits = [(mm, mm_bc)]
        if c >= 2:
            waits.append((sq, sq.n))
        pr.add("vector", lambda e, c=c, res=res: e.tensor_mul(
            res[:], xs[:, c * TOK:(c + 1) * TOK], ps_big[:, 0:TOK]), waits=waits, inc=ve)
        pr.add("vector", lambda e, res=res: e.tensor_sub(res[:], res[:], ps_big[:, TOK:2 * TOK]),
               waits=[], inc=ve)
        pr.dma("sync", outc[c], res[:], waits=[(ve, ve.n)], inc=sq)
    pr.wait("sync", (stq2[0], stq2[0].n), (stq2[1], stq2[1].n))
    pr.build()
    return nc







def _mlp1_in_maps(x3T_slices, wT, bias, consts):
    biasc = np.ascontiguousarray(bias.reshape(NDM, 128).T).astype(np.float32)
    return [dict(
        x3T=np.ascontiguousarray(x3T_slices[cid]),
        wT=wT, biasc=biasc,
        ones128_bf=consts["ones128_bf"],
        ones1x128_f=consts["ones1x128_f"],
        scal=consts["scal"],
    ) for cid in range(NCORES)]


def _slice_stream(xT_full):
    out = []
    for cid in range(NCORES):
        b, q = cid // 4, cid % 4
        out.append(np.ascontiguousarray(xT_full[b][:, q * TOK:(q + 1) * TOK]))
    return out



from contextlib import ExitStack
import ml_dtypes

SEQ = 4096
HQ = 4
DH = 128
NCK0 = SEQ // 128
GATE_NORM = 16.0
RSC = float(1.0 / np.sqrt(128.0))


def build_attn0():
    nc = bass.Bass("TRN2", target_bir_lowering=False, debug=False)

    def di(n, sh, dt=F32):
        return nc.dram_tensor(n, sh, dt, kind="ExternalInput").ap()
    x0T = di("x0T", [DM, SEQ])
    wq = di("wq", [DM, 512])
    wk = di("wk", [DM, 128])
    wv = di("wv", [DM, 128])
    cosT = di("cosT", [128, SEQ], BF16)
    sinT = di("sinT", [128, SEQ], BF16)
    rotm = di("rotm", [128, 128], BF16)
    maskc = di("maskc", [128, 4 * 512], BF16)
    triu = di("triu", [128, 128])
    ident = di("ident", [128, 128], BF16)
    ident_f32 = di("ident_f32", [128, 128], F32)
    ones_f = di("ones_f", [128, 128], F32)
    ones128_bf = di("ones128_bf", [128, 1], BF16)
    ones1x128_f = di("ones1x128_f", [1, 128], F32)
    scal = di("scal", [128, len(SCALAR_VALS)])
    oT_d = nc.dram_tensor("oT_d", [512, SEQ], BF16, kind="ExternalOutput").ap()

    a = nc.alloc_sbuf_tensor
    sc_sb = a("sc_sb", [128, len(SCALAR_VALS)], F32)
    idb = a("idb", [128, 128], BF16)
    idf = a("idf", [128, 128], F32)
    onesf = a("onesf", [128, 128], F32)
    one_bf = a("one_bf", [128, 1], BF16)
    one1f = a("one1f", [1, 128], F32)
    tri_sb = a("tri_sb", [128, 128], F32)
    maskc_sb = a("maskc_sb", [128, 4 * 512], BF16)
    cos_sb = a("cos_sb", [128, SEQ], BF16)
    sin_sb = a("sin_sb", [128, SEQ], BF16)
    rot_sb = a("rot_sb", [128, 128], BF16)
    qT = a("qT", [128, HQ * SEQ], BF16)
    kT = a("kT", [128, SEQ], BF16)
    v_sb = a("v_sb", [128, NCK0 * 129], BF16)
    spc = a("spc", [128, 128], F32)
    csc = a("csc", [128, 128], F32)
    tkc = a("tkc", [128, 128], F32)
    tqc = a("tqc", [128, 128], F32)
    negc16 = a("negc16", [128, 1], F32)
    ktl = a("ktl", [128, 128], BF16)
    kt2 = a("kt2", [128, 128], BF16)
    kt2t = a("kt2t", [128, 128], BF16)
    qtl = a("qtl", [128, 128], BF16)
    ebf = a("ebf", [128, 128], BF16)
    mgl = a("mgl", [128, 128], BF16)
    s_st = a("s_st", [128, 128], F32)
    s_bf = a("s_bf", [128, 128], BF16)
    rrow = a("rrow", [1, 128], F32)
    dcol = a("dcol", [128, 1], F32)
    cend = a("cend", [128, 1], F32)
    tmpb = a("tmpb", [128, 512], BF16)
    den_col = a("den_col", [128, 1], F32)
    y_sb = a("y_sb", [128, 128], F32)
    ocmb = a("ocmb", [128, 128], BF16)
    obuf = a("obuf", [128, 512], BF16)

    p = nc.alloc_psum_tensor
    ps_a = p("ps_a", [128, 512], F32)
    ps_b = p("ps_b", [128, 512], F32)
    ps_t0 = p("ps_t0", [128, 512], BF16)
    ps_t1 = p("ps_t1", [128, 512], BF16)
    ps_sc = p("ps_sc", [128, 2048], F32)

    pr = Prog(nc)
    ldc = pr.sem("ldc")
    ldx = pr.sem("ldx")
    mm = pr.sem("mm")
    ve = pr.engine_sem("vector")
    se = pr.engine_sem("scalar")
    so = pr.sem("so")

    pr.dma("sync", sc_sb[:], scal[:], inc=ldc)
    pr.dma("sync", idf[:], ident_f32[:], inc=ldc)
    pr.dma("sync", onesf[:], ones_f[:], inc=ldc)
    pr.dma("sync", one1f[:], ones1x128_f[:], inc=ldc)
    pr.dma("sync", tri_sb[:], triu[:], inc=ldc)
    pr.dma("sync", cos_sb[:], cosT[:], inc=ldc)
    pr.dma("sync", sin_sb[:], sinT[:], inc=ldc)
    pr.dma("sync", rot_sb[:], rotm[:], inc=ldc)
    ldc_all = ldc.n

    ctx0 = ExitStack()
    wq_sb = ctx0.enter_context(nc.sbuf_tensor("wq_sb", [128, NDM * 512], BF16))
    wk_sb = ctx0.enter_context(nc.sbuf_tensor("wk_sb", [128, NDM * 128], BF16))
    wv_sb = ctx0.enter_context(nc.sbuf_tensor("wv_sb", [128, NDM * 128], BF16))
    xtg = [ctx0.enter_context(nc.sbuf_tensor(f"xtg{s}", [128, NDM * 512], BF16))
           for s in range(2)]

    pr.dma("gpsimd", idb[:], ident[:], inc=ldx)
    pr.dma("gpsimd", one_bf[:], ones128_bf[:], inc=ldx)
    pr.dma("gpsimd", maskc_sb[:], maskc[:], inc=ldx)
    wqc = wq.rearrange("(c p) t -> c p t", p=128)
    wkc = wk.rearrange("(c p) t -> c p t", p=128)
    wvc = wv.rearrange("(c p) t -> c p t", p=128)
    for c in range(NDM):
        pr.dma("gpsimd", wq_sb[:, c * 512:(c + 1) * 512], wqc[c], inc=ldx)
        pr.dma("gpsimd", wk_sb[:, c * 128:(c + 1) * 128], wkc[c], inc=ldx)
        pr.dma("gpsimd", wv_sb[:, c * 128:(c + 1) * 128], wvc[c], inc=ldx)
    ldx_all = ldx.n
    x0c = x0T.rearrange("(c p) t -> c p t", p=128)
    lx = [pr.sem("lx0"), pr.sem("lx1")]

    pr.add("vector", lambda e: e.memset(v_sb[:], 1.0), waits=[], inc=ve)
    pr.add("vector", lambda e: e.memset(s_st[:], 0.0), waits=[], inc=ve)
    pr.add("vector", lambda e: e.memset(s_bf[:], 0.0), waits=[], inc=ve)

    mm_tg_done = []
    for tg in range(8):
        sl = tg % 2
        for c in range(NDM):
            w = [(mm, mm_tg_done[tg - 2])] if (tg >= 2 and c == 0) else []
            pr.dma("gpsimd", xtg[sl][:, c * 512:(c + 1) * 512],
                   x0c[c][:, 512 * tg:512 * (tg + 1)], waits=w, inc=lx[sl])
        lx_full = lx[sl].n
        xv = xtg[sl]
        pp = 0
        for hs in range(HQ):
            ps = ps_a if pp % 2 == 0 else ps_b
            pp += 1

            def qfn(e, hs=hs, xv=xv, ps=ps):
                last = None
                for ic in range(NDM):
                    last = e.matmul(ps[:],
                                    wq_sb[:, ic * 512 + 128 * hs: ic * 512 + 128 * (hs + 1)],
                                    xv[:, ic * 512:(ic + 1) * 512],
                                    start=(ic == 0), stop=(ic == NDM - 1))
                return last
            pr.add("tensor", qfn,
                   waits=[(ldx, ldx_all), (lx[sl], lx_full), (ve, ve.n)], inc=mm)
            pr.add("vector", lambda e, hs=hs, tg=tg, ps=ps: e.tensor_copy(
                qT[:, hs * SEQ + 512 * tg: hs * SEQ + 512 * (tg + 1)], ps[:]),
                   waits=[(mm, mm.n)], inc=ve)
        ps = ps_a if pp % 2 == 0 else ps_b
        pp += 1

        def kfn(e, xv=xv, ps=ps):
            last = None
            for ic in range(NDM):
                last = e.matmul(ps[:],
                                wk_sb[:, ic * 128:(ic + 1) * 128],
                                xv[:, ic * 512:(ic + 1) * 512],
                                start=(ic == 0), stop=(ic == NDM - 1))
            return last
        pr.add("tensor", kfn, waits=[(ve, ve.n)], inc=mm)
        pr.add("vector", lambda e, tg=tg, ps=ps: e.tensor_copy(
            kT[:, 512 * tg:512 * (tg + 1)], ps[:]), waits=[(mm, mm.n)], inc=ve)
        ps = ps_a if pp % 2 == 0 else ps_b

        def vfn(e, xv=xv, ps=ps):
            last = None
            for ic in range(NDM):
                last = e.matmul(ps[:],
                                wv_sb[:, ic * 128:(ic + 1) * 128],
                                xv[:, ic * 512:(ic + 1) * 512],
                                start=(ic == 0), stop=(ic == NDM - 1))
            return last
        pr.add("tensor", vfn, waits=[(ve, ve.n)], inc=mm)
        pr.add("vector", lambda e, ps=ps: e.tensor_copy(obuf[:], ps[:]),
               waits=[(mm, mm.n)], inc=ve)
        for j in range(4):
            ck = 4 * tg + j
            pst = ps_t0 if j % 2 == 0 else ps_t1
            pr.add("tensor", lambda e, j=j, pst=pst: e.transpose(
                pst[:, 0:128], obuf[:, 128 * j:128 * (j + 1)], idb[:]),
                   waits=[(ve, ve.n)], inc=mm)
            pr.add("vector", lambda e, ck=ck, pst=pst: e.tensor_copy(
                v_sb[:, ck * 129:ck * 129 + 128], pst[:, 0:128]),
                   waits=[(mm, mm.n)], inc=ve)
        mm_tg_done.append(mm.n)
    ve_proj = ve.n
    ctx0.close()

    o_gla = a("o_gla", [128, NCK0 * HQ * 128], BF16)
    rq_col = a("rq_col", [128, NCK0 * HQ], F32)
    p_exp = a("p_exp", [128, 32 * 512], BF16)

    for ck in range(NCK0):
        ksl = np.s_[:, 128 * ck:128 * (ck + 1)]
        pr.add("scalar", lambda e, ksl=ksl: e.activation(spc[:], kT[ksl], AF.Exp,
                                                         scale=-1.0),
               waits=[(ve, ve_proj), (ldc, ldc_all)], inc=se)
        pr.add("scalar", lambda e: e.activation(
            spc[:], spc[:], AF.Ln, bias=sc_sb[:, SCALARS["one"]:SCALARS["one"] + 1]),
               waits=[], inc=se)
        pr.add("vector", lambda e: e.tensor_tensor_scan(
            csc[:], onesf[:], spc[:], 0.0, ALU.mult, ALU.add),
               waits=[(se, se.n)], inc=ve)
        pr.add("vector", lambda e: e.tensor_copy(cend[:], csc[:, 127:128]), waits=[], inc=ve)
        pr.add("vector", lambda e: e.tensor_scalar_mul(negc16[:], cend[:],
                                                       -1.0 / GATE_NORM),
               waits=[], inc=ve)
        pr.add("scalar", lambda e: e.activation(dcol[:], cend[:], AF.Exp,
                                                scale=-1.0 / GATE_NORM),
               waits=[(ve, ve.n)], inc=se)
        pr.add("scalar", lambda e, ksl=ksl: e.activation(ebf[:], kT[ksl], AF.Exp),
               waits=[], inc=se)
        pr.add("tensor", lambda e: e.matmul(ps_sc[0:1, 0:128], one_bf[:], ebf[:],
                                            start=True, stop=True),
               waits=[(se, se.n), (ve, ve.n)], inc=mm)
        pr.add("vector", lambda e: e.reciprocal(rrow[:], ps_sc[0:1, 0:128]),
               waits=[(mm, mm.n)], inc=ve)
        pr.add("tensor", lambda e: e.matmul(ps_sc[:, 1024:1152], one1f[:], rrow[:],
                                            start=True, stop=True),
               waits=[(ve, ve.n)], inc=mm)
        mm_rk = mm.n
        pr.add("vector", lambda e, ksl=ksl: e.scalar_tensor_tensor(
            tkc[:], csc[:], 1.0 / GATE_NORM, kT[ksl], ALU.mult, ALU.add),
               waits=[], inc=ve)
        pr.add("scalar", lambda e: e.activation(ktl[:], tkc[:], AF.Exp),
               waits=[(ve, ve.n)], inc=se)
        pr.add("vector", lambda e: e.tensor_tensor(ktl[:], ktl[:], ps_sc[:, 1024:1152],
                                                   ALU.mult),
               waits=[(se, se.n), (mm, mm_rk)], inc=ve)
        pr.add("scalar", lambda e: e.activation(kt2[:], tkc[:], AF.Exp,
                                                bias=negc16[:, 0:1]),
               waits=[(ve, ve.n)], inc=se)
        pr.add("vector", lambda e: e.tensor_tensor(kt2[:], kt2[:], ps_sc[:, 1024:1152],
                                                   ALU.mult),
               waits=[(se, se.n)], inc=ve)
        pr.add("tensor", lambda e: e.transpose(ps_t0[:, 0:128], kt2[:], idb[:]),
               waits=[(ve, ve.n)], inc=mm)
        pr.add("vector", lambda e: e.tensor_copy(kt2t[:], ps_t0[:, 0:128]),
               waits=[(mm, mm.n)], inc=ve)
        for h in range(HQ):
            qsl = np.s_[:, h * SEQ + 128 * ck: h * SEQ + 128 * (ck + 1)]
            pr.add("scalar", lambda e, qsl=qsl: e.activation(ebf[:], qT[qsl], AF.Exp),
                   waits=[(ve, ve.n)], inc=se)
            pr.add("tensor", lambda e: e.matmul(ps_sc[0:1, 0:128], one_bf[:], ebf[:],
                                                start=True, stop=True),
                   waits=[(se, se.n)], inc=mm)
            pr.add("vector", lambda e: e.reciprocal(rrow[:], ps_sc[0:1, 0:128]),
                   waits=[(mm, mm.n)], inc=ve)
            pr.add("tensor", lambda e: e.transpose(ps_sc[:, 1024:1025], rrow[:],
                                                   idf[0:1, 0:1]),
                   waits=[(ve, ve.n)], inc=mm)
            pr.add("vector", lambda e, ck=ck, h=h: e.tensor_copy(
                rq_col[:, ck * HQ + h: ck * HQ + h + 1], ps_sc[:, 1024:1025]),
                   waits=[(mm, mm.n)], inc=ve)
            pr.add("vector", lambda e, qsl=qsl: e.scalar_tensor_tensor(
                tqc[:], csc[:], -1.0 / GATE_NORM, qT[qsl], ALU.mult, ALU.add),
                   waits=[], inc=ve)
            pr.add("scalar", lambda e: e.activation(
                qtl[:], tqc[:], AF.Exp,
                bias=sc_sb[:, SCALARS["ln05"]:SCALARS["ln05"] + 1]),
                   waits=[(ve, ve.n)], inc=se)
            pr.add("tensor", lambda e: e.matmul(ps_sc[:, 256:384], ktl[:], qtl[:],
                                                start=True, stop=True),
                   waits=[(se, se.n)], inc=mm)
            pr.add("vector", lambda e: e.tensor_tensor(mgl[:], ps_sc[:, 256:384],
                                                       tri_sb[:], ALU.mult),
                   waits=[(mm, mm.n)], inc=ve)

            def ofn(e, ck=ck):
                e.matmul(ps_sc[:, 512:640], mgl[:], v_sb[:, ck * 129:ck * 129 + 128],
                         start=True, stop=False)
                return e.matmul(ps_sc[:, 512:640], qtl[:], s_bf[:],
                                start=False, stop=True)
            pr.add("tensor", ofn, waits=[(ve, ve.n), (se, se.n)], inc=mm)
            pr.add("scalar", lambda e, ck=ck, h=h: e.activation(
                o_gla[:, (ck * HQ + h) * 128:(ck * HQ + h + 1) * 128],
                ps_sc[:, 512:640], AF.Copy), waits=[(mm, mm.n)], inc=se)
        def cfn(e, ck=ck):
            return e.matmul(ps_sc[:, 1536:1664], kt2t[:],
                            v_sb[:, ck * 129:ck * 129 + 128], start=True, stop=True)
        pr.add("tensor", cfn, waits=[(ve, ve.n), (se, se.n)], inc=mm)
        pr.add("vector", lambda e: e.scalar_tensor_tensor(
            s_st[:], s_st[:], dcol[:, 0:1], ps_sc[:, 1536:1664], ALU.mult, ALU.add),
               waits=[(mm, mm.n)], inc=ve)
        pr.add("vector", lambda e: e.tensor_copy(s_bf[:], s_st[:]), waits=[], inc=ve)
    ve_gla = ve.n
    se_gla = se.n

    chunks = [(qT, h * SEQ + 512 * J, 512 * J) for h in range(HQ) for J in range(8)]
    chunks += [(kT, 512 * J, 512 * J) for J in range(8)]
    tmp_ve = []
    for i, (buf, c0, t0) in enumerate(chunks):
        ps = ps_a if i % 2 == 0 else ps_b
        sl = np.s_[:, c0:c0 + 512]
        if i < 2:
            w = [(ve, ve_gla), (se, se_gla)]
        else:
            w = [(ve, tmp_ve[i - 2])]
        pr.add("tensor", lambda e, sl=sl, ps=ps, buf=buf: e.matmul(
            ps[:], rot_sb[:], buf[sl], start=True, stop=True), waits=w, inc=mm)
        pr.add("vector", lambda e, sl=sl, buf=buf, t0=t0: e.tensor_tensor(
            buf[sl], buf[sl], cos_sb[:, t0:t0 + 512], ALU.mult),
               waits=[(mm, mm.n)], inc=ve)
        pr.add("vector", lambda e, ps=ps, t0=t0: e.tensor_tensor(
            tmpb[:], sin_sb[:, t0:t0 + 512], ps[:], ALU.mult),
               waits=[], inc=ve)
        tmp_ve.append(ve.n)
        pr.add("vector", lambda e, sl=sl, buf=buf: e.tensor_tensor(
            buf[sl], buf[sl], tmpb[:], ALU.add), waits=[], inc=ve)
    ve_rope = ve.n

    last_av_mm = [mm.n]
    for h in range(HQ):
        for J in range(8):
            for kg in range(J + 1):
                def sfn(e, h=h, J=J, kg=kg):
                    last = None
                    for s4 in range(4):
                        ki = 4 * kg + s4
                        last = e.matmul(ps_sc[:, 512 * s4:512 * (s4 + 1)],
                                        kT[:, 128 * ki:128 * (ki + 1)],
                                        qT[:, h * SEQ + 512 * J: h * SEQ + 512 * (J + 1)],
                                        start=True, stop=(kg != J))
                    if kg == J:
                        for s4 in range(4):
                            last = e.matmul(ps_sc[:, 512 * s4:512 * (s4 + 1)], idb[:],
                                            maskc_sb[:, 512 * s4:512 * (s4 + 1)],
                                            start=False, stop=True)
                    return last
                w = [(ve, ve_rope), (se, se.n)]
                if kg == 0:
                    w.append((mm, last_av_mm[-1]))
                pr.add("tensor", sfn, waits=w, inc=mm)
                mm_s = mm.n
                for half in range(2):
                    pr.add("scalar", lambda e, kg=kg, half=half: e.activation(
                        p_exp[:, 2048 * kg + 1024 * half: 2048 * kg + 1024 * (half + 1)],
                        ps_sc[:, 1024 * half:1024 * (half + 1)], AF.Exp, scale=RSC),
                           waits=[(mm, mm_s)], inc=se)
            se_exp = se.n
            for s in range(4):
                nki = 4 * J + s + 1
                ps = ps_a if s % 2 == 0 else ps_b

                def avfn(e, s=s, nki=nki, ps=ps):
                    last = None
                    for ki in range(nki):
                        last = e.matmul(
                            ps[:, 0:129],
                            p_exp[:, 512 * ki + 128 * s: 512 * ki + 128 * (s + 1)],
                            v_sb[:, ki * 129:(ki + 1) * 129],
                            start=(ki == 0), stop=(ki == nki - 1))
                    return last
                pr.add("tensor", avfn, waits=[(se, se_exp), (ve, ve.n)], inc=mm)
                last_av_mm.append(mm.n)
                pr.add("vector", lambda e, ps=ps: e.reciprocal(den_col[:], ps[:, 128:129]),
                       waits=[(mm, mm.n)], inc=ve)
                pr.add("vector", lambda e: e.tensor_scalar_mul(den_col[:], den_col[:], 0.5),
                       waits=[], inc=ve)
                pr.add("scalar", lambda e, ps=ps: e.activation(
                    y_sb[:], ps[:, 0:128], AF.Copy, scale=den_col[:, 0:1]),
                       waits=[(ve, ve.n)], inc=se)
                ck_g = 4 * J + s
                pr.add("vector", lambda e, ck_g=ck_g, h=h: e.scalar_tensor_tensor(
                    ocmb[:], o_gla[:, (ck_g * HQ + h) * 128:(ck_g * HQ + h + 1) * 128],
                    rq_col[:, ck_g * HQ + h: ck_g * HQ + h + 1], y_sb[:],
                    ALU.mult, ALU.add), waits=[(se, se.n)], inc=ve)
                pst = ps_t0 if s % 2 == 0 else ps_t1
                pr.add("tensor", lambda e, pst=pst: e.transpose(pst[:, 0:128], ocmb[:],
                                                                idb[:]),
                       waits=[(ve, ve.n)], inc=mm)
                pr.add("vector", lambda e, s=s, pst=pst: e.tensor_copy(
                    obuf[:, 128 * s:128 * (s + 1)], pst[:, 0:128]),
                       waits=[(mm, mm.n), (so, so.n)], inc=ve)
            pr.dma("sync", oT_d[128 * h:128 * (h + 1), 512 * J:512 * (J + 1)], obuf[:],
                   waits=[(ve, ve.n)], inc=so)
    pr.wait("sync", (so, so.n))
    pr.build()
    return nc


def attn0_in_maps(x0, inputs, consts):
    Wq = np.asarray(inputs["L0_attn_Wq"], np.float32)
    Wk = np.asarray(inputs["L0_attn_Wk"], np.float32)
    Wv = np.asarray(inputs["L0_attn_Wv"], np.float32)
    inv = 1.0 / (10000.0 ** (np.arange(0, 128, 2, dtype=np.float32) / 128.0))
    freqs = np.arange(SEQ, dtype=np.float32)[:, None] * inv[None, :]
    cos64 = np.ascontiguousarray(np.cos(freqs).T)
    sin64 = np.ascontiguousarray(np.sin(freqs).T)
    cosT = np.concatenate([cos64, cos64], 0).astype(ml_dtypes.bfloat16)
    sinT = np.concatenate([sin64, sin64], 0).astype(ml_dtypes.bfloat16)
    rotv = np.zeros((128, 128), np.float32)
    rotv[np.arange(64) + 64, np.arange(64)] = -1.0
    rotv[np.arange(64), np.arange(64) + 64] = 1.0
    kk = np.arange(128)
    qq = np.arange(512)
    maskv = np.zeros((128, 4 * 512), np.float32)
    for s in range(4):
        maskv[:, 512 * s:512 * (s + 1)] = np.where(
            (128 * s + kk[:, None]) > qq[None, :], BIG_NEG, 0.0)
    x0T = [np.ascontiguousarray(np.asarray(x0[b], np.float32).T) for b in range(2)]
    maps = []
    for cid in range(8):
        b, g = cid // 4, cid % 4
        maps.append(dict(
            x0T=x0T[b],
            wq=np.ascontiguousarray(Wq[:, 512 * g:512 * (g + 1)]),
            wk=np.ascontiguousarray(Wk[:, 128 * g:128 * (g + 1)]),
            wv=np.ascontiguousarray(Wv[:, 128 * g:128 * (g + 1)]),
            cosT=cosT, sinT=sinT, rotm=rotv.astype(ml_dtypes.bfloat16),
            maskc=maskv.astype(ml_dtypes.bfloat16),
            triu=np.triu(np.ones((128, 128), np.float32)),
            ident=consts["ident_bf"], ident_f32=np.eye(128, dtype=np.float32),
            ones_f=np.ones((128, 128), np.float32),
            ones128_bf=consts["ones128_bf"], ones1x128_f=consts["ones1x128_f"],
            scal=consts["scal"]))
    return maps

from contextlib import ExitStack

DFF = 5632
NFC = 44
NBLK, BFC = 4, 11


def build_mlp0(debug=False):
    nc = bass.Bass("TRN2", target_bir_lowering=False, debug=False)

    def di(n, sh, dt=F32):
        return nc.dram_tensor(n, sh, dt, kind="ExternalInput").ap()
    oT = di("oT", [DM, TOK], BF16)
    x0T = di("x0T", [DM, TOK])
    Wo_n = di("Wo_n", [DM, DM])
    gup = di("gup", [2 * NFC, DM, 128])
    agT = di("agT", [DM, 128])
    auT = di("auT", [DM, 128])
    wrT = di("wrT", [DM, 8])
    bgT = di("bgT", [16, 8 * DFF])
    buT = di("buT", [16, 8 * DFF])
    adT_r = di("adT_r", [128, DFF])
    bdT = di("bdT", [128, DM])
    wdT_q = di("wdT_q", [4, NFC, 128, 512])
    pattern2 = di("pattern2", [8, 128], BF16)
    ident_f32 = di("ident_f32", [128, 128], F32)
    ident = di("ident", [128, 128], BF16)
    ones128_bf = di("ones128_bf", [128, 1], BF16)
    ones1x128_f = di("ones1x128_f", [1, 128], F32)
    scal = di("scal", [128, len(SCALAR_VALS)])
    x2T = nc.dram_tensor("x2T", [DM, TOK], F32, kind="ExternalOutput").ap()
    kind_dbg = "ExternalOutput" if debug else "Internal"
    x1spill = nc.dram_tensor("x1spill", [DM, TOK], F32, kind=kind_dbg).ap()
    y2spill = nc.dram_tensor("y2spill", [DM, TOK], F32, kind=kind_dbg).ap()
    hspill = nc.dram_tensor("hspill", [DFF, 512], BF16, kind="Internal").ap()

    a = nc.alloc_sbuf_tensor

    pat2 = a("pat2", [8, 128], BF16)
    idb = a("idb", [128, 128], BF16)
    idf = a("idf", [128, 128], F32)
    one_bf = a("one_bf", [128, 1], BF16)
    one_f = a("one_f", [1, 128], F32)
    sc_sb = a("sc_sb", [128, len(SCALAR_VALS)], F32)
    stat_m = a("stat_m", [1, TOK], F32)
    stat_v = a("stat_v", [1, TOK], F32)
    rstd = a("rstd", [1, TOK], F32)
    mrstd = a("mrstd", [1, TOK], F32)
    ybf = a("ybf", [128, TOK], BF16)
    ysq = a("ysq", [128, TOK], BF16)
    xb = a("xb", [128, NDM * TOK], BF16)

    p = nc.alloc_psum_tensor
    ps_a = p("ps_a", [128, 512], F32)
    ps_b = p("ps_b", [128, 512], F32)
    ps_cf = p("ps_cf", [128, 512], F32)
    ps_tr = p("ps_tr", [128, 1024], BF16)
    ps_dn = p("ps_dn", [128, 2048], F32)

    pr = Prog(nc)
    ldc = pr.sem("ldc")
    ldga = pr.sem("ldga")
    ldwo = pr.sem("ldwo")
    ldsm = pr.sem("ldsm")
    gws = [pr.sem(f"gw{i}") for i in range(4)]
    bgs = pr.sem("bgs")
    bus = pr.sem("bus")
    wds = [pr.sem(f"wds{i}") for i in range(2)]
    hhs = [pr.sem(f"hhs{i}") for i in range(2)]
    xls = [pr.sem(f"xls{i}") for i in range(2)]
    stx1 = pr.sem("stx1")
    sth = pr.sem("sth")
    sy2 = [pr.sem(f"sy2{i}") for i in range(2)]
    sout = pr.sem("sout")
    ldy2 = pr.sem("ldy2")
    mm = pr.sem("mm")
    ve = pr.engine_sem("vector")
    se = pr.engine_sem("scalar")

    def barrier(*waits):
        for eng in ("sync", "scalar", "vector", "tensor", "gpsimd"):
            pr.wait(eng, *waits)

    x0c = x0T.rearrange("(c p) t -> c p t", p=128)
    oc_ = oT.rearrange("(c p) t -> c p t", p=128)
    woc = Wo_n.rearrange("(c p) t -> c p t", p=128)

    def emit_ln(buf, resid_ready, cast_to=None, spill=None, spill_sem=None):
        stat_mms = []
        for c in range(NDM):
            pr.add("vector", lambda e, c=c: e.tensor_copy(ybf[:], buf[:, c * TOK:(c + 1) * TOK]),
                   waits=[resid_ready] + ([(mm, stat_mms[-1])] if c else []), inc=ve)
            pr.add("scalar", lambda e: e.activation(ysq[:], ybf[:], AF.Square),
                   waits=[(ve, ve.n)], inc=se)

            def statfn(e, c=c):
                last = None
                for j in range(2):
                    last = e.matmul(ps_dn[0:1, 512 * j:512 * (j + 1)], one_bf[:],
                                    ybf[:, 512 * j:512 * (j + 1)],
                                    start=(c == 0), stop=(c == NDM - 1))
                for j in range(2):
                    last = e.matmul(ps_dn[0:1, TOK + 512 * j:TOK + 512 * (j + 1)], one_bf[:],
                                    ysq[:, 512 * j:512 * (j + 1)],
                                    start=(c == 0), stop=(c == NDM - 1))
                return last
            pr.add("tensor", statfn, waits=[(se, se.n), (ve, ve.n)], inc=mm)
            stat_mms.append(mm.n)
        pr.add("vector", lambda e: e.tensor_scalar_mul(stat_m[:], ps_dn[0:1, 0:TOK], 1.0 / DM),
               waits=[(mm, mm.n)], inc=ve)
        pr.add("vector", lambda e: e.tensor_scalar_mul(stat_v[:], ps_dn[0:1, TOK:2 * TOK],
                                                       1.0 / DM), waits=[], inc=ve)
        pr.add("vector", lambda e: e.tensor_mul(mrstd[:], stat_m[:], stat_m[:]), waits=[], inc=ve)
        pr.add("vector", lambda e: e.tensor_sub(stat_v[:], stat_v[:], mrstd[:]), waits=[], inc=ve)
        pr.add("scalar", lambda e: e.activation(
            rstd[:], stat_v[:], AF.Sqrt, bias=sc_sb[0:1, SCALARS["eps"]:SCALARS["eps"] + 1]),
               waits=[(ve, ve.n), (ldc, ldc.n)], inc=se)
        pr.add("vector", lambda e: e.reciprocal(rstd[:], rstd[:]), waits=[(se, se.n)], inc=ve)
        pr.add("vector", lambda e: e.tensor_mul(mrstd[:], stat_m[:], rstd[:]), waits=[], inc=ve)

        def bcfn(e):
            last = None
            for j in range(2):
                last = e.matmul(ps_dn[:, 512 * j:512 * (j + 1)], one_f[:],
                                rstd[0:1, 512 * j:512 * (j + 1)], start=True, stop=True)
            for j in range(2):
                last = e.matmul(ps_dn[:, TOK + 512 * j:TOK + 512 * (j + 1)], one_f[:],
                                mrstd[0:1, 512 * j:512 * (j + 1)], start=True, stop=True)
            return last
        pr.add("tensor", bcfn, waits=[(ve, ve.n)], inc=mm)
        mm_bc = mm.n
        for c in range(NDM):
            sl = np.s_[:, c * TOK:(c + 1) * TOK]
            pr.add("vector", lambda e, sl=sl: e.tensor_mul(buf[sl], buf[sl], ps_dn[:, 0:TOK]),
                   waits=[(mm, mm_bc)], inc=ve)
            pr.add("vector", lambda e, sl=sl: e.tensor_sub(buf[sl], buf[sl],
                                                           ps_dn[:, TOK:2 * TOK]),
                   waits=[], inc=ve)
            if cast_to is not None:
                pr.add("vector", lambda e, sl=sl: e.tensor_copy(cast_to[sl], buf[sl]),
                       waits=[], inc=ve)
            if spill is not None:
                pr.dma("sync", spill.rearrange("(c p) t -> c p t", p=128)[c], buf[sl],
                       waits=[(ve, ve.n)], inc=spill_sem)

    ctxA = ExitStack()
    xs = ctxA.enter_context(nc.sbuf_tensor("xs", [128, NDM * TOK], F32))
    ob = ctxA.enter_context(nc.sbuf_tensor("ob", [128, NDM * TOK], BF16))
    woh = ctxA.enter_context(nc.sbuf_tensor("woh", [128, NDM * TOK], BF16))

    for c in range(NDM):
        pr.dma("sync", xs[:, c * TOK:(c + 1) * TOK], x0c[c], inc=ldc)
    pr.dma("sync", one_f[:], ones1x128_f[:], inc=ldc)
    pr.dma("sync", sc_sb[:], scal[:], inc=ldc)
    pr.dma("sync", idf[:], ident_f32[:], inc=ldc)
    ld_x0 = ldc.n
    pr.dma("gpsimd", one_bf[:], ones128_bf[:], inc=ldga)
    pr.dma("gpsimd", idb[:], ident[:], inc=ldga)
    pr.dma("gpsimd", pat2[:], pattern2[:], inc=ldga)
    for c in range(NDM):
        pr.dma("gpsimd", ob[:, c * TOK:(c + 1) * TOK], oc_[c], inc=ldga)
    ldga_all = ldga.n

    resid_mm = []
    resid_ve = []
    k = 0
    for h in range(2):
        for c in range(NDM):
            w = [(mm, resid_mm[-1])] if (h == 1 and c == 0) else []
            pr.dma("gpsimd", woh[:, c * TOK:(c + 1) * TOK],
                   woc[c][:, 1024 * h:1024 * (h + 1)], waits=w, inc=ldwo)
        ldwo_h = ldwo.n
        for dmc_i in range(8):
            dmc = 8 * h + dmc_i
            for tt in range(2):
                ps = ps_a if k % 2 == 0 else ps_b

                def mmfn(e, dmc_i=dmc_i, tt=tt, ps=ps):
                    last = None
                    for ocn in range(NDM):
                        last = e.matmul(
                            ps[:],
                            woh[:, ocn * TOK + 128 * dmc_i: ocn * TOK + 128 * (dmc_i + 1)],
                            ob[:, ocn * TOK + 512 * tt: ocn * TOK + 512 * (tt + 1)],
                            start=(ocn == 0), stop=(ocn == NDM - 1))
                    return last
                waits = [(ldwo, ldwo_h), (ldga, ldga_all)]
                if k >= 2:
                    waits.append((ve, resid_ve[k - 2]))
                pr.add("tensor", mmfn, waits=waits, inc=mm)
                resid_mm.append(mm.n)
                sl = np.s_[:, dmc * TOK + 512 * tt: dmc * TOK + 512 * (tt + 1)]
                pr.add("vector", lambda e, sl=sl, ps=ps: e.tensor_add(xs[sl], xs[sl], ps[:]),
                       waits=[(mm, mm.n), (ldc, ld_x0)], inc=ve)
                resid_ve.append(ve.n)
                k += 1

    emit_ln(xs, (ve, ve.n), cast_to=xb, spill=x1spill, spill_sem=stx1)
    ve_ln1 = ve.n
    stx1_all = stx1.n
    ctxA.close()

    barrier((ve, ve_ln1), (stx1, stx1_all))
    ctxB = ExitStack()

    def B(n, sh, dt):
        return ctxB.enter_context(nc.sbuf_tensor(n, sh, dt))
    gb = B("gb", [128, BFC * 512], BF16)
    ub = B("ub", [128, BFC * 512], BF16)
    agb = B("agb", [128, NDM * 128], BF16)
    aub = B("aub", [128, NDM * 128], BF16)
    wrb = B("wrb", [128, NDM * 8], BF16)
    bgb = B("bgb", [16, BFC * 128], BF16)
    bub = B("bub", [16, BFC * 128], BF16)
    adb = B("adb", [128, DFF], BF16)
    bdb = B("bdb", [128, DM], BF16)
    cg_sb = B("cg_sb", [16, 8 * 512], BF16)
    cu_sb = B("cu_sb", [16, 8 * 512], BF16)
    he_sb = [B(f"he{i}", [128, 512], BF16) for i in range(2)]
    tg_sb = [B(f"tg{i}", [128, 512], BF16) for i in range(2)]
    wpat = B("wpat", [128, 4 * 128], F32)
    cw_sb = B("cw_sb", [128, 4 * 128], BF16)
    cwT_sb = B("cwT_sb", [128, 4 * 128], BF16)
    gw = [B(f"gw{i}", [128, DM], BF16) for i in range(4)]
    wd_t = [B(f"wdt{i}", [128, 512], BF16) for i in range(2)]
    hh_t = [B(f"hht{i}", [128, 512], BF16) for i in range(2)]
    rlog = B("rlog", [128, 8], F32)
    rsum = B("rsum", [128, 1], F32)
    rmax = B("rmax", [128, 1], F32)
    rp2 = B("rp2", [128, 8], F32)
    rmask = B("rmask", [128, 8], F32)
    rmask2 = B("rmask2", [128, 8], F32)
    rw_sb = B("rw_sb", [128, 8], F32)
    wfm = B("wfm", [8, 512], BF16)
    x1rl = [B(f"x1rl{i}", [128, 512], F32) for i in range(2)]

    hs_d = hspill.rearrange("(c p) t -> c p t", p=128)
    agc = agT.rearrange("(c p) t -> c p t", p=128)
    auc = auT.rearrange("(c p) t -> c p t", p=128)
    wrc = wrT.rearrange("(c p) t -> c p t", p=128)
    for c in range(NDM):
        pr.dma("gpsimd", agb[:, c * 128:(c + 1) * 128], agc[c], inc=ldsm)
        pr.dma("gpsimd", aub[:, c * 128:(c + 1) * 128], auc[c], inc=ldsm)
        pr.dma("gpsimd", wrb[:, c * 8:(c + 1) * 8], wrc[c], inc=ldsm)
    pr.dma("gpsimd", adb[:], adT_r[:], inc=ldsm)
    pr.dma("gpsimd", bdb[:], bdT[:], inc=ldsm)
    ldsm_all = ldsm.n

    x1s = x1spill.rearrange("(c p) t -> c p t", p=128)
    y2s_d = y2spill.rearrange("(c p) t -> c p t", p=128)

    for tt in range(2):
        barrier((ve, ve.n), (mm, mm.n), (se, se.n),
                (sth, sth.n), (sy2[0], sy2[0].n), (sy2[1], sy2[1].n))

        first_r = True
        for ttt in range(4):
            tok0 = 512 * tt + 128 * ttt

            def rlfn(e, tok0=tok0):
                last = None
                for ic in range(NDM):
                    last = e.matmul(ps_cf[:, 0:8],
                                    xb[:, ic * TOK + tok0: ic * TOK + tok0 + 128],
                                    wrb[:, ic * 8:(ic + 1) * 8],
                                    start=(ic == 0), stop=(ic == NDM - 1))
                return last
            waits = [(ldsm, ldsm_all), (ve, ve.n)] if first_r else [(ve, ve.n)]
            first_r = False
            pr.add("tensor", rlfn, waits=waits, inc=mm)
            pr.add("scalar", lambda e: e.activation(rlog[:], ps_cf[:, 0:8], AF.Exp),
                   waits=[(mm, mm.n)], inc=se)
            pr.add("vector", lambda e: e.reduce_sum(rsum[:], rlog[:], axis=AX.X),
                   waits=[(se, se.n)], inc=ve)
            pr.add("vector", lambda e: e.reciprocal(rsum[:], rsum[:]), waits=[], inc=ve)
            pr.add("vector", lambda e: e.tensor_scalar_mul(rlog[:], rlog[:], rsum[:, 0:1]),
                   waits=[], inc=ve)
            pr.add("vector", lambda e: e.reduce_max(rmax[:], rlog[:], axis=AX.X),
                   waits=[], inc=ve)
            pr.add("vector", lambda e: e.tensor_single_scalar(rmask[:], rlog[:], rmax[:, 0:1],
                                                              ALU.is_ge), waits=[], inc=ve)
            pr.add("vector", lambda e: e.scalar_tensor_tensor(rp2[:], rmask[:], BIG_NEG,
                                                              rlog[:], ALU.mult, ALU.add),
                   waits=[], inc=ve)
            pr.add("vector", lambda e: e.reduce_max(rmax[:], rp2[:], axis=AX.X),
                   waits=[], inc=ve)
            pr.add("vector", lambda e: e.tensor_single_scalar(rmask2[:], rp2[:], rmax[:, 0:1],
                                                              ALU.is_ge), waits=[], inc=ve)
            pr.add("vector", lambda e: e.tensor_add(rmask[:], rmask[:], rmask2[:]),
                   waits=[], inc=ve)
            pr.add("vector", lambda e: e.tensor_mul(rw_sb[:], rlog[:], rmask[:]),
                   waits=[], inc=ve)
            ve_rw = ve.n
            pr.add("tensor", lambda e: e.transpose(ps_cf[0:8, 128:256], rw_sb[:], idf[:]),
                   waits=[(ve, ve_rw)], inc=mm)
            pr.add("vector", lambda e, ttt=ttt: e.tensor_copy(
                wfm[:, 128 * ttt:128 * (ttt + 1)], ps_cf[0:8, 128:256]),
                   waits=[(mm, mm.n)], inc=ve)
        for ttt in range(4):
            pr.add("tensor", lambda e, ttt=ttt: e.matmul(
                ps_cf[:, 128:256], wfm[:, 128 * ttt:128 * (ttt + 1)], pat2[:],
                start=True, stop=True), waits=[(ve, ve.n)], inc=mm)
            pr.add("vector", lambda e, ttt=ttt: e.tensor_copy(
                wpat[:, 128 * ttt:128 * (ttt + 1)], ps_cf[:, 128:256]),
                   waits=[(mm, mm.n)], inc=ve)
        ve_wpat = ve.n

        cf_done = []
        for e_i in range(8):
            def cgfn(e, e_i=e_i, tt=tt):
                last = None
                for ic in range(NDM):
                    last = e.matmul(ps_a[0:16, :],
                                    agb[:, ic * 128 + 16 * e_i: ic * 128 + 16 * e_i + 16],
                                    xb[:, ic * TOK + 512 * tt: ic * TOK + 512 * (tt + 1)],
                                    start=(ic == 0), stop=(ic == NDM - 1))
                return last

            def cufn(e, e_i=e_i, tt=tt):
                last = None
                for ic in range(NDM):
                    last = e.matmul(ps_b[0:16, :],
                                    aub[:, ic * 128 + 16 * e_i: ic * 128 + 16 * e_i + 16],
                                    xb[:, ic * TOK + 512 * tt: ic * TOK + 512 * (tt + 1)],
                                    start=(ic == 0), stop=(ic == NDM - 1))
                return last
            waits = [(se, cf_done[-1])] if e_i else [(ve, ve_wpat)]
            pr.add("tensor", cgfn, waits=waits, inc=mm)
            pr.add("tensor", cufn, waits=[], inc=mm)
            mm_c = mm.n
            pr.add("scalar", lambda e, e_i=e_i: e.activation(
                cg_sb[:, 512 * e_i:512 * (e_i + 1)], ps_a[0:16, :], AF.Copy, scale=2.0),
                   waits=[(mm, mm_c)], inc=se)
            pr.add("scalar", lambda e, e_i=e_i: e.activation(
                cu_sb[:, 512 * e_i:512 * (e_i + 1)], ps_b[0:16, :], AF.Copy, scale=2.0),
                   waits=[], inc=se)
            cf_done.append(se.n)
        se_cf = se.n

        first_cf = True
        cf_mms = []
        for blk in range(NBLK):
            base_ve = []
            base_mms = []
            for j in range(BFC):
                fc = blk * BFC + j
                sg, su = gws[2 * (j % 2)], gws[2 * (j % 2) + 1]
                bg, bu = gw[2 * (j % 2)], gw[2 * (j % 2) + 1]
                w = [(mm, base_mms[j - 2])] if j >= 2 else [(ve, ve.n), (mm, mm.n)]
                pr.dma("gpsimd", bg[:].rearrange("p (c f) -> p c f", f=128),
                       gup[2 * fc].rearrange("(c p) f -> p c f", p=128), waits=w, inc=sg)
                pr.dma("gpsimd", bu[:].rearrange("p (c f) -> p c f", f=128),
                       gup[2 * fc + 1].rearrange("(c p) f -> p c f", p=128), inc=su)

                def gmm(e, bg=bg, tt=tt):
                    last = None
                    for ic in range(NDM):
                        last = e.matmul(ps_a[:], bg[:, ic * 128:(ic + 1) * 128],
                                        xb[:, ic * TOK + 512 * tt: ic * TOK + 512 * (tt + 1)],
                                        start=(ic == 0), stop=(ic == NDM - 1))
                    return last

                def umm(e, bu=bu, tt=tt):
                    last = None
                    for ic in range(NDM):
                        last = e.matmul(ps_b[:], bu[:, ic * 128:(ic + 1) * 128],
                                        xb[:, ic * TOK + 512 * tt: ic * TOK + 512 * (tt + 1)],
                                        start=(ic == 0), stop=(ic == NDM - 1))
                    return last
                waits = [(sg, sg.n), (su, su.n), (se, se_cf)]
                if j >= 1:
                    waits.append((ve, base_ve[-1]))
                pr.add("tensor", gmm, waits=waits, inc=mm)
                pr.add("tensor", umm, waits=[], inc=mm)
                mm_b = mm.n
                base_mms.append(mm_b)
                pr.add("vector", lambda e, j=j: e.tensor_copy(
                    gb[:, 512 * j:512 * (j + 1)], ps_a[:]),
                       waits=[(mm, mm_b), (sth, sth.n)], inc=ve)
                pr.add("vector", lambda e, j=j: e.tensor_copy(
                    ub[:, 512 * j:512 * (j + 1)], ps_b[:]), waits=[], inc=ve)
                base_ve.append(ve.n)
            ve_base = ve.n

            ex_ve = []
            for e_i in range(8):
                pr.dma("gpsimd", bgb[:], bgT[0:16, e_i * DFF + blk * BFC * 128:
                                             e_i * DFF + (blk + 1) * BFC * 128],
                       waits=[(mm, mm.n)], inc=bgs)
                pr.dma("gpsimd", bub[:], buT[0:16, e_i * DFF + blk * BFC * 128:
                                             e_i * DFF + (blk + 1) * BFC * 128],
                       inc=bus)
                ldg_be = [(bgs, bgs.n), (bus, bus.n)]
                for j in range(BFC):
                    fc = blk * BFC + j

                    def gemm(e, e_i=e_i, j=j):
                        e.matmul(ps_a[:], idb[:], gb[:, 512 * j:512 * (j + 1)],
                                 start=True, stop=False)
                        return e.matmul(ps_a[:], bgb[0:16, 128 * j:128 * (j + 1)],
                                        cg_sb[:, 512 * e_i:512 * (e_i + 1)],
                                        start=False, stop=True)

                    def uemm(e, e_i=e_i, j=j):
                        e.matmul(ps_b[:], idb[:], ub[:, 512 * j:512 * (j + 1)],
                                 start=True, stop=False)
                        return e.matmul(ps_b[:], bub[0:16, 128 * j:128 * (j + 1)],
                                        cu_sb[:, 512 * e_i:512 * (e_i + 1)],
                                        start=False, stop=True)
                    waits = [(ve, ve_base)] + ldg_be
                    if ex_ve:
                        waits.append((ve, ex_ve[-1]))
                    pr.add("tensor", gemm, waits=waits, inc=mm)
                    pr.add("tensor", uemm, waits=[], inc=mm)
                    mm_e = mm.n
                    par = j % 2
                    pr.add("scalar", lambda e, par=par: e.activation(
                        tg_sb[par][:], ps_a[:], AF.Silu), waits=[(mm, mm_e)], inc=se)
                    he_w = [(se, se.n)]
                    if len(cf_mms) >= 2:
                        he_w.append((mm, cf_mms[-2]))
                    pr.add("vector", lambda e, par=par: e.tensor_tensor(
                        he_sb[par][:], tg_sb[par][:], ps_b[:], ALU.mult),
                           waits=he_w, inc=ve)
                    ex_ve.append(ve.n)
                    is_last = (blk == NBLK - 1 and e_i == 7 and j == BFC - 1)

                    def cfmm(e, e_i=e_i, fc=fc, par=par, first=first_cf, is_last=is_last):
                        last = None
                        for ttt in range(4):
                            last = e.matmul(
                                ps_cf[:, 128 * ttt + 16 * e_i: 128 * ttt + 16 * e_i + 16],
                                he_sb[par][:, 128 * ttt:128 * (ttt + 1)],
                                adb[:, 128 * fc + 16 * e_i: 128 * fc + 16 * e_i + 16],
                                start=(first and ttt == 0), stop=(is_last and ttt == 3))
                        return last
                    pr.add("tensor", cfmm, waits=[(ve, ve.n)], inc=mm)
                    cf_mms.append(mm.n)
                    first_cf = False

            h_ve = []
            for j in range(BFC):
                fc = blk * BFC + j
                par = j % 2
                hw = [(ve, ex_ve[-1])]
                if j >= 2:
                    hw.append((ve, h_ve[-2]))
                pr.add("scalar", lambda e, j=j, par=par: e.activation(
                    tg_sb[par][:], gb[:, 512 * j:512 * (j + 1)], AF.Silu),
                       waits=hw, inc=se)
                pr.add("vector", lambda e, j=j, par=par: e.tensor_tensor(
                    gb[:, 512 * j:512 * (j + 1)], tg_sb[par][:],
                    ub[:, 512 * j:512 * (j + 1)], ALU.mult), waits=[(se, se.n)], inc=ve)
                h_ve.append(ve.n)
                pr.dma("sync", hs_d[fc], gb[:, 512 * j:512 * (j + 1)],
                       waits=[(ve, ve.n)], inc=sth)
        sth_all = sth.n

        mm_cf = cf_mms[-1]
        for ttt in range(4):
            pr.add("vector", lambda e, ttt=ttt: e.tensor_tensor(
                cw_sb[:, 128 * ttt:128 * (ttt + 1)], ps_cf[:, 128 * ttt:128 * (ttt + 1)],
                wpat[:, 128 * ttt:128 * (ttt + 1)], ALU.mult),
                   waits=[(mm, mm_cf)], inc=ve)
            pr.add("tensor", lambda e, ttt=ttt: e.transpose(
                ps_tr[:, 128 * ttt:128 * (ttt + 1)], cw_sb[:, 128 * ttt:128 * (ttt + 1)],
                idb[:]), waits=[(ve, ve.n)], inc=mm)
            pr.add("vector", lambda e, ttt=ttt: e.tensor_copy(
                cwT_sb[:, 128 * ttt:128 * (ttt + 1)], ps_tr[:, 128 * ttt:128 * (ttt + 1)]),
                   waits=[(mm, mm.n)], inc=ve)
        ve_cw = ve.n

        dn_all = []
        for q in range(4):
            for fc in range(NFC):
                buf = wd_t[fc % 2]
                hbuf = hh_t[fc % 2]
                sw = wds[fc % 2]
                sh_ = hhs[fc % 2]
                if len(dn_all) >= 2:
                    w = [(mm, dn_all[-2] if fc % 2 == len(dn_all) % 2 else dn_all[-1])]
                elif dn_all:
                    w = [(mm, dn_all[-1])]
                else:
                    w = []
                if fc < 2:
                    w = w + [(ve, ve_cw), (sth, sth_all)]
                pr.dma("gpsimd", buf[:], wdT_q[q, fc], waits=w, inc=sw)
                pr.dma("scalar", hbuf[:], hs_d[fc], waits=w, inc=sh_)

                def dmm(e, fc=fc, buf=buf, hbuf=hbuf):
                    last = None
                    for i in range(4):
                        last = e.matmul(ps_dn[:, 512 * i:512 * (i + 1)],
                                        buf[:, 128 * i:128 * (i + 1)],
                                        hbuf[:],
                                        start=(fc == 0), stop=False)
                    return last
                dw = [(sw, sw.n), (sh_, sh_.n)]
                if fc == 0:
                    dw.append((ve, ve.n))
                pr.add("tensor", dmm, waits=dw, inc=mm)
                dn_all.append(mm.n)

            def delmm(e, q=q):
                last = None
                for i in range(4):
                    for ttt in range(4):
                        last = e.matmul(
                            ps_dn[:, 512 * i + 128 * ttt: 512 * i + 128 * (ttt + 1)],
                            bdb[:, (4 * q + i) * 128:(4 * q + i + 1) * 128],
                            cwT_sb[:, 128 * ttt:128 * (ttt + 1)],
                            start=False, stop=(ttt == 3))
                return last
            pr.add("tensor", delmm, waits=[(ve, ve_cw)], inc=mm)
            mm_q = mm.n
            for i in range(4):
                dmc = 4 * q + i
                buf = x1rl[i % 2]
                sx = xls[i % 2]
                sy = sy2[i % 2]
                pr.dma("sync", buf[:], x1s[dmc][:, 512 * tt:512 * (tt + 1)],
                       waits=[(sy, sy.n)], inc=sx)
                pr.add("vector", lambda e, i=i, buf=buf: e.tensor_add(
                    buf[:], buf[:], ps_dn[:, 512 * i:512 * (i + 1)]),
                       waits=[(mm, mm_q), (sx, sx.n)], inc=ve)
                pr.dma("sync", y2s_d[dmc][:, 512 * tt:512 * (tt + 1)], buf[:],
                       waits=[(ve, ve.n)], inc=sy)

    ctxB.close()

    barrier((sy2[0], sy2[0].n), (sy2[1], sy2[1].n), (ve, ve.n), (mm, mm.n), (se, se.n))
    ctxC = ExitStack()
    y2s = ctxC.enter_context(nc.sbuf_tensor("y2s", [128, NDM * TOK], F32))
    for c in range(NDM):
        pr.dma("sync", y2s[:, c * TOK:(c + 1) * TOK], y2s_d[c], inc=ldy2)
    emit_ln(y2s, (ldy2, ldy2.n), cast_to=None, spill=x2T, spill_sem=sout)
    pr.wait("sync", (sout, sout.n))
    ctxC.close()
    pr.build()
    return nc


def mlp0_in_maps(oT_slices, x0T_slices, inputs, consts):
    import ml_dtypes
    Wo = np.asarray(inputs["L0_attn_Wo"], np.float32)
    Wg = np.asarray(inputs["L0_ffn_Wg"], np.float32)
    Wu = np.asarray(inputs["L0_ffn_Wu"], np.float32)
    Wd = np.asarray(inputs["L0_ffn_Wd"], np.float32)
    Wr = np.asarray(inputs["L0_ffn_Wr"], np.float32)
    Ag = np.asarray(inputs["L0_ffn_Ag"], np.float32)
    Bg = np.asarray(inputs["L0_ffn_Bg"], np.float32)
    Au = np.asarray(inputs["L0_ffn_Au"], np.float32)
    Bu = np.asarray(inputs["L0_ffn_Bu"], np.float32)
    Ad = np.asarray(inputs["L0_ffn_Ad"], np.float32)
    Bd = np.asarray(inputs["L0_ffn_Bd"], np.float32)

    WgT = Wg.T
    WuT = Wu.T
    g_ch = WgT.reshape(DM, NFC, 128).transpose(1, 0, 2)
    u_ch = WuT.reshape(DM, NFC, 128).transpose(1, 0, 2)
    gupv = np.empty((2 * NFC, DM, 128), np.float32)
    gupv[0::2] = g_ch
    gupv[1::2] = u_ch
    agT = np.concatenate([Ag[e].T for e in range(8)], axis=1)
    auT = np.concatenate([Au[e].T for e in range(8)], axis=1)
    wrT = Wr.T
    bgT = np.concatenate([Bg[e].T for e in range(8)], axis=1)
    buT = np.concatenate([Bu[e].T for e in range(8)], axis=1)
    adT = np.concatenate([Ad[e].T for e in range(8)], axis=1)
    adT_r = np.ascontiguousarray(
        adT.reshape(NFC, 128, 128).transpose(1, 0, 2).reshape(128, NFC * 128))
    bdT = np.concatenate([Bd[e].T for e in range(8)], axis=0)
    wdT = Wd.T
    wdT_q = np.ascontiguousarray(
        wdT.reshape(NFC, 128, 4, 512).transpose(2, 0, 1, 3))
    pattern2 = np.zeros((8, 128), np.float32)
    for e in range(8):
        pattern2[e, 16 * e:16 * (e + 1)] = 2.0
    shared = dict(
        Wo_n=Wo, gup=gupv, agT=np.ascontiguousarray(agT), auT=np.ascontiguousarray(auT),
        wrT=np.ascontiguousarray(wrT), bgT=np.ascontiguousarray(bgT),
        buT=np.ascontiguousarray(buT), adT_r=adT_r, bdT=np.ascontiguousarray(bdT),
        wdT_q=wdT_q, pattern2=pattern2.astype(ml_dtypes.bfloat16),
        ident_f32=np.eye(128, dtype=np.float32),
        ident=consts["ident_bf"], ones128_bf=consts["ones128_bf"],
        ones1x128_f=consts["ones1x128_f"], scal=consts["scal"])
    maps = []
    for cid in range(8):
        m = dict(shared)
        m["oT"] = np.ascontiguousarray(oT_slices[cid]).astype(ml_dtypes.bfloat16) \
            if oT_slices[cid].dtype != ml_dtypes.bfloat16 else np.ascontiguousarray(oT_slices[cid])
        m["x0T"] = np.ascontiguousarray(x0T_slices[cid])
        maps.append(m)
    return maps

from contextlib import ExitStack
import ml_dtypes

INTER = 4096
NH = 64
HD = 64
NST = 128
CONV_DIM = INTER + 2 * NST
D_IN = 2 * INTER + 2 * NST + NH
NCH = CONV_DIM // 128
NCK = 8
HALO = 128
TOKH = TOK + HALO


def build_attn1a():
    nc = bass.Bass("TRN2", target_bir_lowering=False, debug=False)

    def di(n, sh, dt=F32):
        return nc.dram_tensor(n, sh, dt, kind="ExternalInput").ap()
    x2Th = di("x2Th", [DM, TOKH])
    w_in = di("w_in", [DM, D_IN])
    convw = di("convw", [128, NCH * 4])
    convb = di("convb", [128, NCH])
    dtb_bc = di("dtb_bc", [128, NH])
    a_col = di("a_col", [NH, 1])
    triu = di("triu", [128, 128])
    masklt = di("masklt", [128, 128])
    ident = di("ident", [128, 128], BF16)
    ident_f32 = di("ident_f32", [128, 128], F32)
    ones_f = di("ones_f", [128, 128], F32)
    ones1x128_f = di("ones1x128_f", [1, 128], F32)
    scal = di("scal", [128, len(SCALAR_VALS)])

    def do(n, sh, dt=F32):
        return nc.dram_tensor(n, sh, dt, kind="ExternalOutput").ap()
    gate_d = do("gate_d", [NCK, 128, INTER], BF16)
    xtok_d = do("xtok_d", [NCK, 128, INTER], BF16)
    intra_d = do("intra_d", [NCK, 128, INTER], BF16)
    cfm_d = do("cfm_d", [128, TOK], BF16)
    gtok_d = do("gtok_d", [NCK, 128, NH])
    tc_d = do("tc_d", [NCK, 128, INTER])
    dch_d = do("dch_d", [NCK, 128, NH])
    uspan_d = do("uspan_d", [128, INTER])
    dspan_d = do("dspan_d", [128, NH])

    a = nc.alloc_sbuf_tensor
    sc_sb = a("sc_sb", [128, len(SCALAR_VALS)], F32)
    idb = a("idb", [128, 128], BF16)
    idf = a("idf", [128, 128], F32)
    onef = a("onef", [128, 128], F32)
    one1f = a("one1f", [1, 128], F32)
    tri_sb = a("tri_sb", [128, 128], F32)
    cw_sb = a("cw_sb", [128, NCH * 4], F32)
    cb_sb = a("cb_sb", [128, NCH], F32)
    dtb_sb = a("dtb_sb", [128, NH], F32)
    acol_sb = a("acol_sb", [NH, 1], F32)
    dt_tok = a("dt_tok", [128, NCK * NH], F32)
    dt_fm = a("dt_fm", [NH, TOK], F32)
    da_fm = a("da_fm", [NH, TOK], F32)
    ca_fm = a("ca_fm", [NH, TOK], F32)
    gt_tmp = a("gt_tmp", [128, NH], F32)
    ca_tok = a("ca_tok", [128, NCK * NH], F32)
    tfm = a("tfm", [NH, 128], F32)
    mlt_sb = a("mlt_sb", [128, 128], F32)
    cs_sb = a("cs_sb", [128, 128], F32)
    d1_sb = a("d1_sb", [128, 128], F32)
    l_sb = a("l_sb", [128, 128], F32)
    m_sb = a("m_sb", [128, 128], BF16)
    scores_sb = a("scores_sb", [128, 128], F32)
    ftok = a("ftok", [128, NCK * NH], F32)
    dce_col = a("dce_col", [NH, 1], F32)
    dch_sb = a("dch_sb", [128, NCK * NH], F32)
    drow64 = a("drow64", [1, NH], F32)
    dsp_sb = a("dsp_sb", [128, NH], F32)
    zrow = a("zrow", [128, NH], F32)
    z2 = a("z2", [128, NH], F32)
    z3 = a("z3", [128, NH], F32)
    ev1 = a("ev1", [128, 512], BF16)
    ev2 = a("ev2", [128, 512], BF16)
    xconv = a("xconv", [128, NCH * TOK], BF16)
    xbc_dram = nc.dram_tensor("xbc_dram", [NCH, 128, TOKH], BF16, kind="Internal").ap()

    p = nc.alloc_psum_tensor
    ps_a = p("ps_a", [128, 512], F32)
    ps_b = p("ps_b", [128, 512], F32)
    ps_t2 = [p("ps_t0", [128, 512], BF16), p("ps_t1", [128, 512], BF16)]
    ps_f = p("ps_f", [128, 512], F32)
    ps_g = p("ps_g", [128, 1536], F32)

    pr = Prog(nc)
    ldc = pr.sem("ldc")
    ldx = pr.sem("ldx")
    wins = [pr.sem(f"win{i}") for i in range(2)]
    mm = pr.sem("mm")
    ve = pr.engine_sem("vector")
    se = pr.engine_sem("scalar")
    sg = pr.sem("sg")
    sxb = pr.sem("sxb")
    sx = pr.sem("sx")
    si = pr.sem("si")
    stc = pr.sem("stc")
    sm = pr.sem("sm")

    ctx0 = ExitStack()
    xb2 = ctx0.enter_context(nc.sbuf_tensor("xb2", [128, NDM * TOKH], BF16))
    win_t = [ctx0.enter_context(nc.sbuf_tensor(f"winb{i}", [128, NDM * 512], BF16))
             for i in range(2)]

    x2c = x2Th.rearrange("(c p) t -> c p t", p=128)
    for c in range(NDM):
        pr.dma("gpsimd", xb2[:, c * TOKH:(c + 1) * TOKH], x2c[c], inc=ldx)
    pr.dma("gpsimd", idb[:], ident[:], inc=ldx)
    ldx_all = ldx.n
    pr.dma("sync", sc_sb[:], scal[:], inc=ldc)
    pr.dma("sync", idf[:], ident_f32[:], inc=ldc)
    pr.dma("sync", onef[:], ones_f[:], inc=ldc)
    pr.dma("sync", one1f[:], ones1x128_f[:], inc=ldc)
    pr.dma("sync", tri_sb[:], triu[:], inc=ldc)
    pr.dma("sync", mlt_sb[:], masklt[:], inc=ldc)
    pr.dma("sync", cw_sb[:], convw[:], inc=ldc)
    pr.dma("sync", cb_sb[:], convb[:], inc=ldc)
    pr.dma("sync", dtb_sb[:], dtb_bc[:], inc=ldc)
    pr.dma("sync", acol_sb[:], a_col[:], inc=ldc)
    ldc_all = ldc.n

    winc = w_in.rearrange("(c p) t -> c p t", p=128)

    last_gate_mm = []
    for g in range(8):
        buf = win_t[g % 2]
        sem = wins[g % 2]
        w = [(mm, last_gate_mm[g - 2])] if g >= 2 else []
        for ic in range(NDM):
            pr.dma("gpsimd", buf[:, ic * 512:(ic + 1) * 512],
                   winc[ic][:, 512 * g:512 * (g + 1)], waits=w if ic == 0 else [], inc=sem)
        for ttt in range(NCK):
            ps = ps_a if ttt % 2 == 0 else ps_b

            def gfn(e, ttt=ttt, buf=buf, ps=ps):
                last = None
                for ic in range(NDM):
                    last = e.matmul(ps[:],
                                    xb2[:, ic * TOKH + HALO + 128 * ttt:
                                        ic * TOKH + HALO + 128 * (ttt + 1)],
                                    buf[:, ic * 512:(ic + 1) * 512],
                                    start=(ic == 0), stop=(ic == NDM - 1))
                return last
            waits = [(sem, sem.n), (ldx, ldx_all), (se, se.n)]
            pr.add("tensor", gfn, waits=waits, inc=mm)
            buf_ev = ev1 if ttt % 2 == 0 else ev2
            pr.add("scalar", lambda e, ps=ps, buf_ev=buf_ev: e.activation(
                buf_ev[:], ps[:], AF.Silu), waits=[(mm, mm.n), (sg, sg.n)], inc=se)
            pr.dma("sync", gate_d[ttt][:, 512 * g:512 * (g + 1)], buf_ev[:],
                   waits=[(se, se.n)], inc=sg)
        last_gate_mm.append(mm.n)

    tok_tiles = [(0, 512), (512, 512), (1024, 128)]
    last_xbc_mm = []
    for fc in range(NCH):
        buf = win_t[fc % 2]
        sem = wins[fc % 2]
        w = [(mm, last_xbc_mm[fc - 2])] if fc >= 2 else [(mm, mm.n)]
        for ic in range(NDM):
            pr.dma("gpsimd", buf[:, ic * 512:ic * 512 + 128],
                   winc[ic][:, INTER + 128 * fc:INTER + 128 * (fc + 1)],
                   waits=w if ic == 0 else [], inc=sem)
        for ti, (t0, tw) in enumerate(tok_tiles):
            ps = ps_a if ti % 2 == 0 else ps_b

            def xfn(e, t0=t0, tw=tw, buf=buf, ps=ps):
                last = None
                for ic in range(NDM):
                    last = e.matmul(ps[:, 0:tw],
                                    buf[:, ic * 512:ic * 512 + 128],
                                    xb2[:, ic * TOKH + t0: ic * TOKH + t0 + tw],
                                    start=(ic == 0), stop=(ic == NDM - 1))
                return last
            pr.add("tensor", xfn, waits=[(sem, sem.n), (ve, ve.n)], inc=mm)
            evb = ev1 if ti % 2 == 0 else ev2
            pr.add("vector", lambda e, tw=tw, ps=ps, evb=evb: e.tensor_copy(
                evb[:, 0:tw], ps[:, 0:tw]),
                   waits=[(mm, mm.n), (sxb, sxb.n), (sg, sg.n)], inc=ve)
            pr.dma("sync", xbc_dram[fc][:, t0:t0 + tw], evb[:, 0:tw],
                   waits=[(ve, ve.n)], inc=sxb)
        last_xbc_mm.append(mm.n)

    buf = win_t[0]
    sem = wins[0]
    for ic in range(NDM):
        pr.dma("gpsimd", buf[:, ic * 512:ic * 512 + NH],
               winc[ic][:, 2 * INTER + 2 * NST: 2 * INTER + 2 * NST + NH],
               waits=[(mm, mm.n)] if ic == 0 else [], inc=sem)
    for ttt in range(NCK):
        def dfn(e, ttt=ttt, buf=buf):
            last = None
            for ic in range(NDM):
                last = e.matmul(ps_f[:, 0:NH],
                                xb2[:, ic * TOKH + HALO + 128 * ttt:
                                    ic * TOKH + HALO + 128 * (ttt + 1)],
                                buf[:, ic * 512:ic * 512 + NH],
                                start=(ic == 0), stop=(ic == NDM - 1))
            return last
        pr.add("tensor", dfn, waits=[(sem, sem.n), (ve, ve.n)], inc=mm)
        pr.add("vector", lambda e: e.tensor_add(zrow[:], ps_f[:, 0:NH], dtb_sb[:]),
               waits=[(mm, mm.n), (ldc, ldc_all)], inc=ve)
        pr.add("scalar", lambda e: e.activation(z2[:], zrow[:], AF.Abs),
               waits=[(ve, ve.n)], inc=se)
        pr.add("scalar", lambda e: e.activation(z2[:], z2[:], AF.Exp, scale=-1.0),
               waits=[], inc=se)
        pr.add("scalar", lambda e: e.activation(
            z2[:], z2[:], AF.Ln, bias=sc_sb[:, SCALARS["one"]:SCALARS["one"] + 1]),
               waits=[], inc=se)
        pr.add("vector", lambda e: e.tensor_scalar_max(z3[:], zrow[:], 0.0),
               waits=[(se, se.n)], inc=ve)
        pr.add("vector", lambda e, ttt=ttt: e.tensor_add(
            dt_tok[:, ttt * NH:(ttt + 1) * NH], z3[:], z2[:]), waits=[], inc=ve)
    ve_dt = ve.n
    ctx0.close()

    xtokc = a("xtokc", [128, INTER], BF16)
    xt_sb = a("xt_sb", [128, INTER], BF16)
    btok = a("btok", [128, 128], BF16)
    mask_sb = a("mask_sb", [128, 128], BF16)
    evf = [a(f"evf{i}", [128, 512], F32) for i in range(2)]
    s_sb = a("s_sb", [128, INTER], F32)
    convacc = a("convacc", [128, TOK], F32)
    xbt = [a(f"xbt{i}", [128, TOKH], BF16) for i in range(2)]
    lxb = [pr.sem(f"lxb{i}") for i in range(2)]
    conv_ve = []
    for fc in range(NCH):
        buf = xbt[fc % 2]
        sem = lxb[fc % 2]
        w = [(sxb, sxb.n)] if fc < 2 else [(ve, conv_ve[fc - 2])]
        pr.dma("scalar", buf[:], xbc_dram[fc], waits=w, inc=sem)
        pr.add("vector", lambda e, fc=fc, buf=buf: e.tensor_scalar(
            convacc[:], buf[:, 125:125 + TOK],
            cw_sb[:, 4 * fc:4 * fc + 1], cb_sb[:, fc:fc + 1], ALU.mult, ALU.add),
               waits=[(sem, sem.n), (se, se.n)], inc=ve)
        for kk in range(1, 4):
            pr.add("vector", lambda e, fc=fc, kk=kk, buf=buf: e.scalar_tensor_tensor(
                convacc[:], buf[:, 125 + kk: 125 + kk + TOK],
                cw_sb[:, 4 * fc + kk:4 * fc + kk + 1], convacc[:], ALU.mult, ALU.add),
                   waits=[], inc=ve)
        pr.add("scalar", lambda e, fc=fc: e.activation(
            xconv[:, fc * TOK:(fc + 1) * TOK], convacc[:], AF.Silu),
               waits=[(ve, ve.n)], inc=se)
        conv_ve.append(ve.n)
    se_conv = se.n
    pr.dma("sync", cfm_d[:], xconv[:, 33 * TOK:34 * TOK], waits=[(se, se_conv)], inc=sm)

    for ttt in range(NCK):
        pr.add("tensor", lambda e, ttt=ttt: e.transpose(
            ps_f[0:NH, 0:128], dt_tok[:, ttt * NH:(ttt + 1) * NH], idf[:]),
               waits=[(ve, ve.n)], inc=mm)
        pr.add("vector", lambda e, ttt=ttt: e.tensor_copy(
            dt_fm[:, 128 * ttt:128 * (ttt + 1)], ps_f[0:NH, 0:128]),
               waits=[(mm, mm.n)], inc=ve)
    pr.add("vector", lambda e: e.tensor_scalar_mul(da_fm[:], dt_fm[:], acol_sb[:, 0:1]),
           waits=[], inc=ve)
    for ttt in range(NCK):
        pr.add("vector", lambda e, ttt=ttt: e.tensor_tensor_scan(
            ca_fm[:, 128 * ttt:128 * (ttt + 1)], onef[0:NH, :],
            da_fm[:, 128 * ttt:128 * (ttt + 1)], 0.0, ALU.mult, ALU.add),
               waits=[], inc=ve)
    ve_ca = ve.n
    for ttt in range(NCK):
        pr.add("tensor", lambda e, ttt=ttt: e.transpose(
            ps_f[0:128, 0:NH], ca_fm[:, 128 * ttt:128 * (ttt + 1)], idf[0:NH, 0:NH]),
               waits=[(ve, ve_ca), (se, se.n), (ve, ve.n)], inc=mm)
        pr.add("vector", lambda e, ttt=ttt: e.tensor_copy(
            ca_tok[:, ttt * NH:(ttt + 1) * NH], ps_f[0:128, 0:NH]),
               waits=[(mm, mm.n)], inc=ve)
        pr.add("scalar", lambda e, ttt=ttt: e.activation(
            gt_tmp[:], ca_tok[:, ttt * NH:(ttt + 1) * NH], AF.Exp),
               waits=[(ve, ve.n), (sm, sm.n)], inc=se)
        pr.dma("sync", gtok_d[ttt], gt_tmp[:], waits=[(se, se.n)], inc=sm)
        pr.add("vector", lambda e, ttt=ttt: e.tensor_scalar(
            tfm[:], ca_fm[:, 128 * ttt:128 * (ttt + 1)],
            ca_fm[:, 128 * ttt + 127:128 * ttt + 128], None, ALU.subtract),
               waits=[], inc=ve)
        pr.add("tensor", lambda e: e.transpose(ps_f[:, 64:64 + NH], tfm[:],
                                               idf[0:NH, 0:NH]),
               waits=[(ve, ve.n)], inc=mm)
        pr.add("scalar", lambda e, ttt=ttt: e.activation(
            ftok[:, ttt * NH:(ttt + 1) * NH], ps_f[:, 64:64 + NH], AF.Exp, scale=-1.0),
               waits=[(mm, mm.n)], inc=se)
        pr.add("vector", lambda e, ttt=ttt: e.tensor_mul(
            ftok[:, ttt * NH:(ttt + 1) * NH], ftok[:, ttt * NH:(ttt + 1) * NH],
            dt_tok[:, ttt * NH:(ttt + 1) * NH]), waits=[(se, se.n)], inc=ve)
        pr.add("scalar", lambda e, ttt=ttt: e.activation(
            dce_col[:], ca_fm[:, 128 * ttt + 127:128 * ttt + 128], AF.Exp),
               waits=[(ve, ve_ca)], inc=se)
        pr.add("tensor", lambda e: e.transpose(ps_f[0:1, 128:128 + NH], dce_col[:],
                                               idf[0:NH, 0:NH]),
               waits=[(se, se.n)], inc=mm)
        pr.add("vector", lambda e: e.tensor_copy(drow64[:], ps_f[0:1, 128:128 + NH]),
               waits=[(mm, mm.n)], inc=ve)
        pr.add("tensor", lambda e: e.matmul(ps_f[:, 256:256 + NH], one1f[:], drow64[:],
                                            start=True, stop=True),
               waits=[(ve, ve.n)], inc=mm)
        pr.add("vector", lambda e, ttt=ttt: e.tensor_copy(
            dch_sb[:, ttt * NH:(ttt + 1) * NH], ps_f[:, 256:256 + NH]),
               waits=[(mm, mm.n)], inc=ve)
        pr.dma("sync", dch_d[ttt], dch_sb[:, ttt * NH:(ttt + 1) * NH],
               waits=[(ve, ve.n)], inc=sm)
    pr.add("vector", lambda e: e.tensor_copy(dsp_sb[:], dch_sb[:, 0:NH]), waits=[], inc=ve)
    for ttt in range(1, NCK):
        pr.add("vector", lambda e, ttt=ttt: e.tensor_mul(
            dsp_sb[:], dsp_sb[:], dch_sb[:, ttt * NH:(ttt + 1) * NH]), waits=[], inc=ve)
    pr.dma("sync", dspan_d[:], dsp_sb[:], waits=[(ve, ve.n)], inc=sm)
    pr.add("vector", lambda e: e.memset(s_sb[:], 0.0), waits=[], inc=ve)

    for ttt in range(NCK):
        tr_ves = []
        for j in range(32):
            slot = j % 2
            w = [(se, se_conv)]
            if len(tr_ves) >= 2:
                w.append((ve, tr_ves[-2]))
            pr.add("tensor", lambda e, j=j, ttt=ttt, slot=slot: e.transpose(
                ps_t2[slot][:, 0:128],
                xconv[:, j * TOK + 128 * ttt: j * TOK + 128 * (ttt + 1)], idb[:]),
                   waits=w, inc=mm)
            pr.add("vector", lambda e, j=j, slot=slot: e.tensor_copy(
                xtokc[:, 128 * j:128 * (j + 1)], ps_t2[slot][:, 0:128]),
                   waits=[(mm, mm.n), (sx, sx.n)], inc=ve)
            tr_ves.append(ve.n)
        pr.dma("sync", xtok_d[ttt], xtokc[:], waits=[(ve, ve.n)], inc=sx)
        pr.add("vector", lambda e, ttt=ttt: e.tensor_tensor(
            xt_sb[:].rearrange("p (h q) -> p h q", h=NH),
            xtokc[:].rearrange("p (h q) -> p h q", h=NH),
            ftok[:, ttt * NH:(ttt + 1) * NH].unsqueeze(2).broadcast_to([128, NH, HD]),
            ALU.mult), waits=[], inc=ve)
        ve_xt = ve.n
        pr.add("tensor", lambda e, ttt=ttt: e.transpose(
            ps_t2[0][:, 0:128], xconv[:, 32 * TOK + 128 * ttt:32 * TOK + 128 * (ttt + 1)],
            idb[:]), waits=[(ve, ve.n)], inc=mm)
        pr.add("vector", lambda e: e.tensor_copy(btok[:], ps_t2[0][:, 0:128]),
               waits=[(mm, mm.n)], inc=ve)
        ve_btok = ve.n
        pr.add("vector", lambda e, ttt=ttt: e.tensor_tensor(
            s_sb[:].rearrange("p (h q) -> p h q", h=NH),
            s_sb[:].rearrange("p (h q) -> p h q", h=NH),
            dch_sb[:, ttt * NH:(ttt + 1) * NH].unsqueeze(2).broadcast_to([128, NH, HD]),
            ALU.mult), waits=[(ve, ve_btok)], inc=ve)
        for i in range(8):
            bank = i % 3

            def tcfn(e, i=i, bank=bank):
                return e.matmul(ps_g[:, 512 * bank:512 * (bank + 1)], btok[:],
                                xt_sb[:, 512 * i:512 * (i + 1)], start=True, stop=True)
            w = [(ve, ve_btok)]
            if i >= 3:
                w.append((ve, ve.n))
            pr.add("tensor", tcfn, waits=w, inc=mm)
            mm_tc = mm.n
            pr.add("vector", lambda e, i=i, bank=bank: e.tensor_add(
                s_sb[:, 512 * i:512 * (i + 1)], s_sb[:, 512 * i:512 * (i + 1)],
                ps_g[:, 512 * bank:512 * (bank + 1)]),
                   waits=[(mm, mm_tc)], inc=ve)
            pr.add("vector", lambda e, i=i, bank=bank: e.tensor_copy(
                evf[i % 2][:], ps_g[:, 512 * bank:512 * (bank + 1)]),
                   waits=[(stc, stc.n)], inc=ve)
            pr.dma("sync", tc_d[ttt][:, 512 * i:512 * (i + 1)], evf[i % 2][:],
                   waits=[(ve, ve.n)], inc=stc)
        pr.add("tensor", lambda e, ttt=ttt: e.matmul(
            ps_f[:, 0:128], xconv[:, 32 * TOK + 128 * ttt:32 * TOK + 128 * (ttt + 1)],
            xconv[:, 33 * TOK + 128 * ttt:33 * TOK + 128 * (ttt + 1)],
            start=True, stop=True), waits=[(ve, ve.n)], inc=mm)
        pr.add("vector", lambda e: e.tensor_copy(scores_sb[:], ps_f[:, 0:128]),
               waits=[(mm, mm.n)], inc=ve)
        ve_sc = ve.n
        for grp in range(8):
            ps = ps_a if grp % 2 == 0 else ps_b
            for hh in range(8):
                h = 8 * grp + hh
                pr.add("vector", lambda e, ttt=ttt, h=h: e.tensor_scalar_mul(
                    cs_sb[:], onef[:], ca_tok[:, ttt * NH + h: ttt * NH + h + 1]),
                       waits=[(ve, ve_sc)], inc=ve)
                pr.add("tensor", lambda e: e.transpose(ps_f[:, 128:256], cs_sb[:], idf[:]),
                       waits=[(ve, ve.n)], inc=mm)
                pr.add("vector", lambda e: e.scalar_tensor_tensor(
                    d1_sb[:], cs_sb[:], -1.0, ps_f[:, 128:256], ALU.mult, ALU.add),
                       waits=[(mm, mm.n)], inc=ve)
                pr.add("vector", lambda e: e.tensor_add(d1_sb[:], d1_sb[:], mlt_sb[:]),
                       waits=[], inc=ve)
                pr.add("scalar", lambda e: e.activation(l_sb[:], d1_sb[:], AF.Exp),
                       waits=[(ve, ve.n)], inc=se)
                pr.add("vector", lambda e, ttt=ttt, h=h: e.scalar_tensor_tensor(
                    m_sb[:], l_sb[:], dt_tok[:, ttt * NH + h: ttt * NH + h + 1],
                    scores_sb[:], ALU.mult, ALU.mult),
                       waits=[(se, se.n)], inc=ve)

                def imfn(e, h=h, hh=hh, ps=ps):
                    return e.matmul(ps[:, 64 * hh:64 * (hh + 1)], m_sb[:],
                                    xtokc[:, 64 * h:64 * (h + 1)],
                                    start=(hh == 0), stop=(hh == 7))
                pr.add("tensor", imfn, waits=[(ve, ve.n)], inc=mm)
            buf_ev = ev1 if grp % 2 == 0 else ev2
            pr.add("scalar", lambda e, ps=ps, buf_ev=buf_ev: e.activation(
                buf_ev[:], ps[:], AF.Copy),
                   waits=[(mm, mm.n), (si, si.n), (sg, sg.n), (sxb, sxb.n)], inc=se)
            pr.dma("sync", intra_d[ttt][:, 512 * grp:512 * (grp + 1)], buf_ev[:],
                   waits=[(se, se.n)], inc=si)

    pr.dma("sync", uspan_d[:], s_sb[:], waits=[(ve, ve.n)], inc=sm)
    pr.wait("sync", (sm, sm.n), (sg, sg.n), (sx, sx.n), (si, si.n), (stc, stc.n))
    pr.build()
    return nc


def attn1a_in_maps(x2T_full, inputs, consts):
    W_in = np.asarray(inputs["L1_attn_W_in"], np.float32)
    conv_w = np.asarray(inputs["L1_attn_conv_w"], np.float32)
    conv_b = np.asarray(inputs["L1_attn_conv_b"], np.float32)
    dt_bias = np.asarray(inputs["L1_attn_dt_bias"], np.float32)
    A = np.asarray(inputs["L1_attn_A"], np.float32)
    convw_r = np.ascontiguousarray(
        conv_w.reshape(NCH, 128, 4).transpose(1, 0, 2).reshape(128, NCH * 4))
    convb_r = np.ascontiguousarray(conv_b.reshape(NCH, 128).T)
    dtb = np.tile(dt_bias[None, :], (128, 1)).astype(np.float32)
    triu = np.triu(np.ones((128, 128), np.float32))
    ii = np.arange(128)
    mlt = np.where(ii[:, None] > ii[None, :], np.float32(-1e30), np.float32(0.0))
    shared = dict(
        masklt=np.ascontiguousarray(mlt.astype(np.float32)),
        w_in=W_in, convw=convw_r, convb=convb_r, dtb_bc=dtb,
        a_col=np.ascontiguousarray(A[:, None]),
        triu=triu, ident=consts["ident_bf"],
        ident_f32=np.eye(128, dtype=np.float32),
        ones_f=np.ones((128, 128), np.float32),
        ones1x128_f=consts["ones1x128_f"], scal=consts["scal"])
    maps = []
    for cid in range(8):
        b, q = cid // 4, cid % 4
        m = dict(shared)
        sl = np.zeros((DM, TOKH), np.float32)
        if q > 0:
            sl[:, 0:HALO] = x2T_full[b][:, q * TOK - HALO:q * TOK]
        sl[:, HALO:] = x2T_full[b][:, q * TOK:(q + 1) * TOK]
        m["x2Th"] = sl
        maps.append(m)
    return maps


def build_attn1b():
    nc = bass.Bass("TRN2", target_bir_lowering=False, debug=False)

    def di(n, sh, dt=F32):
        return nc.dram_tensor(n, sh, dt, kind="ExternalInput").ap()
    x2T = di("x2T", [DM, TOK])
    gate_d = di("gate_d", [NCK, 128, INTER], BF16)
    xtok_d = di("xtok_d", [NCK, 128, INTER], BF16)
    intra_d = di("intra_d", [NCK, 128, INTER], BF16)
    cfm_d = di("cfm_d", [128, TOK], BF16)
    gtok_d = di("gtok_d", [NCK, 128, NH])
    tc_d = di("tc_d", [NCK, 128, INTER])
    dch_d = di("dch_d", [NCK, 128, NH])
    u_in = di("u_in", [3, 128, INTER])
    din_bc = di("din_bc", [3, 128, NH])
    d_bc = di("d_bc", [128, NH])
    nw_bc = di("nw_bc", [128, INTER], BF16)
    wout = di("wout", [INTER, DM])
    ident = di("ident", [128, 128], BF16)
    ones128_bf = di("ones128_bf", [128, 1], BF16)
    ones1x128_f = di("ones1x128_f", [1, 128], F32)
    scal = di("scal", [128, len(SCALAR_VALS)])
    x3T = nc.dram_tensor("x3T", [DM, TOK], F32, kind="ExternalOutput").ap()
    zbT_dram = nc.dram_tensor("zbT_dram", [32, 128, TOK], BF16, kind="Internal").ap()

    a = nc.alloc_sbuf_tensor
    sc_sb = a("sc_sb", [128, len(SCALAR_VALS)], F32)
    idb = a("idb", [128, 128], BF16)
    one_bf = a("one_bf", [128, 1], BF16)
    one_f = a("one_f", [1, 128], F32)
    dbc_sb = a("dbc_sb", [128, NH], F32)
    nw_sb = a("nw_sb", [128, INTER], BF16)
    gl_sb = a("gl_sb", [128, NCK * NH], F32)
    dchl_sb = a("dchl_sb", [128, NCK * NH], F32)
    din_sb = a("din_sb", [128, 3 * NH], F32)
    cfm_sb = a("cfm_sb", [128, TOK], BF16)
    s_sb = a("s_sb", [128, INTER], F32)
    sbf = a("sbf", [128, INTER], BF16)
    tcl = a("tcl", [128, INTER], F32)
    u_sb = tcl
    intral = a("intral", [128, INTER], BF16)
    xtokl = a("xtokl", [128, INTER], BF16)
    gatel = a("gatel", [128, INTER], BF16)
    zc = a("zc", [128, INTER], F32)
    tmp512 = a("tmp512", [128, 512], F32)
    zcol = a("zcol", [128, 1], F32)
    znb = a("znb", [128, INTER], BF16)
    zrow_big = a("zrow_big", [128, 32 * 128], BF16)
    y3 = a("y3", [128, NDM * TOK], F32)
    wq = [a(f"wq{i}", [128, 1024], BF16) for i in range(2)]
    zq = [a(f"zq{i}", [128, 512], BF16) for i in range(2)]

    p = nc.alloc_psum_tensor
    ps_y = p("ps_y", [128, 1024], F32)
    ps_t2 = [p("ps_t0", [128, 512], BF16), p("ps_t1", [128, 512], BF16)]
    ps_w = p("ps_w", [128, 2048], F32)

    pr = Prog(nc)
    ldc = pr.sem("ldc")
    ldu = pr.sem("ldu")
    ldk = pr.sem("ldk")
    lw = [pr.sem(f"lw{i}") for i in range(2)]
    lz = [pr.sem(f"lz{i}") for i in range(2)]
    szt = pr.sem("szt")
    sout = pr.sem("sout")
    mm = pr.sem("mm")
    ve = pr.engine_sem("vector")
    se = pr.engine_sem("scalar")
    lnk = LNKit(nc, pr, mm, one_bf, one_f,
                sc_sb[0:1, SCALARS["eps"]:SCALARS["eps"] + 1])

    pr.dma("sync", sc_sb[:], scal[:], inc=ldc)
    pr.dma("sync", dbc_sb[:], d_bc[:], inc=ldc)
    pr.dma("sync", nw_sb[:], nw_bc[:], inc=ldc)
    pr.dma("sync", cfm_sb[:], cfm_d[:], inc=ldc)
    for c in range(NCK):
        pr.dma("sync", gl_sb[:, c * NH:(c + 1) * NH], gtok_d[c], inc=ldc)
        pr.dma("sync", dchl_sb[:, c * NH:(c + 1) * NH], dch_d[c], inc=ldc)
    for jj in range(3):
        pr.dma("sync", din_sb[:, jj * NH:(jj + 1) * NH], din_bc[jj], inc=ldc)
    ldg2 = pr.sem("ldg2")
    pr.dma("gpsimd", idb[:], ident[:], inc=ldg2)
    pr.dma("gpsimd", one_bf[:], ones128_bf[:], inc=ldg2)
    pr.dma("sync", one_f[:], ones1x128_f[:], inc=ldc)
    x2c = x2T.rearrange("(c p) t -> c p t", p=128)
    for c in range(NDM):
        pr.dma("sync", y3[:, c * TOK:(c + 1) * TOK], x2c[c], inc=ldc)
    ldc_all = ldc.n
    pr.wait("tensor", (ldg2, ldg2.n))

    def hview(t, col0):
        return t[:, col0:col0 + NH].unsqueeze(2).broadcast_to([128, NH, HD])

    def hview8(t, col0):
        return t[:, col0:col0 + 8].unsqueeze(2).broadcast_to([128, 8, HD])

    def as_hq(apx, h):
        return apx.rearrange("p (h q) -> p h q", h=h)

    pr.add("vector", lambda e: e.memset(s_sb[:], 0.0), waits=[(ldc, ldc_all)], inc=ve)
    for jj in range(3):
        pr.dma("scalar", u_sb[:], u_in[jj], waits=[(ve, ve.n)], inc=ldu)
        pr.add("vector", lambda e, jj=jj: e.tensor_tensor(
            as_hq(s_sb[:], NH), as_hq(s_sb[:], NH), hview(din_sb, jj * NH), ALU.mult),
               waits=[], inc=ve)
        pr.add("vector", lambda e: e.tensor_add(s_sb[:], s_sb[:], u_sb[:]),
               waits=[(ldu, ldu.n)], inc=ve)

    for ck in range(NCK):
        w = [(ve, ve.n), (mm, mm.n)]
        pr.dma("sync", tcl[:], tc_d[ck], waits=w, inc=ldk)
        pr.dma("scalar", intral[:], intra_d[ck], waits=w, inc=ldk)
        pr.dma("scalar", xtokl[:], xtok_d[ck], waits=w, inc=ldk)
        pr.dma("scalar", gatel[:], gate_d[ck], waits=w, inc=ldk)
        ldk_ck = ldk.n
        pr.add("vector", lambda e: e.tensor_copy(sbf[:], s_sb[:]), waits=[], inc=ve)
        pr.add("vector", lambda e, ck=ck: e.tensor_tensor(
            as_hq(s_sb[:], NH), as_hq(s_sb[:], NH), hview(dchl_sb, ck * NH), ALU.mult),
               waits=[], inc=ve)
        pr.add("vector", lambda e: e.tensor_add(s_sb[:], s_sb[:], tcl[:]),
               waits=[(ldk, ldk_ck)], inc=ve)
        ve_snap = ve.n
        for i in range(8):
            pcol = 512 * (i % 2)

            def yfn(e, ck=ck, i=i, pcol=pcol):
                return e.matmul(ps_y[:, pcol:pcol + 512],
                                cfm_sb[:, 128 * ck:128 * (ck + 1)],
                                sbf[:, 512 * i:512 * (i + 1)], start=True, stop=True)
            yw = [(ve, ve_snap)]
            if i >= 2:
                yw.append((ve, ve.n))
            pr.add("tensor", yfn, waits=yw, inc=mm)
            mm_y = mm.n
            pr.add("vector", lambda e, ck=ck, i=i, pcol=pcol: e.tensor_tensor(
                as_hq(zc[:, 512 * i:512 * (i + 1)], 8),
                as_hq(ps_y[:, pcol:pcol + 512], 8),
                hview8(gl_sb, ck * NH + 8 * i), ALU.mult), waits=[(mm, mm_y)], inc=ve)
            pr.add("vector", lambda e, i=i: e.tensor_add(
                zc[:, 512 * i:512 * (i + 1)], zc[:, 512 * i:512 * (i + 1)],
                intral[:, 512 * i:512 * (i + 1)]), waits=[], inc=ve)
            pr.add("vector", lambda e, i=i: e.tensor_tensor(
                as_hq(tmp512[:], 8), as_hq(xtokl[:, 512 * i:512 * (i + 1)], 8),
                hview8(dbc_sb, 8 * i), ALU.mult), waits=[], inc=ve)
            pr.add("vector", lambda e, i=i: e.tensor_add(
                zc[:, 512 * i:512 * (i + 1)], zc[:, 512 * i:512 * (i + 1)], tmp512[:]),
                   waits=[], inc=ve)
            pr.add("vector", lambda e, i=i: e.tensor_mul(
                zc[:, 512 * i:512 * (i + 1)], zc[:, 512 * i:512 * (i + 1)],
                gatel[:, 512 * i:512 * (i + 1)]), waits=[], inc=ve)
        pr.add("scalar", lambda e: e.activation(intral[:], zc[:], AF.Square),
               waits=[(ve, ve.n)], inc=se)
        pr.add("vector", lambda e: e.reduce_sum(zcol[:], intral[:], axis=AX.X),
               waits=[(se, se.n)], inc=ve)
        pr.add("scalar", lambda e: e.activation(
            zcol[:], zcol[:], AF.Sqrt, scale=1.0 / INTER,
            bias=sc_sb[:, SCALARS["eps"]:SCALARS["eps"] + 1]),
               waits=[(ve, ve.n)], inc=se)
        pr.add("vector", lambda e: e.reciprocal(zcol[:], zcol[:]), waits=[(se, se.n)], inc=ve)
        pr.add("vector", lambda e: e.scalar_tensor_tensor(
            znb[:], zc[:], zcol[:, 0:1], nw_sb[:], ALU.mult, ALU.mult), waits=[], inc=ve)
        ve_znb = ve.n
        tr_ves = []
        for j in range(32):
            slot = j % 2
            tw = [(ve, ve_znb)]
            if len(tr_ves) >= 2:
                tw.append((ve, tr_ves[-2]))
            pr.add("tensor", lambda e, j=j, slot=slot: e.transpose(
                ps_t2[slot][:, 0:128], znb[:, 128 * j:128 * (j + 1)],
                idb[:]), waits=tw, inc=mm)
            pr.add("vector", lambda e, j=j, slot=slot: e.tensor_copy(
                zrow_big[:, 128 * j:128 * (j + 1)], ps_t2[slot][:, 0:128]),
                   waits=[(mm, mm.n), (szt, szt.n)], inc=ve)
            tr_ves.append(ve.n)
        pr.dma("sync", zbT_dram[:, :, 128 * ck:128 * (ck + 1)].rearrange("c p f -> p c f"),
               zrow_big[:].rearrange("p (c f) -> p c f", f=128),
               waits=[(ve, ve.n)], inc=szt)

    woutc = wout.rearrange("(c p) t -> c p t", p=128)
    wv_all = []
    for tt2 in range(2):
        for qd in range(4):
            for ic in range(32):
                bw = wq[ic % 2]
                bz = zq[ic % 2]
                sw = lw[ic % 2]
                sz = lz[ic % 2]
                w = [(mm, wv_all[-2])] if len(wv_all) >= 2 else [(szt, szt.n)]
                pr.dma("gpsimd", bw[:, 0:512], woutc[ic][:, 512 * qd:512 * (qd + 1)],
                       waits=w, inc=sw)
                pr.dma("scalar", bz[:], zbT_dram[ic][:, 512 * tt2:512 * (tt2 + 1)],
                       waits=w, inc=sz)

                def wfn(e, ic=ic, bw=bw, bz=bz):
                    last = None
                    for t4 in range(4):
                        last = e.matmul(ps_w[:, 512 * t4:512 * (t4 + 1)],
                                        bw[:, 128 * t4:128 * (t4 + 1)], bz[:],
                                        start=(ic == 0), stop=(ic == 31))
                    return last
                ww = [(sw, sw.n), (sz, sz.n)]
                if ic == 0:
                    ww.append((ve, ve.n))
                pr.add("tensor", wfn, waits=ww, inc=mm)
                wv_all.append(mm.n)
            mm_pass = mm.n
            for t4 in range(4):
                dmc = 4 * qd + t4
                pr.add("vector", lambda e, dmc=dmc, t4=t4, tt2=tt2: e.tensor_add(
                    y3[:, dmc * TOK + 512 * tt2: dmc * TOK + 512 * (tt2 + 1)],
                    y3[:, dmc * TOK + 512 * tt2: dmc * TOK + 512 * (tt2 + 1)],
                    ps_w[:, 512 * t4:512 * (t4 + 1)]), waits=[(mm, mm_pass)], inc=ve)

    lnk.emit(y3, ps_w, (ve, ve.n), spill=x3T, spill_sem=sout)
    pr.wait("sync", (sout, sout.n))
    pr.build()
    return nc


def attn1b_in_maps(x2T_full, a_outs, inputs, consts):
    D = np.asarray(inputs["L1_attn_D"], np.float32)
    nw = np.asarray(inputs["L1_attn_norm_weight"], np.float32)
    W_out = np.asarray(inputs["L1_attn_W_out"], np.float32)
    d_bc = np.tile(D[None, :], (128, 1)).astype(np.float32)
    nw_bc = np.tile(nw[None, :], (128, 1)).astype(ml_dtypes.bfloat16)
    maps = []
    for cid in range(8):
        b, q = cid // 4, cid % 4
        grp = [4 * b + j for j in range(4)]
        u_in = np.zeros((3, 128, INTER), np.float32)
        din = np.ones((3, 128, NH), np.float32)
        srcs = [a_outs[grp[c]] for c in range(4)]
        if q >= 1:
            for j in range(q):
                u_in[3 - q + j] = srcs[j]["uspan_d"]
            for j in range(1, q):
                din[3 - q + j] = srcs[j]["dspan_d"]
        m = dict(
            x2T=np.ascontiguousarray(x2T_full[b][:, q * TOK:(q + 1) * TOK]),
            gate_d=a_outs[cid]["gate_d"], xtok_d=a_outs[cid]["xtok_d"],
            intra_d=a_outs[cid]["intra_d"], cfm_d=a_outs[cid]["cfm_d"],
            gtok_d=a_outs[cid]["gtok_d"], tc_d=a_outs[cid]["tc_d"],
            dch_d=a_outs[cid]["dch_d"], u_in=u_in, din_bc=din,
            d_bc=d_bc, nw_bc=nw_bc, wout=W_out,
            ident=consts["ident_bf"], ones128_bf=consts["ones128_bf"],
            ones1x128_f=consts["ones1x128_f"], scal=consts["scal"])
        maps.append(m)
    return maps

from contextlib import ExitStack
from concourse.aot_env import allow_ici_all_to_all
import ml_dtypes

RG8 = [[0, 1, 2, 3, 4, 5, 6, 7]]
RG4 = [[0, 1, 2, 3], [4, 5, 6, 7]]
CM_WLO, CM_WHI = 0, 1
CM_HSEL = 2
CM_S1, CM_C1, CM_S2, CM_C2 = 6, 7, 8, 9
CM_M0, CM_M1, CM_M2 = 10, 11, 12


def build_full(debug=False):
    nc = bass.Bass("TRN2", target_bir_lowering=False, debug=False)

    def di(n, sh, dt=F32):
        return nc.dram_tensor(n, sh, dt, kind="ExternalInput").ap()

    kdbg = "ExternalOutput" if debug else "Internal"

    def dint(n, sh, dt=F32, dbg=False):
        return nc.dram_tensor(n, sh, dt, kind=kdbg if dbg else "Internal").ap()

    triu = di("triu", [128, 128])
    masklt = di("masklt", [128, 128])
    ident = di("ident", [128, 128], BF16)
    ident_f32 = di("ident_f32", [128, 128], F32)
    ones_f = di("ones_f", [128, 128], F32)
    ones128_bf = di("ones128_bf", [128, 1], BF16)
    ones1x128_f = di("ones1x128_f", [1, 128], F32)
    scal = di("scal", [128, len(SCALAR_VALS)])
    cmask = di("cmask", [128, 16])
    x0T = di("x0T", [DM, SEQ])
    wq = di("wq", [DM, 512])
    wk = di("wk", [DM, 128])
    wv = di("wv", [DM, 128])
    cosT = di("cosT", [128, SEQ], BF16)
    sinT = di("sinT", [128, SEQ], BF16)
    rotm = di("rotm", [128, 128], BF16)
    maskc = di("maskc", [128, 4 * 512], BF16)
    x0Ts = di("x0Ts", [DM, TOK])
    Wo_n = di("Wo_n", [DM, DM])
    gup = di("gup", [2 * NFC, DM, 128])
    agT = di("agT", [DM, 128])
    auT = di("auT", [DM, 128])
    wrT = di("wrT", [DM, 8])
    bgT = di("bgT", [16, 8 * DFF])
    buT = di("buT", [16, 8 * DFF])
    adT_r = di("adT_r", [128, DFF])
    bdT = di("bdT", [128, DM])
    wdT_q = di("wdT_q", [4, NFC, 128, 512])
    pattern2 = di("pattern2", [8, 128], BF16)
    w_in = di("w_in", [DM, D_IN])
    convw = di("convw", [128, NCH * 4])
    convb = di("convb", [128, NCH])
    dtb_bc = di("dtb_bc", [128, NH])
    a_col = di("a_col", [NH, 1])
    d_bc = di("d_bc", [128, NH])
    nw_bc = di("nw_bc", [128, INTER], BF16)
    wout = di("wout", [INTER, DM])
    wT = di("wT", [DM, DM])
    biasc = di("biasc", [128, NDM])

    oT_snd = dint("oT_snd", [8 * 512, TOK], BF16)
    oT_rcv = dint("oT_rcv", [8 * 512, TOK], BF16)
    x1spill = dint("x1spill", [DM, TOK])
    y2spill = dint("y2spill", [DM, TOK])
    hspill = dint("hspill", [DFF, 512], BF16)
    x2T_d = dint("x2T_d", [DM, TOK], dbg=True)
    halo_snd = dint("halo_snd", [DM, HALO])
    halo_gat = dint("halo_gat", [4, DM, HALO])
    xbc_dram = dint("xbc_dram", [NCH, 128, TOKH], BF16)
    gate_d = dint("gate_d", [NCK, 128, INTER], BF16)
    xtok_d = dint("xtok_d", [NCK, 128, INTER], BF16)
    intra_d = dint("intra_d", [NCK, 128, INTER], BF16)
    cfm_d = dint("cfm_d", [128, TOK], BF16)
    gtok_d = dint("gtok_d", [NCK, 128, NH])
    tc_d = dint("tc_d", [NCK, 128, INTER])
    dch_d = dint("dch_d", [NCK, 128, NH])
    uspan_d = dint("uspan_d", [128, INTER])
    dspan_d = dint("dspan_d", [128, NH])
    uspan_gat = dint("uspan_gat", [4, 128, INTER])
    dspan_gat = dint("dspan_gat", [4, 128, NH])
    dbg_sin = dint("dbg_sin", [128, INTER], dbg=True)
    dbg_wl = dint("dbg_wl", [128, 4 * NH + 3 * NH], dbg=True)
    zbT_dram = dint("zbT_dram", [32, 128, TOK], BF16)
    x3T_d = dint("x3T_d", [DM, TOK], dbg=True)
    outT = nc.dram_tensor("outT", [DM, TOK], F32, kind="ExternalOutput").ap()

    pr = Prog(nc)
    ve = pr.engine_sem("vector")
    se = pr.engine_sem("scalar")
    cc = pr.sem("cc")

    a = nc.alloc_sbuf_tensor
    sc_sb = a("sc_sb", [128, len(SCALAR_VALS)], F32)
    idb = a("idb", [128, 128], BF16)
    idf = a("idf", [128, 128], F32)
    onesf = a("onesf", [128, 128], F32)
    one_bf = a("one_bf", [128, 1], BF16)
    one1f = a("one1f", [1, 128], F32)
    tri_sb = a("tri_sb", [128, 128], F32)
    mlt_sb = a("mlt_sb", [128, 128], F32)
    cm_sb = a("cm_sb", [128, 16], F32)
    ldc0 = pr.sem("ldc0")
    ldg0 = pr.sem("ldg0")
    pr.dma("sync", sc_sb[:], scal[:], inc=ldc0)
    pr.dma("sync", idf[:], ident_f32[:], inc=ldc0)
    pr.dma("sync", onesf[:], ones_f[:], inc=ldc0)
    pr.dma("sync", one1f[:], ones1x128_f[:], inc=ldc0)
    pr.dma("sync", tri_sb[:], triu[:], inc=ldc0)
    pr.dma("sync", mlt_sb[:], masklt[:], inc=ldc0)
    pr.dma("sync", cm_sb[:], cmask[:], inc=ldc0)
    ldc0_all = ldc0.n
    pr.dma("gpsimd", idb[:], ident[:], inc=ldg0)
    pr.dma("gpsimd", one_bf[:], ones128_bf[:], inc=ldg0)
    ldg0_all = ldg0.n

    def cmcol(j):
        return cm_sb[:, j:j + 1]

    def barrier(*waits):
        for eng in ("sync", "scalar", "vector", "tensor", "gpsimd"):
            pr.wait(eng, *waits)

    def phase_attn0():
        ctxP = ExitStack()

        def a(n, sh, dt):
            return ctxP.enter_context(nc.sbuf_tensor("a0" + n, sh, dt))
        maskc_sb = a("maskc_sb", [128, 4 * 512], BF16)
        cos_sb = a("cos_sb", [128, SEQ], BF16)
        sin_sb = a("sin_sb", [128, SEQ], BF16)
        rot_sb = a("rot_sb", [128, 128], BF16)
        qT = a("qT", [128, HQ * SEQ], BF16)
        kT = a("kT", [128, SEQ], BF16)
        v_sb = a("v_sb", [128, NCK0 * 129], BF16)
        spc = a("spc", [128, 128], F32)
        csc = a("csc", [128, 128], F32)
        tkc = a("tkc", [128, 128], F32)
        tqc = a("tqc", [128, 128], F32)
        negc16 = a("negc16", [128, 1], F32)
        ktl = a("ktl", [128, 128], BF16)
        kt2 = a("kt2", [128, 128], BF16)
        kt2t = a("kt2t", [128, 128], BF16)
        qtl = a("qtl", [128, 128], BF16)
        ebf = a("ebf", [128, 128], BF16)
        mgl = a("mgl", [128, 128], BF16)
        s_st = a("s_st", [128, 128], F32)
        s_bf = a("s_bf", [128, 128], BF16)
        rrow = a("rrow", [1, 128], F32)
        dcol = a("dcol", [128, 1], F32)
        cend = a("cend", [128, 1], F32)
        tmpb = a("tmpb", [128, 512], BF16)
        den_col = a("den_col", [128, 1], F32)
        y_sb = a("y_sb", [128, 128], F32)
        ocmb = a("ocmb", [128, 128], BF16)
        obuf = a("obuf", [128, 512], BF16)

        def pp(n, sh, dt):
            return ctxP.enter_context(nc.psum_tensor("a0" + n, sh, dt))
        ps_a = pp("ps_a", [128, 512], F32)
        ps_b = pp("ps_b", [128, 512], F32)
        ps_t0 = pp("ps_t0", [128, 512], BF16)
        ps_t1 = pp("ps_t1", [128, 512], BF16)
        ps_sc = pp("ps_sc", [128, 2048], F32)

        ldc = pr.sem("a0ldc")
        ldx = pr.sem("a0ldx")
        mm = pr.sem("a0mm")
        so = pr.sem("a0so")

        pr.dma("sync", cos_sb[:], cosT[:], inc=ldc)
        pr.dma("sync", sin_sb[:], sinT[:], inc=ldc)
        pr.dma("sync", rot_sb[:], rotm[:], inc=ldc)
        ldc_all = ldc.n

        ctx0 = ExitStack()
        wq_sb = ctx0.enter_context(nc.sbuf_tensor("a0wq_sb", [128, NDM * 512], BF16))
        wk_sb = ctx0.enter_context(nc.sbuf_tensor("a0wk_sb", [128, NDM * 128], BF16))
        wv_sb = ctx0.enter_context(nc.sbuf_tensor("a0wv_sb", [128, NDM * 128], BF16))
        xtg = [ctx0.enter_context(nc.sbuf_tensor(f"a0xtg{s}", [128, NDM * 512], BF16))
               for s in range(2)]

        pr.dma("gpsimd", maskc_sb[:], maskc[:], inc=ldx)
        wqc = wq.rearrange("(c p) t -> c p t", p=128)
        wkc = wk.rearrange("(c p) t -> c p t", p=128)
        wvc = wv.rearrange("(c p) t -> c p t", p=128)
        for c in range(NDM):
            pr.dma("gpsimd", wq_sb[:, c * 512:(c + 1) * 512], wqc[c], inc=ldx)
            pr.dma("gpsimd", wk_sb[:, c * 128:(c + 1) * 128], wkc[c], inc=ldx)
            pr.dma("gpsimd", wv_sb[:, c * 128:(c + 1) * 128], wvc[c], inc=ldx)
        ldx_all = ldx.n
        x0c = x0T.rearrange("(c p) t -> c p t", p=128)
        lx = [pr.sem("a0lx0"), pr.sem("a0lx1")]

        pr.add("vector", lambda e: e.memset(v_sb[:], 1.0), waits=[], inc=ve)
        pr.add("vector", lambda e: e.memset(s_st[:], 0.0), waits=[], inc=ve)
        pr.add("vector", lambda e: e.memset(s_bf[:], 0.0), waits=[], inc=ve)

        mm_tg_done = []
        for tg in range(8):
            sl = tg % 2
            for c in range(NDM):
                w = [(mm, mm_tg_done[tg - 2])] if (tg >= 2 and c == 0) else []
                pr.dma("gpsimd", xtg[sl][:, c * 512:(c + 1) * 512],
                       x0c[c][:, 512 * tg:512 * (tg + 1)], waits=w, inc=lx[sl])
            lx_full = lx[sl].n
            xv = xtg[sl]
            pp_i = 0
            for hs in range(HQ):
                ps = ps_a if pp_i % 2 == 0 else ps_b
                pp_i += 1

                def qfn(e, hs=hs, xv=xv, ps=ps):
                    last = None
                    for ic in range(NDM):
                        last = e.matmul(ps[:],
                                        wq_sb[:, ic * 512 + 128 * hs: ic * 512 + 128 * (hs + 1)],
                                        xv[:, ic * 512:(ic + 1) * 512],
                                        start=(ic == 0), stop=(ic == NDM - 1))
                    return last
                pr.add("tensor", qfn,
                       waits=[(ldx, ldx_all), (ldg0, ldg0_all), (lx[sl], lx_full),
                              (ve, ve.n)], inc=mm)
                pr.add("vector", lambda e, hs=hs, tg=tg, ps=ps: e.tensor_copy(
                    qT[:, hs * SEQ + 512 * tg: hs * SEQ + 512 * (tg + 1)], ps[:]),
                       waits=[(mm, mm.n)], inc=ve)
            ps = ps_a if pp_i % 2 == 0 else ps_b
            pp_i += 1

            def kfn(e, xv=xv, ps=ps):
                last = None
                for ic in range(NDM):
                    last = e.matmul(ps[:],
                                    wk_sb[:, ic * 128:(ic + 1) * 128],
                                    xv[:, ic * 512:(ic + 1) * 512],
                                    start=(ic == 0), stop=(ic == NDM - 1))
                return last
            pr.add("tensor", kfn, waits=[(ve, ve.n)], inc=mm)
            pr.add("vector", lambda e, tg=tg, ps=ps: e.tensor_copy(
                kT[:, 512 * tg:512 * (tg + 1)], ps[:]), waits=[(mm, mm.n)], inc=ve)
            ps = ps_a if pp_i % 2 == 0 else ps_b

            def vfn(e, xv=xv, ps=ps):
                last = None
                for ic in range(NDM):
                    last = e.matmul(ps[:],
                                    wv_sb[:, ic * 128:(ic + 1) * 128],
                                    xv[:, ic * 512:(ic + 1) * 512],
                                    start=(ic == 0), stop=(ic == NDM - 1))
                return last
            pr.add("tensor", vfn, waits=[(ve, ve.n)], inc=mm)
            pr.add("vector", lambda e, ps=ps: e.tensor_copy(obuf[:], ps[:]),
                   waits=[(mm, mm.n)], inc=ve)
            for j in range(4):
                ck = 4 * tg + j
                pst = ps_t0 if j % 2 == 0 else ps_t1
                pr.add("tensor", lambda e, j=j, pst=pst: e.transpose(
                    pst[:, 0:128], obuf[:, 128 * j:128 * (j + 1)], idb[:]),
                       waits=[(ve, ve.n)], inc=mm)
                pr.add("vector", lambda e, ck=ck, pst=pst: e.tensor_copy(
                    v_sb[:, ck * 129:ck * 129 + 128], pst[:, 0:128]),
                       waits=[(mm, mm.n)], inc=ve)
            mm_tg_done.append(mm.n)
        ve_proj = ve.n
        ctx0.close()

        o_gla = a("o_gla", [128, NCK0 * HQ * 128], BF16)
        rq_col = a("rq_col", [128, NCK0 * HQ], F32)
        p_exp = a("p_exp", [128, 32 * 512], BF16)

        for ck in range(NCK0):
            ksl = np.s_[:, 128 * ck:128 * (ck + 1)]
            pr.add("scalar", lambda e, ksl=ksl: e.activation(spc[:], kT[ksl], AF.Exp,
                                                             scale=-1.0),
                   waits=[(ve, ve_proj), (ldc, ldc_all), (ldc0, ldc0_all)], inc=se)
            pr.add("scalar", lambda e: e.activation(
                spc[:], spc[:], AF.Ln, bias=sc_sb[:, SCALARS["one"]:SCALARS["one"] + 1]),
                   waits=[], inc=se)
            pr.add("vector", lambda e: e.tensor_tensor_scan(
                csc[:], onesf[:], spc[:], 0.0, ALU.mult, ALU.add),
                   waits=[(se, se.n)], inc=ve)
            pr.add("vector", lambda e: e.tensor_copy(cend[:], csc[:, 127:128]),
                   waits=[], inc=ve)
            pr.add("vector", lambda e: e.tensor_scalar_mul(negc16[:], cend[:],
                                                           -1.0 / GATE_NORM),
                   waits=[], inc=ve)
            pr.add("scalar", lambda e: e.activation(dcol[:], cend[:], AF.Exp,
                                                    scale=-1.0 / GATE_NORM),
                   waits=[(ve, ve.n)], inc=se)
            pr.add("scalar", lambda e, ksl=ksl: e.activation(ebf[:], kT[ksl], AF.Exp),
                   waits=[], inc=se)
            pr.add("tensor", lambda e: e.matmul(ps_sc[0:1, 0:128], one_bf[:], ebf[:],
                                                start=True, stop=True),
                   waits=[(se, se.n), (ve, ve.n)], inc=mm)
            pr.add("vector", lambda e: e.reciprocal(rrow[:], ps_sc[0:1, 0:128]),
                   waits=[(mm, mm.n)], inc=ve)
            pr.add("tensor", lambda e: e.matmul(ps_sc[:, 1024:1152], one1f[:], rrow[:],
                                                start=True, stop=True),
                   waits=[(ve, ve.n)], inc=mm)
            mm_rk = mm.n
            pr.add("vector", lambda e, ksl=ksl: e.scalar_tensor_tensor(
                tkc[:], csc[:], 1.0 / GATE_NORM, kT[ksl], ALU.mult, ALU.add),
                   waits=[], inc=ve)
            pr.add("scalar", lambda e: e.activation(ktl[:], tkc[:], AF.Exp),
                   waits=[(ve, ve.n)], inc=se)
            pr.add("vector", lambda e: e.tensor_tensor(ktl[:], ktl[:],
                                                       ps_sc[:, 1024:1152], ALU.mult),
                   waits=[(se, se.n), (mm, mm_rk)], inc=ve)
            pr.add("scalar", lambda e: e.activation(kt2[:], tkc[:], AF.Exp,
                                                    bias=negc16[:, 0:1]),
                   waits=[(ve, ve.n)], inc=se)
            pr.add("vector", lambda e: e.tensor_tensor(kt2[:], kt2[:],
                                                       ps_sc[:, 1024:1152], ALU.mult),
                   waits=[(se, se.n)], inc=ve)
            pr.add("tensor", lambda e: e.transpose(ps_t0[:, 0:128], kt2[:], idb[:]),
                   waits=[(ve, ve.n)], inc=mm)
            pr.add("vector", lambda e: e.tensor_copy(kt2t[:], ps_t0[:, 0:128]),
                   waits=[(mm, mm.n)], inc=ve)
            for h in range(HQ):
                qsl = np.s_[:, h * SEQ + 128 * ck: h * SEQ + 128 * (ck + 1)]
                pr.add("scalar", lambda e, qsl=qsl: e.activation(ebf[:], qT[qsl], AF.Exp),
                       waits=[(ve, ve.n)], inc=se)
                pr.add("tensor", lambda e: e.matmul(ps_sc[0:1, 0:128], one_bf[:], ebf[:],
                                                    start=True, stop=True),
                       waits=[(se, se.n)], inc=mm)
                pr.add("vector", lambda e: e.reciprocal(rrow[:], ps_sc[0:1, 0:128]),
                       waits=[(mm, mm.n)], inc=ve)
                pr.add("tensor", lambda e: e.transpose(ps_sc[:, 1024:1025], rrow[:],
                                                       idf[0:1, 0:1]),
                       waits=[(ve, ve.n)], inc=mm)
                pr.add("vector", lambda e, ck=ck, h=h: e.tensor_copy(
                    rq_col[:, ck * HQ + h: ck * HQ + h + 1], ps_sc[:, 1024:1025]),
                       waits=[(mm, mm.n)], inc=ve)
                pr.add("vector", lambda e, qsl=qsl: e.scalar_tensor_tensor(
                    tqc[:], csc[:], -1.0 / GATE_NORM, qT[qsl], ALU.mult, ALU.add),
                       waits=[], inc=ve)
                pr.add("scalar", lambda e: e.activation(
                    qtl[:], tqc[:], AF.Exp,
                    bias=sc_sb[:, SCALARS["ln05"]:SCALARS["ln05"] + 1]),
                       waits=[(ve, ve.n)], inc=se)
                pr.add("tensor", lambda e: e.matmul(ps_sc[:, 256:384], ktl[:], qtl[:],
                                                    start=True, stop=True),
                       waits=[(se, se.n)], inc=mm)
                pr.add("vector", lambda e: e.tensor_tensor(mgl[:], ps_sc[:, 256:384],
                                                           tri_sb[:], ALU.mult),
                       waits=[(mm, mm.n)], inc=ve)

                def ofn(e, ck=ck):
                    e.matmul(ps_sc[:, 512:640], mgl[:],
                             v_sb[:, ck * 129:ck * 129 + 128], start=True, stop=False)
                    return e.matmul(ps_sc[:, 512:640], qtl[:], s_bf[:],
                                    start=False, stop=True)
                pr.add("tensor", ofn, waits=[(ve, ve.n), (se, se.n)], inc=mm)
                pr.add("scalar", lambda e, ck=ck, h=h: e.activation(
                    o_gla[:, (ck * HQ + h) * 128:(ck * HQ + h + 1) * 128],
                    ps_sc[:, 512:640], AF.Copy), waits=[(mm, mm.n)], inc=se)

            def cfn(e, ck=ck):
                return e.matmul(ps_sc[:, 1536:1664], kt2t[:],
                                v_sb[:, ck * 129:ck * 129 + 128], start=True, stop=True)
            pr.add("tensor", cfn, waits=[(ve, ve.n), (se, se.n)], inc=mm)
            pr.add("vector", lambda e: e.scalar_tensor_tensor(
                s_st[:], s_st[:], dcol[:, 0:1], ps_sc[:, 1536:1664], ALU.mult, ALU.add),
                   waits=[(mm, mm.n)], inc=ve)
            pr.add("vector", lambda e: e.tensor_copy(s_bf[:], s_st[:]), waits=[], inc=ve)
        ve_gla = ve.n
        se_gla = se.n

        chunks = [(qT, h * SEQ + 512 * J, 512 * J) for h in range(HQ) for J in range(8)]
        chunks += [(kT, 512 * J, 512 * J) for J in range(8)]
        tmp_ve = []
        for i, (buf, c0, t0) in enumerate(chunks):
            ps = ps_a if i % 2 == 0 else ps_b
            sl = np.s_[:, c0:c0 + 512]
            if i < 2:
                w = [(ve, ve_gla), (se, se_gla)]
            else:
                w = [(ve, tmp_ve[i - 2])]
            pr.add("tensor", lambda e, sl=sl, ps=ps, buf=buf: e.matmul(
                ps[:], rot_sb[:], buf[sl], start=True, stop=True), waits=w, inc=mm)
            pr.add("vector", lambda e, sl=sl, buf=buf, t0=t0: e.tensor_tensor(
                buf[sl], buf[sl], cos_sb[:, t0:t0 + 512], ALU.mult),
                   waits=[(mm, mm.n)], inc=ve)
            pr.add("vector", lambda e, ps=ps, t0=t0: e.tensor_tensor(
                tmpb[:], sin_sb[:, t0:t0 + 512], ps[:], ALU.mult),
                   waits=[], inc=ve)
            tmp_ve.append(ve.n)
            pr.add("vector", lambda e, sl=sl, buf=buf: e.tensor_tensor(
                buf[sl], buf[sl], tmpb[:], ALU.add), waits=[], inc=ve)
        ve_rope = ve.n

        last_av_mm = [mm.n]
        for h in range(HQ):
            for J in range(8):
                for kg in range(J + 1):
                    def sfn(e, h=h, J=J, kg=kg):
                        last = None
                        for s4 in range(4):
                            ki = 4 * kg + s4
                            last = e.matmul(ps_sc[:, 512 * s4:512 * (s4 + 1)],
                                            kT[:, 128 * ki:128 * (ki + 1)],
                                            qT[:, h * SEQ + 512 * J: h * SEQ + 512 * (J + 1)],
                                            start=True, stop=(kg != J))
                        if kg == J:
                            for s4 in range(4):
                                last = e.matmul(ps_sc[:, 512 * s4:512 * (s4 + 1)], idb[:],
                                                maskc_sb[:, 512 * s4:512 * (s4 + 1)],
                                                start=False, stop=True)
                        return last
                    w = [(ve, ve_rope), (se, se.n)]
                    if kg == 0:
                        w.append((mm, last_av_mm[-1]))
                    pr.add("tensor", sfn, waits=w, inc=mm)
                    mm_s = mm.n
                    for half in range(2):
                        pr.add("scalar", lambda e, kg=kg, half=half: e.activation(
                            p_exp[:, 2048 * kg + 1024 * half: 2048 * kg + 1024 * (half + 1)],
                            ps_sc[:, 1024 * half:1024 * (half + 1)], AF.Exp, scale=RSC),
                               waits=[(mm, mm_s)], inc=se)
                se_exp = se.n
                for s in range(4):
                    nki = 4 * J + s + 1
                    ps = ps_a if s % 2 == 0 else ps_b

                    def avfn(e, s=s, nki=nki, ps=ps):
                        last = None
                        for ki in range(nki):
                            last = e.matmul(
                                ps[:, 0:129],
                                p_exp[:, 512 * ki + 128 * s: 512 * ki + 128 * (s + 1)],
                                v_sb[:, ki * 129:(ki + 1) * 129],
                                start=(ki == 0), stop=(ki == nki - 1))
                        return last
                    pr.add("tensor", avfn, waits=[(se, se_exp), (ve, ve.n)], inc=mm)
                    last_av_mm.append(mm.n)
                    pr.add("vector", lambda e, ps=ps: e.reciprocal(den_col[:],
                                                                   ps[:, 128:129]),
                           waits=[(mm, mm.n)], inc=ve)
                    pr.add("vector", lambda e: e.tensor_scalar_mul(den_col[:],
                                                                   den_col[:], 0.5),
                           waits=[], inc=ve)
                    pr.add("scalar", lambda e, ps=ps: e.activation(
                        y_sb[:], ps[:, 0:128], AF.Copy, scale=den_col[:, 0:1]),
                           waits=[(ve, ve.n)], inc=se)
                    ck_g = 4 * J + s
                    pr.add("vector", lambda e, ck_g=ck_g, h=h: e.scalar_tensor_tensor(
                        ocmb[:], o_gla[:, (ck_g * HQ + h) * 128:(ck_g * HQ + h + 1) * 128],
                        rq_col[:, ck_g * HQ + h: ck_g * HQ + h + 1], y_sb[:],
                        ALU.mult, ALU.add), waits=[(se, se.n)], inc=ve)
                    pst = ps_t0 if s % 2 == 0 else ps_t1
                    pr.add("tensor", lambda e, pst=pst: e.transpose(
                        pst[:, 0:128], ocmb[:], idb[:]), waits=[(ve, ve.n)], inc=mm)
                    pr.add("vector", lambda e, s=s, pst=pst: e.tensor_copy(
                        obuf[:, 128 * s:128 * (s + 1)], pst[:, 0:128]),
                           waits=[(mm, mm.n), (so, so.n)], inc=ve)
                jb = J // 2
                off = 512 * (J % 2)
                r0 = 512 * jb + 128 * h
                pr.dma("sync", oT_snd[r0:r0 + 128, off:off + 512], obuf[:],
                       waits=[(ve, ve.n)], inc=so)
                pr.dma("sync", oT_snd[2048 + r0:2048 + r0 + 128, off:off + 512], obuf[:],
                       waits=[], inc=so)
        ctxP.close()
        return so

    a0so = phase_attn0()
    barrier((a0so, a0so.n))

    def cc_a2a(e):
        with allow_ici_all_to_all():
            return e.collective_compute(
                "AllToAll", mybir.AluOpType.bypass, replica_groups=RG8,
                ins=[oT_snd.tensor.ap().opt()], outs=[oT_rcv.tensor.ap().opt()])
    pr.add("gpsimd", cc_a2a, waits=[(a0so, a0so.n)], inc=cc, inc_n=1, is_dma=True)

    def phase_mlp0():
        ctxP = ExitStack()

        def a(n, sh, dt):
            return ctxP.enter_context(nc.sbuf_tensor("m0" + n, sh, dt))
        pat2 = a("pat2", [8, 128], BF16)
        stat_m = a("stat_m", [1, TOK], F32)
        stat_v = a("stat_v", [1, TOK], F32)
        rstd = a("rstd", [1, TOK], F32)
        mrstd = a("mrstd", [1, TOK], F32)
        ybf = a("ybf", [128, TOK], BF16)
        ysq = a("ysq", [128, TOK], BF16)
        xb = a("xb", [128, NDM * TOK], BF16)

        def pp(n, sh, dt):
            return ctxP.enter_context(nc.psum_tensor("m0" + n, sh, dt))
        ps_a = pp("ps_a", [128, 512], F32)
        ps_b = pp("ps_b", [128, 512], F32)
        ps_cf = pp("ps_cf", [128, 512], F32)
        ps_tr = pp("ps_tr", [128, 1024], BF16)
        ps_dn = pp("ps_dn", [128, 2048], F32)

        ldc = pr.sem("m0ldc")
        ldga = pr.sem("m0ldga")
        ldwo = pr.sem("m0ldwo")
        ldsm = pr.sem("m0ldsm")
        gws = [pr.sem(f"m0gw{i}") for i in range(4)]
        bgs = pr.sem("m0bgs")
        bus = pr.sem("m0bus")
        wds = [pr.sem(f"m0wds{i}") for i in range(2)]
        hhs = [pr.sem(f"m0hhs{i}") for i in range(2)]
        xls = [pr.sem(f"m0xls{i}") for i in range(2)]
        stx1 = pr.sem("m0stx1")
        sth = pr.sem("m0sth")
        sy2 = [pr.sem(f"m0sy2{i}") for i in range(2)]
        sout = pr.sem("m0sout")
        shalo = pr.sem("m0shalo")
        ldy2 = pr.sem("m0ldy2")
        mm = pr.sem("m0mm")

        x0c = x0Ts.rearrange("(c p) t -> c p t", p=128)
        olo = oT_rcv[0:2048].rearrange("(c p) t -> c p t", p=128)
        ohi = oT_rcv[2048:4096].rearrange("(c p) t -> c p t", p=128)
        woc = Wo_n.rearrange("(c p) t -> c p t", p=128)

        def emit_ln(buf, resid_ready, cast_to=None, spill=None, spill_sem=None):
            stat_mms = []
            for c in range(NDM):
                pr.add("vector", lambda e, c=c: e.tensor_copy(
                    ybf[:], buf[:, c * TOK:(c + 1) * TOK]),
                       waits=[resid_ready] + ([(mm, stat_mms[-1])] if c else []), inc=ve)
                pr.add("scalar", lambda e: e.activation(ysq[:], ybf[:], AF.Square),
                       waits=[(ve, ve.n)], inc=se)

                def statfn(e, c=c):
                    last = None
                    for j in range(2):
                        last = e.matmul(ps_dn[0:1, 512 * j:512 * (j + 1)], one_bf[:],
                                        ybf[:, 512 * j:512 * (j + 1)],
                                        start=(c == 0), stop=(c == NDM - 1))
                    for j in range(2):
                        last = e.matmul(ps_dn[0:1, TOK + 512 * j:TOK + 512 * (j + 1)],
                                        one_bf[:], ysq[:, 512 * j:512 * (j + 1)],
                                        start=(c == 0), stop=(c == NDM - 1))
                    return last
                pr.add("tensor", statfn, waits=[(se, se.n), (ve, ve.n)], inc=mm)
                stat_mms.append(mm.n)
            pr.add("vector", lambda e: e.tensor_scalar_mul(stat_m[:], ps_dn[0:1, 0:TOK],
                                                           1.0 / DM),
                   waits=[(mm, mm.n)], inc=ve)
            pr.add("vector", lambda e: e.tensor_scalar_mul(
                stat_v[:], ps_dn[0:1, TOK:2 * TOK], 1.0 / DM), waits=[], inc=ve)
            pr.add("vector", lambda e: e.tensor_mul(mrstd[:], stat_m[:], stat_m[:]),
                   waits=[], inc=ve)
            pr.add("vector", lambda e: e.tensor_sub(stat_v[:], stat_v[:], mrstd[:]),
                   waits=[], inc=ve)
            pr.add("scalar", lambda e: e.activation(
                rstd[:], stat_v[:], AF.Sqrt,
                bias=sc_sb[0:1, SCALARS["eps"]:SCALARS["eps"] + 1]),
                   waits=[(ve, ve.n), (ldc0, ldc0_all)], inc=se)
            pr.add("vector", lambda e: e.reciprocal(rstd[:], rstd[:]),
                   waits=[(se, se.n)], inc=ve)
            pr.add("vector", lambda e: e.tensor_mul(mrstd[:], stat_m[:], rstd[:]),
                   waits=[], inc=ve)

            def bcfn(e):
                last = None
                for j in range(2):
                    last = e.matmul(ps_dn[:, 512 * j:512 * (j + 1)], one1f[:],
                                    rstd[0:1, 512 * j:512 * (j + 1)],
                                    start=True, stop=True)
                for j in range(2):
                    last = e.matmul(ps_dn[:, TOK + 512 * j:TOK + 512 * (j + 1)], one1f[:],
                                    mrstd[0:1, 512 * j:512 * (j + 1)],
                                    start=True, stop=True)
                return last
            pr.add("tensor", bcfn, waits=[(ve, ve.n)], inc=mm)
            mm_bc = mm.n
            for c in range(NDM):
                sl = np.s_[:, c * TOK:(c + 1) * TOK]
                pr.add("vector", lambda e, sl=sl: e.tensor_mul(buf[sl], buf[sl],
                                                               ps_dn[:, 0:TOK]),
                       waits=[(mm, mm_bc)], inc=ve)
                pr.add("vector", lambda e, sl=sl: e.tensor_sub(buf[sl], buf[sl],
                                                               ps_dn[:, TOK:2 * TOK]),
                       waits=[], inc=ve)
                if cast_to is not None:
                    pr.add("vector", lambda e, sl=sl: e.tensor_copy(cast_to[sl], buf[sl]),
                           waits=[], inc=ve)
                if spill is not None:
                    pr.dma("sync", spill.rearrange("(c p) t -> c p t", p=128)[c], buf[sl],
                           waits=[(ve, ve.n)], inc=spill_sem)

        ctxA = ExitStack()
        xs = ctxA.enter_context(nc.sbuf_tensor("m0xs", [128, NDM * TOK], F32))
        ob = ctxA.enter_context(nc.sbuf_tensor("m0ob", [128, NDM * TOK], BF16))
        obt = [ctxA.enter_context(nc.sbuf_tensor(f"m0obt{i}", [128, TOK], BF16))
               for i in range(2)]
        woh = ctxA.enter_context(nc.sbuf_tensor("m0woh", [128, NDM * TOK], BF16))

        for c in range(NDM):
            pr.dma("sync", xs[:, c * TOK:(c + 1) * TOK], x0c[c], inc=ldc)
        ld_x0 = ldc.n
        pr.dma("gpsimd", pat2[:], pattern2[:], inc=ldga)
        lo_s = pr.sem("m0lo")
        ohs = [pr.sem(f"m0oh{i}") for i in range(2)]
        cmb_ve = []
        for c in range(NDM):
            sl = np.s_[:, c * TOK:(c + 1) * TOK]
            slot = c % 2
            w = [(cc, 1)] if c == 0 else []
            pr.dma("gpsimd", ob[sl], olo[c], waits=w, inc=lo_s)
            lo_c = lo_s.n
            wh = [(ve, cmb_ve[c - 2])] if c >= 2 else []
            pr.dma("gpsimd", obt[slot][:], ohi[c], waits=wh, inc=ohs[slot])
            hi_c = ohs[slot].n
            pr.add("vector", lambda e, sl=sl: e.tensor_scalar_mul(
                ob[sl], ob[sl], cmcol(CM_WLO)),
                   waits=[(lo_s, lo_c), (ldc0, ldc0_all)], inc=ve)
            pr.add("vector", lambda e, sl=sl, slot=slot: e.scalar_tensor_tensor(
                ob[sl], obt[slot][:], cmcol(CM_WHI), ob[sl], ALU.mult, ALU.add),
                   waits=[(ohs[slot], hi_c)], inc=ve)
            cmb_ve.append(ve.n)
        ldga_all = ldga.n
        ve_cmb = ve.n

        resid_mm = []
        resid_ve = []
        k = 0
        for h in range(2):
            for c in range(NDM):
                w = [(mm, resid_mm[-1])] if (h == 1 and c == 0) else []
                pr.dma("gpsimd", woh[:, c * TOK:(c + 1) * TOK],
                       woc[c][:, 1024 * h:1024 * (h + 1)], waits=w, inc=ldwo)
            ldwo_h = ldwo.n
            for dmc_i in range(8):
                dmc = 8 * h + dmc_i
                for tt in range(2):
                    ps = ps_a if k % 2 == 0 else ps_b

                    def mmfn(e, dmc_i=dmc_i, tt=tt, ps=ps):
                        last = None
                        for ocn in range(NDM):
                            last = e.matmul(
                                ps[:],
                                woh[:, ocn * TOK + 128 * dmc_i: ocn * TOK + 128 * (dmc_i + 1)],
                                ob[:, ocn * TOK + 512 * tt: ocn * TOK + 512 * (tt + 1)],
                                start=(ocn == 0), stop=(ocn == NDM - 1))
                        return last
                    waits = [(ldwo, ldwo_h), (ldga, ldga_all), (ve, ve_cmb)]
                    if k >= 2:
                        waits.append((ve, resid_ve[k - 2]))
                    pr.add("tensor", mmfn, waits=waits, inc=mm)
                    resid_mm.append(mm.n)
                    sl = np.s_[:, dmc * TOK + 512 * tt: dmc * TOK + 512 * (tt + 1)]
                    pr.add("vector", lambda e, sl=sl, ps=ps: e.tensor_add(xs[sl], xs[sl],
                                                                          ps[:]),
                           waits=[(mm, mm.n), (ldc, ld_x0)], inc=ve)
                    resid_ve.append(ve.n)
                    k += 1

        emit_ln(xs, (ve, ve.n), cast_to=xb, spill=x1spill, spill_sem=stx1)
        ve_ln1 = ve.n
        stx1_all = stx1.n
        ctxA.close()

        barrier((ve, ve_ln1), (stx1, stx1_all))
        ctxB = ExitStack()

        def B(n, sh, dt):
            return ctxB.enter_context(nc.sbuf_tensor("m0" + n, sh, dt))
        gb = B("gb", [128, BFC * 512], BF16)
        ub = B("ub", [128, BFC * 512], BF16)
        agb = B("agb", [128, NDM * 128], BF16)
        aub = B("aub", [128, NDM * 128], BF16)
        wrb = B("wrb", [128, NDM * 8], BF16)
        bgb = B("bgb", [16, BFC * 128], BF16)
        bub = B("bub", [16, BFC * 128], BF16)
        adb = B("adb", [128, DFF], BF16)
        bdb = B("bdb", [128, DM], BF16)
        cg_sb = B("cg_sb", [16, 8 * 512], BF16)
        cu_sb = B("cu_sb", [16, 8 * 512], BF16)
        he_sb = [B(f"he{i}", [128, 512], BF16) for i in range(2)]
        tg_sb = [B(f"tg{i}", [128, 512], BF16) for i in range(2)]
        wpat = B("wpat", [128, 4 * 128], F32)
        cw_sb = B("cw_sb", [128, 4 * 128], BF16)
        cwT_sb = B("cwT_sb", [128, 4 * 128], BF16)
        gw = [B(f"gw{i}", [128, DM], BF16) for i in range(4)]
        wd_t = [B(f"wdt{i}", [128, 512], BF16) for i in range(2)]
        hh_t = [B(f"hht{i}", [128, 512], BF16) for i in range(2)]
        rlog = B("rlog", [128, 8], F32)
        rsum = B("rsum", [128, 1], F32)
        rmax = B("rmax", [128, 1], F32)
        rp2 = B("rp2", [128, 8], F32)
        rmask = B("rmask", [128, 8], F32)
        rmask2 = B("rmask2", [128, 8], F32)
        rw_sb = B("rw_sb", [128, 8], F32)
        wfm = B("wfm", [8, 512], BF16)
        x1rl = [B(f"x1rl{i}", [128, 512], F32) for i in range(2)]

        hs_d = hspill.rearrange("(c p) t -> c p t", p=128)
        agc = agT.rearrange("(c p) t -> c p t", p=128)
        auc = auT.rearrange("(c p) t -> c p t", p=128)
        wrc = wrT.rearrange("(c p) t -> c p t", p=128)
        for c in range(NDM):
            pr.dma("gpsimd", agb[:, c * 128:(c + 1) * 128], agc[c], inc=ldsm)
            pr.dma("gpsimd", aub[:, c * 128:(c + 1) * 128], auc[c], inc=ldsm)
            pr.dma("gpsimd", wrb[:, c * 8:(c + 1) * 8], wrc[c], inc=ldsm)
        pr.dma("gpsimd", adb[:], adT_r[:], inc=ldsm)
        pr.dma("gpsimd", bdb[:], bdT[:], inc=ldsm)
        ldsm_all = ldsm.n

        x1s = x1spill.rearrange("(c p) t -> c p t", p=128)
        y2s_d = y2spill.rearrange("(c p) t -> c p t", p=128)

        for tt in range(2):
            barrier((ve, ve.n), (mm, mm.n), (se, se.n),
                    (sth, sth.n), (sy2[0], sy2[0].n), (sy2[1], sy2[1].n))
            first_r = True
            for ttt in range(4):
                tok0 = 512 * tt + 128 * ttt

                def rlfn(e, tok0=tok0):
                    last = None
                    for ic in range(NDM):
                        last = e.matmul(ps_cf[:, 0:8],
                                        xb[:, ic * TOK + tok0: ic * TOK + tok0 + 128],
                                        wrb[:, ic * 8:(ic + 1) * 8],
                                        start=(ic == 0), stop=(ic == NDM - 1))
                    return last
                waits = [(ldsm, ldsm_all), (ve, ve.n)] if first_r else [(ve, ve.n)]
                first_r = False
                pr.add("tensor", rlfn, waits=waits, inc=mm)
                pr.add("scalar", lambda e: e.activation(rlog[:], ps_cf[:, 0:8], AF.Exp),
                       waits=[(mm, mm.n)], inc=se)
                pr.add("vector", lambda e: e.reduce_sum(rsum[:], rlog[:], axis=AX.X),
                       waits=[(se, se.n)], inc=ve)
                pr.add("vector", lambda e: e.reciprocal(rsum[:], rsum[:]), waits=[], inc=ve)
                pr.add("vector", lambda e: e.tensor_scalar_mul(rlog[:], rlog[:],
                                                               rsum[:, 0:1]),
                       waits=[], inc=ve)
                pr.add("vector", lambda e: e.reduce_max(rmax[:], rlog[:], axis=AX.X),
                       waits=[], inc=ve)
                pr.add("vector", lambda e: e.tensor_single_scalar(
                    rmask[:], rlog[:], rmax[:, 0:1], ALU.is_ge), waits=[], inc=ve)
                pr.add("vector", lambda e: e.scalar_tensor_tensor(
                    rp2[:], rmask[:], BIG_NEG, rlog[:], ALU.mult, ALU.add),
                       waits=[], inc=ve)
                pr.add("vector", lambda e: e.reduce_max(rmax[:], rp2[:], axis=AX.X),
                       waits=[], inc=ve)
                pr.add("vector", lambda e: e.tensor_single_scalar(
                    rmask2[:], rp2[:], rmax[:, 0:1], ALU.is_ge), waits=[], inc=ve)
                pr.add("vector", lambda e: e.tensor_add(rmask[:], rmask[:], rmask2[:]),
                       waits=[], inc=ve)
                pr.add("vector", lambda e: e.tensor_mul(rw_sb[:], rlog[:], rmask[:]),
                       waits=[], inc=ve)
                ve_rw = ve.n
                pr.add("tensor", lambda e: e.transpose(ps_cf[0:8, 128:256], rw_sb[:],
                                                       idf[:]),
                       waits=[(ve, ve_rw)], inc=mm)
                pr.add("vector", lambda e, ttt=ttt: e.tensor_copy(
                    wfm[:, 128 * ttt:128 * (ttt + 1)], ps_cf[0:8, 128:256]),
                       waits=[(mm, mm.n)], inc=ve)
            for ttt in range(4):
                pr.add("tensor", lambda e, ttt=ttt: e.matmul(
                    ps_cf[:, 128:256], wfm[:, 128 * ttt:128 * (ttt + 1)], pat2[:],
                    start=True, stop=True), waits=[(ve, ve.n)], inc=mm)
                pr.add("vector", lambda e, ttt=ttt: e.tensor_copy(
                    wpat[:, 128 * ttt:128 * (ttt + 1)], ps_cf[:, 128:256]),
                       waits=[(mm, mm.n)], inc=ve)
            ve_wpat = ve.n

            cf_done = []
            for e_i in range(8):
                psA = ps_a if e_i % 2 == 0 else ps_dn[:, 0:512]
                psB = ps_b if e_i % 2 == 0 else ps_dn[:, 512:1024]

                def cgfn(e, e_i=e_i, tt=tt, psA=psA):
                    last = None
                    for ic in range(NDM):
                        last = e.matmul(psA[0:16, :],
                                        agb[:, ic * 128 + 16 * e_i: ic * 128 + 16 * e_i + 16],
                                        xb[:, ic * TOK + 512 * tt: ic * TOK + 512 * (tt + 1)],
                                        start=(ic == 0), stop=(ic == NDM - 1))
                    return last

                def cufn(e, e_i=e_i, tt=tt, psB=psB):
                    last = None
                    for ic in range(NDM):
                        last = e.matmul(psB[0:16, :],
                                        aub[:, ic * 128 + 16 * e_i: ic * 128 + 16 * e_i + 16],
                                        xb[:, ic * TOK + 512 * tt: ic * TOK + 512 * (tt + 1)],
                                        start=(ic == 0), stop=(ic == NDM - 1))
                    return last
                waits = [(se, cf_done[-2])] if e_i >= 2 else [(ve, ve_wpat)]
                pr.add("tensor", cgfn, waits=waits, inc=mm)
                pr.add("tensor", cufn, waits=[], inc=mm)
                mm_c = mm.n
                pr.add("scalar", lambda e, e_i=e_i, psA=psA: e.activation(
                    cg_sb[:, 512 * e_i:512 * (e_i + 1)], psA[0:16, :], AF.Copy,
                    scale=2.0), waits=[(mm, mm_c)], inc=se)
                pr.add("scalar", lambda e, e_i=e_i, psB=psB: e.activation(
                    cu_sb[:, 512 * e_i:512 * (e_i + 1)], psB[0:16, :], AF.Copy,
                    scale=2.0), waits=[], inc=se)
                cf_done.append(se.n)
            se_cf = se.n

            first_cf = True
            cf_mms = []
            for blk in range(NBLK):
                base_ve = []
                base_mms = []
                for j in range(BFC):
                    fc = blk * BFC + j
                    psA = ps_a if j % 2 == 0 else ps_dn[:, 0:512]
                    psB = ps_b if j % 2 == 0 else ps_dn[:, 512:1024]
                    sg_, su = gws[2 * (j % 2)], gws[2 * (j % 2) + 1]
                    bg, bu = gw[2 * (j % 2)], gw[2 * (j % 2) + 1]
                    w = [(mm, base_mms[j - 2])] if j >= 2 else [(ve, ve.n), (mm, mm.n)]
                    pr.dma("gpsimd", bg[:].rearrange("p (c f) -> p c f", f=128),
                           gup[2 * fc].rearrange("(c p) f -> p c f", p=128),
                           waits=w, inc=sg_)
                    pr.dma("gpsimd", bu[:].rearrange("p (c f) -> p c f", f=128),
                           gup[2 * fc + 1].rearrange("(c p) f -> p c f", p=128), inc=su)

                    def gmm(e, bg=bg, tt=tt, psA=psA):
                        last = None
                        for ic in range(NDM):
                            last = e.matmul(psA[:], bg[:, ic * 128:(ic + 1) * 128],
                                            xb[:, ic * TOK + 512 * tt: ic * TOK + 512 * (tt + 1)],
                                            start=(ic == 0), stop=(ic == NDM - 1))
                        return last

                    def umm(e, bu=bu, tt=tt, psB=psB):
                        last = None
                        for ic in range(NDM):
                            last = e.matmul(psB[:], bu[:, ic * 128:(ic + 1) * 128],
                                            xb[:, ic * TOK + 512 * tt: ic * TOK + 512 * (tt + 1)],
                                            start=(ic == 0), stop=(ic == NDM - 1))
                        return last
                    waits = [(sg_, sg_.n), (su, su.n), (se, se_cf)]
                    if j >= 2:
                        waits.append((ve, base_ve[-2]))
                    pr.add("tensor", gmm, waits=waits, inc=mm)
                    pr.add("tensor", umm, waits=[], inc=mm)
                    mm_b = mm.n
                    base_mms.append(mm_b)
                    pr.add("vector", lambda e, j=j, psA=psA: e.tensor_copy(
                        gb[:, 512 * j:512 * (j + 1)], psA[:]),
                           waits=[(mm, mm_b), (sth, sth.n)], inc=ve)
                    pr.add("vector", lambda e, j=j, psB=psB: e.tensor_copy(
                        ub[:, 512 * j:512 * (j + 1)], psB[:]), waits=[], inc=ve)
                    base_ve.append(ve.n)
                ve_base = ve.n

                ex_ve = []
                for e_i in range(8):
                    pr.dma("gpsimd", bgb[:], bgT[0:16, e_i * DFF + blk * BFC * 128:
                                                 e_i * DFF + (blk + 1) * BFC * 128],
                           waits=[(mm, mm.n)], inc=bgs)
                    pr.dma("gpsimd", bub[:], buT[0:16, e_i * DFF + blk * BFC * 128:
                                                 e_i * DFF + (blk + 1) * BFC * 128],
                           inc=bus)
                    ldg_be = [(bgs, bgs.n), (bus, bus.n)]
                    for j in range(BFC):
                        fc = blk * BFC + j
                        psA = ps_a if j % 2 == 0 else ps_dn[:, 0:512]
                        psB = ps_b if j % 2 == 0 else ps_dn[:, 512:1024]

                        def gemm(e, e_i=e_i, j=j, psA=psA):
                            e.matmul(psA[:], idb[:], gb[:, 512 * j:512 * (j + 1)],
                                     start=True, stop=False)
                            return e.matmul(psA[:], bgb[0:16, 128 * j:128 * (j + 1)],
                                            cg_sb[:, 512 * e_i:512 * (e_i + 1)],
                                            start=False, stop=True)

                        def uemm(e, e_i=e_i, j=j, psB=psB):
                            e.matmul(psB[:], idb[:], ub[:, 512 * j:512 * (j + 1)],
                                     start=True, stop=False)
                            return e.matmul(psB[:], bub[0:16, 128 * j:128 * (j + 1)],
                                            cu_sb[:, 512 * e_i:512 * (e_i + 1)],
                                            start=False, stop=True)
                        waits = [(ve, ve_base)] + ldg_be
                        if len(ex_ve) >= 2:
                            waits.append((ve, ex_ve[-2]))
                        pr.add("tensor", gemm, waits=waits, inc=mm)
                        pr.add("tensor", uemm, waits=[], inc=mm)
                        mm_e = mm.n
                        par = j % 2
                        pr.add("scalar", lambda e, par=par, psA=psA: e.activation(
                            tg_sb[par][:], psA[:], AF.Silu), waits=[(mm, mm_e)], inc=se)
                        he_w = [(se, se.n), (mm, mm_e)]
                        if len(cf_mms) >= 2:
                            he_w.append((mm, cf_mms[-2]))
                        pr.add("vector", lambda e, par=par, psB=psB: e.tensor_tensor(
                            he_sb[par][:], tg_sb[par][:], psB[:], ALU.mult),
                               waits=he_w, inc=ve)
                        ex_ve.append(ve.n)
                        is_last = (blk == NBLK - 1 and e_i == 7 and j == BFC - 1)

                        def cfmm(e, e_i=e_i, fc=fc, par=par, first=first_cf,
                                 is_last=is_last):
                            last = None
                            for ttt in range(4):
                                last = e.matmul(
                                    ps_cf[:, 128 * ttt + 16 * e_i: 128 * ttt + 16 * e_i + 16],
                                    he_sb[par][:, 128 * ttt:128 * (ttt + 1)],
                                    adb[:, 128 * fc + 16 * e_i: 128 * fc + 16 * e_i + 16],
                                    start=(first and ttt == 0), stop=(is_last and ttt == 3))
                            return last
                        pr.add("tensor", cfmm, waits=[(ve, ve.n)], inc=mm)
                        cf_mms.append(mm.n)
                        first_cf = False

                h_ve = []
                for j in range(BFC):
                    fc = blk * BFC + j
                    par = j % 2
                    hw = [(ve, ex_ve[-1])]
                    if j >= 2:
                        hw.append((ve, h_ve[-2]))
                    pr.add("scalar", lambda e, j=j, par=par: e.activation(
                        tg_sb[par][:], gb[:, 512 * j:512 * (j + 1)], AF.Silu),
                           waits=hw, inc=se)
                    pr.add("vector", lambda e, j=j, par=par: e.tensor_tensor(
                        gb[:, 512 * j:512 * (j + 1)], tg_sb[par][:],
                        ub[:, 512 * j:512 * (j + 1)], ALU.mult),
                           waits=[(se, se.n)], inc=ve)
                    h_ve.append(ve.n)
                    pr.dma("sync", hs_d[fc], gb[:, 512 * j:512 * (j + 1)],
                           waits=[(ve, ve.n)], inc=sth)
            sth_all = sth.n

            mm_cf = cf_mms[-1]
            for ttt in range(4):
                pr.add("vector", lambda e, ttt=ttt: e.tensor_tensor(
                    cw_sb[:, 128 * ttt:128 * (ttt + 1)], ps_cf[:, 128 * ttt:128 * (ttt + 1)],
                    wpat[:, 128 * ttt:128 * (ttt + 1)], ALU.mult),
                       waits=[(mm, mm_cf)], inc=ve)
                pr.add("tensor", lambda e, ttt=ttt: e.transpose(
                    ps_tr[:, 128 * ttt:128 * (ttt + 1)], cw_sb[:, 128 * ttt:128 * (ttt + 1)],
                    idb[:]), waits=[(ve, ve.n)], inc=mm)
                pr.add("vector", lambda e, ttt=ttt: e.tensor_copy(
                    cwT_sb[:, 128 * ttt:128 * (ttt + 1)],
                    ps_tr[:, 128 * ttt:128 * (ttt + 1)]),
                       waits=[(mm, mm.n)], inc=ve)
            ve_cw = ve.n

            dn_all = []
            for q in range(4):
                for fc in range(NFC):
                    buf = wd_t[fc % 2]
                    hbuf = hh_t[fc % 2]
                    sw = wds[fc % 2]
                    sh_ = hhs[fc % 2]
                    if len(dn_all) >= 2:
                        w = [(mm, dn_all[-2] if fc % 2 == len(dn_all) % 2 else dn_all[-1])]
                    elif dn_all:
                        w = [(mm, dn_all[-1])]
                    else:
                        w = []
                    if fc < 2:
                        w = w + [(ve, ve_cw), (sth, sth_all)]
                    pr.dma("gpsimd", buf[:], wdT_q[q, fc], waits=w, inc=sw)
                    pr.dma("scalar", hbuf[:], hs_d[fc], waits=w, inc=sh_)

                    def dmm(e, fc=fc, buf=buf, hbuf=hbuf):
                        last = None
                        for i in range(4):
                            last = e.matmul(ps_dn[:, 512 * i:512 * (i + 1)],
                                            buf[:, 128 * i:128 * (i + 1)],
                                            hbuf[:],
                                            start=(fc == 0), stop=False)
                        return last
                    dw = [(sw, sw.n), (sh_, sh_.n)]
                    if fc == 0:
                        dw.append((ve, ve.n))
                    pr.add("tensor", dmm, waits=dw, inc=mm)
                    dn_all.append(mm.n)

                def delmm(e, q=q):
                    last = None
                    for i in range(4):
                        for ttt in range(4):
                            last = e.matmul(
                                ps_dn[:, 512 * i + 128 * ttt: 512 * i + 128 * (ttt + 1)],
                                bdb[:, (4 * q + i) * 128:(4 * q + i + 1) * 128],
                                cwT_sb[:, 128 * ttt:128 * (ttt + 1)],
                                start=False, stop=(ttt == 3))
                    return last
                pr.add("tensor", delmm, waits=[(ve, ve_cw)], inc=mm)
                mm_q = mm.n
                for i in range(4):
                    dmc = 4 * q + i
                    buf = x1rl[i % 2]
                    sx_ = xls[i % 2]
                    sy = sy2[i % 2]
                    pr.dma("sync", buf[:], x1s[dmc][:, 512 * tt:512 * (tt + 1)],
                           waits=[(sy, sy.n)], inc=sx_)
                    pr.add("vector", lambda e, i=i, buf=buf: e.tensor_add(
                        buf[:], buf[:], ps_dn[:, 512 * i:512 * (i + 1)]),
                           waits=[(mm, mm_q), (sx_, sx_.n)], inc=ve)
                    pr.dma("sync", y2s_d[dmc][:, 512 * tt:512 * (tt + 1)], buf[:],
                           waits=[(ve, ve.n)], inc=sy)

        ctxB.close()

        barrier((sy2[0], sy2[0].n), (sy2[1], sy2[1].n), (ve, ve.n), (mm, mm.n),
                (se, se.n))
        ctxC = ExitStack()
        y2s = ctxC.enter_context(nc.sbuf_tensor("m0y2s", [128, NDM * TOK], F32))
        for c in range(NDM):
            pr.dma("sync", y2s[:, c * TOK:(c + 1) * TOK], y2s_d[c], inc=ldy2)
        emit_ln(y2s, (ldy2, ldy2.n), cast_to=None, spill=x2T_d, spill_sem=sout)
        hc = halo_snd.rearrange("(c p) t -> c p t", p=128)
        for c in range(NDM):
            pr.dma("sync", hc[c], y2s[:, c * TOK + TOK - HALO: (c + 1) * TOK],
                   waits=[(ve, ve.n)], inc=shalo)
        ctxC.close()
        ctxP.close()
        return sout, shalo

    m0sout, m0shalo = phase_mlp0()
    barrier((m0sout, m0sout.n), (m0shalo, m0shalo.n))

    def cc_halo(e):
        return e.collective_compute(
            "AllGather", mybir.AluOpType.bypass, replica_groups=RG4,
            ins=[halo_snd.tensor.ap().opt()], outs=[halo_gat.tensor.ap().opt()])
    pr.add("gpsimd", cc_halo, waits=[(m0shalo, m0shalo.n)], inc=cc, inc_n=1, is_dma=True)

    def phase_attn1a():
        ctxP = ExitStack()

        def a(n, sh, dt):
            return ctxP.enter_context(nc.sbuf_tensor("a1" + n, sh, dt))
        cw_sb = a("cw_sb", [128, NCH * 4], F32)
        cb_sb = a("cb_sb", [128, NCH], F32)
        dtb_sb = a("dtb_sb", [128, NH], F32)
        acol_sb = a("acol_sb", [NH, 1], F32)
        dt_tok = a("dt_tok", [128, NCK * NH], F32)
        dt_fm = a("dt_fm", [NH, TOK], F32)
        da_fm = a("da_fm", [NH, TOK], F32)
        ca_fm = a("ca_fm", [NH, TOK], F32)
        gt_tmp = a("gt_tmp", [128, NH], F32)
        ca_tok = a("ca_tok", [128, NCK * NH], F32)
        tfm = a("tfm", [NH, 128], F32)
        cs_sb = a("cs_sb", [128, 128], F32)
        d1_sb = a("d1_sb", [128, 128], F32)
        l_sb = a("l_sb", [128, 128], F32)
        m_sb = a("m_sb", [128, 128], BF16)
        scores_sb = a("scores_sb", [128, 128], F32)
        ftok = a("ftok", [128, NCK * NH], F32)
        dce_col = a("dce_col", [NH, 1], F32)
        dch_sb = a("dch_sb", [128, NCK * NH], F32)
        drow64 = a("drow64", [1, NH], F32)
        dsp_sb = a("dsp_sb", [128, NH], F32)
        zrow = a("zrow", [128, NH], F32)
        z2 = a("z2", [128, NH], F32)
        z3 = a("z3", [128, NH], F32)
        ev1 = a("ev1", [128, 512], BF16)
        ev2 = a("ev2", [128, 512], BF16)
        xconv = a("xconv", [128, NCH * TOK], BF16)

        def pp(n, sh, dt):
            return ctxP.enter_context(nc.psum_tensor("a1" + n, sh, dt))
        ps_a = pp("ps_a", [128, 512], F32)
        ps_b = pp("ps_b", [128, 512], F32)
        ps_t2 = [pp("ps_t0", [128, 512], BF16), pp("ps_t1", [128, 512], BF16)]
        ps_f = pp("ps_f", [128, 512], F32)
        ps_g = pp("ps_g", [128, 1536], F32)

        ldc = pr.sem("a1ldc")
        ldx = pr.sem("a1ldx")
        wins = [pr.sem(f"a1win{i}") for i in range(2)]
        mm = pr.sem("a1mm")
        sg = pr.sem("a1sg")
        sxb = pr.sem("a1sxb")
        sx = pr.sem("a1sx")
        si = pr.sem("a1si")
        stc = pr.sem("a1stc")
        sm = pr.sem("a1sm")

        ctx0 = ExitStack()
        xb2 = ctx0.enter_context(nc.sbuf_tensor("a1xb2", [128, NDM * TOKH], BF16))
        win_t = [ctx0.enter_context(nc.sbuf_tensor(f"a1winb{i}", [128, NDM * 512], BF16))
                 for i in range(2)]
        hg_sb = ctx0.enter_context(nc.sbuf_tensor("a1hg_sb", [128, 4 * NDM * HALO], BF16))

        x2c = x2T_d.rearrange("(c p) t -> c p t", p=128)
        for c in range(NDM):
            pr.dma("gpsimd", xb2[:, c * TOKH + HALO:(c + 1) * TOKH], x2c[c], inc=ldx)
        for m in range(4):
            pr.dma("gpsimd", hg_sb[:, m * NDM * HALO:(m + 1) * NDM * HALO].rearrange(
                "p (c t) -> p c t", t=HALO),
                   halo_gat[m].rearrange("(c p) t -> p c t", p=128),
                   waits=[(cc, 2)] if m == 0 else [], inc=ldx)
        ldx_all = ldx.n
        for c in range(NDM):
            dst = np.s_[:, c * TOKH: c * TOKH + HALO]
            pr.add("vector", lambda e, dst=dst, c=c: e.tensor_scalar_mul(
                xb2[dst], hg_sb[:, c * HALO:(c + 1) * HALO], cmcol(CM_HSEL)),
                   waits=[(ldx, ldx_all), (ldc0, ldc0_all)], inc=ve)
            for m in range(1, 4):
                pr.add("vector", lambda e, dst=dst, c=c, m=m: e.scalar_tensor_tensor(
                    xb2[dst], hg_sb[:, m * NDM * HALO + c * HALO:
                                    m * NDM * HALO + (c + 1) * HALO],
                    cmcol(CM_HSEL + m), xb2[dst], ALU.mult, ALU.add),
                       waits=[], inc=ve)
        ve_halo = ve.n
        pr.dma("sync", cw_sb[:], convw[:], inc=ldc)
        pr.dma("sync", cb_sb[:], convb[:], inc=ldc)
        pr.dma("sync", dtb_sb[:], dtb_bc[:], inc=ldc)
        pr.dma("sync", acol_sb[:], a_col[:], inc=ldc)
        ldc_all = ldc.n

        winc = w_in.rearrange("(c p) t -> c p t", p=128)

        last_gate_mm = []
        for g in range(8):
            buf = win_t[g % 2]
            sem = wins[g % 2]
            w = [(mm, last_gate_mm[g - 2])] if g >= 2 else []
            for ic in range(NDM):
                pr.dma("gpsimd", buf[:, ic * 512:(ic + 1) * 512],
                       winc[ic][:, 512 * g:512 * (g + 1)],
                       waits=w if ic == 0 else [], inc=sem)
            for ttt in range(NCK):
                ps = ps_a if ttt % 2 == 0 else ps_b

                def gfn(e, ttt=ttt, buf=buf, ps=ps):
                    last = None
                    for ic in range(NDM):
                        last = e.matmul(ps[:],
                                        xb2[:, ic * TOKH + HALO + 128 * ttt:
                                            ic * TOKH + HALO + 128 * (ttt + 1)],
                                        buf[:, ic * 512:(ic + 1) * 512],
                                        start=(ic == 0), stop=(ic == NDM - 1))
                    return last
                waits = [(sem, sem.n), (ldx, ldx_all), (ve, ve_halo), (se, se.n)]
                pr.add("tensor", gfn, waits=waits, inc=mm)
                buf_ev = ev1 if ttt % 2 == 0 else ev2
                pr.add("scalar", lambda e, ps=ps, buf_ev=buf_ev: e.activation(
                    buf_ev[:], ps[:], AF.Silu), waits=[(mm, mm.n), (sg, sg.n)], inc=se)
                pr.dma("sync", gate_d[ttt][:, 512 * g:512 * (g + 1)], buf_ev[:],
                       waits=[(se, se.n)], inc=sg)
            last_gate_mm.append(mm.n)

        tok_tiles = [(0, 512), (512, 512), (1024, 128)]
        last_xbc_mm = []
        for fc in range(NCH):
            buf = win_t[fc % 2]
            sem = wins[fc % 2]
            w = [(mm, last_xbc_mm[fc - 2])] if fc >= 2 else [(mm, mm.n)]
            for ic in range(NDM):
                pr.dma("gpsimd", buf[:, ic * 512:ic * 512 + 128],
                       winc[ic][:, INTER + 128 * fc:INTER + 128 * (fc + 1)],
                       waits=w if ic == 0 else [], inc=sem)
            for ti, (t0, tw) in enumerate(tok_tiles):
                ps = ps_a if ti % 2 == 0 else ps_b

                def xfn(e, t0=t0, tw=tw, buf=buf, ps=ps):
                    last = None
                    for ic in range(NDM):
                        last = e.matmul(ps[:, 0:tw],
                                        buf[:, ic * 512:ic * 512 + 128],
                                        xb2[:, ic * TOKH + t0: ic * TOKH + t0 + tw],
                                        start=(ic == 0), stop=(ic == NDM - 1))
                    return last
                pr.add("tensor", xfn, waits=[(sem, sem.n), (ve, ve.n)], inc=mm)
                evb = ev1 if ti % 2 == 0 else ev2
                pr.add("vector", lambda e, tw=tw, ps=ps, evb=evb: e.tensor_copy(
                    evb[:, 0:tw], ps[:, 0:tw]),
                       waits=[(mm, mm.n), (sxb, sxb.n), (sg, sg.n)], inc=ve)
                pr.dma("sync", xbc_dram[fc][:, t0:t0 + tw], evb[:, 0:tw],
                       waits=[(ve, ve.n)], inc=sxb)
            last_xbc_mm.append(mm.n)

        buf = win_t[0]
        sem = wins[0]
        for ic in range(NDM):
            pr.dma("gpsimd", buf[:, ic * 512:ic * 512 + NH],
                   winc[ic][:, 2 * INTER + 2 * NST: 2 * INTER + 2 * NST + NH],
                   waits=[(mm, mm.n)] if ic == 0 else [], inc=sem)
        for ttt in range(NCK):
            def dfn(e, ttt=ttt, buf=buf):
                last = None
                for ic in range(NDM):
                    last = e.matmul(ps_f[:, 0:NH],
                                    xb2[:, ic * TOKH + HALO + 128 * ttt:
                                        ic * TOKH + HALO + 128 * (ttt + 1)],
                                    buf[:, ic * 512:ic * 512 + NH],
                                    start=(ic == 0), stop=(ic == NDM - 1))
                return last
            pr.add("tensor", dfn, waits=[(sem, sem.n), (ve, ve.n)], inc=mm)
            pr.add("vector", lambda e: e.tensor_add(zrow[:], ps_f[:, 0:NH], dtb_sb[:]),
                   waits=[(mm, mm.n), (ldc, ldc_all)], inc=ve)
            pr.add("scalar", lambda e: e.activation(z2[:], zrow[:], AF.Abs),
                   waits=[(ve, ve.n)], inc=se)
            pr.add("scalar", lambda e: e.activation(z2[:], z2[:], AF.Exp, scale=-1.0),
                   waits=[], inc=se)
            pr.add("scalar", lambda e: e.activation(
                z2[:], z2[:], AF.Ln, bias=sc_sb[:, SCALARS["one"]:SCALARS["one"] + 1]),
                   waits=[], inc=se)
            pr.add("vector", lambda e: e.tensor_scalar_max(z3[:], zrow[:], 0.0),
                   waits=[(se, se.n)], inc=ve)
            pr.add("vector", lambda e, ttt=ttt: e.tensor_add(
                dt_tok[:, ttt * NH:(ttt + 1) * NH], z3[:], z2[:]), waits=[], inc=ve)
        ctx0.close()

        xtokc = a("xtokc", [128, INTER], BF16)
        xt_sb = a("xt_sb", [128, INTER], BF16)
        btok = a("btok", [128, 128], BF16)
        evf = [a(f"evf{i}", [128, 512], F32) for i in range(2)]
        s_sb = a("s_sb", [128, INTER], F32)
        convacc = a("convacc", [128, TOK], F32)
        xbt = [a(f"xbt{i}", [128, TOKH], BF16) for i in range(2)]
        lxb = [pr.sem(f"a1lxb{i}") for i in range(2)]
        conv_ve = []
        for fc in range(NCH):
            buf = xbt[fc % 2]
            sem = lxb[fc % 2]
            w = [(sxb, sxb.n)] if fc < 2 else [(ve, conv_ve[fc - 2])]
            pr.dma("scalar", buf[:], xbc_dram[fc], waits=w, inc=sem)
            pr.add("vector", lambda e, fc=fc, buf=buf: e.tensor_scalar(
                convacc[:], buf[:, 125:125 + TOK],
                cw_sb[:, 4 * fc:4 * fc + 1], cb_sb[:, fc:fc + 1], ALU.mult, ALU.add),
                   waits=[(sem, sem.n), (se, se.n)], inc=ve)
            for kk in range(1, 4):
                pr.add("vector", lambda e, fc=fc, kk=kk, buf=buf: e.scalar_tensor_tensor(
                    convacc[:], buf[:, 125 + kk: 125 + kk + TOK],
                    cw_sb[:, 4 * fc + kk:4 * fc + kk + 1], convacc[:], ALU.mult, ALU.add),
                       waits=[], inc=ve)
            pr.add("scalar", lambda e, fc=fc: e.activation(
                xconv[:, fc * TOK:(fc + 1) * TOK], convacc[:], AF.Silu),
                   waits=[(ve, ve.n)], inc=se)
            conv_ve.append(ve.n)
        se_conv = se.n
        pr.dma("sync", cfm_d[:], xconv[:, 33 * TOK:34 * TOK], waits=[(se, se_conv)],
               inc=sm)

        for ttt in range(NCK):
            pr.add("tensor", lambda e, ttt=ttt: e.transpose(
                ps_f[0:NH, 0:128], dt_tok[:, ttt * NH:(ttt + 1) * NH], idf[:]),
                   waits=[(ve, ve.n)], inc=mm)
            pr.add("vector", lambda e, ttt=ttt: e.tensor_copy(
                dt_fm[:, 128 * ttt:128 * (ttt + 1)], ps_f[0:NH, 0:128]),
                   waits=[(mm, mm.n)], inc=ve)
        pr.add("vector", lambda e: e.tensor_scalar_mul(da_fm[:], dt_fm[:],
                                                       acol_sb[:, 0:1]),
               waits=[], inc=ve)
        for ttt in range(NCK):
            pr.add("vector", lambda e, ttt=ttt: e.tensor_tensor_scan(
                ca_fm[:, 128 * ttt:128 * (ttt + 1)], onesf[0:NH, :],
                da_fm[:, 128 * ttt:128 * (ttt + 1)], 0.0, ALU.mult, ALU.add),
                   waits=[], inc=ve)
        ve_ca = ve.n
        for ttt in range(NCK):
            pr.add("tensor", lambda e, ttt=ttt: e.transpose(
                ps_f[0:128, 0:NH], ca_fm[:, 128 * ttt:128 * (ttt + 1)], idf[0:NH, 0:NH]),
                   waits=[(ve, ve_ca), (se, se.n), (ve, ve.n)], inc=mm)
            pr.add("vector", lambda e, ttt=ttt: e.tensor_copy(
                ca_tok[:, ttt * NH:(ttt + 1) * NH], ps_f[0:128, 0:NH]),
                   waits=[(mm, mm.n)], inc=ve)
            pr.add("scalar", lambda e, ttt=ttt: e.activation(
                gt_tmp[:], ca_tok[:, ttt * NH:(ttt + 1) * NH], AF.Exp),
                   waits=[(ve, ve.n), (sm, sm.n)], inc=se)
            pr.dma("sync", gtok_d[ttt], gt_tmp[:], waits=[(se, se.n)], inc=sm)
            pr.add("vector", lambda e, ttt=ttt: e.tensor_scalar(
                tfm[:], ca_fm[:, 128 * ttt:128 * (ttt + 1)],
                ca_fm[:, 128 * ttt + 127:128 * ttt + 128], None, ALU.subtract),
                   waits=[], inc=ve)
            pr.add("tensor", lambda e: e.transpose(ps_f[:, 64:64 + NH], tfm[:],
                                                   idf[0:NH, 0:NH]),
                   waits=[(ve, ve.n)], inc=mm)
            pr.add("scalar", lambda e, ttt=ttt: e.activation(
                ftok[:, ttt * NH:(ttt + 1) * NH], ps_f[:, 64:64 + NH], AF.Exp,
                scale=-1.0), waits=[(mm, mm.n)], inc=se)
            pr.add("vector", lambda e, ttt=ttt: e.tensor_mul(
                ftok[:, ttt * NH:(ttt + 1) * NH], ftok[:, ttt * NH:(ttt + 1) * NH],
                dt_tok[:, ttt * NH:(ttt + 1) * NH]), waits=[(se, se.n)], inc=ve)
            pr.add("scalar", lambda e, ttt=ttt: e.activation(
                dce_col[:], ca_fm[:, 128 * ttt + 127:128 * ttt + 128], AF.Exp),
                   waits=[(ve, ve_ca)], inc=se)
            pr.add("tensor", lambda e: e.transpose(ps_f[0:1, 128:128 + NH], dce_col[:],
                                                   idf[0:NH, 0:NH]),
                   waits=[(se, se.n)], inc=mm)
            pr.add("vector", lambda e: e.tensor_copy(drow64[:], ps_f[0:1, 128:128 + NH]),
                   waits=[(mm, mm.n)], inc=ve)
            pr.add("tensor", lambda e: e.matmul(ps_f[:, 256:256 + NH], one1f[:],
                                                drow64[:], start=True, stop=True),
                   waits=[(ve, ve.n)], inc=mm)
            pr.add("vector", lambda e, ttt=ttt: e.tensor_copy(
                dch_sb[:, ttt * NH:(ttt + 1) * NH], ps_f[:, 256:256 + NH]),
                   waits=[(mm, mm.n)], inc=ve)
            pr.dma("sync", dch_d[ttt], dch_sb[:, ttt * NH:(ttt + 1) * NH],
                   waits=[(ve, ve.n)], inc=sm)
        pr.add("vector", lambda e: e.tensor_copy(dsp_sb[:], dch_sb[:, 0:NH]),
               waits=[], inc=ve)
        for ttt in range(1, NCK):
            pr.add("vector", lambda e, ttt=ttt: e.tensor_mul(
                dsp_sb[:], dsp_sb[:], dch_sb[:, ttt * NH:(ttt + 1) * NH]),
                   waits=[], inc=ve)
        pr.dma("sync", dspan_d[:], dsp_sb[:], waits=[(ve, ve.n)], inc=sm)
        pr.add("vector", lambda e: e.memset(s_sb[:], 0.0), waits=[], inc=ve)

        for ttt in range(NCK):
            tr_ves = []
            for j in range(32):
                slot = j % 2
                w = [(se, se_conv)]
                if len(tr_ves) >= 2:
                    w.append((ve, tr_ves[-2]))
                pr.add("tensor", lambda e, j=j, ttt=ttt, slot=slot: e.transpose(
                    ps_t2[slot][:, 0:128],
                    xconv[:, j * TOK + 128 * ttt: j * TOK + 128 * (ttt + 1)], idb[:]),
                       waits=w, inc=mm)
                pr.add("vector", lambda e, j=j, slot=slot: e.tensor_copy(
                    xtokc[:, 128 * j:128 * (j + 1)], ps_t2[slot][:, 0:128]),
                       waits=[(mm, mm.n), (sx, sx.n)], inc=ve)
                tr_ves.append(ve.n)
            pr.dma("sync", xtok_d[ttt], xtokc[:], waits=[(ve, ve.n)], inc=sx)
            pr.add("vector", lambda e, ttt=ttt: e.tensor_tensor(
                xt_sb[:].rearrange("p (h q) -> p h q", h=NH),
                xtokc[:].rearrange("p (h q) -> p h q", h=NH),
                ftok[:, ttt * NH:(ttt + 1) * NH].unsqueeze(2).broadcast_to([128, NH, HD]),
                ALU.mult), waits=[], inc=ve)
            pr.add("tensor", lambda e, ttt=ttt: e.transpose(
                ps_t2[0][:, 0:128],
                xconv[:, 32 * TOK + 128 * ttt:32 * TOK + 128 * (ttt + 1)],
                idb[:]), waits=[(ve, ve.n)], inc=mm)
            pr.add("vector", lambda e: e.tensor_copy(btok[:], ps_t2[0][:, 0:128]),
                   waits=[(mm, mm.n)], inc=ve)
            ve_btok = ve.n
            pr.add("vector", lambda e, ttt=ttt: e.tensor_tensor(
                s_sb[:].rearrange("p (h q) -> p h q", h=NH),
                s_sb[:].rearrange("p (h q) -> p h q", h=NH),
                dch_sb[:, ttt * NH:(ttt + 1) * NH].unsqueeze(2).broadcast_to(
                    [128, NH, HD]),
                ALU.mult), waits=[(ve, ve_btok)], inc=ve)
            for i in range(8):
                bank = i % 3

                def tcfn(e, i=i, bank=bank):
                    return e.matmul(ps_g[:, 512 * bank:512 * (bank + 1)], btok[:],
                                    xt_sb[:, 512 * i:512 * (i + 1)],
                                    start=True, stop=True)
                w = [(ve, ve_btok)]
                if i >= 3:
                    w.append((ve, ve.n))
                pr.add("tensor", tcfn, waits=w, inc=mm)
                mm_tc = mm.n
                pr.add("vector", lambda e, i=i, bank=bank: e.tensor_add(
                    s_sb[:, 512 * i:512 * (i + 1)], s_sb[:, 512 * i:512 * (i + 1)],
                    ps_g[:, 512 * bank:512 * (bank + 1)]),
                       waits=[(mm, mm_tc)], inc=ve)
                pr.add("vector", lambda e, i=i, bank=bank: e.tensor_copy(
                    evf[i % 2][:], ps_g[:, 512 * bank:512 * (bank + 1)]),
                       waits=[(stc, stc.n)], inc=ve)
                pr.dma("sync", tc_d[ttt][:, 512 * i:512 * (i + 1)], evf[i % 2][:],
                       waits=[(ve, ve.n)], inc=stc)
            pr.add("tensor", lambda e, ttt=ttt: e.matmul(
                ps_f[:, 0:128],
                xconv[:, 32 * TOK + 128 * ttt:32 * TOK + 128 * (ttt + 1)],
                xconv[:, 33 * TOK + 128 * ttt:33 * TOK + 128 * (ttt + 1)],
                start=True, stop=True), waits=[(ve, ve.n)], inc=mm)
            pr.add("vector", lambda e: e.tensor_copy(scores_sb[:], ps_f[:, 0:128]),
                   waits=[(mm, mm.n)], inc=ve)
            ve_sc = ve.n
            for grp in range(8):
                ps = ps_a if grp % 2 == 0 else ps_b
                for hh in range(8):
                    h = 8 * grp + hh
                    pr.add("vector", lambda e, ttt=ttt, h=h: e.tensor_scalar_mul(
                        cs_sb[:], onesf[:], ca_tok[:, ttt * NH + h: ttt * NH + h + 1]),
                           waits=[(ve, ve_sc)], inc=ve)
                    pr.add("tensor", lambda e: e.transpose(ps_f[:, 128:256], cs_sb[:],
                                                           idf[:]),
                           waits=[(ve, ve.n)], inc=mm)
                    pr.add("vector", lambda e: e.scalar_tensor_tensor(
                        d1_sb[:], cs_sb[:], -1.0, ps_f[:, 128:256], ALU.mult, ALU.add),
                           waits=[(mm, mm.n)], inc=ve)
                    pr.add("vector", lambda e: e.tensor_add(d1_sb[:], d1_sb[:],
                                                            mlt_sb[:]),
                           waits=[], inc=ve)
                    pr.add("scalar", lambda e: e.activation(l_sb[:], d1_sb[:], AF.Exp),
                           waits=[(ve, ve.n)], inc=se)
                    pr.add("vector", lambda e, ttt=ttt, h=h: e.scalar_tensor_tensor(
                        m_sb[:], l_sb[:], dt_tok[:, ttt * NH + h: ttt * NH + h + 1],
                        scores_sb[:], ALU.mult, ALU.mult),
                           waits=[(se, se.n)], inc=ve)

                    def imfn(e, h=h, hh=hh, ps=ps):
                        return e.matmul(ps[:, 64 * hh:64 * (hh + 1)], m_sb[:],
                                        xtokc[:, 64 * h:64 * (h + 1)],
                                        start=(hh == 0), stop=(hh == 7))
                    pr.add("tensor", imfn, waits=[(ve, ve.n)], inc=mm)
                buf_ev = ev1 if grp % 2 == 0 else ev2
                pr.add("scalar", lambda e, ps=ps, buf_ev=buf_ev: e.activation(
                    buf_ev[:], ps[:], AF.Copy),
                       waits=[(mm, mm.n), (si, si.n), (sg, sg.n), (sxb, sxb.n)], inc=se)
                pr.dma("sync", intra_d[ttt][:, 512 * grp:512 * (grp + 1)], buf_ev[:],
                       waits=[(se, se.n)], inc=si)

        pr.dma("sync", uspan_d[:], s_sb[:], waits=[(ve, ve.n)], inc=sm)
        ctxP.close()
        return sm, sg, sx, si, stc, sxb

    a1sm, a1sg, a1sx, a1si, a1stc, a1sxb = phase_attn1a()
    barrier((a1sm, a1sm.n), (a1sg, a1sg.n), (a1sx, a1sx.n), (a1si, a1si.n),
            (a1stc, a1stc.n), (a1sxb, a1sxb.n))

    def cc_usp(e):
        return e.collective_compute(
            "AllGather", mybir.AluOpType.bypass, replica_groups=RG4,
            ins=[uspan_d.tensor.ap().opt()], outs=[uspan_gat.tensor.ap().opt()])
    pr.add("gpsimd", cc_usp, waits=[(a1sm, a1sm.n)], inc=cc, inc_n=1, is_dma=True)

    def cc_dsp(e):
        return e.collective_compute(
            "AllGather", mybir.AluOpType.bypass, replica_groups=RG4,
            ins=[dspan_d.tensor.ap().opt()], outs=[dspan_gat.tensor.ap().opt()])
    pr.add("gpsimd", cc_dsp, waits=[], inc=cc, inc_n=1, is_dma=True)

    def phase_attn1b():
        ctxP = ExitStack()
        mm = pr.sem("b1mm")
        lnk = LNKit(nc, pr, mm, one_bf, one1f,
                    sc_sb[0:1, SCALARS["eps"]:SCALARS["eps"] + 1], ctx=ctxP)

        def a(n, sh, dt):
            return ctxP.enter_context(nc.sbuf_tensor("b1" + n, sh, dt))
        dbc_sb = a("dbc_sb", [128, NH], F32)
        nw_sb = a("nw_sb", [128, INTER], BF16)
        gl_sb = a("gl_sb", [128, NCK * NH], F32)
        dchl_sb = a("dchl_sb", [128, NCK * NH], F32)
        dg_sb = a("dg_sb", [128, 4 * NH], F32)
        wl_sb = a("wl_sb", [128, 3 * NH], F32)
        f2_sb = a("f2_sb", [128, NH], F32)
        cfm_sb = a("cfm_sb", [128, TOK], BF16)
        s_sb = a("s_sb", [128, INTER], F32)
        sbf = a("sbf", [128, INTER], BF16)
        tcl = a("tcl", [128, INTER], F32)
        u_sb = tcl
        intral = a("intral", [128, INTER], BF16)
        xtokl = a("xtokl", [128, INTER], BF16)
        gatel = a("gatel", [128, INTER], BF16)
        zc = a("zc", [128, INTER], F32)
        tmp512 = a("tmp512", [128, 512], F32)
        zcol = a("zcol", [128, 1], F32)
        znb = a("znb", [128, INTER], BF16)
        zrow_big = a("zrow_big", [128, 32 * 128], BF16)
        y3 = a("y3", [128, NDM * TOK], F32)
        wq_ = [a(f"wq{i}", [128, 1024], BF16) for i in range(2)]
        zq = [a(f"zq{i}", [128, 512], BF16) for i in range(2)]

        def pp(n, sh, dt):
            return ctxP.enter_context(nc.psum_tensor("b1" + n, sh, dt))
        ps_y = pp("ps_y", [128, 1024], F32)
        ps_t2 = [pp("ps_t0", [128, 512], BF16), pp("ps_t1", [128, 512], BF16)]
        ps_w = pp("ps_w", [128, 2048], F32)

        ldc = pr.sem("b1ldc")
        ldu = pr.sem("b1ldu")
        ldk = pr.sem("b1ldk")
        lw = [pr.sem(f"b1lw{i}") for i in range(2)]
        lz = [pr.sem(f"b1lz{i}") for i in range(2)]
        szt = pr.sem("b1szt")
        sout = pr.sem("b1sout")

        pr.dma("sync", dbc_sb[:], d_bc[:], inc=ldc)
        pr.dma("sync", nw_sb[:], nw_bc[:], inc=ldc)
        pr.dma("sync", cfm_sb[:], cfm_d[:], inc=ldc)
        for c in range(NCK):
            pr.dma("sync", gl_sb[:, c * NH:(c + 1) * NH], gtok_d[c], inc=ldc)
            pr.dma("sync", dchl_sb[:, c * NH:(c + 1) * NH], dch_d[c], inc=ldc)
        for m in range(4):
            pr.dma("sync", dg_sb[:, m * NH:(m + 1) * NH], dspan_gat[m],
                   waits=[(cc, 4)] if m == 0 else [], inc=ldc)
        x2c = x2T_d.rearrange("(c p) t -> c p t", p=128)
        for c in range(NDM):
            pr.dma("sync", y3[:, c * TOK:(c + 1) * TOK], x2c[c], inc=ldc)
        ldc_all = ldc.n

        def hview(t, col0):
            return t[:, col0:col0 + NH].unsqueeze(2).broadcast_to([128, NH, HD])

        def hview8(t, col0):
            return t[:, col0:col0 + 8].unsqueeze(2).broadcast_to([128, 8, HD])

        def as_hq(apx, h):
            return apx.rearrange("p (h q) -> p h q", h=h)

        pr.add("vector", lambda e: e.tensor_scalar(
            wl_sb[:, 0:NH], dg_sb[:, NH:2 * NH], cmcol(CM_S1), cmcol(CM_C1),
            ALU.mult, ALU.add), waits=[(ldc, ldc_all), (ldc0, ldc0_all)], inc=ve)
        pr.add("vector", lambda e: e.tensor_scalar(
            f2_sb[:], dg_sb[:, 2 * NH:3 * NH], cmcol(CM_S2), cmcol(CM_C2),
            ALU.mult, ALU.add), waits=[], inc=ve)
        pr.add("vector", lambda e: e.tensor_mul(wl_sb[:, 0:NH], wl_sb[:, 0:NH],
                                                f2_sb[:]),
               waits=[], inc=ve)
        pr.add("vector", lambda e: e.tensor_scalar_mul(wl_sb[:, 0:NH], wl_sb[:, 0:NH],
                                                       cmcol(CM_M0)), waits=[], inc=ve)
        pr.add("vector", lambda e: e.tensor_scalar_mul(wl_sb[:, NH:2 * NH], f2_sb[:],
                                                       cmcol(CM_M1)), waits=[], inc=ve)
        pr.add("vector", lambda e: e.tensor_scalar_mul(
            wl_sb[:, 2 * NH:3 * NH], onesf[:, 0:NH], cmcol(CM_M2)), waits=[], inc=ve)

        pr.add("vector", lambda e: e.memset(s_sb[:], 0.0), waits=[], inc=ve)
        for m in range(3):
            pr.dma("scalar", u_sb[:], uspan_gat[m],
                   waits=[(cc, 3), (ve, ve.n)], inc=ldu)
            pr.add("vector", lambda e, m=m: e.tensor_tensor(
                as_hq(u_sb[:], NH), as_hq(u_sb[:], NH), hview(wl_sb, m * NH), ALU.mult),
                   waits=[(ldu, ldu.n)], inc=ve)
            pr.add("vector", lambda e: e.tensor_add(s_sb[:], s_sb[:], u_sb[:]),
                   waits=[], inc=ve)

        sdbg = pr.sem("b1sdbg")
        if debug:
            pr.dma("sync", dbg_sin[:], s_sb[:], waits=[(ve, ve.n)], inc=sdbg)
            pr.dma("sync", dbg_wl[:, 0:4 * NH], dg_sb[:], waits=[(ve, ve.n)], inc=sdbg)
            pr.dma("sync", dbg_wl[:, 4 * NH:], wl_sb[:], waits=[(ve, ve.n)], inc=sdbg)

        for ck in range(NCK):
            w = [(ve, ve.n), (mm, mm.n), (sdbg, sdbg.n)]
            pr.dma("sync", tcl[:], tc_d[ck], waits=w, inc=ldk)
            pr.dma("scalar", intral[:], intra_d[ck], waits=w, inc=ldk)
            pr.dma("scalar", xtokl[:], xtok_d[ck], waits=w, inc=ldk)
            pr.dma("scalar", gatel[:], gate_d[ck], waits=w, inc=ldk)
            ldk_ck = ldk.n
            pr.add("vector", lambda e: e.tensor_copy(sbf[:], s_sb[:]),
                   waits=[(sdbg, sdbg.n)], inc=ve)
            pr.add("vector", lambda e, ck=ck: e.tensor_tensor(
                as_hq(s_sb[:], NH), as_hq(s_sb[:], NH), hview(dchl_sb, ck * NH),
                ALU.mult), waits=[], inc=ve)
            pr.add("vector", lambda e: e.tensor_add(s_sb[:], s_sb[:], tcl[:]),
                   waits=[(ldk, ldk_ck)], inc=ve)
            ve_snap = ve.n
            for i in range(8):
                pcol = 512 * (i % 2)

                def yfn(e, ck=ck, i=i, pcol=pcol):
                    return e.matmul(ps_y[:, pcol:pcol + 512],
                                    cfm_sb[:, 128 * ck:128 * (ck + 1)],
                                    sbf[:, 512 * i:512 * (i + 1)],
                                    start=True, stop=True)
                yw = [(ve, ve_snap)]
                if i >= 2:
                    yw.append((ve, ve.n))
                pr.add("tensor", yfn, waits=yw, inc=mm)
                mm_y = mm.n
                pr.add("vector", lambda e, ck=ck, i=i, pcol=pcol: e.tensor_tensor(
                    as_hq(zc[:, 512 * i:512 * (i + 1)], 8),
                    as_hq(ps_y[:, pcol:pcol + 512], 8),
                    hview8(gl_sb, ck * NH + 8 * i), ALU.mult), waits=[(mm, mm_y)],
                       inc=ve)
                pr.add("vector", lambda e, i=i: e.tensor_add(
                    zc[:, 512 * i:512 * (i + 1)], zc[:, 512 * i:512 * (i + 1)],
                    intral[:, 512 * i:512 * (i + 1)]), waits=[], inc=ve)
                pr.add("vector", lambda e, i=i: e.tensor_tensor(
                    as_hq(tmp512[:], 8), as_hq(xtokl[:, 512 * i:512 * (i + 1)], 8),
                    hview8(dbc_sb, 8 * i), ALU.mult), waits=[], inc=ve)
                pr.add("vector", lambda e, i=i: e.tensor_add(
                    zc[:, 512 * i:512 * (i + 1)], zc[:, 512 * i:512 * (i + 1)],
                    tmp512[:]), waits=[], inc=ve)
                pr.add("vector", lambda e, i=i: e.tensor_mul(
                    zc[:, 512 * i:512 * (i + 1)], zc[:, 512 * i:512 * (i + 1)],
                    gatel[:, 512 * i:512 * (i + 1)]), waits=[], inc=ve)
            pr.add("scalar", lambda e: e.activation(intral[:], zc[:], AF.Square),
                   waits=[(ve, ve.n)], inc=se)
            pr.add("vector", lambda e: e.reduce_sum(zcol[:], intral[:], axis=AX.X),
                   waits=[(se, se.n)], inc=ve)
            pr.add("scalar", lambda e: e.activation(
                zcol[:], zcol[:], AF.Sqrt, scale=1.0 / INTER,
                bias=sc_sb[:, SCALARS["eps"]:SCALARS["eps"] + 1]),
                   waits=[(ve, ve.n)], inc=se)
            pr.add("vector", lambda e: e.reciprocal(zcol[:], zcol[:]),
                   waits=[(se, se.n)], inc=ve)
            pr.add("vector", lambda e: e.scalar_tensor_tensor(
                znb[:], zc[:], zcol[:, 0:1], nw_sb[:], ALU.mult, ALU.mult),
                   waits=[], inc=ve)
            ve_znb = ve.n
            tr_ves = []
            for j in range(32):
                slot = j % 2
                tw = [(ve, ve_znb)]
                if len(tr_ves) >= 2:
                    tw.append((ve, tr_ves[-2]))
                pr.add("tensor", lambda e, j=j, slot=slot: e.transpose(
                    ps_t2[slot][:, 0:128], znb[:, 128 * j:128 * (j + 1)],
                    idb[:]), waits=tw, inc=mm)
                pr.add("vector", lambda e, j=j, slot=slot: e.tensor_copy(
                    zrow_big[:, 128 * j:128 * (j + 1)], ps_t2[slot][:, 0:128]),
                       waits=[(mm, mm.n), (szt, szt.n)], inc=ve)
                tr_ves.append(ve.n)
            pr.dma("sync",
                   zbT_dram[:, :, 128 * ck:128 * (ck + 1)].rearrange("c p f -> p c f"),
                   zrow_big[:].rearrange("p (c f) -> p c f", f=128),
                   waits=[(ve, ve.n)], inc=szt)

        woutc = wout.rearrange("(c p) t -> c p t", p=128)
        wv_all = []
        for tt2 in range(2):
            for qd in range(4):
                for ic in range(32):
                    bw = wq_[ic % 2]
                    bz = zq[ic % 2]
                    sw = lw[ic % 2]
                    sz = lz[ic % 2]
                    w = [(mm, wv_all[-2])] if len(wv_all) >= 2 else [(szt, szt.n)]
                    pr.dma("gpsimd", bw[:, 0:512], woutc[ic][:, 512 * qd:512 * (qd + 1)],
                           waits=w, inc=sw)
                    pr.dma("scalar", bz[:], zbT_dram[ic][:, 512 * tt2:512 * (tt2 + 1)],
                           waits=w, inc=sz)

                    def wfn(e, ic=ic, bw=bw, bz=bz):
                        last = None
                        for t4 in range(4):
                            last = e.matmul(ps_w[:, 512 * t4:512 * (t4 + 1)],
                                            bw[:, 128 * t4:128 * (t4 + 1)], bz[:],
                                            start=(ic == 0), stop=(ic == 31))
                        return last
                    ww = [(sw, sw.n), (sz, sz.n)]
                    if ic == 0:
                        ww.append((ve, ve.n))
                    pr.add("tensor", wfn, waits=ww, inc=mm)
                    wv_all.append(mm.n)
                mm_pass = mm.n
                for t4 in range(4):
                    dmc = 4 * qd + t4
                    pr.add("vector", lambda e, dmc=dmc, t4=t4, tt2=tt2: e.tensor_add(
                        y3[:, dmc * TOK + 512 * tt2: dmc * TOK + 512 * (tt2 + 1)],
                        y3[:, dmc * TOK + 512 * tt2: dmc * TOK + 512 * (tt2 + 1)],
                        ps_w[:, 512 * t4:512 * (t4 + 1)]), waits=[(mm, mm_pass)],
                           inc=ve)

        lnk.emit(y3, ps_w, (ve, ve.n), spill=x3T_d, spill_sem=sout)
        ctxP.close()
        return sout

    b1sout = phase_attn1b()
    barrier((b1sout, b1sout.n))

    def phase_mlp1():
        ctxP = ExitStack()

        def a(n, sh, dt):
            return ctxP.enter_context(nc.sbuf_tensor("m1" + n, sh, dt))
        xs = a("xs", [128, NDM * TOK], F32)
        xb = a("xb", [128, NDM * TOK], BF16)
        wb = a("wb", [128, NDM * DM], BF16)
        sgn = a("sgn", [128, DM], BF16)
        absw = a("absw", [128, DM], BF16)
        bias_sb = a("bias_sb", [128, NDM], F32)
        al_rowf = a("al_rowf", [1, DM], F32)
        stat_m = a("stat_m", [1, TOK], F32)
        stat_v = a("stat_v", [1, TOK], F32)
        rstd = a("rstd", [1, TOK], F32)
        mrstd = a("mrstd", [1, TOK], F32)
        ybf = a("ybf", [128, TOK], BF16)
        ysq = a("ysq", [128, TOK], BF16)
        res2 = [a("res_a", [128, TOK], F32), a("res_b", [128, TOK], F32)]

        def pp(n, sh, dt):
            return ctxP.enter_context(nc.psum_tensor("m1" + n, sh, dt))
        ps_d = [pp("ps_d0", [128, 512], F32), pp("ps_d1", [128, 512], F32)]
        ps_big = pp("ps_big", [128, 2048], F32)

        ld = pr.sem("m1ld")
        ldg = pr.sem("m1ldg")
        mm = pr.sem("m1mm")
        stq2 = [pr.sem("m1stq_a"), pr.sem("m1stq_b")]

        x3c = x3T_d.rearrange("(c p) t -> c p t", p=128)
        wc = wT.rearrange("(c p) o -> c p o", p=128)
        for c in range(NDM):
            pr.dma("sync", xs[:, c * TOK:(c + 1) * TOK], x3c[c], inc=ld)
        pr.dma("sync", bias_sb[:], biasc[:], inc=ld)
        ld_all = ld.n
        for c in range(NDM):
            pr.dma("gpsimd", xb[:, c * TOK:(c + 1) * TOK], x3c[c], inc=ldg)
        ldg_x = ldg.n
        for c in range(NDM):
            pr.dma("gpsimd", wb[:, c * DM:(c + 1) * DM], wc[c], inc=ldg)
        ldg_w = ldg.n

        alpha_mm = []
        for c in range(NDM):
            pr.add("scalar",
                   lambda e, c=c: e.activation(absw[:], wb[:, c * DM:(c + 1) * DM],
                                               AF.Abs),
                   waits=[(ldg, ldg_w)] + ([(mm, alpha_mm[-1])] if c else []), inc=se)
            pr.add("tensor", lambda e, c=c: [
                e.matmul(ps_big[0:1, 512 * j:512 * (j + 1)], one_bf[:],
                         absw[:, 512 * j:512 * (j + 1)],
                         start=(c == 0), stop=(c == NDM - 1)) for j in range(4)][-1],
                   waits=[(se, se.n)], inc=mm)
            alpha_mm.append(mm.n)
        pr.add("vector", lambda e: e.tensor_scalar_mul(al_rowf[:], ps_big[0:1, :],
                                                       1.0 / DM),
               waits=[(mm, mm.n)], inc=ve)
        pr.add("tensor", lambda e: [
            e.matmul(ps_big[:, 512 * j:512 * (j + 1)], one1f[:],
                     al_rowf[0:1, 512 * j:512 * (j + 1)], start=True, stop=True)
            for j in range(4)][-1],
               waits=[(ve, ve.n)], inc=mm)
        mm_albc = mm.n

        wbin_ve = []
        for c in range(NDM):
            pr.add("scalar",
                   lambda e, c=c: e.activation(sgn[:], wb[:, c * DM:(c + 1) * DM],
                                               AF.Sign),
                   waits=[(ve, wbin_ve[-1])] if c else [], inc=se)
            pr.add("vector", lambda e, c=c: e.tensor_tensor(
                wb[:, c * DM:(c + 1) * DM], sgn[:], ps_big[:, :DM], ALU.mult),
                   waits=[(se, se.n), (mm, mm_albc)], inc=ve)
            wbin_ve.append(ve.n)
        ve_wbin = ve.n

        resid_ve = []
        k = 0
        for oc in range(NDM):
            for tt in range(2):
                ps = ps_d[k % 2]

                def mmfn(e, oc=oc, tt=tt, ps=ps):
                    last = None
                    for ic in range(NDM):
                        last = e.matmul(ps[:],
                                        wb[:, ic * DM + 128 * oc: ic * DM + 128 * (oc + 1)],
                                        xb[:, ic * TOK + 512 * tt: ic * TOK + 512 * (tt + 1)],
                                        start=(ic == 0), stop=(ic == NDM - 1))
                    return last
                waits = [(ve, ve_wbin), (ldg, ldg_w)]
                if k >= 2:
                    waits.append((ve, resid_ve[k - 2]))
                pr.add("tensor", mmfn, waits=waits, inc=mm)
                sl = np.s_[:, oc * TOK + 512 * tt: oc * TOK + 512 * (tt + 1)]
                pr.add("vector", lambda e, sl=sl, oc=oc, ps=ps: e.scalar_tensor_tensor(
                    xs[sl], xs[sl], bias_sb[:, oc:oc + 1], ps[:], ALU.add, ALU.add),
                       waits=[(mm, mm.n), (ld, ld_all)], inc=ve)
                resid_ve.append(ve.n)
                k += 1
        ve_resid = ve.n

        stat_mm = []
        for c in range(NDM):
            pr.add("vector", lambda e, c=c: e.tensor_copy(ybf[:],
                                                          xs[:, c * TOK:(c + 1) * TOK]),
                   waits=[(ve, ve_resid)] + ([(mm, stat_mm[-1])] if c else
                                             [(mm, mm_albc)]),
                   inc=ve)
            pr.add("scalar", lambda e: e.activation(ysq[:], ybf[:], AF.Square),
                   waits=[(ve, ve.n)], inc=se)

            def statfn(e, c=c):
                last = None
                for j in range(2):
                    last = e.matmul(ps_big[0:1, 512 * j:512 * (j + 1)], one_bf[:],
                                    ybf[:, 512 * j:512 * (j + 1)],
                                    start=(c == 0), stop=(c == NDM - 1))
                for j in range(2):
                    last = e.matmul(ps_big[0:1, TOK + 512 * j:TOK + 512 * (j + 1)],
                                    one_bf[:], ysq[:, 512 * j:512 * (j + 1)],
                                    start=(c == 0), stop=(c == NDM - 1))
                return last
            pr.add("tensor", statfn, waits=[(se, se.n), (ve, ve.n)], inc=mm)
            stat_mm.append(mm.n)

        pr.add("vector", lambda e: e.tensor_scalar_mul(stat_m[:], ps_big[0:1, 0:TOK],
                                                       1.0 / DM),
               waits=[(mm, mm.n)], inc=ve)
        pr.add("vector", lambda e: e.tensor_scalar_mul(stat_v[:],
                                                       ps_big[0:1, TOK:2 * TOK],
                                                       1.0 / DM), waits=[], inc=ve)
        pr.add("vector", lambda e: e.tensor_mul(mrstd[:], stat_m[:], stat_m[:]),
               waits=[], inc=ve)
        pr.add("vector", lambda e: e.tensor_sub(stat_v[:], stat_v[:], mrstd[:]),
               waits=[], inc=ve)
        pr.add("scalar", lambda e: e.activation(
            rstd[:], stat_v[:], AF.Sqrt,
            bias=sc_sb[0:1, SCALARS["eps"]:SCALARS["eps"] + 1]),
               waits=[(ve, ve.n), (ld, ld_all)], inc=se)
        pr.add("vector", lambda e: e.reciprocal(rstd[:], rstd[:]), waits=[(se, se.n)],
               inc=ve)
        pr.add("vector", lambda e: e.tensor_mul(mrstd[:], stat_m[:], rstd[:]),
               waits=[], inc=ve)
        ve_stats = ve.n

        def bcfn(e):
            last = None
            for j in range(2):
                last = e.matmul(ps_big[:, 512 * j:512 * (j + 1)], one1f[:],
                                rstd[0:1, 512 * j:512 * (j + 1)], start=True, stop=True)
            for j in range(2):
                last = e.matmul(ps_big[:, TOK + 512 * j:TOK + 512 * (j + 1)], one1f[:],
                                mrstd[0:1, 512 * j:512 * (j + 1)], start=True, stop=True)
            return last
        pr.add("tensor", bcfn, waits=[(ve, ve_stats)], inc=mm)
        mm_bc = mm.n

        outc = outT.rearrange("(c p) t -> c p t", p=128)
        for c in range(NDM):
            res = res2[c % 2]
            sq = stq2[c % 2]
            waits = [(mm, mm_bc)]
            if c >= 2:
                waits.append((sq, sq.n))
            pr.add("vector", lambda e, c=c, res=res: e.tensor_mul(
                res[:], xs[:, c * TOK:(c + 1) * TOK], ps_big[:, 0:TOK]),
                   waits=waits, inc=ve)
            pr.add("vector", lambda e, res=res: e.tensor_sub(res[:], res[:],
                                                             ps_big[:, TOK:2 * TOK]),
                   waits=[], inc=ve)
            pr.dma("sync", outc[c], res[:], waits=[(ve, ve.n)], inc=sq)
        pr.wait("sync", (stq2[0], stq2[0].n), (stq2[1], stq2[1].n))
        ctxP.close()

    phase_mlp1()
    pr.build()
    return nc


def full_in_maps(inputs, consts):
    x0 = np.asarray(inputs["L0_attn_hidden_states"], np.float32)
    x0T = [np.ascontiguousarray(x0[b].T) for b in range(2)]
    a0maps = attn0_in_maps(x0, inputs, consts)
    dummy = [np.zeros((DM, TOK), np.float32)] * NCORES
    m0maps = mlp0_in_maps(dummy, dummy, inputs, consts)
    a1maps = attn1a_in_maps([np.zeros((DM, SEQ), np.float32)] * 2, inputs, consts)
    D = np.asarray(inputs["L1_attn_D"], np.float32)
    nw = np.asarray(inputs["L1_attn_norm_weight"], np.float32)
    W_out = np.asarray(inputs["L1_attn_W_out"], np.float32)
    wT = np.ascontiguousarray(np.asarray(inputs["L1_ffn_weight"], np.float32).T)
    bias = np.asarray(inputs["L1_ffn_bias"], np.float32)
    biasc = np.ascontiguousarray(bias.reshape(NDM, 128).T).astype(np.float32)
    maps = []
    for cid in range(NCORES):
        b, q = cid // 4, cid % 4
        m = dict(a0maps[cid])
        m0 = dict(m0maps[cid])
        del m0["oT"]
        m0["x0Ts"] = np.ascontiguousarray(x0T[b][:, q * TOK:(q + 1) * TOK])
        del m0["x0T"]
        m.update(m0)
        a1 = dict(a1maps[cid])
        del a1["x2Th"]
        del a1["triu"]
        m.update(a1)
        m["d_bc"] = np.tile(D[None, :], (128, 1)).astype(np.float32)
        m["nw_bc"] = np.tile(nw[None, :], (128, 1)).astype(ml_dtypes.bfloat16)
        m["wout"] = W_out
        m["wT"] = wT
        m["biasc"] = biasc
        cm = np.zeros((16,), np.float32)
        cm[CM_WLO] = 1.0 if b == 0 else 0.0
        cm[CM_WHI] = 1.0 - cm[CM_WLO]
        if q >= 1:
            cm[CM_HSEL + (q - 1)] = 1.0
        cm[CM_S1] = 1.0 if 1 < q else 0.0
        cm[CM_C1] = 1.0 - cm[CM_S1]
        cm[CM_S2] = 1.0 if 2 < q else 0.0
        cm[CM_C2] = 1.0 - cm[CM_S2]
        cm[CM_M0] = 1.0 if 0 < q else 0.0
        cm[CM_M1] = 1.0 if 1 < q else 0.0
        cm[CM_M2] = 1.0 if 2 < q else 0.0
        m["cmask"] = np.tile(cm[None, :], (128, 1)).astype(np.float32)
        maps.append(m)
    return maps


def kernel(**inputs):
    consts = host_consts()
    nc = build_full()
    maps = full_in_maps(inputs, consts)
    res = run_bass_kernel_spmd(nc, maps, core_ids=list(range(NCORES)))
    out = np.zeros((B, SEQ, DM), np.float32)
    for cid in range(NCORES):
        b, q = cid // 4, cid % 4
        out[b, TOK * q:TOK * (q + 1), :] = np.asarray(res.results[cid]["outT"],
                                                      np.float32).T
    return out
